# version 14
# baseline (speedup 1.0000x reference)
import os
import numpy as np
from contextlib import ExitStack

import concourse.bass as bass
import concourse.mybir as mybir
from concourse.bass_utils import run_bass_kernel_spmd
from concourse.alu_op_type import AluOpType as OP

F32 = mybir.dt.float32
BF16 = mybir.dt.bfloat16
ACT = mybir.ActivationFunctionType
AX = mybir.AxisListType

_DBG = os.environ.get("KDBG") == "1"

NCORE = 8
S, DM, H, DH, LM, SEG = 16384, 1024, 16, 64, 64, 256
E8, R4, FF, KSZ = 8, 4, 4096, 65
SLOC = S // NCORE
NT = SLOC // 128
NTP = NT + 2
TPAD = NTP * 128
SCALE = float(DH) ** -0.25
EPS = 1e-5
TC = 512
NTC = SLOC // TC


class Sched:
    ENGS = ("tensor", "vector", "scalar", "gpsimd", "sync")
    BAR_SEMS = ("T", "V", "Sc", "LDG", "LDS", "STQ")

    def __init__(self):
        self.steps = {e: [] for e in self.ENGS}
        self.cnt = {}

    def sem(self, name):
        self.cnt.setdefault(name, 0)

    def raw(self, eng, fn):
        self.steps[eng].append(fn)

    def inc(self, name, k):
        self.cnt[name] += k
        return self.cnt[name]

    def wait(self, eng, name, val=None):
        v = self.cnt[name] if val is None else val
        if v <= 0:
            return
        self.steps[eng].append(("W", name, v))

    def barrier(self):
        snap = [(s, self.cnt[s]) for s in self.BAR_SEMS if self.cnt[s] > 0]
        for e in self.ENGS:
            for s, v in snap:
                self.steps[e].append(("W", s, v))


class Arena:

    def __init__(self, name):
        self.name = name
        self.regions = []

    def add(self, name, size, p0, p1):
        self.regions.append((name, size, p0, p1))

    def pack(self):
        def run(order):
            placed = {}
            for name, size, p0, p1 in order:
                pad = (size + 15) // 16 * 16
                spans = sorted((v for v in placed.values()
                                if not (v[3] < p0 or v[2] > p1)), key=lambda v: v[0])
                off = 0
                for (o2, s2, _, _, _) in spans:
                    if off + pad <= o2:
                        break
                    off = max(off, o2 + s2)
                placed[name] = (off, pad, p0, p1, size)
            tot = max((o + s for o, s, _, _, _ in placed.values()), default=0)
            return placed, tot
        best = None
        for key in (lambda r: (-r[1],), lambda r: (r[2], -r[1]),
                    lambda r: (-(r[3] - r[2]), -r[1]), lambda r: (r[2], r[3], -r[1])):
            placed, tot = run(sorted(self.regions, key=key))
            if best is None or tot < best[1]:
                best = (placed, tot)
        rng = np.random.default_rng(12345)
        base = sorted(self.regions, key=lambda r: (-r[1],))
        for _ in range(400):
            order = list(base)
            rng.shuffle(order)
            placed, tot = run(order)
            if tot < best[1]:
                best = (placed, tot)
        self.placed, self.total = best
        return self.total


def ts(i, n):
    return slice(i * n, (i + 1) * n)


class _StopBuild(Exception):
    pass


def build_program(stop_phase=None):
    nc = bass.Bass("TRN2", target_bir_lowering=False)
    P = Sched()
    SEMS = ("T", "V", "Sc", "LDG", "LDS", "STQ", "CC", "CCD",
            "LDW0", "LDW1", "LDW2", "LDW3")
    for s_ in SEMS:
        P.sem(s_)

    def din(name, shape, dt=F32):
        return nc.dram_tensor(name, list(shape), dt, kind="ExternalInput").ap()

    xT_in = din("xT", (DM, TPAD))
    xtm_in = din("xtm", (SLOC, DM))
    wq_in = din("wq", (DM, DM))
    wk_in = din("wk", (DM, DM))
    wv_in = din("wv", (DM, DM))
    wiT_in = din("wiT", (128, 32 * 1024))
    woT_in = din("woT", (FF, DM))
    gateT_in = din("gateT", (DM, E8))
    acatT_in = din("acatT", (DM, 32))
    bcatT_in = din("bcatT", (32, FF))
    tconv_in = din("tconv", (128, H * 3 * 64))
    eye_in = din("eye", (128, 128))
    izf_in = din("izf", (128, 16 * H))
    izff_in = din("izff", (128, 256))
    ipair_in = din("ipair", (16, 1024))
    im4_in = din("im4", (E8, 32))
    i7_in = din("i7", (128, 1024))
    i15_in = din("i15", (128, 1024))
    i13_in = din("i13", (128, 1024))
    bvb_in = din("bvb", (128, DM))
    bvlo_in = din("bvlo", (128, DM))
    bvhi_in = din("bvhi", (128, DM))
    bqc_in = din("bqc", (128, 8))
    bkc_in = din("bkc", (128, 8))
    wibc_in = din("wibc", (128, 32))
    gb16_in = din("gb16", (128, 128))
    p216_in = din("p216", (128, 128))
    wob_in = din("wob", (1, DM))
    y_out = nc.dram_tensor("y", [SLOC, DM], F32, kind="ExternalOutput").ap()

    x1b = nc.dram_tensor("x1b", [SLOC, DM], F32).ap()
    cc1i = nc.dram_tensor("cc1i", [128, 64], F32)
    cc1o = nc.dram_tensor("cc1o", [NCORE * 128, 64], F32)
    cc2i = nc.dram_tensor("cc2i", [128, 520], F32)
    cc2o = nc.dram_tensor("cc2o", [NCORE * 128, 520], F32)

    if _DBG:
        dbg = {nm: nc.dram_tensor("dbg_" + nm, list(shp), F32, kind="ExternalOutput").ap()
               for nm, shp in (("ql", (128, 512)), ("kl", (128, 512)),
                               ("vti", (64, 1024)), ("k3v", (128, 512)),
                               ("w2", (128, 512)), ("sumb", (128, 520)),
                               ("blob", (128, 520)))}

    BF = Arena("bf")
    FP = Arena("fp")
    for nm, sz, p0, p1 in (
        ("ln1", 8 * TPAD, 0, 4), ("ln2", 8 * SLOC, 11, 15),
        ("q", 8 * SLOC, 1, 7), ("k", 8 * SLOC, 2, 6), ("v", NTP * DM, 4, 9),
        ("inter", 32 * TC, 13, 15), ("vA", 17 * DM, 9, 10),
        ("e1", 8 * SLOC, 7, 10), ("woT", 32 * DM, 10, 15),
        ("w3", 8 * DM, 0, 3), ("wvh", 8 * 512, 4, 4),
        ("xbf", 512, 0, 0), ("xsq", 512, 0, 0),
        ("e3t0", 1024, 4.6, 4.6), ("e3t1", 1024, 4.6, 4.6),
        ("wib0", 1024, 13, 15), ("wib1", 1024, 13, 15),
        ("wib2", 1024, 13, 15), ("wib3", 1024, 13, 15),
        ("z", SLOC, 12, 15), ("mfm", SLOC, 12, 12), ("u", SLOC, 12, 12),
        ("ql", 512, 3, 9), ("kl", 512, 3, 7), ("w2", 512, 9, 10),
        ("lf_all", 512, 0, 3), ("tconv", H * 3 * 64, 8, 10),
        ("izbf", 16 * H, 0, 8), ("im4", 32, 12, 12), ("gateT", 8 * E8, 11, 12),
        ("acatT", 8 * 32, 11, 12), ("bcatT", FF, 12, 15), ("wob", DM, 12, 15),
        ("trash", DM, 10, 10), ("ones1b", 128, 0, 15), ("o128b", 16, 0, 15),
        ("zrow", 512, 0, 15),
    ):
        BF.add(nm, sz, p0, p1)
    for nm, sz, p0, p1 in (
        ("xch0", 2048, 0, 0), ("xch1", 2048, 0, 0), ("rows", 512, 0, 0),
        ("tmp384", 256, 0, 0), ("lf_loc", 64, 0, 3),
        ("e2", 1024, 5.0, 5.25), ("k2n", 1024, 5.0, 5.45),
        ("k2nt", 1024, 5.4, 5.9),
        ("vi", 1024, 5.45, 5.95), ("vti", 1024, 5.45, 5.95), ("kv", 1024, 5.5, 5.9),
        ("kvt", 1024, 5.5, 5.9), ("t1", 1024, 5.5, 5.9), ("t3", 1024, 5.5, 5.9),
        ("t5", 1024, 5.5, 5.9), ("vtd", 512, 5.9, 9),
        ("i7", 1024, 4, 5.9), ("i15", 1024, 4, 5.9), ("i13", 1024, 4, 5.9),
        ("rmcol", 16, 5.0, 5.9), ("rs8", 16, 5.0, 5.25), ("izff", 256, 0, 5.3),
        ("eye", 128, 0, 15), ("ipair", 1024, 6.5, 8),
        ("rz1", SLOC, 8, 8), ("blob", 528, 4.6, 9),
        ("g2a", 528, 4.6, 9), ("g2b", 528, 4.6, 9), ("sumb", 528, 4.6, 9), ("rz3", 8, 9, 9),
        ("k3v", 512, 9, 9), ("bvb", DM, 4, 4), ("bvlo", DM, 4, 4),
        ("bvhi", DM, 4, 4), ("bqc", 8, 0, 3), ("bkc", 8, 0, 3),
        ("wibc", 32, 12, 15), ("gb16", 128, 11, 12), ("p216", 128, 11, 12),
        ("xtm0", DM, 10, 10), ("xtm1", DM, 10, 10), ("ctmp", 512, 10, 10),
        ("x1c0", DM, 10, 10), ("x1c1", DM, 10, 10),
        ("s1", 16, 10, 11), ("s2", 16, 10, 11), ("st16", 256, 11, 11),
        ("r2rows", 2 * SLOC, 11, 11), ("r2mu", 256, 11, 11),
        ("r2sig", 256, 11, 11),
        ("x1r0", DM, 11, 15), ("x1r1", DM, 11, 15),
        ("x1r2", DM, 11, 15), ("x1r3", DM, 11, 15),
        ("tmp512", 512, 7, 11), ("glog", SLOC, 12, 12),
        ("lg", 128, 12, 12), ("eqw", 128, 12, 12), ("mask", 128, 12, 12),
        ("mx", 16, 11, 12),
        ("out0", DM, 13, 15), ("out1", DM, 13, 15), ("ones1f", 128, 0, 15),
        ("consts", 4, 0, 15),
    ):
        FP.add(nm, sz, p0, p1)
    nbf = BF.pack()
    nfp = FP.pack()
    assert 2 * nbf + 4 * nfp <= 208 * 1024, (nbf, nfp, 2 * nbf + 4 * nfp)

    es = ExitStack()
    BFA = es.enter_context(nc.sbuf_tensor("BFA", [128, nbf], BF16))
    FPA = es.enter_context(nc.sbuf_tensor("FPA", [128, nfp], F32))

    def R(name, parts=128):
        arena, t = (BF, BFA) if name in BF.placed else (FP, FPA)
        off, _, _, _, size = arena.placed[name]
        return t[0:parts, off:off + size]

    def Rs(name, lo, n, parts=(0, 128)):
        arena, t = (BF, BFA) if name in BF.placed else (FP, FPA)
        off = arena.placed[name][0]
        return t[parts[0]:parts[1], off + lo:off + lo + n]

    pb = [es.enter_context(nc.psum_tensor(f"pb{i}", [128, 512], F32)) for i in range(8)]
    sems = {}
    for s_ in SEMS:
        sems[s_] = es.enter_context(nc.semaphore(s_))
    SM = sems
    nc.const_aps.register(F32, EPS, Rs("consts", 0, 1))
    nc.const_aps.register(F32, -0.5, Rs("consts", 1, 1))

    bank_guard = {}

    def guard_bank(*bs):
        for b in bs:
            g = bank_guard.get(b)
            if g:
                P.wait("tensor", g[0], g[1])

    def consume_bank(*bs, sem="V"):
        for b in bs:
            bank_guard[b] = (sem, P.cnt[sem])

    def t_op(fn):
        P.raw("tensor", lambda e, fn=fn: fn(e).then_inc(SM["T"], 1))
        return P.inc("T", 1)

    def v_op(fn):
        P.wait("vector", "V", P.cnt["V"])
        P.raw("vector", lambda e, fn=fn: fn(e).then_inc(SM["V"], 1))
        return P.inc("V", 1)

    def s_op(fn):
        P.wait("scalar", "Sc", P.cnt["Sc"])
        P.raw("scalar", lambda e, fn=fn: fn(e).then_inc(SM["Sc"], 1))
        return P.inc("Sc", 1)

    def g_dma(fn):
        P.wait("gpsimd", "LDG", P.cnt["LDG"])
        P.wait("gpsimd", "CCD", P.cnt["CCD"])
        P.raw("gpsimd", fn)
        return P.inc("LDG", 16)

    def c_dma(fn):
        P.wait("gpsimd", "LDG", P.cnt["LDG"])
        P.wait("gpsimd", "CCD", P.cnt["CCD"])
        P.raw("gpsimd", fn)
        return P.inc("CCD", 16)

    def sy_dma(fn):
        P.wait("sync", "LDS", P.cnt["LDS"])
        P.raw("sync", fn)
        return P.inc("LDS", 16)

    def sc_dma(fn):
        P.wait("scalar", "STQ", P.cnt["STQ"])
        P.raw("scalar", fn)
        return P.inc("STQ", 16)

    def gload(dst, src):
        return g_dma(lambda e, d=dst, s=src: e.dma_start(out=d, in_=s).then_inc(SM["LDG"], 16))

    if _DBG:
        def dump(nm, ap_):
            for s_ in ("T", "V", "Sc", "LDS", "STQ"):
                P.wait("gpsimd", s_, P.cnt[s_])
            g_dma(lambda e, d=dbg[nm], s=ap_: e.dma_start(out=d, in_=s).then_inc(SM["LDG"], 16))
            for e_ in ("tensor", "vector", "scalar", "sync"):
                P.wait(e_, "LDG", P.cnt["LDG"])
    else:
        def dump(nm, ap_):
            pass

    def ckpt(n):
        if stop_phase is not None and n >= stop_phase:
            raise _StopBuild()

    def pslot(h):
        return ((h % 2) * 64, (h % 2) * 64 + 64, (h // 2) * 64)

    try:
        for nm, src_ in (("eye", eye_in), ("izff", izff_in),
                         ("bqc", bqc_in), ("bkc", bkc_in)):
            sy_dma(lambda e, d=R(nm, {"ipair": 16}.get(nm, 128)), s=src_:
                   e.dma_start(out=d, in_=s).then_inc(SM["LDS"], 16))
        ld_const = P.cnt["LDS"]

        ld_wq = gload(R("w3").rearrange("p (dc d) -> p dc d", dc=8),
                      wq_in.rearrange("(dc p) d -> p dc d", p=128))
        gload(R("izbf"), izf_in)

        def _memsets(e):
            e.memset(R("ones1b", 1), 1.0).then_inc(SM["V"], 1)
            e.memset(Rs("o128b", 0, 1), 1.0).then_inc(SM["V"], 1)
            e.memset(R("ones1f", 1), 1.0).then_inc(SM["V"], 1)
            e.memset(Rs("consts", 0, 1), EPS).then_inc(SM["V"], 1)
            e.memset(Rs("consts", 1, 1), -0.5).then_inc(SM["V"], 1)
            e.memset(R("zrow"), 0.0).then_inc(SM["V"], 1)
        P.raw("vector", _memsets)
        P.inc("V", 6)
        mset_v = P.cnt["V"]

        NCH = 9
        CW = TPAD // NCH
        war_x, war_xb = {}, {}
        for c in range(NCH):
            xb = "xch%d" % (c % 2)
            if c >= 2:
                P.wait("sync", "V", war_x[c % 2])
            sy_dma(lambda e, b=R(xb), c=c: e.dma_start(
                out=b.rearrange("p (dc t) -> p dc t", dc=8),
                in_=xT_in.rearrange("(dc p) t -> p dc t", p=128)[:, :, ts(c, CW)],
            ).then_inc(SM["LDS"], 16))
            P.wait("tensor", "V", mset_v)
            guard_bank(0, 2)
            for dc in range(8):
                sl = (dc % 2) * CW
                if c + dc >= 2:
                    P.wait("scalar", "T", war_xb[dc % 2])
                P.wait("scalar", "LDS", P.cnt["LDS"])
                s_op(lambda e, b=xb, dc=dc, sl=sl: e.copy(Rs("xbf", sl, CW), Rs(b, dc * CW, CW)))
                s_op(lambda e, b=xb, dc=dc, sl=sl: e.activation(Rs("xsq", sl, CW), Rs(b, dc * CW, CW), ACT.Square))
                P.wait("tensor", "Sc", P.cnt["Sc"])
                t_op(lambda e, sl=sl, st=(dc == 0), sp=(dc == 7): e.matmul(
                    pb[0][0:1, 0:CW], Rs("o128b", 0, 1), Rs("xbf", sl, CW), start=st, stop=sp))
                t_op(lambda e, sl=sl, st=(dc == 0), sp=(dc == 7): e.matmul(
                    pb[2][0:1, 0:CW], Rs("o128b", 0, 1), Rs("xsq", sl, CW), start=st, stop=sp))
                war_xb[dc % 2] = P.cnt["T"]
            P.wait("vector", "T", P.cnt["T"])
            v_op(lambda e: e.tensor_scalar(Rs("rows", 0, CW, (0, 1)), pb[0][0:1, 0:CW], 1.0 / DM, None, OP.mult))
            v_op(lambda e: e.tensor_scalar(Rs("rows", CW, CW, (0, 1)), pb[2][0:1, 0:CW], 1.0 / DM, None, OP.mult))
            v_op(lambda e: e.tensor_tensor(Rs("tmp384", 0, CW, (0, 1)), Rs("rows", 0, CW, (0, 1)), Rs("rows", 0, CW, (0, 1)), OP.mult))
            v_op(lambda e: e.tensor_tensor(Rs("rows", CW, CW, (0, 1)), Rs("rows", CW, CW, (0, 1)), Rs("tmp384", 0, CW, (0, 1)), OP.subtract))
            consume_bank(0, 2)
            P.wait("scalar", "V", P.cnt["V"])
            s_op(lambda e: e.activation(Rs("rows", CW, CW, (0, 1)), Rs("rows", CW, CW, (0, 1)), ACT.Ln, bias=EPS))
            s_op(lambda e: e.activation(Rs("rows", CW, CW, (0, 1)), Rs("rows", CW, CW, (0, 1)), ACT.Exp, scale=-0.5))
            P.wait("tensor", "Sc", P.cnt["Sc"])
            P.wait("tensor", "LDS", ld_const)
            guard_bank(4, 6)
            t_op(lambda e: e.matmul(pb[4][:, 0:CW], R("ones1f", 1), Rs("rows", 0, CW, (0, 1)), start=True, stop=True))
            t_op(lambda e: e.matmul(pb[6][:, 0:CW], R("ones1f", 1), Rs("rows", CW, CW, (0, 1)), start=True, stop=True))
            P.wait("vector", "T", P.cnt["T"])
            for dc in range(8):
                v_op(lambda e, b=xb, dc=dc: e.tensor_tensor(Rs("tmp384", 0, CW), Rs(b, dc * CW, CW), pb[4][:, 0:CW], OP.subtract))
                v_op(lambda e, dc=dc, c=c: e.tensor_tensor(
                    Rs("ln1", dc * TPAD + c * CW, CW), Rs("tmp384", 0, CW), pb[6][:, 0:CW], OP.mult))
            consume_bank(4, 6)
            war_x[c % 2] = P.cnt["V"]

        for dc in range(8):
            v_op(lambda e, dc=dc: e.reduce_sum(
                Rs("lf_loc", dc * 8, 8),
                Rs("ln1", dc * TPAD + 128, SLOC).rearrange("p (g t) -> p g t", g=8),
                axis=AX.X))
        v_op(lambda e: e.tensor_scalar(R("lf_loc"), R("lf_loc"), 1.0 / SEG, None, OP.mult))
        P.wait("gpsimd", "V", P.cnt["V"])
        c_dma(lambda e: e.dma_start(out=cc1i.ap(), in_=R("lf_loc")).then_inc(SM["CCD"], 16))
        P.wait("gpsimd", "CCD", P.cnt["CCD"])

        def _cc1(e):
            e.collective_compute("AllGather", OP.bypass, replica_groups=[list(range(NCORE))],
                                 ins=[cc1i.ap().opt()], outs=[cc1o.ap().opt()]).then_inc(SM["CC"], 1)
        P.raw("gpsimd", _cc1)
        P.inc("CC", 1)
        P.wait("gpsimd", "CC", P.cnt["CC"])
        c_dma(lambda e: e.dma_start(
            out=R("lf_all").rearrange("p (s c) -> p s c", s=8),
            in_=cc1o.ap().rearrange("(s p) c -> p s c", p=128)).then_inc(SM["CCD"], 16))
        cc1_done = P.cnt["CCD"]

        P.barrier()
        ckpt(1)

        def proj_qk(wld, out_nm, bias_nm):
            P.wait("tensor", "LDG", wld)
            for pp in range(8):
                for n in range(4):
                    b = [0, 2][(pp * 4 + n) % 2]
                    guard_bank(b)
                    for dc in range(8):
                        t_op(lambda e, b=b, dc=dc, pp=pp, n=n, st=(dc == 0), sp=(dc == 7): e.matmul(
                            pb[b][:],
                            Rs("w3", dc * 1024 + pp * 128, 128),
                            Rs("ln1", dc * TPAD + 128 + n * 512, 512),
                            start=st, stop=sp))
                    P.wait("vector", "T", P.cnt["T"])
                    v_op(lambda e, b=b, pp=pp, n=n, o=out_nm, bs=bias_nm: e.tensor_scalar(
                        Rs(o, pp * SLOC + n * 512, 512),
                        pb[b][:], Rs(bs, pp, 1).opt(), SCALE, OP.add, OP.mult))
                    consume_bank(b)

        proj_qk(ld_wq, "q", "bqc")
        P.wait("gpsimd", "T", P.cnt["T"])
        ld_wk = gload(R("w3").rearrange("p (dc d) -> p dc d", dc=8),
                      wk_in.rearrange("(dc p) d -> p dc d", p=128))
        proj_qk(ld_wk, "k", "bkc")

        P.barrier()
        ckpt(3)

        P.wait("tensor", "CCD", cc1_done)

        def proj_l(out_nm, bias_nm):
            for pp in range(8):
                b = [4, 6][pp % 2]
                guard_bank(b)
                for dc in range(8):
                    t_op(lambda e, b=b, dc=dc, pp=pp, st=(dc == 0), sp=(dc == 7): e.matmul(
                        pb[b][:, 0:64],
                        Rs("w3", dc * 1024 + pp * 128, 128),
                        R("lf_all").rearrange("p (s c g) -> p s c g", s=8, c=8)[:, :, dc, :],
                        start=st, stop=sp))
                P.wait("vector", "T", P.cnt["T"])
                v_op(lambda e, b=b, pp=pp, o=out_nm, bs=bias_nm: e.tensor_scalar(
                    Rs(o, pp * 64, 64), pb[b][:, 0:64], Rs(bs, pp, 1).opt(), SCALE, OP.add, OP.mult))
                consume_bank(b)

        proj_l("kl", "bkc")
        P.wait("gpsimd", "T", P.cnt["T"])
        ld_wq2 = gload(R("w3").rearrange("p (dc d) -> p dc d", dc=8),
                       wq_in.rearrange("(dc p) d -> p dc d", p=128))
        P.wait("tensor", "LDG", ld_wq2)
        proj_l("ql", "bqc")
        dump("kl", R("kl"))
        dump("ql", R("ql"))

        P.barrier()
        ckpt(4)

        for nm, src_ in (("i7", i7_in), ("i15", i15_in), ("i13", i13_in),
                         ("bvb", bvb_in), ("bvlo", bvlo_in), ("bvhi", bvhi_in)):
            sy_dma(lambda e, d=R(nm), s=src_: e.dma_start(out=d, in_=s).then_inc(SM["LDS"], 16))
        ld_i = P.cnt["LDS"]
        P.wait("vector", "LDS", ld_i)
        for hf in range(2):
            P.wait("gpsimd", "T", P.cnt["T"])
            ld_wvh = gload(R("wvh").rearrange("p (dc d) -> p dc d", dc=8),
                           wv_in.rearrange("(dc p) d -> p dc d", p=128)[:, :, ts(hf, 512)])
            P.wait("tensor", "LDG", ld_wvh)
            for g in range(NTP):
                b = [0, 2][g % 2]
                guard_bank(b)
                for dc in range(8):
                    t_op(lambda e, hf=hf, b=b, g=g, dc=dc, st=(dc == 0), sp=(dc == 7): e.matmul(
                        pb[b][:],
                        Rs("ln1", dc * TPAD + g * 128, 128),
                        Rs("wvh", dc * 512, 512),
                        start=st, stop=sp))
                P.wait("vector", "T", P.cnt["T"])
                bv = "bvlo" if g == 0 else ("bvhi" if g == NTP - 1 else "bvb")
                v_op(lambda e, b=b, g=g, hf=hf, bv=bv: e.tensor_tensor(
                    Rs("v", g * DM + hf * 512, 512), pb[b][:], Rs(bv, hf * 512, 512), OP.add))
                consume_bank(b)

        P.barrier()
        ckpt(5)

        for sc in range(NT):
            enm = "e3t%d" % (sc % 2)
            guard_bank(0, 1)
            for h in range(H):
                r0, r1, c0 = pslot(h)
                bk = h % 2
                t_op(lambda e, bk=bk, h=h, sc=sc, r0=r0, r1=r1: e.matmul(
                    pb[bk][:, ts(h // 2, 64)],
                    Rs("k", (h // 2) * SLOC + sc * 128, 128, (r0, r1)),
                    Rs("ql", (h // 2) * 64, 64, (r0, r1)),
                    start=True, stop=True, tile_position=(r0, 0)))
            P.wait("scalar", "T", P.cnt["T"])
            s_op(lambda e, enm=enm: e.activation(
                Rs(enm, 0, 1024).rearrange("p (s two c) -> p s c two", two=2, c=64)[:, :, :, 0],
                pb[0][:], ACT.Exp))
            s_op(lambda e, enm=enm: e.activation(
                Rs(enm, 0, 1024).rearrange("p (s two c) -> p s c two", two=2, c=64)[:, :, :, 1],
                pb[1][:], ACT.Exp))
            consume_bank(0, 1, sem="Sc")
            P.wait("tensor", "Sc", P.cnt["Sc"])
            guard_bank(2, 4)
            for h in range(H):
                r0, r1, c0 = pslot(h)
                t_op(lambda e, enm=enm, h=h, r0=r0, c0=c0, sc=sc: e.matmul(
                    pb[4][r0:r0 + 64, c0:c0 + 64],
                    Rs(enm, (h % 8) * 64 + (h // 8) * 512, 64),
                    Rs("v", (sc + 1) * DM + h * 64, 64),
                    start=True, stop=True, tile_position=(0, r0)))
                t_op(lambda e, enm=enm, h=h, r0=r0: e.matmul(
                    pb[2][r0:r0 + 64, (h // 2):(h // 2) + 1],
                    Rs(enm, (h % 8) * 64 + (h // 8) * 512, 64),
                    R("o128b", 128)[:, 0:1],
                    start=True, stop=True, tile_position=(0, r0)))
            P.wait("vector", "T", P.cnt["T"])
            if sc == 0:
                v_op(lambda e: e.tensor_copy(Rs("blob", 0, 512), pb[4][:]))
                v_op(lambda e: e.tensor_copy(Rs("blob", 512, 8), pb[2][:, 0:8]))
            else:
                v_op(lambda e: e.tensor_tensor(Rs("blob", 0, 512), Rs("blob", 0, 512), pb[4][:], OP.add))
                v_op(lambda e: e.tensor_tensor(Rs("blob", 512, 8), Rs("blob", 512, 8), pb[2][:, 0:8], OP.add))
            consume_bank(2, 4)
        dump("blob", Rs("blob", 0, 520))
        P.wait("gpsimd", "V", P.cnt["V"])
        c_dma(lambda e: e.dma_start(out=cc2i.ap(), in_=Rs("blob", 0, 520)).then_inc(SM["CCD"], 16))
        P.wait("gpsimd", "CCD", P.cnt["CCD"])

        def _cc2(e):
            e.collective_compute("AllGather", OP.bypass, replica_groups=[list(range(NCORE))],
                                 ins=[cc2i.ap().opt()], outs=[cc2o.ap().opt()]).then_inc(SM["CC"], 1)
        P.raw("gpsimd", _cc2)
        P.inc("CC", 1)
        cc2_done = P.cnt["CC"]
        ckpt(7)

        P.barrier()
        ckpt(6)

        guard_bank(0, 1)
        for h in range(H):
            r0, r1, c0 = pslot(h)
            bk = h % 2
            t_op(lambda e, bk=bk, r0=r0, r1=r1, h=h: e.matmul(
                pb[bk][0:64, ts(h // 2, 64)],
                Rs("ql", (h // 2) * 64, 64, (r0, r1)),
                Rs("kl", (h // 2) * 64, 64, (r0, r1)),
                start=True, stop=True, tile_position=(r0, 0)))
        ckpt(5.05)
        P.wait("scalar", "T", P.cnt["T"])
        s_op(lambda e: e.activation(
            Rs("e2", 0, 1024, (0, 64)).rearrange("p (s two c) -> p s c two", two=2, c=64)[:, :, :, 0],
            pb[0][0:64, :], ACT.Exp))
        s_op(lambda e: e.activation(
            Rs("e2", 0, 1024, (0, 64)).rearrange("p (s two c) -> p s c two", two=2, c=64)[:, :, :, 1],
            pb[1][0:64, :], ACT.Exp))
        consume_bank(0, 1, sem="Sc")
        P.wait("vector", "Sc", P.cnt["Sc"])
        v_op(lambda e: e.reduce_sum(Rs("rs8", 0, 16, (0, 64)),
                                    Rs("e2", 0, 1024, (0, 64)).rearrange("p (a m) -> p a m", a=16), axis=AX.X))
        v_op(lambda e: e.reciprocal(Rs("rs8", 0, 16, (0, 64)), Rs("rs8", 0, 16, (0, 64))))
        for h in range(H):
            v_op(lambda e, h=h: e.tensor_scalar(Rs("k2n", h * 64, 64, (0, 64)), Rs("e2", h * 64, 64, (0, 64)),
                                                Rs("rs8", h, 1, (0, 64)).opt(), None, OP.mult))
        ckpt(5.1)
        P.wait("tensor", "V", P.cnt["V"])
        guard_bank(2)
        for h in range(H):
            t_op(lambda e, h=h: e.matmul(
                pb[2][0:16, 0:64],
                Rs("izff", h * 16, 16, (0, 64)),
                Rs("k2n", h * 64, 64, (0, 64)),
                start=(h == 0), stop=(h == H - 1)))
        P.wait("vector", "T", P.cnt["T"])
        v_op(lambda e: e.reduce_max(Rs("rmcol", 0, 1, (0, 16)), pb[2][0:16, 0:64], axis=AX.X))
        consume_bank(2)
        P.wait("tensor", "V", P.cnt["V"])
        guard_bank(4)
        t_op(lambda e: e.transpose(pb[4][0:1, 0:16], Rs("rmcol", 0, 1, (0, 16)), Rs("eye", 0, 16, (0, 16))))
        P.wait("vector", "T", P.cnt["T"])
        v_op(lambda e: e.reduce_max(Rs("rmcol", 1, 1, (0, 1)), pb[4][0:1, 0:16], axis=AX.X))
        consume_bank(4)
        v_op(lambda e: e.reciprocal(Rs("rmcol", 1, 1, (0, 1)), Rs("rmcol", 1, 1, (0, 1))))
        P.wait("tensor", "V", P.cnt["V"])
        guard_bank(6)
        t_op(lambda e: e.matmul(pb[6][:, 0:1], R("ones1f", 1), Rs("rmcol", 1, 1, (0, 1)), start=True, stop=True))
        P.wait("vector", "T", P.cnt["T"])
        v_op(lambda e: e.tensor_copy(Rs("rmcol", 2, 1), pb[6][:, 0:1]))
        consume_bank(6)
        ckpt(5.3)

        def mm16f(lhs_nm, rhs_nm, banks):
            guard_bank(*banks)
            for h in range(H):
                bk = banks[h // 8]
                t_op(lambda e, bk=bk, h=h, ln=lhs_nm, rn=rhs_nm: e.matmul(
                    pb[bk][0:64, ts(h % 8, 64)],
                    Rs(ln, h * 64, 64, (0, 64)),
                    Rs(rn, h * 64, 64, (0, 64)),
                    start=True, stop=True))

        def tr16f(in_nm, banks):
            guard_bank(*banks)
            for h in range(H):
                bk = banks[h // 8]
                t_op(lambda e, bk=bk, h=h, nm=in_nm: e.transpose(
                    pb[bk][0:64, ts(h % 8, 64)],
                    Rs(nm, h * 64, 64, (0, 64)),
                    Rs("eye", 0, 64, (0, 64))))

        def ev2(dst_nm, banks, fn):
            for i, bk in enumerate(banks):
                v_op(lambda e, i=i, bk=bk, d=dst_nm, fn=fn: fn(
                    e, Rs(d, i * 512, 512, (0, 64)), pb[bk][0:64, :], i))
            consume_bank(*banks)

        P.wait("tensor", "V", P.cnt["V"])
        tr16f("k2n", (4, 6))
        P.wait("vector", "T", P.cnt["T"])
        ev2("k2nt", (4, 6), lambda e, d, p, i: e.tensor_copy(d, p))
        v_op(lambda e: e.tensor_scalar(Rs("vi", 0, 1024, (0, 64)), Rs("k2nt", 0, 1024, (0, 64)),
                                       Rs("rmcol", 2, 1, (0, 64)).opt(), None, OP.mult))
        v_op(lambda e: e.tensor_scalar(Rs("vti", 0, 1024, (0, 64)), Rs("k2n", 0, 1024, (0, 64)),
                                       Rs("rmcol", 2, 1, (0, 64)).opt(), None, OP.mult))

        for it in range(6):
            ckpt(5.4 + it / 100.0)
            P.wait("tensor", "V", P.cnt["V"])
            mm16f("k2nt", "vi", (0, 2))
            P.wait("vector", "T", P.cnt["T"])
            ev2("kv", (0, 2), lambda e, d, p, i: e.tensor_copy(d, p))
            for i in range(2):
                v_op(lambda e, i=i: e.scalar_tensor_tensor(
                    Rs("t1", i * 512, 512, (0, 64)), pb[2 * i][0:64, :], -1.0,
                    Rs("i7", i * 512, 512, (0, 64)), OP.mult, OP.add))
            P.wait("tensor", "V", P.cnt["V"])
            tr16f("kv", (4, 6))
            P.wait("vector", "T", P.cnt["T"])
            ev2("kvt", (4, 6), lambda e, d, p, i: e.tensor_copy(d, p))
            P.wait("tensor", "V", P.cnt["V"])
            mm16f("kvt", "t1", (0, 2))
            P.wait("vector", "T", P.cnt["T"])
            ev2("t3", (0, 2), lambda e, d, p, i: e.scalar_tensor_tensor(
                d, p, -1.0, Rs("i15", i * 512, 512, (0, 64)), OP.mult, OP.add))
            P.wait("tensor", "V", P.cnt["V"])
            mm16f("kvt", "t3", (4, 6))
            P.wait("vector", "T", P.cnt["T"])
            ev2("t5", (4, 6), lambda e, d, p, i: e.scalar_tensor_tensor(
                d, p, -1.0, Rs("i13", i * 512, 512, (0, 64)), OP.mult, OP.add))
            P.wait("tensor", "V", P.cnt["V"])
            mm16f("vti", "t5", (0, 2))
            P.wait("vector", "T", P.cnt["T"])
            ev2("vi", (0, 2), lambda e, d, p, i: e.tensor_scalar(d, p, 0.25, None, OP.mult))
            P.wait("tensor", "V", P.cnt["V"])
            tr16f("vi", (4, 6))
            P.wait("vector", "T", P.cnt["T"])
            ev2("vti", (4, 6), lambda e, d, p, i: e.tensor_copy(d, p))

        ckpt(5.8)
        dump("vti", Rs("vti", 0, 1024, (0, 64)))
        P.wait("sync", "V", P.cnt["V"])
        for par in range(2):
            sy_dma(lambda e, par=par: e.dma_start(
                out=FPA[par * 64:(par + 1) * 64,
                        FP.placed["vtd"][0]:FP.placed["vtd"][0] + 512],
                in_=Rs("vti", 0, 1024, (0, 64)).rearrange(
                    "p (a two c) -> p a two c", a=8, two=2)[:, :, par, :],
            ).then_inc(SM["LDS"], 16))
        vtd_done = P.cnt["LDS"]


        ld_ipair = sy_dma(lambda e: e.dma_start(out=R("ipair", 16), in_=ipair_in).then_inc(SM["LDS"], 16))
        for n in range(4):
            for pp in range(8):
                be, bo = [(4, 1), (6, 3)][(n * 8 + pp) % 2]
                guard_bank(be, bo)
                for par in range(2):
                    b = be if par == 0 else bo
                    t_op(lambda e, b=b, pp=pp, n=n, par=par: e.matmul(
                        pb[b][par * 64:par * 64 + 64, :],
                        Rs("kl", pp * 64, 64, (par * 64, par * 64 + 64)),
                        Rs("q", pp * SLOC + n * 512, 512, (par * 64, par * 64 + 64)),
                        start=True, stop=True, tile_position=(par * 64, par * 64)))
                P.wait("scalar", "T", P.cnt["T"])
                s_op(lambda e, b=be, pp=pp, n=n: e.activation(
                    Rs("e1", pp * SLOC + n * 512, 512, (0, 64)), pb[b][0:64, :], ACT.Exp))
                s_op(lambda e, b=bo, pp=pp, n=n: e.activation(
                    Rs("e1", pp * SLOC + n * 512, 512, (64, 128)), pb[b][64:128, :], ACT.Exp))
                consume_bank(be, bo, sem="Sc")
        P.wait("tensor", "Sc", P.cnt["Sc"])
        ckpt(8)
        for n in range(4):
            be, bo = [(0, 1), (2, 3)][n % 2]
            guard_bank(be, bo)
            for h in range(H):
                r0, r1, c0 = pslot(h)
                b = be if h % 2 == 0 else bo
                t_op(lambda e, b=b, h=h, n=n, r0=r0, r1=r1, st=(h < 2), sp=(h >= H - 2): e.matmul(
                    pb[b][0:16, :],
                    Rs("izbf", h * 16, 16, (r0, r1)),
                    Rs("e1", (h // 2) * SLOC + n * 512, 512, (r0, r1)),
                    start=st, stop=sp, tile_position=(r0, 0)))
            P.wait("scalar", "T", P.cnt["T"])
            P.wait("scalar", "V", P.cnt["V"])
            s_op(lambda e, bo=bo: e.activation(Rs("tmp512", 0, 512, (0, 16)),
                                               pb[bo][0:16, :], ACT.Copy))
            P.wait("vector", "T", P.cnt["T"])
            P.wait("vector", "Sc", P.cnt["Sc"])
            v_op(lambda e, be=be: e.tensor_tensor(Rs("tmp512", 0, 512, (0, 16)),
                                                  pb[be][0:16, :], Rs("tmp512", 0, 512, (0, 16)), OP.add))
            v_op(lambda e, n=n: e.reciprocal(Rs("rz1", n * 512, 512, (0, 16)), Rs("tmp512", 0, 512, (0, 16))))
            consume_bank(be, bo)
        P.wait("tensor", "V", P.cnt["V"])
        P.wait("tensor", "LDS", ld_ipair)
        for pp in range(8):
            for n in range(4):
                b = [4, 6][(pp * 4 + n) % 2]
                guard_bank(b)
                t_op(lambda e, b=b, pp=pp, n=n: e.matmul(
                    pb[b][:], Rs("ipair", pp * 128, 128, (0, 16)), Rs("rz1", n * 512, 512, (0, 16)),
                    start=True, stop=True))
                P.wait("vector", "T", P.cnt["T"])
                v_op(lambda e, b=b, pp=pp, n=n: e.tensor_tensor(
                    Rs("e1", pp * SLOC + n * 512, 512),
                    Rs("e1", pp * SLOC + n * 512, 512),
                    pb[b][:], OP.mult))
                consume_bank(b)
        ckpt(9)

        P.wait("gpsimd", "CC", cc2_done)
        g2cnt, g2add = {}, {}
        for s8 in range(8):
            gb = "g2a" if s8 % 2 == 0 else "g2b"
            if s8 >= 2:
                P.wait("gpsimd", "V", g2add[s8 - 2])
            g2cnt[s8] = c_dma(lambda e, gb=gb, s8=s8: e.dma_start(
                out=Rs(gb, 0, 520),
                in_=cc2o.ap().rearrange("(s p) c -> p s c", p=128)[:, s8, :]).then_inc(SM["CCD"], 16))
            P.wait("vector", "CCD", g2cnt[s8])
            if s8 == 0:
                v_op(lambda e, gb=gb: e.tensor_copy(Rs("sumb", 0, 520), Rs(gb, 0, 520)))
            else:
                v_op(lambda e, gb=gb: e.tensor_tensor(Rs("sumb", 0, 520), Rs("sumb", 0, 520),
                                                      Rs(gb, 0, 520), OP.add))
            g2add[s8] = P.cnt["V"]
        v_op(lambda e: e.reciprocal(R("rz3"), Rs("sumb", 512, 8)))
        for a in range(8):
            v_op(lambda e, a=a: e.tensor_scalar(Rs("k3v", a * 64, 64), Rs("sumb", a * 64, 64),
                                                Rs("rz3", a, 1).opt(), None, OP.mult))
        dump("sumb", Rs("sumb", 0, 520))
        dump("k3v", Rs("k3v", 0, 512))
        P.wait("tensor", "V", P.cnt["V"])
        P.wait("tensor", "LDS", vtd_done)
        guard_bank(0, 1)
        for h in range(H):
            r0, r1, c0 = pslot(h)
            bk = h % 2
            t_op(lambda e, bk=bk, r0=r0, r1=r1, c0=c0: e.matmul(
                pb[bk][r0:r1, c0:c0 + 64],
                Rs("vtd", c0, 64, (r0, r1)),
                Rs("k3v", c0, 64, (r0, r1)),
                start=True, stop=True, tile_position=(r0, r0)))
        P.wait("vector", "T", P.cnt["T"])
        v_op(lambda e: e.tensor_copy(Rs("w2", 0, 512, (0, 64)), pb[0][0:64, :]))
        v_op(lambda e: e.tensor_copy(Rs("w2", 0, 512, (64, 128)), pb[1][64:128, :]))
        consume_bank(0, 1)
        dump("w2", R("w2"))

        P.wait("sync", "V", P.cnt["V"])
        sy_dma(lambda e: e.dma_start(
            out=BFA[0:32, BF.placed["vA"][0]:BF.placed["vA"][0] + 17 * DM],
            in_=Rs("v", 0, 17 * DM, (96, 128))).then_inc(SM["LDS"], 16))
        sy_dma(lambda e: e.dma_start(
            out=BFA[32:128, BF.placed["vA"][0]:BF.placed["vA"][0] + 17 * DM],
            in_=Rs("v", DM, 17 * DM, (0, 96))).then_inc(SM["LDS"], 16))
        va_done = P.cnt["LDS"]

        P.barrier()
        ckpt(10)

        ld_tconv = gload(R("tconv"), tconv_in)
        ld_wo = gload(R("woT").rearrange("p (fc d) -> p fc d", fc=32),
                      woT_in.rearrange("(fc p) d -> p fc d", p=128))
        P.wait("tensor", "LDG", ld_tconv)
        P.wait("tensor", "LDS", va_done)

        war_xtm, war_x1c = {}, {}
        for sc in range(NT):
            xnm = "xtm%d" % (sc % 2)
            if sc >= 2:
                P.wait("sync", "V", war_xtm[sc % 2])
            sy_dma(lambda e, d=R(xnm), sc=sc: e.dma_start(out=d, in_=xtm_in[ts(sc, 128), :]).then_inc(SM["LDS"], 16))
            bm = (0, 1, 2, 3) if sc % 2 == 0 else (4, 5, 6, 7)
            guard_bank(*bm)
            for par in range(2):
                for s in range(8):
                    h = 2 * s + par
                    t_op(lambda e, bk=bm[par], h=h, s=s, row=par * 64, sc=sc, st=(s == 0): e.matmul(
                        pb[bk][:, ts(s, 64)],
                        Rs("e1", (h // 2) * SLOC + sc * 128, 128, (row, row + 64)),
                        Rs("w2", (h // 2) * 64, 64, (row, row + 64)),
                        start=st, stop=False, tile_position=(row, 0)))
            t_op(lambda e, bk=bm[2]: e.matmul(pb[bk][:], Rs("zrow", 0, 128, (0, 1)),
                                    Rs("zrow", 0, 512, (0, 1)), start=True, stop=False))
            t_op(lambda e, bk=bm[3]: e.matmul(pb[bk][:], Rs("zrow", 0, 128, (64, 65)),
                                    Rs("zrow", 0, 512, (64, 65)), start=True, stop=False,
                                    tile_position=(64, 0)))
            for h in range(H):
                for m_par in range(2):
                    m = sc * 2 + m_par
                    if m % 2 == 0:
                        pieces = ((0, 0, 128, m // 2),)
                    else:
                        pieces = ((1, 64, 64, (m - 1) // 2), (2, 0, 64, (m + 1) // 2))
                    for (cls, base, kk, j) in pieces:
                        row = 64 if base == 64 else 0
                        bk = (0 if h % 2 == 0 else 2) if row == 0 else (1 if h % 2 == 1 else 3)
                        t_op(lambda e, bk=bm[bk], h=h, s=h // 2, ps0=m_par * 64, cls=cls,
                             base=base, kk=kk, j=j: e.matmul(
                            pb[bk][ps0:ps0 + 64, ts(s, 64)],
                            Rs("tconv", (h * 3 + cls) * 64, 64, (base, base + kk)),
                            Rs("vA", j * DM + h * 64, 64, (base, base + kk)),
                            start=False, stop=False, tile_position=(base, ps0)))
            t_op(lambda e, bk=bm[0]: e.matmul(pb[bk][:], Rs("zrow", 0, 128, (0, 1)),
                                    Rs("zrow", 0, 512, (0, 1)), start=False, stop=True))
            t_op(lambda e, bk=bm[2]: e.matmul(pb[bk][:], Rs("zrow", 0, 128, (0, 1)),
                                    Rs("zrow", 0, 512, (0, 1)), start=False, stop=True))
            t_op(lambda e, bk=bm[1]: e.matmul(pb[bk][:], Rs("zrow", 0, 128, (64, 65)),
                                    Rs("zrow", 0, 512, (64, 65)), start=False, stop=True,
                                    tile_position=(64, 0)))
            t_op(lambda e, bk=bm[3]: e.matmul(pb[bk][:], Rs("zrow", 0, 128, (64, 65)),
                                    Rs("zrow", 0, 512, (64, 65)), start=False, stop=True,
                                    tile_position=(64, 0)))
            P.wait("vector", "T", P.cnt["T"])
            P.wait("vector", "LDS", P.cnt["LDS"])
            cnm = "x1c%d" % (sc % 2)
            if sc >= 2:
                P.wait("vector", "STQ", war_x1c[sc % 2])

            def pstr(nm, par):
                return Rs(nm, 0, DM).rearrange("p (s two c) -> p s c two", two=2, c=64)[:, :, :, par]
            v_op(lambda e, xnm=xnm, bk=bm[0]: e.tensor_tensor(R("ctmp"), pstr(xnm, 0), pb[bk][:], OP.add))
            v_op(lambda e, cnm=cnm, bk=bm[3]: e.tensor_tensor(pstr(cnm, 0), R("ctmp"), pb[bk][:], OP.add))
            v_op(lambda e, xnm=xnm, bk=bm[1]: e.tensor_tensor(R("ctmp"), pstr(xnm, 1), pb[bk][:], OP.add))
            v_op(lambda e, cnm=cnm, bk=bm[2]: e.tensor_tensor(pstr(cnm, 1), R("ctmp"), pb[bk][:], OP.add))
            consume_bank(*bm)
            war_xtm[sc % 2] = P.cnt["V"]
            v_op(lambda e, cnm=cnm, sc=sc: e.reduce_sum(Rs("s1", sc, 1), Rs(cnm, 0, DM), axis=AX.X))
            P.wait("scalar", "V", P.cnt["V"])
            s_op(lambda e, cnm=cnm: e.activation(R("trash"), Rs(cnm, 0, DM), ACT.Square))
            P.wait("vector", "Sc", P.cnt["Sc"])
            v_op(lambda e, sc=sc: e.reduce_sum(Rs("s2", sc, 1), R("trash"), axis=AX.X))
            sc_dma(lambda e, cnm=cnm, sc=sc: e.dma_start(out=x1b[ts(sc, 128), :], in_=Rs(cnm, 0, DM)).then_inc(SM["STQ"], 16))
            war_x1c[sc % 2] = P.cnt["STQ"]

        P.barrier()
        ckpt(11)

        P.wait("vector", "STQ", P.cnt["STQ"])
        v_op(lambda e: e.tensor_scalar(R("s1"), R("s1"), 1.0 / DM, None, OP.mult))
        v_op(lambda e: e.tensor_tensor(R("mx"), R("s1"), R("s1"), OP.mult))
        v_op(lambda e: e.tensor_scalar(R("s2"), R("s2"), 1.0 / DM, None, OP.mult))
        v_op(lambda e: e.tensor_tensor(R("s2"), R("s2"), R("mx"), OP.subtract))
        P.wait("scalar", "V", P.cnt["V"])
        s_op(lambda e: e.activation(R("s2"), R("s2"), ACT.Ln, bias=EPS))
        s_op(lambda e: e.activation(R("s2"), R("s2"), ACT.Exp, scale=-0.5))
        P.wait("tensor", "Sc", P.cnt["Sc"])
        guard_bank(0)
        t_op(lambda e: e.transpose(pb[0][0:16, 0:128], R("s1"), R("eye")))
        t_op(lambda e: e.transpose(pb[0][0:16, 128:256], R("s2"), R("eye")))
        P.wait("vector", "T", P.cnt["T"])
        v_op(lambda e: e.tensor_copy(Rs("st16", 0, 256, (0, 16)), pb[0][0:16, 0:256]))
        consume_bank(0)
        P.wait("sync", "V", P.cnt["V"])
        sy_dma(lambda e: e.dma_start(out=Rs("r2rows", 0, SLOC, (0, 1)), in_=Rs("st16", 0, 128, (0, 16))).then_inc(SM["LDS"], 16))
        sy_dma(lambda e: e.dma_start(out=Rs("r2rows", SLOC, SLOC, (0, 1)), in_=Rs("st16", 128, 128, (0, 16))).then_inc(SM["LDS"], 16))
        P.wait("tensor", "LDS", P.cnt["LDS"])

        war_x1r, x1r_cnt = {}, {}
        for scg in range(8):
            for w in range(2):
                b = [4, 6][w]
                guard_bank(b)
                t_op(lambda e, b=b, scg=scg, w=w: e.matmul(
                    pb[b][:, 0:256], R("ones1f", 1), Rs("r2rows", w * SLOC + scg * 256, 256, (0, 1)),
                    start=True, stop=True))
                P.wait("vector", "T", P.cnt["T"])
                dnm = "r2mu" if w == 0 else "r2sig"
                v_op(lambda e, b=b, d=dnm: e.tensor_copy(Rs(d, 0, 256), pb[b][:, 0:256]))
                consume_bank(b)
            for j in range(2):
                bi = scg * 2 + j
                slot = bi % 4
                qeng, qsem = [("sync", "LDS"), ("scalar", "STQ"),
                              ("gpsimd", "LDG")][bi % 3]
                if bi >= 4:
                    P.wait(qeng, "T", war_x1r[slot])
                if qeng == "sync":
                    x1r_cnt[bi] = (qsem, sy_dma(lambda e, slot=slot, bi=bi: e.dma_start(
                        out=R("x1r%d" % slot), in_=x1b[ts(bi, 128), :]).then_inc(SM["LDS"], 16)))
                elif qeng == "scalar":
                    x1r_cnt[bi] = (qsem, sc_dma(lambda e, slot=slot, bi=bi: e.dma_start(
                        out=R("x1r%d" % slot), in_=x1b[ts(bi, 128), :]).then_inc(SM["STQ"], 16)))
                else:
                    x1r_cnt[bi] = (qsem, gload(R("x1r%d" % slot), x1b[ts(bi, 128), :]))
            for j in range(2):
                qsem, cnt = x1r_cnt[scg * 2 + j]
                P.wait("tensor", qsem, cnt)
            for dc in range(8):
                b = [0, 2][dc % 2]
                guard_bank(b)
                for j in range(2):
                    t_op(lambda e, b=b, dc=dc, j=j, slot=(scg * 2 + j) % 4: e.transpose(
                        pb[b][:, ts(j, 128)], Rs("x1r%d" % slot, dc * 128, 128), R("eye")))
                P.wait("vector", "T", P.cnt["T"])
                v_op(lambda e, b=b: e.tensor_tensor(
                    Rs("tmp512", 0, 256), pb[b][:, 0:256], Rs("r2mu", 0, 256), OP.subtract))
                v_op(lambda e, dc=dc, scg=scg: e.tensor_tensor(
                    Rs("ln2", dc * SLOC + scg * 256, 256),
                    Rs("tmp512", 0, 256), Rs("r2sig", 0, 256), OP.mult))
                consume_bank(b)
            for j in range(2):
                war_x1r[(scg * 2 + j) % 4] = P.cnt["T"]

        P.barrier()
        ckpt(12)

        ld_gate = gload(R("gateT").rearrange("p (dc e) -> p dc e", dc=8),
                        gateT_in.rearrange("(dc p) e -> p dc e", p=128))
        ld_acat = gload(R("acatT").rearrange("p (dc r) -> p dc r", dc=8),
                        acatT_in.rearrange("(dc p) r -> p dc r", p=128))
        ld_im4 = gload(R("im4", 8), im4_in)
        for nm, src_ in (("gb16", gb16_in), ("p216", p216_in)):
            sy_dma(lambda e, d=R(nm), s=src_: e.dma_start(out=d, in_=s).then_inc(SM["LDS"], 16))
        P.wait("tensor", "LDS", P.cnt["LDS"])
        P.wait("tensor", "LDG", ld_acat)
        for n in range(4):
            b = [0, 2][n % 2]
            guard_bank(b)
            for dc in range(8):
                t_op(lambda e, b=b, dc=dc, n=n, st=(dc == 0), sp=(dc == 7): e.matmul(
                    pb[b][0:8, :], Rs("gateT", dc * 8, 8),
                    Rs("ln2", dc * SLOC + n * 512, 512), start=st, stop=sp))
            P.wait("vector", "T", P.cnt["T"])
            v_op(lambda e, b=b, n=n: e.tensor_copy(Rs("glog", n * 512, 512, (0, 8)), pb[b][0:8, :]))
            consume_bank(b)
            b2 = [4, 6][n % 2]
            guard_bank(b2)
            for dc in range(8):
                t_op(lambda e, b=b2, dc=dc, n=n, st=(dc == 0), sp=(dc == 7): e.matmul(
                    pb[b][0:32, :], Rs("acatT", dc * 32, 32),
                    Rs("ln2", dc * SLOC + n * 512, 512), start=st, stop=sp))
            P.wait("vector", "T", P.cnt["T"])
            v_op(lambda e, b=b2, n=n: e.tensor_copy(Rs("u", n * 512, 512, (0, 32)), pb[b][0:32, :]))
            consume_bank(b2)
        P.wait("tensor", "V", P.cnt["V"])
        guard_bank(0)
        for sc in range(NT):
            t_op(lambda e, sc=sc: e.transpose(pb[0][:, ts(sc, 8)], Rs("glog", sc * 128, 128, (0, 8)),
                                              Rs("eye", 0, 8, (0, 8))))
        P.wait("vector", "T", P.cnt["T"])
        v_op(lambda e: e.tensor_tensor(R("lg"), pb[0][:, 0:128], R("gb16"), OP.add))
        consume_bank(0)
        v_op(lambda e: e.reduce_max(R("mx"), R("lg").rearrange("p (a b) -> p a b", a=16), axis=AX.X))
        for sc in range(NT):
            v_op(lambda e, sc=sc: e.tensor_scalar(Rs("eqw", sc * 8, 8), Rs("lg", sc * 8, 8),
                                                  Rs("mx", sc, 1).opt(), None, OP.is_ge))
        v_op(lambda e: e.tensor_tensor(R("eqw"), R("eqw"), R("p216"), OP.mult))
        v_op(lambda e: e.reduce_max(R("mx"), R("eqw").rearrange("p (a b) -> p a b", a=16), axis=AX.X))
        for sc in range(NT):
            v_op(lambda e, sc=sc: e.tensor_scalar(Rs("mask", sc * 8, 8), Rs("eqw", sc * 8, 8),
                                                  Rs("mx", sc, 1).opt(), None, OP.is_ge))
        P.wait("tensor", "V", P.cnt["V"])
        for grp in range(4):
            b = [4, 6][grp % 2]
            guard_bank(b)
            for j in range(4):
                sc = grp * 4 + j
                t_op(lambda e, b=b, j=j, sc=sc: e.transpose(
                    pb[b][0:8, ts(j, 128)], Rs("mask", sc * 8, 8), R("eye")))
            P.wait("vector", "T", P.cnt["T"])
            v_op(lambda e, b=b, grp=grp: e.tensor_copy(Rs("mfm", grp * 512, 512, (0, 8)), pb[b][0:8, :]))
            consume_bank(b)
        P.wait("tensor", "V", P.cnt["V"])
        P.wait("tensor", "LDG", ld_im4)
        for n in range(4):
            b = [0, 2][n % 2]
            guard_bank(b)
            t_op(lambda e, b=b, n=n: e.matmul(pb[b][0:32, :], R("im4", 8), Rs("mfm", n * 512, 512, (0, 8)),
                                              start=True, stop=True))
            P.wait("vector", "T", P.cnt["T"])
            v_op(lambda e, b=b, n=n: e.tensor_tensor(Rs("z", n * 512, 512, (0, 32)),
                                                     Rs("u", n * 512, 512, (0, 32)), pb[b][0:32, :], OP.mult))
            consume_bank(b)

        P.barrier()
        ckpt(13)

        ld_bcat = gload(R("bcatT", 32), bcatT_in)
        ld_wob = gload(R("wob", 1), wob_in)
        sy_dma(lambda e: e.dma_start(out=R("wibc"), in_=wibc_in).then_inc(SM["LDS"], 16))
        ld_wibc = P.cnt["LDS"]
        P.wait("tensor", "LDG", ld_wo)
        P.wait("tensor", "LDG", ld_wob)
        P.wait("scalar", "LDS", ld_wibc)
        wi_guard = {}
        war_wo, war_out = {}, {}
        for tc in range(NTC):
            for fc in range(32):
                slot = fc % 4
                if tc * 32 + fc >= 4:
                    P.wait("gpsimd", "T", wi_guard[slot])
                P.raw("gpsimd", lambda e, fc=fc, slot=slot: e.dma_start(
                    out=R("wib%d" % slot), in_=wiT_in[:, ts(fc, 1024)]).then_inc(SM["LDW%d" % slot], 16))
                P.inc("LDW%d" % slot, 16)
                P.wait("tensor", "LDW%d" % slot, P.cnt["LDW%d" % slot])
                b = [0, 2][fc % 2]
                guard_bank(b)
                for dc in range(8):
                    t_op(lambda e, b=b, slot=slot, dc=dc, tc=tc, st=(dc == 0): e.matmul(
                        pb[b][:], Rs("wib%d" % slot, dc * 128, 128),
                        Rs("ln2", dc * SLOC + tc * TC, TC), start=st, stop=False))
                t_op(lambda e, b=b, fc=fc, tc=tc: e.matmul(
                    pb[b][:], Rs("bcatT", fc * 128, 128, (0, 32)),
                    Rs("z", tc * TC, TC, (0, 32)), start=False, stop=True))
                wi_guard[slot] = P.cnt["T"]
                P.wait("scalar", "T", P.cnt["T"])
                s_op(lambda e, b=b, fc=fc: e.activation(
                    Rs("inter", fc * TC, TC), pb[b][:], ACT.Relu, bias=Rs("wibc", fc, 1).opt()))
                consume_bank(b, sem="Sc")
            P.wait("tensor", "Sc", P.cnt["Sc"])
            for tt in range(TC // 128):
                sc = tc * (TC // 128) + tt
                xj = sc % 4
                if sc >= 4:
                    P.wait("sync", "V", war_wo[xj])
                sy_dma(lambda e, xj=xj, sc=sc: e.dma_start(out=R("x1r%d" % xj), in_=x1b[ts(sc, 128), :]).then_inc(SM["LDS"], 16))
                for dh in range(2):
                    b = [4, 6][(tt * 2 + dh) % 2]
                    guard_bank(b)
                    for fc in range(32):
                        t_op(lambda e, b=b, fc=fc, tt=tt, dh=dh, st=(fc == 0): e.matmul(
                            pb[b][:], Rs("inter", fc * TC + tt * 128, 128),
                            Rs("woT", fc * DM + dh * 512, 512), start=st, stop=False))
                    t_op(lambda e, b=b, dh=dh: e.matmul(
                        pb[b][:], R("ones1b", 1), Rs("wob", dh * 512, 512, (0, 1)), start=False, stop=True))
                    P.wait("vector", "T", P.cnt["T"])
                    P.wait("vector", "LDS", P.cnt["LDS"])
                    onm = "out%d" % (sc % 2)
                    if sc >= 2 and dh == 0:
                        P.wait("vector", "STQ", war_out[sc % 2])
                    v_op(lambda e, b=b, onm=onm, xj=xj, dh=dh: e.tensor_tensor(
                        Rs(onm, dh * 512, 512), Rs("x1r%d" % xj, dh * 512, 512), pb[b][:], OP.add))
                    consume_bank(b)
                war_wo[xj] = P.cnt["V"]
                P.wait("scalar", "V", P.cnt["V"])
                sc_dma(lambda e, onm="out%d" % (sc % 2), sc=sc: e.dma_start(
                    out=y_out[ts(sc, 128), :], in_=Rs(onm, 0, DM)).then_inc(SM["STQ"], 16))
                war_out[sc % 2] = P.cnt["STQ"]

        P.wait("scalar", "STQ", P.cnt["STQ"])
    except _StopBuild:
        pass

    with nc.Block() as block:
        def run(eng_name):
            def body(e):
                for st_ in P.steps[eng_name]:
                    if isinstance(st_, tuple):
                        e.wait_ge(sems[st_[1]], st_[2])
                    else:
                        st_(e)
            return body
        block.tensor(run("tensor"))
        block.vector(run("vector"))
        block.scalar(run("scalar"))
        block.gpsimd(run("gpsimd"))
        block.sync(run("sync"))

    _elide_ldweights(nc)
    es.close()
    return nc


_MM_FLDS = ("replication_resolution", "replication_shift_amnt", "replication_num_rows",
            "start_tensor_calc", "stop_tensor_calc", "is_transpose", "is_fmap_onezero",
            "is_weight_onezero", "ifmap_quant_offset", "weights_quant_offset", "tile_size",
            "tile_position", "perf_mode", "bass_skip_group_check", "name", "engine",
            "debug", "ins", "outs", "sync_info", "emission_index", "descendants")


def _elide_ldweights(nc):
    n = 0
    for fn in nc.m.functions:
        for bl in fn.blocks:
            if "_PE_" not in bl.name:
                continue
            il = bl.instructions
            prev = None
            for i, ins in enumerate(il):
                tn = type(ins).__name__
                if tn == "InstEventSemaphore":
                    continue
                if tn != "InstMatmult" or ins.is_transpose:
                    prev = None
                    continue
                try:
                    a = ins.ins[1]
                    if "bfloat16" not in str(a.dtype) or "bfloat16" not in str(ins.ins[0].dtype):
                        prev = None
                        continue
                    if a.ap[1][1] != ins.outs[0].ap[0][1]:
                        prev = None
                        continue
                    k = (a.memref, a.offset, str(a.ap), str(a.dtype), ins.tile_position)
                except Exception:
                    prev = None
                    continue
                if k == prev:
                    kw = {f: getattr(ins, f) for f in _MM_FLDS}
                    kw["ldweights"] = False
                    il[i] = mybir.InstMatmult(**kw)
                    n += 1
                else:
                    prev = k
    return n


_CACHE = {}


def _host_stage(inputs):
    f32 = np.float32
    x = np.asarray(inputs["L0_attn_hidden_states"], f32)[0]
    Wq = np.asarray(inputs["L0_attn_Wq"], f32)
    bq = np.asarray(inputs["L0_attn_bq"], f32)
    Wk = np.asarray(inputs["L0_attn_Wk"], f32)
    bk = np.asarray(inputs["L0_attn_bk"], f32)
    Wv = np.asarray(inputs["L0_attn_Wv"], f32)
    bv = np.asarray(inputs["L0_attn_bv"], f32)
    conv_w = np.asarray(inputs["L0_attn_conv_w"], f32)
    gW = np.asarray(inputs["L0_ffn_gate_W"], f32)
    gb = np.asarray(inputs["L0_ffn_gate_b"], f32)
    wiW = np.asarray(inputs["L0_ffn_wi_W"], f32)
    wib = np.asarray(inputs["L0_ffn_wi_b"], f32)
    woW = np.asarray(inputs["L0_ffn_wo_W"], f32)
    wob = np.asarray(inputs["L0_ffn_wo_b"], f32)
    lA = np.asarray(inputs["L0_ffn_lora_A"], f32)
    lB = np.asarray(inputs["L0_ffn_lora_B"], f32)

    xpad = np.zeros((S + 256, DM), f32)
    xpad[128:128 + S] = x

    wiT = np.ascontiguousarray(wiW.T)
    wiT_pm = np.ascontiguousarray(
        wiT.reshape(8, 128, 32, 128).transpose(1, 2, 0, 3)).reshape(128, 32 * 1024)
    woT = np.ascontiguousarray(woW.T)
    gateT = np.ascontiguousarray(gW.T)
    acatT = np.ascontiguousarray(lA.reshape(32, DM).T)
    bcatT = np.ascontiguousarray(lB.transpose(0, 2, 1).reshape(32, FF))

    w65 = conv_w[:, 0, :, 0]
    Tfull = np.zeros((H, 128, 64), f32)
    for p in range(64):
        Tfull[:, p:p + 65, p] = w65
    tconv = np.zeros((H, 3, 128, 64), f32)
    tconv[:, 0] = Tfull
    tconv[:, 1, 64:128] = Tfull[:, 0:64]
    tconv[:, 2, 0:64] = Tfull[:, 64:128]
    tconv_pm = np.ascontiguousarray(tconv.transpose(2, 0, 1, 3)).reshape(128, H * 3 * 64)

    eye = np.eye(128, dtype=f32)
    izf = np.zeros((128, 16 * H), f32)
    for h in range(H):
        izf[(h % 2) * 64:(h % 2) * 64 + 64, 16 * h + h] = 1.0
    izff = np.zeros((128, 256), f32)
    for h in range(H):
        izff[0:64, 16 * h + h] = 1.0
    ipair = np.zeros((16, 8, 128), f32)
    for pp in range(8):
        ipair[2 * pp, pp, 0:64] = 1.0
        ipair[2 * pp + 1, pp, 64:128] = 1.0
    ipair = ipair.reshape(16, 1024)
    im4 = np.zeros((E8, 32), f32)
    for e_ in range(E8):
        im4[e_, 4 * e_:4 * e_ + 4] = 1.0
    eyep = np.zeros((128, 1024), f32)
    for a in range(16):
        eyep[0:64, a * 64:(a + 1) * 64] = np.eye(64)
    bvb = np.broadcast_to(bv, (128, DM)).copy()
    bqc = np.ascontiguousarray(bq.reshape(8, 128).T)
    bkc = np.ascontiguousarray(bk.reshape(8, 128).T)
    wibc = np.ascontiguousarray(wib.reshape(32, 128).T)
    gb16 = np.tile(np.broadcast_to(gb, (128, 8)), (1, 16))
    p216 = np.tile((2.0 ** -np.arange(8, dtype=f32))[None, :], (128, 16))

    shared = dict(wq=Wq, wk=Wk, wv=Wv, wiT=wiT_pm, woT=woT, gateT=gateT,
                  acatT=acatT, bcatT=bcatT, tconv=tconv_pm, eye=eye, izf=izf,
                  izff=izff, ipair=ipair, im4=im4, i7=7.0 * eyep,
                  i15=15.0 * eyep, i13=13.0 * eyep, bvb=bvb, bqc=bqc, bkc=bkc,
                  wibc=wibc, gb16=gb16, p216=p216, wob=wob.reshape(1, DM))
    in_maps = []
    for c in range(NCORE):
        sl = xpad[c * SLOC: c * SLOC + TPAD]
        m = dict(shared)
        m["xT"] = np.ascontiguousarray(sl.T)
        m["xtm"] = np.ascontiguousarray(x[c * SLOC:(c + 1) * SLOC])
        m["bvlo"] = np.zeros_like(bvb) if c == 0 else bvb
        m["bvhi"] = np.zeros_like(bvb) if c == NCORE - 1 else bvb
        in_maps.append(m)
    return in_maps


def kernel(**inputs):
    in_maps = _host_stage(inputs)
    if "nc" not in _CACHE:
        _CACHE["nc"] = build_program()
    res = run_bass_kernel_spmd(_CACHE["nc"], in_maps, core_ids=list(range(NCORE)))
    out = np.concatenate([np.asarray(r["y"], np.float32) for r in res.results], axis=0)
    return out.reshape(1, S, DM)


# version 15
# speedup vs baseline: 1.0025x; 1.0025x over previous
import os
import numpy as np
from contextlib import ExitStack

import concourse.bass as bass
import concourse.mybir as mybir
from concourse.bass_utils import run_bass_kernel_spmd
from concourse.alu_op_type import AluOpType as OP

F32 = mybir.dt.float32
BF16 = mybir.dt.bfloat16
ACT = mybir.ActivationFunctionType
AX = mybir.AxisListType

_DBG = os.environ.get("KDBG") == "1"

NCORE = 8
S, DM, H, DH, LM, SEG = 16384, 1024, 16, 64, 64, 256
E8, R4, FF, KSZ = 8, 4, 4096, 65
SLOC = S // NCORE
NT = SLOC // 128
NTP = NT + 2
TPAD = NTP * 128
SCALE = float(DH) ** -0.25
EPS = 1e-5
TC = 512
NTC = SLOC // TC


class Sched:
    ENGS = ("tensor", "vector", "scalar", "gpsimd", "sync")
    BAR_SEMS = ("T", "V", "Sc", "LDG", "LDS", "STQ")

    def __init__(self):
        self.steps = {e: [] for e in self.ENGS}
        self.cnt = {}

    def sem(self, name):
        self.cnt.setdefault(name, 0)

    def raw(self, eng, fn):
        self.steps[eng].append(fn)

    def inc(self, name, k):
        self.cnt[name] += k
        return self.cnt[name]

    def wait(self, eng, name, val=None):
        v = self.cnt[name] if val is None else val
        if v <= 0:
            return
        self.steps[eng].append(("W", name, v))

    def barrier(self):
        snap = [(s, self.cnt[s]) for s in self.BAR_SEMS if self.cnt[s] > 0]
        for e in self.ENGS:
            for s, v in snap:
                self.steps[e].append(("W", s, v))


class Arena:

    def __init__(self, name):
        self.name = name
        self.regions = []

    def add(self, name, size, p0, p1):
        self.regions.append((name, size, p0, p1))

    def pack(self):
        def run(order):
            placed = {}
            for name, size, p0, p1 in order:
                pad = (size + 15) // 16 * 16
                spans = sorted((v for v in placed.values()
                                if not (v[3] < p0 or v[2] > p1)), key=lambda v: v[0])
                off = 0
                for (o2, s2, _, _, _) in spans:
                    if off + pad <= o2:
                        break
                    off = max(off, o2 + s2)
                placed[name] = (off, pad, p0, p1, size)
            tot = max((o + s for o, s, _, _, _ in placed.values()), default=0)
            return placed, tot
        best = None
        for key in (lambda r: (-r[1],), lambda r: (r[2], -r[1]),
                    lambda r: (-(r[3] - r[2]), -r[1]), lambda r: (r[2], r[3], -r[1])):
            placed, tot = run(sorted(self.regions, key=key))
            if best is None or tot < best[1]:
                best = (placed, tot)
        rng = np.random.default_rng(12345)
        base = sorted(self.regions, key=lambda r: (-r[1],))
        for _ in range(400):
            order = list(base)
            rng.shuffle(order)
            placed, tot = run(order)
            if tot < best[1]:
                best = (placed, tot)
        self.placed, self.total = best
        return self.total


def ts(i, n):
    return slice(i * n, (i + 1) * n)


class _StopBuild(Exception):
    pass


def build_program(stop_phase=None):
    nc = bass.Bass("TRN2", target_bir_lowering=False)
    P = Sched()
    SEMS = ("T", "V", "Sc", "LDG", "LDS", "STQ", "CC", "CCD",
            "LDW0", "LDW1", "LDW2", "LDW3")
    for s_ in SEMS:
        P.sem(s_)

    def din(name, shape, dt=F32):
        return nc.dram_tensor(name, list(shape), dt, kind="ExternalInput").ap()

    xT_in = din("xT", (DM, TPAD))
    xtm_in = din("xtm", (SLOC, DM))
    wq_in = din("wq", (DM, DM))
    wk_in = din("wk", (DM, DM))
    wv_in = din("wv", (DM, DM))
    wiT_in = din("wiT", (128, 32 * 1024))
    woT_in = din("woT", (FF, DM))
    gateT_in = din("gateT", (DM, E8))
    acatT_in = din("acatT", (DM, 32))
    bcatT_in = din("bcatT", (32, FF))
    tconv_in = din("tconv", (128, H * 3 * 64))
    eye_in = din("eye", (128, 128))
    izf_in = din("izf", (128, 16 * H))
    izff_in = din("izff", (128, 256))
    ipair_in = din("ipair", (16, 1024))
    im4_in = din("im4", (E8, 32))
    i7_in = din("i7", (128, 1024))
    i15_in = din("i15", (128, 1024))
    i13_in = din("i13", (128, 1024))
    bvb_in = din("bvb", (128, DM))
    bvlo_in = din("bvlo", (128, DM))
    bvhi_in = din("bvhi", (128, DM))
    bqc_in = din("bqc", (128, 8))
    bkc_in = din("bkc", (128, 8))
    wibc_in = din("wibc", (128, 32))
    gb16_in = din("gb16", (128, 128))
    p216_in = din("p216", (128, 128))
    wob_in = din("wob", (1, DM))
    y_out = nc.dram_tensor("y", [SLOC, DM], F32, kind="ExternalOutput").ap()

    x1b = nc.dram_tensor("x1b", [SLOC, DM], F32).ap()
    cc1i = nc.dram_tensor("cc1i", [128, 64], F32)
    cc1o = nc.dram_tensor("cc1o", [NCORE * 128, 64], F32)
    cc2i = nc.dram_tensor("cc2i", [128, 520], F32)
    cc2o = nc.dram_tensor("cc2o", [NCORE * 128, 520], F32)

    if _DBG:
        dbg = {nm: nc.dram_tensor("dbg_" + nm, list(shp), F32, kind="ExternalOutput").ap()
               for nm, shp in (("ql", (128, 512)), ("kl", (128, 512)),
                               ("vti", (64, 1024)), ("k3v", (128, 512)),
                               ("w2", (128, 512)), ("sumb", (128, 520)),
                               ("blob", (128, 520)))}

    BF = Arena("bf")
    FP = Arena("fp")
    for nm, sz, p0, p1 in (
        ("ln1", 8 * TPAD, 0, 4), ("ln2", 8 * SLOC, 11, 15),
        ("q", 8 * SLOC, 1, 7), ("k", 8 * SLOC, 2, 6), ("v", NTP * DM, 4, 9),
        ("inter", 32 * TC, 13, 15), ("vA", 17 * DM, 9, 10),
        ("e1", 8 * SLOC, 7, 10), ("woT", 32 * DM, 10, 15),
        ("w3", 8 * DM, 0, 3), ("wvh", 8 * 512, 4, 4),
        ("xbf", 512, 0, 0), ("xsq", 512, 0, 0),
        ("e3t0", 1024, 4.6, 4.6), ("e3t1", 1024, 4.6, 4.6),
        ("wib0", 1024, 13, 15), ("wib1", 1024, 13, 15),
        ("wib2", 1024, 13, 15), ("wib3", 1024, 13, 15),
        ("z", SLOC, 12, 15), ("mfm", SLOC, 12, 12), ("u", SLOC, 12, 12),
        ("ql", 512, 3, 9), ("kl", 512, 3, 7), ("w2", 512, 9, 10),
        ("lf_all", 512, 0, 3), ("tconv", H * 3 * 64, 8, 10),
        ("izbf", 16 * H, 0, 8), ("im4", 32, 12, 12), ("gateT", 8 * E8, 11, 12),
        ("acatT", 8 * 32, 11, 12), ("bcatT", FF, 12, 15), ("wob", DM, 12, 15),
        ("trash", DM, 10, 10), ("ones1b", 128, 0, 15), ("o128b", 16, 0, 15),
        ("zrow", 512, 0, 15),
    ):
        BF.add(nm, sz, p0, p1)
    for nm, sz, p0, p1 in (
        ("xch0", 2048, 0, 0), ("xch1", 2048, 0, 0), ("rows", 512, 0, 0),
        ("tmp384", 256, 0, 0), ("lf_loc", 64, 0, 3),
        ("e2", 1024, 5.0, 5.25), ("k2n", 1024, 5.0, 5.45),
        ("k2nt", 1024, 5.4, 5.9),
        ("vi", 1024, 5.45, 5.95), ("vti", 1024, 5.45, 5.95), ("kv", 1024, 5.5, 5.9),
        ("kvt", 1024, 5.5, 5.9), ("t1", 1024, 5.5, 5.9), ("t3", 1024, 5.5, 5.9),
        ("t5", 1024, 5.5, 5.9), ("vtd", 512, 5.9, 9),
        ("i7", 1024, 4, 5.9), ("i15", 1024, 4, 5.9), ("i13", 1024, 4, 5.9),
        ("rmcol", 16, 5.0, 5.9), ("rs8", 16, 5.0, 5.25), ("izff", 256, 0, 5.3),
        ("eye", 128, 0, 15), ("ipair", 1024, 6.5, 8),
        ("rz1", SLOC, 8, 8), ("blob", 528, 4.6, 9),
        ("g2a", 528, 4.6, 9), ("g2b", 528, 4.6, 9), ("sumb", 528, 4.6, 9), ("rz3", 8, 9, 9),
        ("k3v", 512, 9, 9), ("bvb", DM, 4, 4), ("bvlo", DM, 4, 4),
        ("bvhi", DM, 4, 4), ("bqc", 8, 0, 3), ("bkc", 8, 0, 3),
        ("wibc", 32, 12, 15), ("gb16", 128, 11, 12), ("p216", 128, 11, 12),
        ("xtm0", DM, 10, 10), ("xtm1", DM, 10, 10), ("ctmp", 512, 10, 10),
        ("x1c0", DM, 10, 10), ("x1c1", DM, 10, 10),
        ("s1", 16, 10, 11), ("s2", 16, 10, 11), ("st16", 256, 11, 11),
        ("r2rows", 2 * SLOC, 11, 11), ("r2mu", 256, 11, 11),
        ("r2sig", 256, 11, 11),
        ("x1r0", DM, 11, 15), ("x1r1", DM, 11, 15),
        ("x1r2", DM, 11, 15), ("x1r3", DM, 11, 15),
        ("tmp512", 512, 7, 11), ("glog", SLOC, 12, 12),
        ("lg", 128, 12, 12), ("eqw", 128, 12, 12), ("mask", 128, 12, 12),
        ("mx", 16, 11, 12),
        ("out0", DM, 13, 15), ("out1", DM, 13, 15), ("ones1f", 128, 0, 15),
        ("consts", 4, 0, 15),
    ):
        FP.add(nm, sz, p0, p1)
    nbf = BF.pack()
    nfp = FP.pack()
    assert 2 * nbf + 4 * nfp <= 208 * 1024, (nbf, nfp, 2 * nbf + 4 * nfp)

    es = ExitStack()
    BFA = es.enter_context(nc.sbuf_tensor("BFA", [128, nbf], BF16))
    FPA = es.enter_context(nc.sbuf_tensor("FPA", [128, nfp], F32))

    def R(name, parts=128):
        arena, t = (BF, BFA) if name in BF.placed else (FP, FPA)
        off, _, _, _, size = arena.placed[name]
        return t[0:parts, off:off + size]

    def Rs(name, lo, n, parts=(0, 128)):
        arena, t = (BF, BFA) if name in BF.placed else (FP, FPA)
        off = arena.placed[name][0]
        return t[parts[0]:parts[1], off + lo:off + lo + n]

    pb = [es.enter_context(nc.psum_tensor(f"pb{i}", [128, 512], F32)) for i in range(8)]
    sems = {}
    for s_ in SEMS:
        sems[s_] = es.enter_context(nc.semaphore(s_))
    SM = sems
    nc.const_aps.register(F32, EPS, Rs("consts", 0, 1))
    nc.const_aps.register(F32, -0.5, Rs("consts", 1, 1))

    bank_guard = {}

    def guard_bank(*bs):
        for b in bs:
            g = bank_guard.get(b)
            if g:
                P.wait("tensor", g[0], g[1])

    def consume_bank(*bs, sem="V"):
        for b in bs:
            bank_guard[b] = (sem, P.cnt[sem])

    def t_op(fn):
        P.raw("tensor", lambda e, fn=fn: fn(e).then_inc(SM["T"], 1))
        return P.inc("T", 1)

    def v_op(fn):
        P.wait("vector", "V", P.cnt["V"])
        P.raw("vector", lambda e, fn=fn: fn(e).then_inc(SM["V"], 1))
        return P.inc("V", 1)

    def s_op(fn):
        P.wait("scalar", "Sc", P.cnt["Sc"])
        P.raw("scalar", lambda e, fn=fn: fn(e).then_inc(SM["Sc"], 1))
        return P.inc("Sc", 1)

    def g_dma(fn):
        P.wait("gpsimd", "LDG", P.cnt["LDG"])
        P.wait("gpsimd", "CCD", P.cnt["CCD"])
        P.raw("gpsimd", fn)
        return P.inc("LDG", 16)

    def c_dma(fn):
        P.wait("gpsimd", "LDG", P.cnt["LDG"])
        P.wait("gpsimd", "CCD", P.cnt["CCD"])
        P.raw("gpsimd", fn)
        return P.inc("CCD", 16)

    def sy_dma(fn):
        P.wait("sync", "LDS", P.cnt["LDS"])
        P.raw("sync", fn)
        return P.inc("LDS", 16)

    def sc_dma(fn):
        P.wait("scalar", "STQ", P.cnt["STQ"])
        P.raw("scalar", fn)
        return P.inc("STQ", 16)

    def gload(dst, src):
        return g_dma(lambda e, d=dst, s=src: e.dma_start(out=d, in_=s).then_inc(SM["LDG"], 16))

    if _DBG:
        def dump(nm, ap_):
            for s_ in ("T", "V", "Sc", "LDS", "STQ"):
                P.wait("gpsimd", s_, P.cnt[s_])
            g_dma(lambda e, d=dbg[nm], s=ap_: e.dma_start(out=d, in_=s).then_inc(SM["LDG"], 16))
            for e_ in ("tensor", "vector", "scalar", "sync"):
                P.wait(e_, "LDG", P.cnt["LDG"])
    else:
        def dump(nm, ap_):
            pass

    def ckpt(n):
        if stop_phase is not None and n >= stop_phase:
            raise _StopBuild()

    def pslot(h):
        return ((h % 2) * 64, (h % 2) * 64 + 64, (h // 2) * 64)

    try:
        for nm, src_ in (("eye", eye_in), ("izff", izff_in),
                         ("bqc", bqc_in), ("bkc", bkc_in)):
            sy_dma(lambda e, d=R(nm, {"ipair": 16}.get(nm, 128)), s=src_:
                   e.dma_start(out=d, in_=s).then_inc(SM["LDS"], 16))
        ld_const = P.cnt["LDS"]

        ld_wq = gload(R("w3").rearrange("p (dc d) -> p dc d", dc=8),
                      wq_in.rearrange("(dc p) d -> p dc d", p=128))
        gload(R("izbf"), izf_in)

        def _memsets(e):
            e.memset(R("ones1b", 1), 1.0).then_inc(SM["V"], 1)
            e.memset(Rs("o128b", 0, 1), 1.0).then_inc(SM["V"], 1)
            e.memset(R("ones1f", 1), 1.0).then_inc(SM["V"], 1)
            e.memset(Rs("consts", 0, 1), EPS).then_inc(SM["V"], 1)
            e.memset(Rs("consts", 1, 1), -0.5).then_inc(SM["V"], 1)
            e.memset(R("zrow"), 0.0).then_inc(SM["V"], 1)
        P.raw("vector", _memsets)
        P.inc("V", 6)
        mset_v = P.cnt["V"]

        NCH = 9
        CW = TPAD // NCH
        war_x, war_xb = {}, {}
        for c in range(NCH):
            xb = "xch%d" % (c % 2)
            qeng, qsem = ("sync", "LDS") if c % 2 == 0 else ("scalar", "STQ")
            if c >= 2:
                P.wait(qeng, "V", war_x[c % 2])
            if c % 2 == 0:
                xld = sy_dma(lambda e, b=R(xb), c=c: e.dma_start(
                    out=b.rearrange("p (dc t) -> p dc t", dc=8),
                    in_=xT_in.rearrange("(dc p) t -> p dc t", p=128)[:, :, ts(c, CW)],
                ).then_inc(SM["LDS"], 16))
            else:
                xld = sc_dma(lambda e, b=R(xb), c=c: e.dma_start(
                    out=b.rearrange("p (dc t) -> p dc t", dc=8),
                    in_=xT_in.rearrange("(dc p) t -> p dc t", p=128)[:, :, ts(c, CW)],
                ).then_inc(SM["STQ"], 16))
            P.wait("tensor", "V", mset_v)
            guard_bank(0, 2)
            P.wait("scalar", qsem, xld)
            P.wait("scalar", "LDS", ld_const)
            for dc in range(8):
                sl = (dc % 2) * CW
                if c + dc >= 2:
                    P.wait("scalar", "T", war_xb[dc % 2])
                s_op(lambda e, b=xb, dc=dc, sl=sl: e.copy(Rs("xbf", sl, CW), Rs(b, dc * CW, CW)))
                s_op(lambda e, b=xb, dc=dc, sl=sl: e.activation(Rs("xsq", sl, CW), Rs(b, dc * CW, CW), ACT.Square))
                P.wait("tensor", "Sc", P.cnt["Sc"])
                t_op(lambda e, sl=sl, st=(dc == 0), sp=(dc == 7): e.matmul(
                    pb[0][0:1, 0:CW], Rs("o128b", 0, 1), Rs("xbf", sl, CW), start=st, stop=sp))
                t_op(lambda e, sl=sl, st=(dc == 0), sp=(dc == 7): e.matmul(
                    pb[2][0:1, 0:CW], Rs("o128b", 0, 1), Rs("xsq", sl, CW), start=st, stop=sp))
                war_xb[dc % 2] = P.cnt["T"]
            P.wait("vector", "T", P.cnt["T"])
            v_op(lambda e: e.tensor_scalar(Rs("rows", 0, CW, (0, 1)), pb[0][0:1, 0:CW], 1.0 / DM, None, OP.mult))
            v_op(lambda e: e.tensor_scalar(Rs("rows", CW, CW, (0, 1)), pb[2][0:1, 0:CW], 1.0 / DM, None, OP.mult))
            v_op(lambda e: e.tensor_tensor(Rs("tmp384", 0, CW, (0, 1)), Rs("rows", 0, CW, (0, 1)), Rs("rows", 0, CW, (0, 1)), OP.mult))
            v_op(lambda e: e.tensor_tensor(Rs("rows", CW, CW, (0, 1)), Rs("rows", CW, CW, (0, 1)), Rs("tmp384", 0, CW, (0, 1)), OP.subtract))
            consume_bank(0, 2)
            P.wait("scalar", "V", P.cnt["V"])
            s_op(lambda e: e.activation(Rs("rows", CW, CW, (0, 1)), Rs("rows", CW, CW, (0, 1)), ACT.Ln, bias=EPS))
            s_op(lambda e: e.activation(Rs("rows", CW, CW, (0, 1)), Rs("rows", CW, CW, (0, 1)), ACT.Exp, scale=-0.5))
            P.wait("tensor", "Sc", P.cnt["Sc"])
            P.wait("tensor", "LDS", ld_const)
            guard_bank(4, 6)
            t_op(lambda e: e.matmul(pb[4][:, 0:CW], R("ones1f", 1), Rs("rows", 0, CW, (0, 1)), start=True, stop=True))
            t_op(lambda e: e.matmul(pb[6][:, 0:CW], R("ones1f", 1), Rs("rows", CW, CW, (0, 1)), start=True, stop=True))
            P.wait("vector", "T", P.cnt["T"])
            for dc in range(8):
                v_op(lambda e, b=xb, dc=dc: e.tensor_tensor(Rs("tmp384", 0, CW), Rs(b, dc * CW, CW), pb[4][:, 0:CW], OP.subtract))
                v_op(lambda e, dc=dc, c=c: e.tensor_tensor(
                    Rs("ln1", dc * TPAD + c * CW, CW), Rs("tmp384", 0, CW), pb[6][:, 0:CW], OP.mult))
            consume_bank(4, 6)
            war_x[c % 2] = P.cnt["V"]

        for dc in range(8):
            v_op(lambda e, dc=dc: e.reduce_sum(
                Rs("lf_loc", dc * 8, 8),
                Rs("ln1", dc * TPAD + 128, SLOC).rearrange("p (g t) -> p g t", g=8),
                axis=AX.X))
        v_op(lambda e: e.tensor_scalar(R("lf_loc"), R("lf_loc"), 1.0 / SEG, None, OP.mult))
        P.wait("gpsimd", "V", P.cnt["V"])
        c_dma(lambda e: e.dma_start(out=cc1i.ap(), in_=R("lf_loc")).then_inc(SM["CCD"], 16))
        P.wait("gpsimd", "CCD", P.cnt["CCD"])

        def _cc1(e):
            e.collective_compute("AllGather", OP.bypass, replica_groups=[list(range(NCORE))],
                                 ins=[cc1i.ap().opt()], outs=[cc1o.ap().opt()]).then_inc(SM["CC"], 1)
        P.raw("gpsimd", _cc1)
        P.inc("CC", 1)
        P.wait("gpsimd", "CC", P.cnt["CC"])
        c_dma(lambda e: e.dma_start(
            out=R("lf_all").rearrange("p (s c) -> p s c", s=8),
            in_=cc1o.ap().rearrange("(s p) c -> p s c", p=128)).then_inc(SM["CCD"], 16))
        cc1_done = P.cnt["CCD"]

        P.barrier()
        ckpt(1)

        def proj_qk(wld, out_nm, bias_nm):
            P.wait("tensor", "LDG", wld)
            for pp in range(8):
                for n in range(4):
                    b = [0, 2][(pp * 4 + n) % 2]
                    guard_bank(b)
                    for dc in range(8):
                        t_op(lambda e, b=b, dc=dc, pp=pp, n=n, st=(dc == 0), sp=(dc == 7): e.matmul(
                            pb[b][:],
                            Rs("w3", dc * 1024 + pp * 128, 128),
                            Rs("ln1", dc * TPAD + 128 + n * 512, 512),
                            start=st, stop=sp))
                    P.wait("vector", "T", P.cnt["T"])
                    v_op(lambda e, b=b, pp=pp, n=n, o=out_nm, bs=bias_nm: e.tensor_scalar(
                        Rs(o, pp * SLOC + n * 512, 512),
                        pb[b][:], Rs(bs, pp, 1).opt(), SCALE, OP.add, OP.mult))
                    consume_bank(b)

        proj_qk(ld_wq, "q", "bqc")
        P.wait("gpsimd", "T", P.cnt["T"])
        ld_wk = gload(R("w3").rearrange("p (dc d) -> p dc d", dc=8),
                      wk_in.rearrange("(dc p) d -> p dc d", p=128))
        proj_qk(ld_wk, "k", "bkc")

        P.barrier()
        ckpt(3)

        P.wait("tensor", "CCD", cc1_done)

        def proj_l(out_nm, bias_nm):
            for pp in range(8):
                b = [4, 6][pp % 2]
                guard_bank(b)
                for dc in range(8):
                    t_op(lambda e, b=b, dc=dc, pp=pp, st=(dc == 0), sp=(dc == 7): e.matmul(
                        pb[b][:, 0:64],
                        Rs("w3", dc * 1024 + pp * 128, 128),
                        R("lf_all").rearrange("p (s c g) -> p s c g", s=8, c=8)[:, :, dc, :],
                        start=st, stop=sp))
                P.wait("vector", "T", P.cnt["T"])
                v_op(lambda e, b=b, pp=pp, o=out_nm, bs=bias_nm: e.tensor_scalar(
                    Rs(o, pp * 64, 64), pb[b][:, 0:64], Rs(bs, pp, 1).opt(), SCALE, OP.add, OP.mult))
                consume_bank(b)

        proj_l("kl", "bkc")
        P.wait("gpsimd", "T", P.cnt["T"])
        ld_wq2 = gload(R("w3").rearrange("p (dc d) -> p dc d", dc=8),
                       wq_in.rearrange("(dc p) d -> p dc d", p=128))
        P.wait("tensor", "LDG", ld_wq2)
        proj_l("ql", "bqc")
        dump("kl", R("kl"))
        dump("ql", R("ql"))

        P.barrier()
        ckpt(4)

        for nm, src_ in (("i7", i7_in), ("i15", i15_in), ("i13", i13_in),
                         ("bvb", bvb_in), ("bvlo", bvlo_in), ("bvhi", bvhi_in)):
            sy_dma(lambda e, d=R(nm), s=src_: e.dma_start(out=d, in_=s).then_inc(SM["LDS"], 16))
        ld_i = P.cnt["LDS"]
        P.wait("vector", "LDS", ld_i)
        for hf in range(2):
            P.wait("gpsimd", "T", P.cnt["T"])
            ld_wvh = gload(R("wvh").rearrange("p (dc d) -> p dc d", dc=8),
                           wv_in.rearrange("(dc p) d -> p dc d", p=128)[:, :, ts(hf, 512)])
            P.wait("tensor", "LDG", ld_wvh)
            for g in range(NTP):
                b = [0, 2][g % 2]
                guard_bank(b)
                for dc in range(8):
                    t_op(lambda e, hf=hf, b=b, g=g, dc=dc, st=(dc == 0), sp=(dc == 7): e.matmul(
                        pb[b][:],
                        Rs("ln1", dc * TPAD + g * 128, 128),
                        Rs("wvh", dc * 512, 512),
                        start=st, stop=sp))
                P.wait("vector", "T", P.cnt["T"])
                bv = "bvlo" if g == 0 else ("bvhi" if g == NTP - 1 else "bvb")
                v_op(lambda e, b=b, g=g, hf=hf, bv=bv: e.tensor_tensor(
                    Rs("v", g * DM + hf * 512, 512), pb[b][:], Rs(bv, hf * 512, 512), OP.add))
                consume_bank(b)

        P.barrier()
        ckpt(5)

        for sc in range(NT):
            enm = "e3t%d" % (sc % 2)
            guard_bank(0, 1)
            for h in range(H):
                r0, r1, c0 = pslot(h)
                bk = h % 2
                t_op(lambda e, bk=bk, h=h, sc=sc, r0=r0, r1=r1: e.matmul(
                    pb[bk][:, ts(h // 2, 64)],
                    Rs("k", (h // 2) * SLOC + sc * 128, 128, (r0, r1)),
                    Rs("ql", (h // 2) * 64, 64, (r0, r1)),
                    start=True, stop=True, tile_position=(r0, 0)))
            P.wait("scalar", "T", P.cnt["T"])
            s_op(lambda e, enm=enm: e.activation(
                Rs(enm, 0, 1024).rearrange("p (s two c) -> p s c two", two=2, c=64)[:, :, :, 0],
                pb[0][:], ACT.Exp))
            s_op(lambda e, enm=enm: e.activation(
                Rs(enm, 0, 1024).rearrange("p (s two c) -> p s c two", two=2, c=64)[:, :, :, 1],
                pb[1][:], ACT.Exp))
            consume_bank(0, 1, sem="Sc")
            P.wait("tensor", "Sc", P.cnt["Sc"])
            guard_bank(2, 4)
            for h in range(H):
                r0, r1, c0 = pslot(h)
                t_op(lambda e, enm=enm, h=h, r0=r0, c0=c0, sc=sc: e.matmul(
                    pb[4][r0:r0 + 64, c0:c0 + 64],
                    Rs(enm, (h % 8) * 64 + (h // 8) * 512, 64),
                    Rs("v", (sc + 1) * DM + h * 64, 64),
                    start=True, stop=True, tile_position=(0, r0)))
                t_op(lambda e, enm=enm, h=h, r0=r0: e.matmul(
                    pb[2][r0:r0 + 64, (h // 2):(h // 2) + 1],
                    Rs(enm, (h % 8) * 64 + (h // 8) * 512, 64),
                    R("o128b", 128)[:, 0:1],
                    start=True, stop=True, tile_position=(0, r0)))
            P.wait("vector", "T", P.cnt["T"])
            if sc == 0:
                v_op(lambda e: e.tensor_copy(Rs("blob", 0, 512), pb[4][:]))
                v_op(lambda e: e.tensor_copy(Rs("blob", 512, 8), pb[2][:, 0:8]))
            else:
                v_op(lambda e: e.tensor_tensor(Rs("blob", 0, 512), Rs("blob", 0, 512), pb[4][:], OP.add))
                v_op(lambda e: e.tensor_tensor(Rs("blob", 512, 8), Rs("blob", 512, 8), pb[2][:, 0:8], OP.add))
            consume_bank(2, 4)
        dump("blob", Rs("blob", 0, 520))
        P.wait("gpsimd", "V", P.cnt["V"])
        c_dma(lambda e: e.dma_start(out=cc2i.ap(), in_=Rs("blob", 0, 520)).then_inc(SM["CCD"], 16))
        P.wait("gpsimd", "CCD", P.cnt["CCD"])

        def _cc2(e):
            e.collective_compute("AllGather", OP.bypass, replica_groups=[list(range(NCORE))],
                                 ins=[cc2i.ap().opt()], outs=[cc2o.ap().opt()]).then_inc(SM["CC"], 1)
        P.raw("gpsimd", _cc2)
        P.inc("CC", 1)
        cc2_done = P.cnt["CC"]
        ckpt(7)

        P.barrier()
        ckpt(6)

        guard_bank(0, 1)
        for h in range(H):
            r0, r1, c0 = pslot(h)
            bk = h % 2
            t_op(lambda e, bk=bk, r0=r0, r1=r1, h=h: e.matmul(
                pb[bk][0:64, ts(h // 2, 64)],
                Rs("ql", (h // 2) * 64, 64, (r0, r1)),
                Rs("kl", (h // 2) * 64, 64, (r0, r1)),
                start=True, stop=True, tile_position=(r0, 0)))
        ckpt(5.05)
        P.wait("scalar", "T", P.cnt["T"])
        s_op(lambda e: e.activation(
            Rs("e2", 0, 1024, (0, 64)).rearrange("p (s two c) -> p s c two", two=2, c=64)[:, :, :, 0],
            pb[0][0:64, :], ACT.Exp))
        s_op(lambda e: e.activation(
            Rs("e2", 0, 1024, (0, 64)).rearrange("p (s two c) -> p s c two", two=2, c=64)[:, :, :, 1],
            pb[1][0:64, :], ACT.Exp))
        consume_bank(0, 1, sem="Sc")
        P.wait("vector", "Sc", P.cnt["Sc"])
        v_op(lambda e: e.reduce_sum(Rs("rs8", 0, 16, (0, 64)),
                                    Rs("e2", 0, 1024, (0, 64)).rearrange("p (a m) -> p a m", a=16), axis=AX.X))
        v_op(lambda e: e.reciprocal(Rs("rs8", 0, 16, (0, 64)), Rs("rs8", 0, 16, (0, 64))))
        for h in range(H):
            v_op(lambda e, h=h: e.tensor_scalar(Rs("k2n", h * 64, 64, (0, 64)), Rs("e2", h * 64, 64, (0, 64)),
                                                Rs("rs8", h, 1, (0, 64)).opt(), None, OP.mult))
        ckpt(5.1)
        P.wait("tensor", "V", P.cnt["V"])
        guard_bank(2)
        for h in range(H):
            t_op(lambda e, h=h: e.matmul(
                pb[2][0:16, 0:64],
                Rs("izff", h * 16, 16, (0, 64)),
                Rs("k2n", h * 64, 64, (0, 64)),
                start=(h == 0), stop=(h == H - 1)))
        P.wait("vector", "T", P.cnt["T"])
        v_op(lambda e: e.reduce_max(Rs("rmcol", 0, 1, (0, 16)), pb[2][0:16, 0:64], axis=AX.X))
        consume_bank(2)
        P.wait("tensor", "V", P.cnt["V"])
        guard_bank(4)
        t_op(lambda e: e.transpose(pb[4][0:1, 0:16], Rs("rmcol", 0, 1, (0, 16)), Rs("eye", 0, 16, (0, 16))))
        P.wait("vector", "T", P.cnt["T"])
        v_op(lambda e: e.reduce_max(Rs("rmcol", 1, 1, (0, 1)), pb[4][0:1, 0:16], axis=AX.X))
        consume_bank(4)
        v_op(lambda e: e.reciprocal(Rs("rmcol", 1, 1, (0, 1)), Rs("rmcol", 1, 1, (0, 1))))
        P.wait("tensor", "V", P.cnt["V"])
        guard_bank(6)
        t_op(lambda e: e.matmul(pb[6][:, 0:1], R("ones1f", 1), Rs("rmcol", 1, 1, (0, 1)), start=True, stop=True))
        P.wait("vector", "T", P.cnt["T"])
        v_op(lambda e: e.tensor_copy(Rs("rmcol", 2, 1), pb[6][:, 0:1]))
        consume_bank(6)
        ckpt(5.3)

        def mm16f(lhs_nm, rhs_nm, banks):
            guard_bank(*banks)
            for h in range(H):
                bk = banks[h // 8]
                t_op(lambda e, bk=bk, h=h, ln=lhs_nm, rn=rhs_nm: e.matmul(
                    pb[bk][0:64, ts(h % 8, 64)],
                    Rs(ln, h * 64, 64, (0, 64)),
                    Rs(rn, h * 64, 64, (0, 64)),
                    start=True, stop=True))

        def tr16f(in_nm, banks):
            guard_bank(*banks)
            for h in range(H):
                bk = banks[h // 8]
                t_op(lambda e, bk=bk, h=h, nm=in_nm: e.transpose(
                    pb[bk][0:64, ts(h % 8, 64)],
                    Rs(nm, h * 64, 64, (0, 64)),
                    Rs("eye", 0, 64, (0, 64))))

        def ev2(dst_nm, banks, fn):
            for i, bk in enumerate(banks):
                v_op(lambda e, i=i, bk=bk, d=dst_nm, fn=fn: fn(
                    e, Rs(d, i * 512, 512, (0, 64)), pb[bk][0:64, :], i))
            consume_bank(*banks)

        P.wait("tensor", "V", P.cnt["V"])
        tr16f("k2n", (4, 6))
        P.wait("vector", "T", P.cnt["T"])
        ev2("k2nt", (4, 6), lambda e, d, p, i: e.tensor_copy(d, p))
        v_op(lambda e: e.tensor_scalar(Rs("vi", 0, 1024, (0, 64)), Rs("k2nt", 0, 1024, (0, 64)),
                                       Rs("rmcol", 2, 1, (0, 64)).opt(), None, OP.mult))
        v_op(lambda e: e.tensor_scalar(Rs("vti", 0, 1024, (0, 64)), Rs("k2n", 0, 1024, (0, 64)),
                                       Rs("rmcol", 2, 1, (0, 64)).opt(), None, OP.mult))

        for it in range(6):
            ckpt(5.4 + it / 100.0)
            P.wait("tensor", "V", P.cnt["V"])
            mm16f("k2nt", "vi", (0, 2))
            P.wait("vector", "T", P.cnt["T"])
            ev2("kv", (0, 2), lambda e, d, p, i: e.tensor_copy(d, p))
            for i in range(2):
                v_op(lambda e, i=i: e.scalar_tensor_tensor(
                    Rs("t1", i * 512, 512, (0, 64)), pb[2 * i][0:64, :], -1.0,
                    Rs("i7", i * 512, 512, (0, 64)), OP.mult, OP.add))
            P.wait("tensor", "V", P.cnt["V"])
            tr16f("kv", (4, 6))
            P.wait("vector", "T", P.cnt["T"])
            ev2("kvt", (4, 6), lambda e, d, p, i: e.tensor_copy(d, p))
            P.wait("tensor", "V", P.cnt["V"])
            mm16f("kvt", "t1", (0, 2))
            P.wait("vector", "T", P.cnt["T"])
            ev2("t3", (0, 2), lambda e, d, p, i: e.scalar_tensor_tensor(
                d, p, -1.0, Rs("i15", i * 512, 512, (0, 64)), OP.mult, OP.add))
            P.wait("tensor", "V", P.cnt["V"])
            mm16f("kvt", "t3", (4, 6))
            P.wait("vector", "T", P.cnt["T"])
            ev2("t5", (4, 6), lambda e, d, p, i: e.scalar_tensor_tensor(
                d, p, -1.0, Rs("i13", i * 512, 512, (0, 64)), OP.mult, OP.add))
            P.wait("tensor", "V", P.cnt["V"])
            mm16f("vti", "t5", (0, 2))
            P.wait("vector", "T", P.cnt["T"])
            ev2("vi", (0, 2), lambda e, d, p, i: e.tensor_scalar(d, p, 0.25, None, OP.mult))
            P.wait("tensor", "V", P.cnt["V"])
            tr16f("vi", (4, 6))
            P.wait("vector", "T", P.cnt["T"])
            ev2("vti", (4, 6), lambda e, d, p, i: e.tensor_copy(d, p))

        ckpt(5.8)
        dump("vti", Rs("vti", 0, 1024, (0, 64)))
        P.wait("sync", "V", P.cnt["V"])
        for par in range(2):
            sy_dma(lambda e, par=par: e.dma_start(
                out=FPA[par * 64:(par + 1) * 64,
                        FP.placed["vtd"][0]:FP.placed["vtd"][0] + 512],
                in_=Rs("vti", 0, 1024, (0, 64)).rearrange(
                    "p (a two c) -> p a two c", a=8, two=2)[:, :, par, :],
            ).then_inc(SM["LDS"], 16))
        vtd_done = P.cnt["LDS"]


        ld_ipair = sy_dma(lambda e: e.dma_start(out=R("ipair", 16), in_=ipair_in).then_inc(SM["LDS"], 16))
        for n in range(4):
            for pp in range(8):
                be, bo = [(4, 1), (6, 3)][(n * 8 + pp) % 2]
                guard_bank(be, bo)
                for par in range(2):
                    b = be if par == 0 else bo
                    t_op(lambda e, b=b, pp=pp, n=n, par=par: e.matmul(
                        pb[b][par * 64:par * 64 + 64, :],
                        Rs("kl", pp * 64, 64, (par * 64, par * 64 + 64)),
                        Rs("q", pp * SLOC + n * 512, 512, (par * 64, par * 64 + 64)),
                        start=True, stop=True, tile_position=(par * 64, par * 64)))
                P.wait("scalar", "T", P.cnt["T"])
                s_op(lambda e, b=be, pp=pp, n=n: e.activation(
                    Rs("e1", pp * SLOC + n * 512, 512, (0, 64)), pb[b][0:64, :], ACT.Exp))
                s_op(lambda e, b=bo, pp=pp, n=n: e.activation(
                    Rs("e1", pp * SLOC + n * 512, 512, (64, 128)), pb[b][64:128, :], ACT.Exp))
                consume_bank(be, bo, sem="Sc")
        P.wait("tensor", "Sc", P.cnt["Sc"])
        ckpt(8)
        for n in range(4):
            be, bo = [(0, 1), (2, 3)][n % 2]
            guard_bank(be, bo)
            for h in range(H):
                r0, r1, c0 = pslot(h)
                b = be if h % 2 == 0 else bo
                t_op(lambda e, b=b, h=h, n=n, r0=r0, r1=r1, st=(h < 2), sp=(h >= H - 2): e.matmul(
                    pb[b][0:16, :],
                    Rs("izbf", h * 16, 16, (r0, r1)),
                    Rs("e1", (h // 2) * SLOC + n * 512, 512, (r0, r1)),
                    start=st, stop=sp, tile_position=(r0, 0)))
            P.wait("scalar", "T", P.cnt["T"])
            P.wait("scalar", "V", P.cnt["V"])
            s_op(lambda e, bo=bo: e.activation(Rs("tmp512", 0, 512, (0, 16)),
                                               pb[bo][0:16, :], ACT.Copy))
            P.wait("vector", "T", P.cnt["T"])
            P.wait("vector", "Sc", P.cnt["Sc"])
            v_op(lambda e, be=be: e.tensor_tensor(Rs("tmp512", 0, 512, (0, 16)),
                                                  pb[be][0:16, :], Rs("tmp512", 0, 512, (0, 16)), OP.add))
            v_op(lambda e, n=n: e.reciprocal(Rs("rz1", n * 512, 512, (0, 16)), Rs("tmp512", 0, 512, (0, 16))))
            consume_bank(be, bo)
        P.wait("tensor", "V", P.cnt["V"])
        P.wait("tensor", "LDS", ld_ipair)
        for pp in range(8):
            for n in range(4):
                b = [4, 6][(pp * 4 + n) % 2]
                guard_bank(b)
                t_op(lambda e, b=b, pp=pp, n=n: e.matmul(
                    pb[b][:], Rs("ipair", pp * 128, 128, (0, 16)), Rs("rz1", n * 512, 512, (0, 16)),
                    start=True, stop=True))
                P.wait("vector", "T", P.cnt["T"])
                v_op(lambda e, b=b, pp=pp, n=n: e.tensor_tensor(
                    Rs("e1", pp * SLOC + n * 512, 512),
                    Rs("e1", pp * SLOC + n * 512, 512),
                    pb[b][:], OP.mult))
                consume_bank(b)
        ckpt(9)

        P.wait("gpsimd", "CC", cc2_done)
        g2cnt, g2add = {}, {}
        for s8 in range(8):
            gb = "g2a" if s8 % 2 == 0 else "g2b"
            if s8 >= 2:
                P.wait("gpsimd", "V", g2add[s8 - 2])
            g2cnt[s8] = c_dma(lambda e, gb=gb, s8=s8: e.dma_start(
                out=Rs(gb, 0, 520),
                in_=cc2o.ap().rearrange("(s p) c -> p s c", p=128)[:, s8, :]).then_inc(SM["CCD"], 16))
            P.wait("vector", "CCD", g2cnt[s8])
            if s8 == 0:
                v_op(lambda e, gb=gb: e.tensor_copy(Rs("sumb", 0, 520), Rs(gb, 0, 520)))
            else:
                v_op(lambda e, gb=gb: e.tensor_tensor(Rs("sumb", 0, 520), Rs("sumb", 0, 520),
                                                      Rs(gb, 0, 520), OP.add))
            g2add[s8] = P.cnt["V"]
        v_op(lambda e: e.reciprocal(R("rz3"), Rs("sumb", 512, 8)))
        for a in range(8):
            v_op(lambda e, a=a: e.tensor_scalar(Rs("k3v", a * 64, 64), Rs("sumb", a * 64, 64),
                                                Rs("rz3", a, 1).opt(), None, OP.mult))
        dump("sumb", Rs("sumb", 0, 520))
        dump("k3v", Rs("k3v", 0, 512))
        P.wait("tensor", "V", P.cnt["V"])
        P.wait("tensor", "LDS", vtd_done)
        guard_bank(0, 1)
        for h in range(H):
            r0, r1, c0 = pslot(h)
            bk = h % 2
            t_op(lambda e, bk=bk, r0=r0, r1=r1, c0=c0: e.matmul(
                pb[bk][r0:r1, c0:c0 + 64],
                Rs("vtd", c0, 64, (r0, r1)),
                Rs("k3v", c0, 64, (r0, r1)),
                start=True, stop=True, tile_position=(r0, r0)))
        P.wait("vector", "T", P.cnt["T"])
        v_op(lambda e: e.tensor_copy(Rs("w2", 0, 512, (0, 64)), pb[0][0:64, :]))
        v_op(lambda e: e.tensor_copy(Rs("w2", 0, 512, (64, 128)), pb[1][64:128, :]))
        consume_bank(0, 1)
        dump("w2", R("w2"))

        P.wait("sync", "V", P.cnt["V"])
        sy_dma(lambda e: e.dma_start(
            out=BFA[0:32, BF.placed["vA"][0]:BF.placed["vA"][0] + 17 * DM],
            in_=Rs("v", 0, 17 * DM, (96, 128))).then_inc(SM["LDS"], 16))
        sy_dma(lambda e: e.dma_start(
            out=BFA[32:128, BF.placed["vA"][0]:BF.placed["vA"][0] + 17 * DM],
            in_=Rs("v", DM, 17 * DM, (0, 96))).then_inc(SM["LDS"], 16))
        va_done = P.cnt["LDS"]

        P.barrier()
        ckpt(10)

        ld_tconv = gload(R("tconv"), tconv_in)
        ld_wo = gload(R("woT").rearrange("p (fc d) -> p fc d", fc=32),
                      woT_in.rearrange("(fc p) d -> p fc d", p=128))
        P.wait("tensor", "LDG", ld_tconv)
        P.wait("tensor", "LDS", va_done)

        war_xtm, war_x1c = {}, {}
        for sc in range(NT):
            xnm = "xtm%d" % (sc % 2)
            if sc >= 2:
                P.wait("sync", "V", war_xtm[sc % 2])
            sy_dma(lambda e, d=R(xnm), sc=sc: e.dma_start(out=d, in_=xtm_in[ts(sc, 128), :]).then_inc(SM["LDS"], 16))
            bm = (0, 1, 2, 3) if sc % 2 == 0 else (4, 5, 6, 7)
            guard_bank(*bm)
            for par in range(2):
                for s in range(8):
                    h = 2 * s + par
                    t_op(lambda e, bk=bm[par], h=h, s=s, row=par * 64, sc=sc, st=(s == 0): e.matmul(
                        pb[bk][:, ts(s, 64)],
                        Rs("e1", (h // 2) * SLOC + sc * 128, 128, (row, row + 64)),
                        Rs("w2", (h // 2) * 64, 64, (row, row + 64)),
                        start=st, stop=False, tile_position=(row, 0)))
            t_op(lambda e, bk=bm[2]: e.matmul(pb[bk][:], Rs("zrow", 0, 128, (0, 1)),
                                    Rs("zrow", 0, 512, (0, 1)), start=True, stop=False))
            t_op(lambda e, bk=bm[3]: e.matmul(pb[bk][:], Rs("zrow", 0, 128, (64, 65)),
                                    Rs("zrow", 0, 512, (64, 65)), start=True, stop=False,
                                    tile_position=(64, 0)))
            for h in range(H):
                for m_par in range(2):
                    m = sc * 2 + m_par
                    if m % 2 == 0:
                        pieces = ((0, 0, 128, m // 2),)
                    else:
                        pieces = ((1, 64, 64, (m - 1) // 2), (2, 0, 64, (m + 1) // 2))
                    for (cls, base, kk, j) in pieces:
                        row = 64 if base == 64 else 0
                        bk = (0 if h % 2 == 0 else 2) if row == 0 else (1 if h % 2 == 1 else 3)
                        t_op(lambda e, bk=bm[bk], h=h, s=h // 2, ps0=m_par * 64, cls=cls,
                             base=base, kk=kk, j=j: e.matmul(
                            pb[bk][ps0:ps0 + 64, ts(s, 64)],
                            Rs("tconv", (h * 3 + cls) * 64, 64, (base, base + kk)),
                            Rs("vA", j * DM + h * 64, 64, (base, base + kk)),
                            start=False, stop=False, tile_position=(base, ps0)))
            t_op(lambda e, bk=bm[0]: e.matmul(pb[bk][:], Rs("zrow", 0, 128, (0, 1)),
                                    Rs("zrow", 0, 512, (0, 1)), start=False, stop=True))
            t_op(lambda e, bk=bm[2]: e.matmul(pb[bk][:], Rs("zrow", 0, 128, (0, 1)),
                                    Rs("zrow", 0, 512, (0, 1)), start=False, stop=True))
            t_op(lambda e, bk=bm[1]: e.matmul(pb[bk][:], Rs("zrow", 0, 128, (64, 65)),
                                    Rs("zrow", 0, 512, (64, 65)), start=False, stop=True,
                                    tile_position=(64, 0)))
            t_op(lambda e, bk=bm[3]: e.matmul(pb[bk][:], Rs("zrow", 0, 128, (64, 65)),
                                    Rs("zrow", 0, 512, (64, 65)), start=False, stop=True,
                                    tile_position=(64, 0)))
            P.wait("vector", "T", P.cnt["T"])
            P.wait("vector", "LDS", P.cnt["LDS"])
            cnm = "x1c%d" % (sc % 2)
            if sc >= 2:
                P.wait("vector", "STQ", war_x1c[sc % 2])

            def pstr(nm, par):
                return Rs(nm, 0, DM).rearrange("p (s two c) -> p s c two", two=2, c=64)[:, :, :, par]
            v_op(lambda e, xnm=xnm, bk=bm[0]: e.tensor_tensor(R("ctmp"), pstr(xnm, 0), pb[bk][:], OP.add))
            v_op(lambda e, cnm=cnm, bk=bm[3]: e.tensor_tensor(pstr(cnm, 0), R("ctmp"), pb[bk][:], OP.add))
            v_op(lambda e, xnm=xnm, bk=bm[1]: e.tensor_tensor(R("ctmp"), pstr(xnm, 1), pb[bk][:], OP.add))
            v_op(lambda e, cnm=cnm, bk=bm[2]: e.tensor_tensor(pstr(cnm, 1), R("ctmp"), pb[bk][:], OP.add))
            consume_bank(*bm)
            war_xtm[sc % 2] = P.cnt["V"]
            v_op(lambda e, cnm=cnm, sc=sc: e.reduce_sum(Rs("s1", sc, 1), Rs(cnm, 0, DM), axis=AX.X))
            P.wait("scalar", "V", P.cnt["V"])
            s_op(lambda e, cnm=cnm: e.activation(R("trash"), Rs(cnm, 0, DM), ACT.Square))
            P.wait("vector", "Sc", P.cnt["Sc"])
            v_op(lambda e, sc=sc: e.reduce_sum(Rs("s2", sc, 1), R("trash"), axis=AX.X))
            sc_dma(lambda e, cnm=cnm, sc=sc: e.dma_start(out=x1b[ts(sc, 128), :], in_=Rs(cnm, 0, DM)).then_inc(SM["STQ"], 16))
            war_x1c[sc % 2] = P.cnt["STQ"]

        P.barrier()
        ckpt(11)

        P.wait("vector", "STQ", P.cnt["STQ"])
        v_op(lambda e: e.tensor_scalar(R("s1"), R("s1"), 1.0 / DM, None, OP.mult))
        v_op(lambda e: e.tensor_tensor(R("mx"), R("s1"), R("s1"), OP.mult))
        v_op(lambda e: e.tensor_scalar(R("s2"), R("s2"), 1.0 / DM, None, OP.mult))
        v_op(lambda e: e.tensor_tensor(R("s2"), R("s2"), R("mx"), OP.subtract))
        P.wait("scalar", "V", P.cnt["V"])
        s_op(lambda e: e.activation(R("s2"), R("s2"), ACT.Ln, bias=EPS))
        s_op(lambda e: e.activation(R("s2"), R("s2"), ACT.Exp, scale=-0.5))
        P.wait("tensor", "Sc", P.cnt["Sc"])
        guard_bank(0)
        t_op(lambda e: e.transpose(pb[0][0:16, 0:128], R("s1"), R("eye")))
        t_op(lambda e: e.transpose(pb[0][0:16, 128:256], R("s2"), R("eye")))
        P.wait("vector", "T", P.cnt["T"])
        v_op(lambda e: e.tensor_copy(Rs("st16", 0, 256, (0, 16)), pb[0][0:16, 0:256]))
        consume_bank(0)
        P.wait("sync", "V", P.cnt["V"])
        sy_dma(lambda e: e.dma_start(out=Rs("r2rows", 0, SLOC, (0, 1)), in_=Rs("st16", 0, 128, (0, 16))).then_inc(SM["LDS"], 16))
        sy_dma(lambda e: e.dma_start(out=Rs("r2rows", SLOC, SLOC, (0, 1)), in_=Rs("st16", 128, 128, (0, 16))).then_inc(SM["LDS"], 16))
        P.wait("tensor", "LDS", P.cnt["LDS"])

        war_x1r, x1r_cnt = {}, {}
        for scg in range(8):
            for w in range(2):
                b = [4, 6][w]
                guard_bank(b)
                t_op(lambda e, b=b, scg=scg, w=w: e.matmul(
                    pb[b][:, 0:256], R("ones1f", 1), Rs("r2rows", w * SLOC + scg * 256, 256, (0, 1)),
                    start=True, stop=True))
                P.wait("vector", "T", P.cnt["T"])
                dnm = "r2mu" if w == 0 else "r2sig"
                v_op(lambda e, b=b, d=dnm: e.tensor_copy(Rs(d, 0, 256), pb[b][:, 0:256]))
                consume_bank(b)
            for j in range(2):
                bi = scg * 2 + j
                slot = bi % 4
                qeng, qsem = [("sync", "LDS"), ("scalar", "STQ"),
                              ("gpsimd", "LDG")][bi % 3]
                if bi >= 4:
                    P.wait(qeng, "T", war_x1r[slot])
                if qeng == "sync":
                    x1r_cnt[bi] = (qsem, sy_dma(lambda e, slot=slot, bi=bi: e.dma_start(
                        out=R("x1r%d" % slot), in_=x1b[ts(bi, 128), :]).then_inc(SM["LDS"], 16)))
                elif qeng == "scalar":
                    x1r_cnt[bi] = (qsem, sc_dma(lambda e, slot=slot, bi=bi: e.dma_start(
                        out=R("x1r%d" % slot), in_=x1b[ts(bi, 128), :]).then_inc(SM["STQ"], 16)))
                else:
                    x1r_cnt[bi] = (qsem, gload(R("x1r%d" % slot), x1b[ts(bi, 128), :]))
            for j in range(2):
                qsem, cnt = x1r_cnt[scg * 2 + j]
                P.wait("tensor", qsem, cnt)
            for dc in range(8):
                b = [0, 2][dc % 2]
                guard_bank(b)
                for j in range(2):
                    t_op(lambda e, b=b, dc=dc, j=j, slot=(scg * 2 + j) % 4: e.transpose(
                        pb[b][:, ts(j, 128)], Rs("x1r%d" % slot, dc * 128, 128), R("eye")))
                P.wait("vector", "T", P.cnt["T"])
                v_op(lambda e, b=b: e.tensor_tensor(
                    Rs("tmp512", 0, 256), pb[b][:, 0:256], Rs("r2mu", 0, 256), OP.subtract))
                v_op(lambda e, dc=dc, scg=scg: e.tensor_tensor(
                    Rs("ln2", dc * SLOC + scg * 256, 256),
                    Rs("tmp512", 0, 256), Rs("r2sig", 0, 256), OP.mult))
                consume_bank(b)
            for j in range(2):
                war_x1r[(scg * 2 + j) % 4] = P.cnt["T"]

        P.barrier()
        ckpt(12)

        ld_gate = gload(R("gateT").rearrange("p (dc e) -> p dc e", dc=8),
                        gateT_in.rearrange("(dc p) e -> p dc e", p=128))
        ld_acat = gload(R("acatT").rearrange("p (dc r) -> p dc r", dc=8),
                        acatT_in.rearrange("(dc p) r -> p dc r", p=128))
        ld_im4 = gload(R("im4", 8), im4_in)
        for nm, src_ in (("gb16", gb16_in), ("p216", p216_in)):
            sy_dma(lambda e, d=R(nm), s=src_: e.dma_start(out=d, in_=s).then_inc(SM["LDS"], 16))
        P.wait("tensor", "LDS", P.cnt["LDS"])
        P.wait("tensor", "LDG", ld_acat)
        for n in range(4):
            b = [0, 2][n % 2]
            guard_bank(b)
            for dc in range(8):
                t_op(lambda e, b=b, dc=dc, n=n, st=(dc == 0), sp=(dc == 7): e.matmul(
                    pb[b][0:8, :], Rs("gateT", dc * 8, 8),
                    Rs("ln2", dc * SLOC + n * 512, 512), start=st, stop=sp))
            P.wait("vector", "T", P.cnt["T"])
            v_op(lambda e, b=b, n=n: e.tensor_copy(Rs("glog", n * 512, 512, (0, 8)), pb[b][0:8, :]))
            consume_bank(b)
            b2 = [4, 6][n % 2]
            guard_bank(b2)
            for dc in range(8):
                t_op(lambda e, b=b2, dc=dc, n=n, st=(dc == 0), sp=(dc == 7): e.matmul(
                    pb[b][0:32, :], Rs("acatT", dc * 32, 32),
                    Rs("ln2", dc * SLOC + n * 512, 512), start=st, stop=sp))
            P.wait("vector", "T", P.cnt["T"])
            v_op(lambda e, b=b2, n=n: e.tensor_copy(Rs("u", n * 512, 512, (0, 32)), pb[b][0:32, :]))
            consume_bank(b2)
        P.wait("tensor", "V", P.cnt["V"])
        guard_bank(0)
        for sc in range(NT):
            t_op(lambda e, sc=sc: e.transpose(pb[0][:, ts(sc, 8)], Rs("glog", sc * 128, 128, (0, 8)),
                                              Rs("eye", 0, 8, (0, 8))))
        P.wait("vector", "T", P.cnt["T"])
        v_op(lambda e: e.tensor_tensor(R("lg"), pb[0][:, 0:128], R("gb16"), OP.add))
        consume_bank(0)
        v_op(lambda e: e.reduce_max(R("mx"), R("lg").rearrange("p (a b) -> p a b", a=16), axis=AX.X))
        for sc in range(NT):
            v_op(lambda e, sc=sc: e.tensor_scalar(Rs("eqw", sc * 8, 8), Rs("lg", sc * 8, 8),
                                                  Rs("mx", sc, 1).opt(), None, OP.is_ge))
        v_op(lambda e: e.tensor_tensor(R("eqw"), R("eqw"), R("p216"), OP.mult))
        v_op(lambda e: e.reduce_max(R("mx"), R("eqw").rearrange("p (a b) -> p a b", a=16), axis=AX.X))
        for sc in range(NT):
            v_op(lambda e, sc=sc: e.tensor_scalar(Rs("mask", sc * 8, 8), Rs("eqw", sc * 8, 8),
                                                  Rs("mx", sc, 1).opt(), None, OP.is_ge))
        P.wait("tensor", "V", P.cnt["V"])
        for grp in range(4):
            b = [4, 6][grp % 2]
            guard_bank(b)
            for j in range(4):
                sc = grp * 4 + j
                t_op(lambda e, b=b, j=j, sc=sc: e.transpose(
                    pb[b][0:8, ts(j, 128)], Rs("mask", sc * 8, 8), R("eye")))
            P.wait("vector", "T", P.cnt["T"])
            v_op(lambda e, b=b, grp=grp: e.tensor_copy(Rs("mfm", grp * 512, 512, (0, 8)), pb[b][0:8, :]))
            consume_bank(b)
        P.wait("tensor", "V", P.cnt["V"])
        P.wait("tensor", "LDG", ld_im4)
        for n in range(4):
            b = [0, 2][n % 2]
            guard_bank(b)
            t_op(lambda e, b=b, n=n: e.matmul(pb[b][0:32, :], R("im4", 8), Rs("mfm", n * 512, 512, (0, 8)),
                                              start=True, stop=True))
            P.wait("vector", "T", P.cnt["T"])
            v_op(lambda e, b=b, n=n: e.tensor_tensor(Rs("z", n * 512, 512, (0, 32)),
                                                     Rs("u", n * 512, 512, (0, 32)), pb[b][0:32, :], OP.mult))
            consume_bank(b)

        P.barrier()
        ckpt(13)

        ld_bcat = gload(R("bcatT", 32), bcatT_in)
        ld_wob = gload(R("wob", 1), wob_in)
        sy_dma(lambda e: e.dma_start(out=R("wibc"), in_=wibc_in).then_inc(SM["LDS"], 16))
        ld_wibc = P.cnt["LDS"]
        P.wait("tensor", "LDG", ld_wo)
        P.wait("tensor", "LDG", ld_wob)
        P.wait("scalar", "LDS", ld_wibc)
        wi_guard = {}
        war_wo, war_out = {}, {}
        for tc in range(NTC):
            for fc in range(32):
                slot = fc % 4
                if tc * 32 + fc >= 4:
                    P.wait("gpsimd", "T", wi_guard[slot])
                P.raw("gpsimd", lambda e, fc=fc, slot=slot: e.dma_start(
                    out=R("wib%d" % slot), in_=wiT_in[:, ts(fc, 1024)]).then_inc(SM["LDW%d" % slot], 16))
                P.inc("LDW%d" % slot, 16)
                P.wait("tensor", "LDW%d" % slot, P.cnt["LDW%d" % slot])
                b = [0, 2][fc % 2]
                guard_bank(b)
                for dc in range(8):
                    t_op(lambda e, b=b, slot=slot, dc=dc, tc=tc, st=(dc == 0): e.matmul(
                        pb[b][:], Rs("wib%d" % slot, dc * 128, 128),
                        Rs("ln2", dc * SLOC + tc * TC, TC), start=st, stop=False))
                t_op(lambda e, b=b, fc=fc, tc=tc: e.matmul(
                    pb[b][:], Rs("bcatT", fc * 128, 128, (0, 32)),
                    Rs("z", tc * TC, TC, (0, 32)), start=False, stop=True))
                wi_guard[slot] = P.cnt["T"]
                P.wait("scalar", "T", P.cnt["T"])
                s_op(lambda e, b=b, fc=fc: e.activation(
                    Rs("inter", fc * TC, TC), pb[b][:], ACT.Relu, bias=Rs("wibc", fc, 1).opt()))
                consume_bank(b, sem="Sc")
            P.wait("tensor", "Sc", P.cnt["Sc"])
            for tt in range(TC // 128):
                sc = tc * (TC // 128) + tt
                xj = sc % 4
                if sc >= 4:
                    P.wait("sync", "V", war_wo[xj])
                sy_dma(lambda e, xj=xj, sc=sc: e.dma_start(out=R("x1r%d" % xj), in_=x1b[ts(sc, 128), :]).then_inc(SM["LDS"], 16))
                for dh in range(2):
                    b = [4, 6][(tt * 2 + dh) % 2]
                    guard_bank(b)
                    for fc in range(32):
                        t_op(lambda e, b=b, fc=fc, tt=tt, dh=dh, st=(fc == 0): e.matmul(
                            pb[b][:], Rs("inter", fc * TC + tt * 128, 128),
                            Rs("woT", fc * DM + dh * 512, 512), start=st, stop=False))
                    t_op(lambda e, b=b, dh=dh: e.matmul(
                        pb[b][:], R("ones1b", 1), Rs("wob", dh * 512, 512, (0, 1)), start=False, stop=True))
                    P.wait("vector", "T", P.cnt["T"])
                    P.wait("vector", "LDS", P.cnt["LDS"])
                    onm = "out%d" % (sc % 2)
                    if sc >= 2 and dh == 0:
                        P.wait("vector", "STQ", war_out[sc % 2])
                    v_op(lambda e, b=b, onm=onm, xj=xj, dh=dh: e.tensor_tensor(
                        Rs(onm, dh * 512, 512), Rs("x1r%d" % xj, dh * 512, 512), pb[b][:], OP.add))
                    consume_bank(b)
                war_wo[xj] = P.cnt["V"]
                P.wait("scalar", "V", P.cnt["V"])
                sc_dma(lambda e, onm="out%d" % (sc % 2), sc=sc: e.dma_start(
                    out=y_out[ts(sc, 128), :], in_=Rs(onm, 0, DM)).then_inc(SM["STQ"], 16))
                war_out[sc % 2] = P.cnt["STQ"]

        P.wait("scalar", "STQ", P.cnt["STQ"])
    except _StopBuild:
        pass

    with nc.Block() as block:
        def run(eng_name):
            def body(e):
                for st_ in P.steps[eng_name]:
                    if isinstance(st_, tuple):
                        e.wait_ge(sems[st_[1]], st_[2])
                    else:
                        st_(e)
            return body
        block.tensor(run("tensor"))
        block.vector(run("vector"))
        block.scalar(run("scalar"))
        block.gpsimd(run("gpsimd"))
        block.sync(run("sync"))

    _elide_ldweights(nc)
    es.close()
    return nc


_MM_FLDS = ("replication_resolution", "replication_shift_amnt", "replication_num_rows",
            "start_tensor_calc", "stop_tensor_calc", "is_transpose", "is_fmap_onezero",
            "is_weight_onezero", "ifmap_quant_offset", "weights_quant_offset", "tile_size",
            "tile_position", "perf_mode", "bass_skip_group_check", "name", "engine",
            "debug", "ins", "outs", "sync_info", "emission_index", "descendants")


def _elide_ldweights(nc):
    n = 0
    for fn in nc.m.functions:
        for bl in fn.blocks:
            if "_PE_" not in bl.name:
                continue
            il = bl.instructions
            prev = None
            for i, ins in enumerate(il):
                tn = type(ins).__name__
                if tn == "InstEventSemaphore":
                    continue
                if tn != "InstMatmult" or ins.is_transpose:
                    prev = None
                    continue
                try:
                    a = ins.ins[1]
                    if "bfloat16" not in str(a.dtype) or "bfloat16" not in str(ins.ins[0].dtype):
                        prev = None
                        continue
                    if a.ap[1][1] != ins.outs[0].ap[0][1]:
                        prev = None
                        continue
                    k = (a.memref, a.offset, str(a.ap), str(a.dtype), ins.tile_position)
                except Exception:
                    prev = None
                    continue
                if k == prev:
                    kw = {f: getattr(ins, f) for f in _MM_FLDS}
                    kw["ldweights"] = False
                    il[i] = mybir.InstMatmult(**kw)
                    n += 1
                else:
                    prev = k
    return n


_CACHE = {}


def _host_stage(inputs):
    f32 = np.float32
    x = np.asarray(inputs["L0_attn_hidden_states"], f32)[0]
    Wq = np.asarray(inputs["L0_attn_Wq"], f32)
    bq = np.asarray(inputs["L0_attn_bq"], f32)
    Wk = np.asarray(inputs["L0_attn_Wk"], f32)
    bk = np.asarray(inputs["L0_attn_bk"], f32)
    Wv = np.asarray(inputs["L0_attn_Wv"], f32)
    bv = np.asarray(inputs["L0_attn_bv"], f32)
    conv_w = np.asarray(inputs["L0_attn_conv_w"], f32)
    gW = np.asarray(inputs["L0_ffn_gate_W"], f32)
    gb = np.asarray(inputs["L0_ffn_gate_b"], f32)
    wiW = np.asarray(inputs["L0_ffn_wi_W"], f32)
    wib = np.asarray(inputs["L0_ffn_wi_b"], f32)
    woW = np.asarray(inputs["L0_ffn_wo_W"], f32)
    wob = np.asarray(inputs["L0_ffn_wo_b"], f32)
    lA = np.asarray(inputs["L0_ffn_lora_A"], f32)
    lB = np.asarray(inputs["L0_ffn_lora_B"], f32)

    xpad = np.zeros((S + 256, DM), f32)
    xpad[128:128 + S] = x

    wiT = np.ascontiguousarray(wiW.T)
    wiT_pm = np.ascontiguousarray(
        wiT.reshape(8, 128, 32, 128).transpose(1, 2, 0, 3)).reshape(128, 32 * 1024)
    woT = np.ascontiguousarray(woW.T)
    gateT = np.ascontiguousarray(gW.T)
    acatT = np.ascontiguousarray(lA.reshape(32, DM).T)
    bcatT = np.ascontiguousarray(lB.transpose(0, 2, 1).reshape(32, FF))

    w65 = conv_w[:, 0, :, 0]
    Tfull = np.zeros((H, 128, 64), f32)
    for p in range(64):
        Tfull[:, p:p + 65, p] = w65
    tconv = np.zeros((H, 3, 128, 64), f32)
    tconv[:, 0] = Tfull
    tconv[:, 1, 64:128] = Tfull[:, 0:64]
    tconv[:, 2, 0:64] = Tfull[:, 64:128]
    tconv_pm = np.ascontiguousarray(tconv.transpose(2, 0, 1, 3)).reshape(128, H * 3 * 64)

    eye = np.eye(128, dtype=f32)
    izf = np.zeros((128, 16 * H), f32)
    for h in range(H):
        izf[(h % 2) * 64:(h % 2) * 64 + 64, 16 * h + h] = 1.0
    izff = np.zeros((128, 256), f32)
    for h in range(H):
        izff[0:64, 16 * h + h] = 1.0
    ipair = np.zeros((16, 8, 128), f32)
    for pp in range(8):
        ipair[2 * pp, pp, 0:64] = 1.0
        ipair[2 * pp + 1, pp, 64:128] = 1.0
    ipair = ipair.reshape(16, 1024)
    im4 = np.zeros((E8, 32), f32)
    for e_ in range(E8):
        im4[e_, 4 * e_:4 * e_ + 4] = 1.0
    eyep = np.zeros((128, 1024), f32)
    for a in range(16):
        eyep[0:64, a * 64:(a + 1) * 64] = np.eye(64)
    bvb = np.broadcast_to(bv, (128, DM)).copy()
    bqc = np.ascontiguousarray(bq.reshape(8, 128).T)
    bkc = np.ascontiguousarray(bk.reshape(8, 128).T)
    wibc = np.ascontiguousarray(wib.reshape(32, 128).T)
    gb16 = np.tile(np.broadcast_to(gb, (128, 8)), (1, 16))
    p216 = np.tile((2.0 ** -np.arange(8, dtype=f32))[None, :], (128, 16))

    shared = dict(wq=Wq, wk=Wk, wv=Wv, wiT=wiT_pm, woT=woT, gateT=gateT,
                  acatT=acatT, bcatT=bcatT, tconv=tconv_pm, eye=eye, izf=izf,
                  izff=izff, ipair=ipair, im4=im4, i7=7.0 * eyep,
                  i15=15.0 * eyep, i13=13.0 * eyep, bvb=bvb, bqc=bqc, bkc=bkc,
                  wibc=wibc, gb16=gb16, p216=p216, wob=wob.reshape(1, DM))
    in_maps = []
    for c in range(NCORE):
        sl = xpad[c * SLOC: c * SLOC + TPAD]
        m = dict(shared)
        m["xT"] = np.ascontiguousarray(sl.T)
        m["xtm"] = np.ascontiguousarray(x[c * SLOC:(c + 1) * SLOC])
        m["bvlo"] = np.zeros_like(bvb) if c == 0 else bvb
        m["bvhi"] = np.zeros_like(bvb) if c == NCORE - 1 else bvb
        in_maps.append(m)
    return in_maps


def kernel(**inputs):
    in_maps = _host_stage(inputs)
    if "nc" not in _CACHE:
        _CACHE["nc"] = build_program()
    res = run_bass_kernel_spmd(_CACHE["nc"], in_maps, core_ids=list(range(NCORE)))
    out = np.concatenate([np.asarray(r["y"], np.float32) for r in res.results], axis=0)
    return out.reshape(1, S, DM)


# version 16 (final)
# speedup vs baseline: 1.0066x; 1.0042x over previous
import os
import numpy as np
from contextlib import ExitStack

import concourse.bass as bass
import concourse.mybir as mybir
from concourse.bass_utils import run_bass_kernel_spmd
from concourse.alu_op_type import AluOpType as OP

F32 = mybir.dt.float32
BF16 = mybir.dt.bfloat16
ACT = mybir.ActivationFunctionType
AX = mybir.AxisListType

_DBG = os.environ.get("KDBG") == "1"

NCORE = 8
S, DM, H, DH, LM, SEG = 16384, 1024, 16, 64, 64, 256
E8, R4, FF, KSZ = 8, 4, 4096, 65
SLOC = S // NCORE
NT = SLOC // 128
NTP = NT + 2
TPAD = NTP * 128
SCALE = float(DH) ** -0.25
EPS = 1e-5
TC = 512
NTC = SLOC // TC


class Sched:
    ENGS = ("tensor", "vector", "scalar", "gpsimd", "sync")
    BAR_SEMS = ("T", "V", "Sc", "LDG", "LDS", "STQ")

    def __init__(self):
        self.steps = {e: [] for e in self.ENGS}
        self.cnt = {}

    def sem(self, name):
        self.cnt.setdefault(name, 0)

    def raw(self, eng, fn):
        self.steps[eng].append(fn)

    def inc(self, name, k):
        self.cnt[name] += k
        return self.cnt[name]

    def wait(self, eng, name, val=None):
        v = self.cnt[name] if val is None else val
        if v <= 0:
            return
        self.steps[eng].append(("W", name, v))

    def barrier(self):
        snap = [(s, self.cnt[s]) for s in self.BAR_SEMS if self.cnt[s] > 0]
        for e in self.ENGS:
            for s, v in snap:
                self.steps[e].append(("W", s, v))


class Arena:

    def __init__(self, name):
        self.name = name
        self.regions = []

    def add(self, name, size, p0, p1):
        self.regions.append((name, size, p0, p1))

    def pack(self):
        def run(order):
            placed = {}
            for name, size, p0, p1 in order:
                pad = (size + 15) // 16 * 16
                spans = sorted((v for v in placed.values()
                                if not (v[3] < p0 or v[2] > p1)), key=lambda v: v[0])
                off = 0
                for (o2, s2, _, _, _) in spans:
                    if off + pad <= o2:
                        break
                    off = max(off, o2 + s2)
                placed[name] = (off, pad, p0, p1, size)
            tot = max((o + s for o, s, _, _, _ in placed.values()), default=0)
            return placed, tot
        best = None
        for key in (lambda r: (-r[1],), lambda r: (r[2], -r[1]),
                    lambda r: (-(r[3] - r[2]), -r[1]), lambda r: (r[2], r[3], -r[1])):
            placed, tot = run(sorted(self.regions, key=key))
            if best is None or tot < best[1]:
                best = (placed, tot)
        rng = np.random.default_rng(12345)
        base = sorted(self.regions, key=lambda r: (-r[1],))
        for _ in range(400):
            order = list(base)
            rng.shuffle(order)
            placed, tot = run(order)
            if tot < best[1]:
                best = (placed, tot)
        self.placed, self.total = best
        return self.total


def ts(i, n):
    return slice(i * n, (i + 1) * n)


class _StopBuild(Exception):
    pass


def build_program(stop_phase=None):
    nc = bass.Bass("TRN2", target_bir_lowering=False)
    P = Sched()
    SEMS = ("T", "V", "Sc", "LDG", "LDS", "STQ", "CC", "CCD",
            "LDW0", "LDW1", "LDW2", "LDW3")
    for s_ in SEMS:
        P.sem(s_)

    def din(name, shape, dt=F32):
        return nc.dram_tensor(name, list(shape), dt, kind="ExternalInput").ap()

    xT_in = din("xT", (DM, TPAD))
    xtm_in = din("xtm", (SLOC, DM))
    wq_in = din("wq", (DM, DM))
    wk_in = din("wk", (DM, DM))
    wv_in = din("wv", (DM, DM))
    wiT_in = din("wiT", (128, 32 * 1024))
    woT_in = din("woT", (FF, DM))
    gateT_in = din("gateT", (DM, E8))
    acatT_in = din("acatT", (DM, 32))
    bcatT_in = din("bcatT", (32, FF))
    tconv_in = din("tconv", (128, H * 3 * 64))
    eye_in = din("eye", (128, 128))
    izf_in = din("izf", (128, 16 * H))
    izff_in = din("izff", (128, 256))
    ipair_in = din("ipair", (16, 1024))
    im4_in = din("im4", (E8, 32))
    i7_in = din("i7", (128, 1024))
    i15_in = din("i15", (128, 1024))
    i13_in = din("i13", (128, 1024))
    bvb_in = din("bvb", (128, DM))
    bvlo_in = din("bvlo", (128, DM))
    bvhi_in = din("bvhi", (128, DM))
    bqc_in = din("bqc", (128, 8))
    bkc_in = din("bkc", (128, 8))
    wibc_in = din("wibc", (128, 32))
    gb16_in = din("gb16", (128, 128))
    p216_in = din("p216", (128, 128))
    wob_in = din("wob", (1, DM))
    y_out = nc.dram_tensor("y", [SLOC, DM], F32, kind="ExternalOutput").ap()

    x1b = nc.dram_tensor("x1b", [SLOC, DM], F32).ap()
    cc1i = nc.dram_tensor("cc1i", [128, 64], F32)
    cc1o = nc.dram_tensor("cc1o", [NCORE * 128, 64], F32)
    cc2i = nc.dram_tensor("cc2i", [128, 520], F32)
    cc2o = nc.dram_tensor("cc2o", [NCORE * 128, 520], F32)

    if _DBG:
        dbg = {nm: nc.dram_tensor("dbg_" + nm, list(shp), F32, kind="ExternalOutput").ap()
               for nm, shp in (("ql", (128, 512)), ("kl", (128, 512)),
                               ("vti", (64, 1024)), ("k3v", (128, 512)),
                               ("w2", (128, 512)), ("sumb", (128, 520)),
                               ("blob", (128, 520)))}

    BF = Arena("bf")
    FP = Arena("fp")
    for nm, sz, p0, p1 in (
        ("ln1", 8 * TPAD, 0, 4), ("ln2", 8 * SLOC, 11, 15),
        ("q", 8 * SLOC, 1, 7), ("k", 8 * SLOC, 2, 6), ("v", NTP * DM, 4, 9),
        ("inter", 32 * TC, 13, 15), ("vA", 17 * DM, 9, 10),
        ("e1", 8 * SLOC, 7, 10), ("woT", 32 * DM, 10, 15),
        ("w3", 8 * DM, 0, 3), ("wvh", 8 * 512, 1.5, 4), ("wkx", 4096, 1.5, 3),
        ("xbf", 512, 0, 0), ("xsq", 512, 0, 0),
        ("e3t0", 1024, 4.6, 4.6), ("e3t1", 1024, 4.6, 4.6),
        ("wib0", 1024, 13, 15), ("wib1", 1024, 13, 15),
        ("wib2", 1024, 13, 15), ("wib3", 1024, 13, 15),
        ("z", SLOC, 12, 15), ("mfm", SLOC, 12, 12), ("u", SLOC, 12, 12),
        ("ql", 512, 3, 9), ("kl", 512, 3, 7), ("w2", 512, 9, 10),
        ("lf_all", 512, 0, 3), ("tconv", H * 3 * 64, 8, 10),
        ("izbf", 16 * H, 0, 8), ("im4", 32, 12, 12), ("gateT", 8 * E8, 11, 12),
        ("acatT", 8 * 32, 11, 12), ("bcatT", FF, 12, 15), ("wob", DM, 12, 15),
        ("trash", DM, 10, 10), ("ones1b", 128, 0, 15), ("o128b", 16, 0, 15),
        ("zrow", 512, 0, 15),
    ):
        BF.add(nm, sz, p0, p1)
    for nm, sz, p0, p1 in (
        ("xch0", 2048, 0, 0), ("xch1", 2048, 0, 0), ("rows", 512, 0, 0),
        ("tmp384", 256, 0, 0), ("lf_loc", 64, 0, 3),
        ("e2", 1024, 5.0, 5.25), ("k2n", 1024, 5.0, 5.45),
        ("k2nt", 1024, 5.4, 5.9),
        ("vi", 1024, 5.45, 5.95), ("vti", 1024, 5.45, 5.95), ("kv", 1024, 5.5, 5.9),
        ("kvt", 1024, 5.5, 5.9), ("t1", 1024, 5.5, 5.9), ("t3", 1024, 5.5, 5.9),
        ("t5", 1024, 5.5, 5.9), ("vtd", 512, 5.9, 9),
        ("i7", 1024, 4, 5.9), ("i15", 1024, 4, 5.9), ("i13", 1024, 4, 5.9),
        ("rmcol", 16, 5.0, 5.9), ("rs8", 16, 5.0, 5.25), ("izff", 256, 0, 5.3),
        ("eye", 128, 0, 15), ("ipair", 1024, 6.5, 8),
        ("rz1", SLOC, 8, 8), ("blob", 528, 4.6, 9),
        ("g2a", 528, 4.6, 9), ("g2b", 528, 4.6, 9), ("sumb", 528, 4.6, 9), ("rz3", 8, 9, 9),
        ("k3v", 512, 9, 9), ("bvb", DM, 4, 4), ("bvlo", DM, 4, 4),
        ("bvhi", DM, 4, 4), ("bqc", 8, 0, 3), ("bkc", 8, 0, 3),
        ("wibc", 32, 12, 15), ("gb16", 128, 11, 12), ("p216", 128, 11, 12),
        ("xtm0", DM, 10, 10), ("xtm1", DM, 10, 10), ("ctmp", 512, 10, 10),
        ("x1c0", DM, 10, 10), ("x1c1", DM, 10, 10),
        ("s1", 16, 10, 11), ("s2", 16, 10, 11), ("st16", 256, 11, 11),
        ("r2rows", 2 * SLOC, 11, 11), ("r2mu", 256, 11, 11),
        ("r2sig", 256, 11, 11),
        ("x1r0", DM, 11, 15), ("x1r1", DM, 11, 15),
        ("x1r2", DM, 11, 15), ("x1r3", DM, 11, 15),
        ("tmp512", 512, 7, 11), ("glog", SLOC, 12, 12),
        ("lg", 128, 12, 12), ("eqw", 128, 12, 12), ("mask", 128, 12, 12),
        ("mx", 16, 11, 12),
        ("out0", DM, 13, 15), ("out1", DM, 13, 15), ("ones1f", 128, 0, 15),
        ("consts", 4, 0, 15),
    ):
        FP.add(nm, sz, p0, p1)
    nbf = BF.pack()
    nfp = FP.pack()
    assert 2 * nbf + 4 * nfp <= 208 * 1024, (nbf, nfp, 2 * nbf + 4 * nfp)

    es = ExitStack()
    BFA = es.enter_context(nc.sbuf_tensor("BFA", [128, nbf], BF16))
    FPA = es.enter_context(nc.sbuf_tensor("FPA", [128, nfp], F32))

    def R(name, parts=128):
        arena, t = (BF, BFA) if name in BF.placed else (FP, FPA)
        off, _, _, _, size = arena.placed[name]
        return t[0:parts, off:off + size]

    def Rs(name, lo, n, parts=(0, 128)):
        arena, t = (BF, BFA) if name in BF.placed else (FP, FPA)
        off = arena.placed[name][0]
        return t[parts[0]:parts[1], off + lo:off + lo + n]

    pb = [es.enter_context(nc.psum_tensor(f"pb{i}", [128, 512], F32)) for i in range(8)]
    sems = {}
    for s_ in SEMS:
        sems[s_] = es.enter_context(nc.semaphore(s_))
    SM = sems
    nc.const_aps.register(F32, EPS, Rs("consts", 0, 1))
    nc.const_aps.register(F32, -0.5, Rs("consts", 1, 1))

    bank_guard = {}

    def guard_bank(*bs):
        for b in bs:
            g = bank_guard.get(b)
            if g:
                P.wait("tensor", g[0], g[1])

    def consume_bank(*bs, sem="V"):
        for b in bs:
            bank_guard[b] = (sem, P.cnt[sem])

    def t_op(fn):
        P.raw("tensor", lambda e, fn=fn: fn(e).then_inc(SM["T"], 1))
        return P.inc("T", 1)

    def v_op(fn):
        P.wait("vector", "V", P.cnt["V"])
        P.raw("vector", lambda e, fn=fn: fn(e).then_inc(SM["V"], 1))
        return P.inc("V", 1)

    def s_op(fn):
        P.wait("scalar", "Sc", P.cnt["Sc"])
        P.raw("scalar", lambda e, fn=fn: fn(e).then_inc(SM["Sc"], 1))
        return P.inc("Sc", 1)

    def g_dma(fn):
        P.wait("gpsimd", "LDG", P.cnt["LDG"])
        P.wait("gpsimd", "CCD", P.cnt["CCD"])
        P.raw("gpsimd", fn)
        return P.inc("LDG", 16)

    def c_dma(fn):
        P.wait("gpsimd", "LDG", P.cnt["LDG"])
        P.wait("gpsimd", "CCD", P.cnt["CCD"])
        P.raw("gpsimd", fn)
        return P.inc("CCD", 16)

    def sy_dma(fn):
        P.wait("sync", "LDS", P.cnt["LDS"])
        P.raw("sync", fn)
        return P.inc("LDS", 16)

    def sc_dma(fn):
        P.wait("scalar", "STQ", P.cnt["STQ"])
        P.raw("scalar", fn)
        return P.inc("STQ", 16)

    def gload(dst, src):
        return g_dma(lambda e, d=dst, s=src: e.dma_start(out=d, in_=s).then_inc(SM["LDG"], 16))

    if _DBG:
        def dump(nm, ap_):
            for s_ in ("T", "V", "Sc", "LDS", "STQ"):
                P.wait("gpsimd", s_, P.cnt[s_])
            g_dma(lambda e, d=dbg[nm], s=ap_: e.dma_start(out=d, in_=s).then_inc(SM["LDG"], 16))
            for e_ in ("tensor", "vector", "scalar", "sync"):
                P.wait(e_, "LDG", P.cnt["LDG"])
    else:
        def dump(nm, ap_):
            pass

    def ckpt(n):
        if stop_phase is not None and n >= stop_phase:
            raise _StopBuild()

    def pslot(h):
        return ((h % 2) * 64, (h % 2) * 64 + 64, (h // 2) * 64)

    try:
        for nm, src_ in (("eye", eye_in), ("izff", izff_in),
                         ("bqc", bqc_in), ("bkc", bkc_in)):
            sy_dma(lambda e, d=R(nm, {"ipair": 16}.get(nm, 128)), s=src_:
                   e.dma_start(out=d, in_=s).then_inc(SM["LDS"], 16))
        ld_const = P.cnt["LDS"]

        ld_wq = gload(R("w3").rearrange("p (dc d) -> p dc d", dc=8),
                      wq_in.rearrange("(dc p) d -> p dc d", p=128))
        gload(R("izbf"), izf_in)

        def _memsets(e):
            e.memset(R("ones1b", 1), 1.0).then_inc(SM["V"], 1)
            e.memset(Rs("o128b", 0, 1), 1.0).then_inc(SM["V"], 1)
            e.memset(R("ones1f", 1), 1.0).then_inc(SM["V"], 1)
            e.memset(Rs("consts", 0, 1), EPS).then_inc(SM["V"], 1)
            e.memset(Rs("consts", 1, 1), -0.5).then_inc(SM["V"], 1)
            e.memset(R("zrow"), 0.0).then_inc(SM["V"], 1)
        P.raw("vector", _memsets)
        P.inc("V", 6)
        mset_v = P.cnt["V"]

        NCH = 9
        CW = TPAD // NCH
        war_x, war_xb = {}, {}
        for c in range(NCH):
            xb = "xch%d" % (c % 2)
            qeng, qsem = ("sync", "LDS") if c % 2 == 0 else ("scalar", "STQ")
            if c >= 2:
                P.wait(qeng, "V", war_x[c % 2])
            if c % 2 == 0:
                xld = sy_dma(lambda e, b=R(xb), c=c: e.dma_start(
                    out=b.rearrange("p (dc t) -> p dc t", dc=8),
                    in_=xT_in.rearrange("(dc p) t -> p dc t", p=128)[:, :, ts(c, CW)],
                ).then_inc(SM["LDS"], 16))
            else:
                xld = sc_dma(lambda e, b=R(xb), c=c: e.dma_start(
                    out=b.rearrange("p (dc t) -> p dc t", dc=8),
                    in_=xT_in.rearrange("(dc p) t -> p dc t", p=128)[:, :, ts(c, CW)],
                ).then_inc(SM["STQ"], 16))
            P.wait("tensor", "V", mset_v)
            guard_bank(0, 2)
            P.wait("scalar", qsem, xld)
            P.wait("scalar", "LDS", ld_const)
            for dc in range(8):
                sl = (dc % 2) * CW
                if c + dc >= 2:
                    P.wait("scalar", "T", war_xb[dc % 2])
                s_op(lambda e, b=xb, dc=dc, sl=sl: e.copy(Rs("xbf", sl, CW), Rs(b, dc * CW, CW)))
                s_op(lambda e, b=xb, dc=dc, sl=sl: e.activation(Rs("xsq", sl, CW), Rs(b, dc * CW, CW), ACT.Square))
                P.wait("tensor", "Sc", P.cnt["Sc"])
                t_op(lambda e, sl=sl, st=(dc == 0), sp=(dc == 7): e.matmul(
                    pb[0][0:1, 0:CW], Rs("o128b", 0, 1), Rs("xbf", sl, CW), start=st, stop=sp))
                t_op(lambda e, sl=sl, st=(dc == 0), sp=(dc == 7): e.matmul(
                    pb[2][0:1, 0:CW], Rs("o128b", 0, 1), Rs("xsq", sl, CW), start=st, stop=sp))
                war_xb[dc % 2] = P.cnt["T"]
            P.wait("vector", "T", P.cnt["T"])
            v_op(lambda e: e.tensor_scalar(Rs("rows", 0, CW, (0, 1)), pb[0][0:1, 0:CW], 1.0 / DM, None, OP.mult))
            v_op(lambda e: e.tensor_scalar(Rs("rows", CW, CW, (0, 1)), pb[2][0:1, 0:CW], 1.0 / DM, None, OP.mult))
            v_op(lambda e: e.tensor_tensor(Rs("tmp384", 0, CW, (0, 1)), Rs("rows", 0, CW, (0, 1)), Rs("rows", 0, CW, (0, 1)), OP.mult))
            v_op(lambda e: e.tensor_tensor(Rs("rows", CW, CW, (0, 1)), Rs("rows", CW, CW, (0, 1)), Rs("tmp384", 0, CW, (0, 1)), OP.subtract))
            consume_bank(0, 2)
            P.wait("scalar", "V", P.cnt["V"])
            s_op(lambda e: e.activation(Rs("rows", CW, CW, (0, 1)), Rs("rows", CW, CW, (0, 1)), ACT.Ln, bias=EPS))
            s_op(lambda e: e.activation(Rs("rows", CW, CW, (0, 1)), Rs("rows", CW, CW, (0, 1)), ACT.Exp, scale=-0.5))
            P.wait("tensor", "Sc", P.cnt["Sc"])
            P.wait("tensor", "LDS", ld_const)
            guard_bank(4, 6)
            t_op(lambda e: e.matmul(pb[4][:, 0:CW], R("ones1f", 1), Rs("rows", 0, CW, (0, 1)), start=True, stop=True))
            t_op(lambda e: e.matmul(pb[6][:, 0:CW], R("ones1f", 1), Rs("rows", CW, CW, (0, 1)), start=True, stop=True))
            P.wait("vector", "T", P.cnt["T"])
            for dc in range(8):
                v_op(lambda e, b=xb, dc=dc: e.tensor_tensor(Rs("tmp384", 0, CW), Rs(b, dc * CW, CW), pb[4][:, 0:CW], OP.subtract))
                v_op(lambda e, dc=dc, c=c: e.tensor_tensor(
                    Rs("ln1", dc * TPAD + c * CW, CW), Rs("tmp384", 0, CW), pb[6][:, 0:CW], OP.mult))
            consume_bank(4, 6)
            war_x[c % 2] = P.cnt["V"]

        for dc in range(8):
            v_op(lambda e, dc=dc: e.reduce_sum(
                Rs("lf_loc", dc * 8, 8),
                Rs("ln1", dc * TPAD + 128, SLOC).rearrange("p (g t) -> p g t", g=8),
                axis=AX.X))
        v_op(lambda e: e.tensor_scalar(R("lf_loc"), R("lf_loc"), 1.0 / SEG, None, OP.mult))
        P.wait("gpsimd", "V", P.cnt["V"])
        c_dma(lambda e: e.dma_start(out=cc1i.ap(), in_=R("lf_loc")).then_inc(SM["CCD"], 16))
        P.wait("gpsimd", "CCD", P.cnt["CCD"])

        def _cc1(e):
            e.collective_compute("AllGather", OP.bypass, replica_groups=[list(range(NCORE))],
                                 ins=[cc1i.ap().opt()], outs=[cc1o.ap().opt()]).then_inc(SM["CC"], 1)
        P.raw("gpsimd", _cc1)
        P.inc("CC", 1)
        P.wait("gpsimd", "CC", P.cnt["CC"])
        c_dma(lambda e: e.dma_start(
            out=R("lf_all").rearrange("p (s c) -> p s c", s=8),
            in_=cc1o.ap().rearrange("(s p) c -> p s c", p=128)).then_inc(SM["CCD"], 16))
        cc1_done = P.cnt["CCD"]

        P.barrier()
        ckpt(1)

        wk_src = {dc: ("wvh", dc * 1024) for dc in range(4)}
        wk_src.update({dc: ("wkx", (dc - 4) * 1024) for dc in range(4, 8)})
        ld_wk = 0
        for dc in range(8):
            nm, off = wk_src[dc]
            ld_wk = gload(Rs(nm, off, 1024),
                          wk_in.rearrange("(dc p) d -> p dc d", p=128)[:, dc, :])

        def proj_qk(wld, out_nm, bias_nm, src_of=None):
            P.wait("tensor", "LDG", wld)
            for pp in range(8):
                for n in range(4):
                    b = [0, 2][(pp * 4 + n) % 2]
                    guard_bank(b)
                    for dc in range(8):
                        nm, off = ("w3", dc * 1024) if src_of is None else src_of[dc]
                        t_op(lambda e, b=b, nm=nm, off=off, pp=pp, n=n, dc=dc,
                             st=(dc == 0), sp=(dc == 7): e.matmul(
                            pb[b][:],
                            Rs(nm, off + pp * 128, 128),
                            Rs("ln1", dc * TPAD + 128 + n * 512, 512),
                            start=st, stop=sp))
                    P.wait("vector", "T", P.cnt["T"])
                    v_op(lambda e, b=b, pp=pp, n=n, o=out_nm, bs=bias_nm: e.tensor_scalar(
                        Rs(o, pp * SLOC + n * 512, 512),
                        pb[b][:], Rs(bs, pp, 1).opt(), SCALE, OP.add, OP.mult))
                    consume_bank(b)

        proj_qk(ld_wq, "q", "bqc")
        proj_qk(ld_wk, "k", "bkc", src_of=wk_src)

        P.barrier()
        ckpt(3)

        P.wait("tensor", "CCD", cc1_done)

        def proj_l(out_nm, bias_nm, src_of=None):
            for pp in range(8):
                b = [4, 6][pp % 2]
                guard_bank(b)
                for dc in range(8):
                    nm, off = ("w3", dc * 1024) if src_of is None else src_of[dc]
                    t_op(lambda e, b=b, nm=nm, off=off, pp=pp, dc=dc,
                         st=(dc == 0), sp=(dc == 7): e.matmul(
                        pb[b][:, 0:64],
                        Rs(nm, off + pp * 128, 128),
                        R("lf_all").rearrange("p (s c g) -> p s c g", s=8, c=8)[:, :, dc, :],
                        start=st, stop=sp))
                P.wait("vector", "T", P.cnt["T"])
                v_op(lambda e, b=b, pp=pp, o=out_nm, bs=bias_nm: e.tensor_scalar(
                    Rs(o, pp * 64, 64), pb[b][:, 0:64], Rs(bs, pp, 1).opt(), SCALE, OP.add, OP.mult))
                consume_bank(b)

        proj_l("kl", "bkc", src_of=wk_src)
        proj_l("ql", "bqc")
        dump("kl", R("kl"))
        dump("ql", R("ql"))

        P.barrier()
        ckpt(4)

        for nm, src_ in (("i7", i7_in), ("i15", i15_in), ("i13", i13_in),
                         ("bvb", bvb_in), ("bvlo", bvlo_in), ("bvhi", bvhi_in)):
            sy_dma(lambda e, d=R(nm), s=src_: e.dma_start(out=d, in_=s).then_inc(SM["LDS"], 16))
        ld_i = P.cnt["LDS"]
        P.wait("vector", "LDS", ld_i)
        for hf in range(2):
            P.wait("gpsimd", "T", P.cnt["T"])
            ld_wvh = gload(R("wvh").rearrange("p (dc d) -> p dc d", dc=8),
                           wv_in.rearrange("(dc p) d -> p dc d", p=128)[:, :, ts(hf, 512)])
            P.wait("tensor", "LDG", ld_wvh)
            for g in range(NTP):
                b = [0, 2][g % 2]
                guard_bank(b)
                for dc in range(8):
                    t_op(lambda e, hf=hf, b=b, g=g, dc=dc, st=(dc == 0), sp=(dc == 7): e.matmul(
                        pb[b][:],
                        Rs("ln1", dc * TPAD + g * 128, 128),
                        Rs("wvh", dc * 512, 512),
                        start=st, stop=sp))
                P.wait("vector", "T", P.cnt["T"])
                bv = "bvlo" if g == 0 else ("bvhi" if g == NTP - 1 else "bvb")
                v_op(lambda e, b=b, g=g, hf=hf, bv=bv: e.tensor_tensor(
                    Rs("v", g * DM + hf * 512, 512), pb[b][:], Rs(bv, hf * 512, 512), OP.add))
                consume_bank(b)

        P.barrier()
        ckpt(5)

        for sc in range(NT):
            enm = "e3t%d" % (sc % 2)
            guard_bank(0, 1)
            for h in range(H):
                r0, r1, c0 = pslot(h)
                bk = h % 2
                t_op(lambda e, bk=bk, h=h, sc=sc, r0=r0, r1=r1: e.matmul(
                    pb[bk][:, ts(h // 2, 64)],
                    Rs("k", (h // 2) * SLOC + sc * 128, 128, (r0, r1)),
                    Rs("ql", (h // 2) * 64, 64, (r0, r1)),
                    start=True, stop=True, tile_position=(r0, 0)))
            P.wait("scalar", "T", P.cnt["T"])
            s_op(lambda e, enm=enm: e.activation(
                Rs(enm, 0, 1024).rearrange("p (s two c) -> p s c two", two=2, c=64)[:, :, :, 0],
                pb[0][:], ACT.Exp))
            s_op(lambda e, enm=enm: e.activation(
                Rs(enm, 0, 1024).rearrange("p (s two c) -> p s c two", two=2, c=64)[:, :, :, 1],
                pb[1][:], ACT.Exp))
            consume_bank(0, 1, sem="Sc")
            P.wait("tensor", "Sc", P.cnt["Sc"])
            guard_bank(2, 4)
            for h in range(H):
                r0, r1, c0 = pslot(h)
                t_op(lambda e, enm=enm, h=h, r0=r0, c0=c0, sc=sc: e.matmul(
                    pb[4][r0:r0 + 64, c0:c0 + 64],
                    Rs(enm, (h % 8) * 64 + (h // 8) * 512, 64),
                    Rs("v", (sc + 1) * DM + h * 64, 64),
                    start=True, stop=True, tile_position=(0, r0)))
                t_op(lambda e, enm=enm, h=h, r0=r0: e.matmul(
                    pb[2][r0:r0 + 64, (h // 2):(h // 2) + 1],
                    Rs(enm, (h % 8) * 64 + (h // 8) * 512, 64),
                    R("o128b", 128)[:, 0:1],
                    start=True, stop=True, tile_position=(0, r0)))
            P.wait("vector", "T", P.cnt["T"])
            if sc == 0:
                v_op(lambda e: e.tensor_copy(Rs("blob", 0, 512), pb[4][:]))
                v_op(lambda e: e.tensor_copy(Rs("blob", 512, 8), pb[2][:, 0:8]))
            else:
                v_op(lambda e: e.tensor_tensor(Rs("blob", 0, 512), Rs("blob", 0, 512), pb[4][:], OP.add))
                v_op(lambda e: e.tensor_tensor(Rs("blob", 512, 8), Rs("blob", 512, 8), pb[2][:, 0:8], OP.add))
            consume_bank(2, 4)
        dump("blob", Rs("blob", 0, 520))
        P.wait("gpsimd", "V", P.cnt["V"])
        c_dma(lambda e: e.dma_start(out=cc2i.ap(), in_=Rs("blob", 0, 520)).then_inc(SM["CCD"], 16))
        P.wait("gpsimd", "CCD", P.cnt["CCD"])

        def _cc2(e):
            e.collective_compute("AllGather", OP.bypass, replica_groups=[list(range(NCORE))],
                                 ins=[cc2i.ap().opt()], outs=[cc2o.ap().opt()]).then_inc(SM["CC"], 1)
        P.raw("gpsimd", _cc2)
        P.inc("CC", 1)
        cc2_done = P.cnt["CC"]
        ckpt(7)

        P.barrier()
        ckpt(6)

        guard_bank(0, 1)
        for h in range(H):
            r0, r1, c0 = pslot(h)
            bk = h % 2
            t_op(lambda e, bk=bk, r0=r0, r1=r1, h=h: e.matmul(
                pb[bk][0:64, ts(h // 2, 64)],
                Rs("ql", (h // 2) * 64, 64, (r0, r1)),
                Rs("kl", (h // 2) * 64, 64, (r0, r1)),
                start=True, stop=True, tile_position=(r0, 0)))
        ckpt(5.05)
        P.wait("scalar", "T", P.cnt["T"])
        s_op(lambda e: e.activation(
            Rs("e2", 0, 1024, (0, 64)).rearrange("p (s two c) -> p s c two", two=2, c=64)[:, :, :, 0],
            pb[0][0:64, :], ACT.Exp))
        s_op(lambda e: e.activation(
            Rs("e2", 0, 1024, (0, 64)).rearrange("p (s two c) -> p s c two", two=2, c=64)[:, :, :, 1],
            pb[1][0:64, :], ACT.Exp))
        consume_bank(0, 1, sem="Sc")
        P.wait("vector", "Sc", P.cnt["Sc"])
        v_op(lambda e: e.reduce_sum(Rs("rs8", 0, 16, (0, 64)),
                                    Rs("e2", 0, 1024, (0, 64)).rearrange("p (a m) -> p a m", a=16), axis=AX.X))
        v_op(lambda e: e.reciprocal(Rs("rs8", 0, 16, (0, 64)), Rs("rs8", 0, 16, (0, 64))))
        for h in range(H):
            v_op(lambda e, h=h: e.tensor_scalar(Rs("k2n", h * 64, 64, (0, 64)), Rs("e2", h * 64, 64, (0, 64)),
                                                Rs("rs8", h, 1, (0, 64)).opt(), None, OP.mult))
        ckpt(5.1)
        P.wait("tensor", "V", P.cnt["V"])
        guard_bank(2)
        for h in range(H):
            t_op(lambda e, h=h: e.matmul(
                pb[2][0:16, 0:64],
                Rs("izff", h * 16, 16, (0, 64)),
                Rs("k2n", h * 64, 64, (0, 64)),
                start=(h == 0), stop=(h == H - 1)))
        P.wait("vector", "T", P.cnt["T"])
        v_op(lambda e: e.reduce_max(Rs("rmcol", 0, 1, (0, 16)), pb[2][0:16, 0:64], axis=AX.X))
        consume_bank(2)
        P.wait("tensor", "V", P.cnt["V"])
        guard_bank(4)
        t_op(lambda e: e.transpose(pb[4][0:1, 0:16], Rs("rmcol", 0, 1, (0, 16)), Rs("eye", 0, 16, (0, 16))))
        P.wait("vector", "T", P.cnt["T"])
        v_op(lambda e: e.reduce_max(Rs("rmcol", 1, 1, (0, 1)), pb[4][0:1, 0:16], axis=AX.X))
        consume_bank(4)
        v_op(lambda e: e.reciprocal(Rs("rmcol", 1, 1, (0, 1)), Rs("rmcol", 1, 1, (0, 1))))
        P.wait("tensor", "V", P.cnt["V"])
        guard_bank(6)
        t_op(lambda e: e.matmul(pb[6][:, 0:1], R("ones1f", 1), Rs("rmcol", 1, 1, (0, 1)), start=True, stop=True))
        P.wait("vector", "T", P.cnt["T"])
        v_op(lambda e: e.tensor_copy(Rs("rmcol", 2, 1), pb[6][:, 0:1]))
        consume_bank(6)
        ckpt(5.3)

        def mm16f(lhs_nm, rhs_nm, banks):
            guard_bank(*banks)
            for h in range(H):
                bk = banks[h // 8]
                t_op(lambda e, bk=bk, h=h, ln=lhs_nm, rn=rhs_nm: e.matmul(
                    pb[bk][0:64, ts(h % 8, 64)],
                    Rs(ln, h * 64, 64, (0, 64)),
                    Rs(rn, h * 64, 64, (0, 64)),
                    start=True, stop=True))

        def tr16f(in_nm, banks):
            guard_bank(*banks)
            for h in range(H):
                bk = banks[h // 8]
                t_op(lambda e, bk=bk, h=h, nm=in_nm: e.transpose(
                    pb[bk][0:64, ts(h % 8, 64)],
                    Rs(nm, h * 64, 64, (0, 64)),
                    Rs("eye", 0, 64, (0, 64))))

        def ev2(dst_nm, banks, fn):
            for i, bk in enumerate(banks):
                v_op(lambda e, i=i, bk=bk, d=dst_nm, fn=fn: fn(
                    e, Rs(d, i * 512, 512, (0, 64)), pb[bk][0:64, :], i))
            consume_bank(*banks)

        P.wait("tensor", "V", P.cnt["V"])
        tr16f("k2n", (4, 6))
        P.wait("vector", "T", P.cnt["T"])
        ev2("k2nt", (4, 6), lambda e, d, p, i: e.tensor_copy(d, p))
        v_op(lambda e: e.tensor_scalar(Rs("vi", 0, 1024, (0, 64)), Rs("k2nt", 0, 1024, (0, 64)),
                                       Rs("rmcol", 2, 1, (0, 64)).opt(), None, OP.mult))
        v_op(lambda e: e.tensor_scalar(Rs("vti", 0, 1024, (0, 64)), Rs("k2n", 0, 1024, (0, 64)),
                                       Rs("rmcol", 2, 1, (0, 64)).opt(), None, OP.mult))

        for it in range(6):
            ckpt(5.4 + it / 100.0)
            P.wait("tensor", "V", P.cnt["V"])
            mm16f("k2nt", "vi", (0, 2))
            P.wait("vector", "T", P.cnt["T"])
            ev2("kv", (0, 2), lambda e, d, p, i: e.tensor_copy(d, p))
            for i in range(2):
                v_op(lambda e, i=i: e.scalar_tensor_tensor(
                    Rs("t1", i * 512, 512, (0, 64)), pb[2 * i][0:64, :], -1.0,
                    Rs("i7", i * 512, 512, (0, 64)), OP.mult, OP.add))
            P.wait("tensor", "V", P.cnt["V"])
            tr16f("kv", (4, 6))
            P.wait("vector", "T", P.cnt["T"])
            ev2("kvt", (4, 6), lambda e, d, p, i: e.tensor_copy(d, p))
            P.wait("tensor", "V", P.cnt["V"])
            mm16f("kvt", "t1", (0, 2))
            P.wait("vector", "T", P.cnt["T"])
            ev2("t3", (0, 2), lambda e, d, p, i: e.scalar_tensor_tensor(
                d, p, -1.0, Rs("i15", i * 512, 512, (0, 64)), OP.mult, OP.add))
            P.wait("tensor", "V", P.cnt["V"])
            mm16f("kvt", "t3", (4, 6))
            P.wait("vector", "T", P.cnt["T"])
            ev2("t5", (4, 6), lambda e, d, p, i: e.scalar_tensor_tensor(
                d, p, -1.0, Rs("i13", i * 512, 512, (0, 64)), OP.mult, OP.add))
            P.wait("tensor", "V", P.cnt["V"])
            mm16f("vti", "t5", (0, 2))
            P.wait("vector", "T", P.cnt["T"])
            ev2("vi", (0, 2), lambda e, d, p, i: e.tensor_scalar(d, p, 0.25, None, OP.mult))
            P.wait("tensor", "V", P.cnt["V"])
            tr16f("vi", (4, 6))
            P.wait("vector", "T", P.cnt["T"])
            ev2("vti", (4, 6), lambda e, d, p, i: e.tensor_copy(d, p))

        ckpt(5.8)
        dump("vti", Rs("vti", 0, 1024, (0, 64)))
        P.wait("sync", "V", P.cnt["V"])
        for par in range(2):
            sy_dma(lambda e, par=par: e.dma_start(
                out=FPA[par * 64:(par + 1) * 64,
                        FP.placed["vtd"][0]:FP.placed["vtd"][0] + 512],
                in_=Rs("vti", 0, 1024, (0, 64)).rearrange(
                    "p (a two c) -> p a two c", a=8, two=2)[:, :, par, :],
            ).then_inc(SM["LDS"], 16))
        vtd_done = P.cnt["LDS"]


        ld_ipair = sy_dma(lambda e: e.dma_start(out=R("ipair", 16), in_=ipair_in).then_inc(SM["LDS"], 16))
        for n in range(4):
            for pp in range(8):
                be, bo = [(4, 1), (6, 3)][(n * 8 + pp) % 2]
                guard_bank(be, bo)
                for par in range(2):
                    b = be if par == 0 else bo
                    t_op(lambda e, b=b, pp=pp, n=n, par=par: e.matmul(
                        pb[b][par * 64:par * 64 + 64, :],
                        Rs("kl", pp * 64, 64, (par * 64, par * 64 + 64)),
                        Rs("q", pp * SLOC + n * 512, 512, (par * 64, par * 64 + 64)),
                        start=True, stop=True, tile_position=(par * 64, par * 64)))
                P.wait("scalar", "T", P.cnt["T"])
                s_op(lambda e, b=be, pp=pp, n=n: e.activation(
                    Rs("e1", pp * SLOC + n * 512, 512, (0, 64)), pb[b][0:64, :], ACT.Exp))
                s_op(lambda e, b=bo, pp=pp, n=n: e.activation(
                    Rs("e1", pp * SLOC + n * 512, 512, (64, 128)), pb[b][64:128, :], ACT.Exp))
                consume_bank(be, bo, sem="Sc")
        P.wait("tensor", "Sc", P.cnt["Sc"])
        ckpt(8)
        for n in range(4):
            be, bo = [(0, 1), (2, 3)][n % 2]
            guard_bank(be, bo)
            for h in range(H):
                r0, r1, c0 = pslot(h)
                b = be if h % 2 == 0 else bo
                t_op(lambda e, b=b, h=h, n=n, r0=r0, r1=r1, st=(h < 2), sp=(h >= H - 2): e.matmul(
                    pb[b][0:16, :],
                    Rs("izbf", h * 16, 16, (r0, r1)),
                    Rs("e1", (h // 2) * SLOC + n * 512, 512, (r0, r1)),
                    start=st, stop=sp, tile_position=(r0, 0)))
            P.wait("scalar", "T", P.cnt["T"])
            P.wait("scalar", "V", P.cnt["V"])
            s_op(lambda e, bo=bo: e.activation(Rs("tmp512", 0, 512, (0, 16)),
                                               pb[bo][0:16, :], ACT.Copy))
            P.wait("vector", "T", P.cnt["T"])
            P.wait("vector", "Sc", P.cnt["Sc"])
            v_op(lambda e, be=be: e.tensor_tensor(Rs("tmp512", 0, 512, (0, 16)),
                                                  pb[be][0:16, :], Rs("tmp512", 0, 512, (0, 16)), OP.add))
            v_op(lambda e, n=n: e.reciprocal(Rs("rz1", n * 512, 512, (0, 16)), Rs("tmp512", 0, 512, (0, 16))))
            consume_bank(be, bo)
        P.wait("tensor", "V", P.cnt["V"])
        P.wait("tensor", "LDS", ld_ipair)
        for pp in range(8):
            for n in range(4):
                b = [4, 6][(pp * 4 + n) % 2]
                guard_bank(b)
                t_op(lambda e, b=b, pp=pp, n=n: e.matmul(
                    pb[b][:], Rs("ipair", pp * 128, 128, (0, 16)), Rs("rz1", n * 512, 512, (0, 16)),
                    start=True, stop=True))
                P.wait("vector", "T", P.cnt["T"])
                v_op(lambda e, b=b, pp=pp, n=n: e.tensor_tensor(
                    Rs("e1", pp * SLOC + n * 512, 512),
                    Rs("e1", pp * SLOC + n * 512, 512),
                    pb[b][:], OP.mult))
                consume_bank(b)
        ckpt(9)

        P.wait("gpsimd", "CC", cc2_done)
        g2cnt, g2add = {}, {}
        for s8 in range(8):
            gb = "g2a" if s8 % 2 == 0 else "g2b"
            if s8 >= 2:
                P.wait("gpsimd", "V", g2add[s8 - 2])
            g2cnt[s8] = c_dma(lambda e, gb=gb, s8=s8: e.dma_start(
                out=Rs(gb, 0, 520),
                in_=cc2o.ap().rearrange("(s p) c -> p s c", p=128)[:, s8, :]).then_inc(SM["CCD"], 16))
            P.wait("vector", "CCD", g2cnt[s8])
            if s8 == 0:
                v_op(lambda e, gb=gb: e.tensor_copy(Rs("sumb", 0, 520), Rs(gb, 0, 520)))
            else:
                v_op(lambda e, gb=gb: e.tensor_tensor(Rs("sumb", 0, 520), Rs("sumb", 0, 520),
                                                      Rs(gb, 0, 520), OP.add))
            g2add[s8] = P.cnt["V"]
        v_op(lambda e: e.reciprocal(R("rz3"), Rs("sumb", 512, 8)))
        for a in range(8):
            v_op(lambda e, a=a: e.tensor_scalar(Rs("k3v", a * 64, 64), Rs("sumb", a * 64, 64),
                                                Rs("rz3", a, 1).opt(), None, OP.mult))
        dump("sumb", Rs("sumb", 0, 520))
        dump("k3v", Rs("k3v", 0, 512))
        P.wait("tensor", "V", P.cnt["V"])
        P.wait("tensor", "LDS", vtd_done)
        guard_bank(0, 1)
        for h in range(H):
            r0, r1, c0 = pslot(h)
            bk = h % 2
            t_op(lambda e, bk=bk, r0=r0, r1=r1, c0=c0: e.matmul(
                pb[bk][r0:r1, c0:c0 + 64],
                Rs("vtd", c0, 64, (r0, r1)),
                Rs("k3v", c0, 64, (r0, r1)),
                start=True, stop=True, tile_position=(r0, r0)))
        P.wait("vector", "T", P.cnt["T"])
        v_op(lambda e: e.tensor_copy(Rs("w2", 0, 512, (0, 64)), pb[0][0:64, :]))
        v_op(lambda e: e.tensor_copy(Rs("w2", 0, 512, (64, 128)), pb[1][64:128, :]))
        consume_bank(0, 1)
        dump("w2", R("w2"))

        P.wait("sync", "V", P.cnt["V"])
        sy_dma(lambda e: e.dma_start(
            out=BFA[0:32, BF.placed["vA"][0]:BF.placed["vA"][0] + 17 * DM],
            in_=Rs("v", 0, 17 * DM, (96, 128))).then_inc(SM["LDS"], 16))
        sy_dma(lambda e: e.dma_start(
            out=BFA[32:128, BF.placed["vA"][0]:BF.placed["vA"][0] + 17 * DM],
            in_=Rs("v", DM, 17 * DM, (0, 96))).then_inc(SM["LDS"], 16))
        va_done = P.cnt["LDS"]

        P.barrier()
        ckpt(10)

        ld_tconv = gload(R("tconv"), tconv_in)
        ld_wo = gload(R("woT").rearrange("p (fc d) -> p fc d", fc=32),
                      woT_in.rearrange("(fc p) d -> p fc d", p=128))
        P.wait("tensor", "LDG", ld_tconv)
        P.wait("tensor", "LDS", va_done)

        war_xtm, war_x1c = {}, {}
        for sc in range(NT):
            xnm = "xtm%d" % (sc % 2)
            if sc >= 2:
                P.wait("sync", "V", war_xtm[sc % 2])
            sy_dma(lambda e, d=R(xnm), sc=sc: e.dma_start(out=d, in_=xtm_in[ts(sc, 128), :]).then_inc(SM["LDS"], 16))
            bm = (0, 1, 2, 3) if sc % 2 == 0 else (4, 5, 6, 7)
            guard_bank(*bm)
            for par in range(2):
                for s in range(8):
                    h = 2 * s + par
                    t_op(lambda e, bk=bm[par], h=h, s=s, row=par * 64, sc=sc, st=(s == 0): e.matmul(
                        pb[bk][:, ts(s, 64)],
                        Rs("e1", (h // 2) * SLOC + sc * 128, 128, (row, row + 64)),
                        Rs("w2", (h // 2) * 64, 64, (row, row + 64)),
                        start=st, stop=False, tile_position=(row, 0)))
            t_op(lambda e, bk=bm[2]: e.matmul(pb[bk][:], Rs("zrow", 0, 128, (0, 1)),
                                    Rs("zrow", 0, 512, (0, 1)), start=True, stop=False))
            t_op(lambda e, bk=bm[3]: e.matmul(pb[bk][:], Rs("zrow", 0, 128, (64, 65)),
                                    Rs("zrow", 0, 512, (64, 65)), start=True, stop=False,
                                    tile_position=(64, 0)))
            for h in range(H):
                for m_par in range(2):
                    m = sc * 2 + m_par
                    if m % 2 == 0:
                        pieces = ((0, 0, 128, m // 2),)
                    else:
                        pieces = ((1, 64, 64, (m - 1) // 2), (2, 0, 64, (m + 1) // 2))
                    for (cls, base, kk, j) in pieces:
                        row = 64 if base == 64 else 0
                        bk = (0 if h % 2 == 0 else 2) if row == 0 else (1 if h % 2 == 1 else 3)
                        t_op(lambda e, bk=bm[bk], h=h, s=h // 2, ps0=m_par * 64, cls=cls,
                             base=base, kk=kk, j=j: e.matmul(
                            pb[bk][ps0:ps0 + 64, ts(s, 64)],
                            Rs("tconv", (h * 3 + cls) * 64, 64, (base, base + kk)),
                            Rs("vA", j * DM + h * 64, 64, (base, base + kk)),
                            start=False, stop=False, tile_position=(base, ps0)))
            t_op(lambda e, bk=bm[0]: e.matmul(pb[bk][:], Rs("zrow", 0, 128, (0, 1)),
                                    Rs("zrow", 0, 512, (0, 1)), start=False, stop=True))
            t_op(lambda e, bk=bm[2]: e.matmul(pb[bk][:], Rs("zrow", 0, 128, (0, 1)),
                                    Rs("zrow", 0, 512, (0, 1)), start=False, stop=True))
            t_op(lambda e, bk=bm[1]: e.matmul(pb[bk][:], Rs("zrow", 0, 128, (64, 65)),
                                    Rs("zrow", 0, 512, (64, 65)), start=False, stop=True,
                                    tile_position=(64, 0)))
            t_op(lambda e, bk=bm[3]: e.matmul(pb[bk][:], Rs("zrow", 0, 128, (64, 65)),
                                    Rs("zrow", 0, 512, (64, 65)), start=False, stop=True,
                                    tile_position=(64, 0)))
            P.wait("vector", "T", P.cnt["T"])
            P.wait("vector", "LDS", P.cnt["LDS"])
            cnm = "x1c%d" % (sc % 2)
            if sc >= 2:
                P.wait("vector", "STQ", war_x1c[sc % 2])

            def pstr(nm, par):
                return Rs(nm, 0, DM).rearrange("p (s two c) -> p s c two", two=2, c=64)[:, :, :, par]
            v_op(lambda e, xnm=xnm, bk=bm[0]: e.tensor_tensor(R("ctmp"), pstr(xnm, 0), pb[bk][:], OP.add))
            v_op(lambda e, cnm=cnm, bk=bm[3]: e.tensor_tensor(pstr(cnm, 0), R("ctmp"), pb[bk][:], OP.add))
            v_op(lambda e, xnm=xnm, bk=bm[1]: e.tensor_tensor(R("ctmp"), pstr(xnm, 1), pb[bk][:], OP.add))
            v_op(lambda e, cnm=cnm, bk=bm[2]: e.tensor_tensor(pstr(cnm, 1), R("ctmp"), pb[bk][:], OP.add))
            consume_bank(*bm)
            war_xtm[sc % 2] = P.cnt["V"]
            v_op(lambda e, cnm=cnm, sc=sc: e.reduce_sum(Rs("s1", sc, 1), Rs(cnm, 0, DM), axis=AX.X))
            P.wait("scalar", "V", P.cnt["V"])
            s_op(lambda e, cnm=cnm: e.activation(R("trash"), Rs(cnm, 0, DM), ACT.Square))
            P.wait("vector", "Sc", P.cnt["Sc"])
            v_op(lambda e, sc=sc: e.reduce_sum(Rs("s2", sc, 1), R("trash"), axis=AX.X))
            sc_dma(lambda e, cnm=cnm, sc=sc: e.dma_start(out=x1b[ts(sc, 128), :], in_=Rs(cnm, 0, DM)).then_inc(SM["STQ"], 16))
            war_x1c[sc % 2] = P.cnt["STQ"]

        P.barrier()
        ckpt(11)

        P.wait("vector", "STQ", P.cnt["STQ"])
        v_op(lambda e: e.tensor_scalar(R("s1"), R("s1"), 1.0 / DM, None, OP.mult))
        v_op(lambda e: e.tensor_tensor(R("mx"), R("s1"), R("s1"), OP.mult))
        v_op(lambda e: e.tensor_scalar(R("s2"), R("s2"), 1.0 / DM, None, OP.mult))
        v_op(lambda e: e.tensor_tensor(R("s2"), R("s2"), R("mx"), OP.subtract))
        P.wait("scalar", "V", P.cnt["V"])
        s_op(lambda e: e.activation(R("s2"), R("s2"), ACT.Ln, bias=EPS))
        s_op(lambda e: e.activation(R("s2"), R("s2"), ACT.Exp, scale=-0.5))
        P.wait("tensor", "Sc", P.cnt["Sc"])
        guard_bank(0)
        t_op(lambda e: e.transpose(pb[0][0:16, 0:128], R("s1"), R("eye")))
        t_op(lambda e: e.transpose(pb[0][0:16, 128:256], R("s2"), R("eye")))
        P.wait("vector", "T", P.cnt["T"])
        v_op(lambda e: e.tensor_copy(Rs("st16", 0, 256, (0, 16)), pb[0][0:16, 0:256]))
        consume_bank(0)
        P.wait("sync", "V", P.cnt["V"])
        sy_dma(lambda e: e.dma_start(out=Rs("r2rows", 0, SLOC, (0, 1)), in_=Rs("st16", 0, 128, (0, 16))).then_inc(SM["LDS"], 16))
        sy_dma(lambda e: e.dma_start(out=Rs("r2rows", SLOC, SLOC, (0, 1)), in_=Rs("st16", 128, 128, (0, 16))).then_inc(SM["LDS"], 16))
        P.wait("tensor", "LDS", P.cnt["LDS"])

        war_x1r, x1r_cnt = {}, {}
        for scg in range(8):
            for w in range(2):
                b = [4, 6][w]
                guard_bank(b)
                t_op(lambda e, b=b, scg=scg, w=w: e.matmul(
                    pb[b][:, 0:256], R("ones1f", 1), Rs("r2rows", w * SLOC + scg * 256, 256, (0, 1)),
                    start=True, stop=True))
                P.wait("vector", "T", P.cnt["T"])
                dnm = "r2mu" if w == 0 else "r2sig"
                v_op(lambda e, b=b, d=dnm: e.tensor_copy(Rs(d, 0, 256), pb[b][:, 0:256]))
                consume_bank(b)
            for j in range(2):
                bi = scg * 2 + j
                slot = bi % 4
                qeng, qsem = [("sync", "LDS"), ("scalar", "STQ"),
                              ("gpsimd", "LDG")][bi % 3]
                if bi >= 4:
                    P.wait(qeng, "T", war_x1r[slot])
                if qeng == "sync":
                    x1r_cnt[bi] = (qsem, sy_dma(lambda e, slot=slot, bi=bi: e.dma_start(
                        out=R("x1r%d" % slot), in_=x1b[ts(bi, 128), :]).then_inc(SM["LDS"], 16)))
                elif qeng == "scalar":
                    x1r_cnt[bi] = (qsem, sc_dma(lambda e, slot=slot, bi=bi: e.dma_start(
                        out=R("x1r%d" % slot), in_=x1b[ts(bi, 128), :]).then_inc(SM["STQ"], 16)))
                else:
                    x1r_cnt[bi] = (qsem, gload(R("x1r%d" % slot), x1b[ts(bi, 128), :]))
            for j in range(2):
                qsem, cnt = x1r_cnt[scg * 2 + j]
                P.wait("tensor", qsem, cnt)
            for dc in range(8):
                b = [0, 2][dc % 2]
                guard_bank(b)
                for j in range(2):
                    t_op(lambda e, b=b, dc=dc, j=j, slot=(scg * 2 + j) % 4: e.transpose(
                        pb[b][:, ts(j, 128)], Rs("x1r%d" % slot, dc * 128, 128), R("eye")))
                P.wait("vector", "T", P.cnt["T"])
                v_op(lambda e, b=b: e.tensor_tensor(
                    Rs("tmp512", 0, 256), pb[b][:, 0:256], Rs("r2mu", 0, 256), OP.subtract))
                v_op(lambda e, dc=dc, scg=scg: e.tensor_tensor(
                    Rs("ln2", dc * SLOC + scg * 256, 256),
                    Rs("tmp512", 0, 256), Rs("r2sig", 0, 256), OP.mult))
                consume_bank(b)
            for j in range(2):
                war_x1r[(scg * 2 + j) % 4] = P.cnt["T"]

        P.barrier()
        ckpt(12)

        ld_gate = gload(R("gateT").rearrange("p (dc e) -> p dc e", dc=8),
                        gateT_in.rearrange("(dc p) e -> p dc e", p=128))
        ld_acat = gload(R("acatT").rearrange("p (dc r) -> p dc r", dc=8),
                        acatT_in.rearrange("(dc p) r -> p dc r", p=128))
        ld_im4 = gload(R("im4", 8), im4_in)
        for nm, src_ in (("gb16", gb16_in), ("p216", p216_in)):
            sy_dma(lambda e, d=R(nm), s=src_: e.dma_start(out=d, in_=s).then_inc(SM["LDS"], 16))
        P.wait("tensor", "LDS", P.cnt["LDS"])
        P.wait("tensor", "LDG", ld_acat)
        for n in range(4):
            b = [0, 2][n % 2]
            guard_bank(b)
            for dc in range(8):
                t_op(lambda e, b=b, dc=dc, n=n, st=(dc == 0), sp=(dc == 7): e.matmul(
                    pb[b][0:8, :], Rs("gateT", dc * 8, 8),
                    Rs("ln2", dc * SLOC + n * 512, 512), start=st, stop=sp))
            P.wait("vector", "T", P.cnt["T"])
            v_op(lambda e, b=b, n=n: e.tensor_copy(Rs("glog", n * 512, 512, (0, 8)), pb[b][0:8, :]))
            consume_bank(b)
            b2 = [4, 6][n % 2]
            guard_bank(b2)
            for dc in range(8):
                t_op(lambda e, b=b2, dc=dc, n=n, st=(dc == 0), sp=(dc == 7): e.matmul(
                    pb[b][0:32, :], Rs("acatT", dc * 32, 32),
                    Rs("ln2", dc * SLOC + n * 512, 512), start=st, stop=sp))
            P.wait("vector", "T", P.cnt["T"])
            v_op(lambda e, b=b2, n=n: e.tensor_copy(Rs("u", n * 512, 512, (0, 32)), pb[b][0:32, :]))
            consume_bank(b2)
        P.wait("tensor", "V", P.cnt["V"])
        guard_bank(0)
        for sc in range(NT):
            t_op(lambda e, sc=sc: e.transpose(pb[0][:, ts(sc, 8)], Rs("glog", sc * 128, 128, (0, 8)),
                                              Rs("eye", 0, 8, (0, 8))))
        P.wait("vector", "T", P.cnt["T"])
        v_op(lambda e: e.tensor_tensor(R("lg"), pb[0][:, 0:128], R("gb16"), OP.add))
        consume_bank(0)
        v_op(lambda e: e.reduce_max(R("mx"), R("lg").rearrange("p (a b) -> p a b", a=16), axis=AX.X))
        for sc in range(NT):
            v_op(lambda e, sc=sc: e.tensor_scalar(Rs("eqw", sc * 8, 8), Rs("lg", sc * 8, 8),
                                                  Rs("mx", sc, 1).opt(), None, OP.is_ge))
        v_op(lambda e: e.tensor_tensor(R("eqw"), R("eqw"), R("p216"), OP.mult))
        v_op(lambda e: e.reduce_max(R("mx"), R("eqw").rearrange("p (a b) -> p a b", a=16), axis=AX.X))
        for sc in range(NT):
            v_op(lambda e, sc=sc: e.tensor_scalar(Rs("mask", sc * 8, 8), Rs("eqw", sc * 8, 8),
                                                  Rs("mx", sc, 1).opt(), None, OP.is_ge))
        P.wait("tensor", "V", P.cnt["V"])
        for grp in range(4):
            b = [4, 6][grp % 2]
            guard_bank(b)
            for j in range(4):
                sc = grp * 4 + j
                t_op(lambda e, b=b, j=j, sc=sc: e.transpose(
                    pb[b][0:8, ts(j, 128)], Rs("mask", sc * 8, 8), R("eye")))
            P.wait("vector", "T", P.cnt["T"])
            v_op(lambda e, b=b, grp=grp: e.tensor_copy(Rs("mfm", grp * 512, 512, (0, 8)), pb[b][0:8, :]))
            consume_bank(b)
        P.wait("tensor", "V", P.cnt["V"])
        P.wait("tensor", "LDG", ld_im4)
        for n in range(4):
            b = [0, 2][n % 2]
            guard_bank(b)
            t_op(lambda e, b=b, n=n: e.matmul(pb[b][0:32, :], R("im4", 8), Rs("mfm", n * 512, 512, (0, 8)),
                                              start=True, stop=True))
            P.wait("vector", "T", P.cnt["T"])
            v_op(lambda e, b=b, n=n: e.tensor_tensor(Rs("z", n * 512, 512, (0, 32)),
                                                     Rs("u", n * 512, 512, (0, 32)), pb[b][0:32, :], OP.mult))
            consume_bank(b)

        P.barrier()
        ckpt(13)

        ld_bcat = gload(R("bcatT", 32), bcatT_in)
        ld_wob = gload(R("wob", 1), wob_in)
        sy_dma(lambda e: e.dma_start(out=R("wibc"), in_=wibc_in).then_inc(SM["LDS"], 16))
        ld_wibc = P.cnt["LDS"]
        P.wait("tensor", "LDG", ld_wo)
        P.wait("tensor", "LDG", ld_wob)
        P.wait("scalar", "LDS", ld_wibc)
        wi_guard = {}
        war_wo, war_out = {}, {}
        for tc in range(NTC):
            for fc in range(32):
                slot = fc % 4
                if tc * 32 + fc >= 4:
                    P.wait("gpsimd", "T", wi_guard[slot])
                P.raw("gpsimd", lambda e, fc=fc, slot=slot: e.dma_start(
                    out=R("wib%d" % slot), in_=wiT_in[:, ts(fc, 1024)]).then_inc(SM["LDW%d" % slot], 16))
                P.inc("LDW%d" % slot, 16)
                P.wait("tensor", "LDW%d" % slot, P.cnt["LDW%d" % slot])
                b = [0, 2][fc % 2]
                guard_bank(b)
                for dc in range(8):
                    t_op(lambda e, b=b, slot=slot, dc=dc, tc=tc, st=(dc == 0): e.matmul(
                        pb[b][:], Rs("wib%d" % slot, dc * 128, 128),
                        Rs("ln2", dc * SLOC + tc * TC, TC), start=st, stop=False))
                t_op(lambda e, b=b, fc=fc, tc=tc: e.matmul(
                    pb[b][:], Rs("bcatT", fc * 128, 128, (0, 32)),
                    Rs("z", tc * TC, TC, (0, 32)), start=False, stop=True))
                wi_guard[slot] = P.cnt["T"]
                P.wait("scalar", "T", P.cnt["T"])
                s_op(lambda e, b=b, fc=fc: e.activation(
                    Rs("inter", fc * TC, TC), pb[b][:], ACT.Relu, bias=Rs("wibc", fc, 1).opt()))
                consume_bank(b, sem="Sc")
            P.wait("tensor", "Sc", P.cnt["Sc"])
            for tt in range(TC // 128):
                sc = tc * (TC // 128) + tt
                xj = sc % 4
                if sc >= 4:
                    P.wait("sync", "V", war_wo[xj])
                sy_dma(lambda e, xj=xj, sc=sc: e.dma_start(out=R("x1r%d" % xj), in_=x1b[ts(sc, 128), :]).then_inc(SM["LDS"], 16))
                for dh in range(2):
                    b = [4, 6][(tt * 2 + dh) % 2]
                    guard_bank(b)
                    for fc in range(32):
                        t_op(lambda e, b=b, fc=fc, tt=tt, dh=dh, st=(fc == 0): e.matmul(
                            pb[b][:], Rs("inter", fc * TC + tt * 128, 128),
                            Rs("woT", fc * DM + dh * 512, 512), start=st, stop=False))
                    t_op(lambda e, b=b, dh=dh: e.matmul(
                        pb[b][:], R("ones1b", 1), Rs("wob", dh * 512, 512, (0, 1)), start=False, stop=True))
                    P.wait("vector", "T", P.cnt["T"])
                    P.wait("vector", "LDS", P.cnt["LDS"])
                    onm = "out%d" % (sc % 2)
                    if sc >= 2 and dh == 0:
                        P.wait("vector", "STQ", war_out[sc % 2])
                    v_op(lambda e, b=b, onm=onm, xj=xj, dh=dh: e.tensor_tensor(
                        Rs(onm, dh * 512, 512), Rs("x1r%d" % xj, dh * 512, 512), pb[b][:], OP.add))
                    consume_bank(b)
                war_wo[xj] = P.cnt["V"]
                P.wait("scalar", "V", P.cnt["V"])
                sc_dma(lambda e, onm="out%d" % (sc % 2), sc=sc: e.dma_start(
                    out=y_out[ts(sc, 128), :], in_=Rs(onm, 0, DM)).then_inc(SM["STQ"], 16))
                war_out[sc % 2] = P.cnt["STQ"]

        P.wait("scalar", "STQ", P.cnt["STQ"])
    except _StopBuild:
        pass

    with nc.Block() as block:
        def run(eng_name):
            def body(e):
                for st_ in P.steps[eng_name]:
                    if isinstance(st_, tuple):
                        e.wait_ge(sems[st_[1]], st_[2])
                    else:
                        st_(e)
            return body
        block.tensor(run("tensor"))
        block.vector(run("vector"))
        block.scalar(run("scalar"))
        block.gpsimd(run("gpsimd"))
        block.sync(run("sync"))

    _elide_ldweights(nc)
    es.close()
    return nc


_MM_FLDS = ("replication_resolution", "replication_shift_amnt", "replication_num_rows",
            "start_tensor_calc", "stop_tensor_calc", "is_transpose", "is_fmap_onezero",
            "is_weight_onezero", "ifmap_quant_offset", "weights_quant_offset", "tile_size",
            "tile_position", "perf_mode", "bass_skip_group_check", "name", "engine",
            "debug", "ins", "outs", "sync_info", "emission_index", "descendants")


def _elide_ldweights(nc):
    n = 0
    for fn in nc.m.functions:
        for bl in fn.blocks:
            if "_PE_" not in bl.name:
                continue
            il = bl.instructions
            prev = None
            for i, ins in enumerate(il):
                tn = type(ins).__name__
                if tn == "InstEventSemaphore":
                    continue
                if tn != "InstMatmult" or ins.is_transpose:
                    prev = None
                    continue
                try:
                    a = ins.ins[1]
                    if "bfloat16" not in str(a.dtype) or "bfloat16" not in str(ins.ins[0].dtype):
                        prev = None
                        continue
                    if a.ap[1][1] != ins.outs[0].ap[0][1]:
                        prev = None
                        continue
                    k = (a.memref, a.offset, str(a.ap), str(a.dtype), ins.tile_position)
                except Exception:
                    prev = None
                    continue
                if k == prev:
                    kw = {f: getattr(ins, f) for f in _MM_FLDS}
                    kw["ldweights"] = False
                    il[i] = mybir.InstMatmult(**kw)
                    n += 1
                else:
                    prev = k
    return n


_CACHE = {}


def _host_stage(inputs):
    f32 = np.float32
    x = np.asarray(inputs["L0_attn_hidden_states"], f32)[0]
    Wq = np.asarray(inputs["L0_attn_Wq"], f32)
    bq = np.asarray(inputs["L0_attn_bq"], f32)
    Wk = np.asarray(inputs["L0_attn_Wk"], f32)
    bk = np.asarray(inputs["L0_attn_bk"], f32)
    Wv = np.asarray(inputs["L0_attn_Wv"], f32)
    bv = np.asarray(inputs["L0_attn_bv"], f32)
    conv_w = np.asarray(inputs["L0_attn_conv_w"], f32)
    gW = np.asarray(inputs["L0_ffn_gate_W"], f32)
    gb = np.asarray(inputs["L0_ffn_gate_b"], f32)
    wiW = np.asarray(inputs["L0_ffn_wi_W"], f32)
    wib = np.asarray(inputs["L0_ffn_wi_b"], f32)
    woW = np.asarray(inputs["L0_ffn_wo_W"], f32)
    wob = np.asarray(inputs["L0_ffn_wo_b"], f32)
    lA = np.asarray(inputs["L0_ffn_lora_A"], f32)
    lB = np.asarray(inputs["L0_ffn_lora_B"], f32)

    xpad = np.zeros((S + 256, DM), f32)
    xpad[128:128 + S] = x

    wiT = np.ascontiguousarray(wiW.T)
    wiT_pm = np.ascontiguousarray(
        wiT.reshape(8, 128, 32, 128).transpose(1, 2, 0, 3)).reshape(128, 32 * 1024)
    woT = np.ascontiguousarray(woW.T)
    gateT = np.ascontiguousarray(gW.T)
    acatT = np.ascontiguousarray(lA.reshape(32, DM).T)
    bcatT = np.ascontiguousarray(lB.transpose(0, 2, 1).reshape(32, FF))

    w65 = conv_w[:, 0, :, 0]
    Tfull = np.zeros((H, 128, 64), f32)
    for p in range(64):
        Tfull[:, p:p + 65, p] = w65
    tconv = np.zeros((H, 3, 128, 64), f32)
    tconv[:, 0] = Tfull
    tconv[:, 1, 64:128] = Tfull[:, 0:64]
    tconv[:, 2, 0:64] = Tfull[:, 64:128]
    tconv_pm = np.ascontiguousarray(tconv.transpose(2, 0, 1, 3)).reshape(128, H * 3 * 64)

    eye = np.eye(128, dtype=f32)
    izf = np.zeros((128, 16 * H), f32)
    for h in range(H):
        izf[(h % 2) * 64:(h % 2) * 64 + 64, 16 * h + h] = 1.0
    izff = np.zeros((128, 256), f32)
    for h in range(H):
        izff[0:64, 16 * h + h] = 1.0
    ipair = np.zeros((16, 8, 128), f32)
    for pp in range(8):
        ipair[2 * pp, pp, 0:64] = 1.0
        ipair[2 * pp + 1, pp, 64:128] = 1.0
    ipair = ipair.reshape(16, 1024)
    im4 = np.zeros((E8, 32), f32)
    for e_ in range(E8):
        im4[e_, 4 * e_:4 * e_ + 4] = 1.0
    eyep = np.zeros((128, 1024), f32)
    for a in range(16):
        eyep[0:64, a * 64:(a + 1) * 64] = np.eye(64)
    bvb = np.broadcast_to(bv, (128, DM)).copy()
    bqc = np.ascontiguousarray(bq.reshape(8, 128).T)
    bkc = np.ascontiguousarray(bk.reshape(8, 128).T)
    wibc = np.ascontiguousarray(wib.reshape(32, 128).T)
    gb16 = np.tile(np.broadcast_to(gb, (128, 8)), (1, 16))
    p216 = np.tile((2.0 ** -np.arange(8, dtype=f32))[None, :], (128, 16))

    shared = dict(wq=Wq, wk=Wk, wv=Wv, wiT=wiT_pm, woT=woT, gateT=gateT,
                  acatT=acatT, bcatT=bcatT, tconv=tconv_pm, eye=eye, izf=izf,
                  izff=izff, ipair=ipair, im4=im4, i7=7.0 * eyep,
                  i15=15.0 * eyep, i13=13.0 * eyep, bvb=bvb, bqc=bqc, bkc=bkc,
                  wibc=wibc, gb16=gb16, p216=p216, wob=wob.reshape(1, DM))
    in_maps = []
    for c in range(NCORE):
        sl = xpad[c * SLOC: c * SLOC + TPAD]
        m = dict(shared)
        m["xT"] = np.ascontiguousarray(sl.T)
        m["xtm"] = np.ascontiguousarray(x[c * SLOC:(c + 1) * SLOC])
        m["bvlo"] = np.zeros_like(bvb) if c == 0 else bvb
        m["bvhi"] = np.zeros_like(bvb) if c == NCORE - 1 else bvb
        in_maps.append(m)
    return in_maps


def kernel(**inputs):
    in_maps = _host_stage(inputs)
    if "nc" not in _CACHE:
        _CACHE["nc"] = build_program()
    res = run_bass_kernel_spmd(_CACHE["nc"], in_maps, core_ids=list(range(NCORE)))
    out = np.concatenate([np.asarray(r["y"], np.float32) for r in res.results], axis=0)
    return out.reshape(1, S, DM)


# version 17
# speedup vs baseline: 1.0066x; 1.0000x over previous
import os
import numpy as np
from contextlib import ExitStack

import concourse.bass as bass
import concourse.mybir as mybir
from concourse.bass_utils import run_bass_kernel_spmd
from concourse.alu_op_type import AluOpType as OP

F32 = mybir.dt.float32
BF16 = mybir.dt.bfloat16
ACT = mybir.ActivationFunctionType
AX = mybir.AxisListType

_DBG = os.environ.get("KDBG") == "1"

NCORE = 8
S, DM, H, DH, LM, SEG = 16384, 1024, 16, 64, 64, 256
E8, R4, FF, KSZ = 8, 4, 4096, 65
SLOC = S // NCORE
NT = SLOC // 128
NTP = NT + 2
TPAD = NTP * 128
SCALE = float(DH) ** -0.25
EPS = 1e-5
TC = 512
NTC = SLOC // TC


class Sched:
    ENGS = ("tensor", "vector", "scalar", "gpsimd", "sync")
    BAR_SEMS = ("T", "V", "Sc", "LDG", "LDS", "STQ")

    def __init__(self):
        self.steps = {e: [] for e in self.ENGS}
        self.cnt = {}

    def sem(self, name):
        self.cnt.setdefault(name, 0)

    def raw(self, eng, fn):
        self.steps[eng].append(fn)

    def inc(self, name, k):
        self.cnt[name] += k
        return self.cnt[name]

    def wait(self, eng, name, val=None):
        v = self.cnt[name] if val is None else val
        if v <= 0:
            return
        self.steps[eng].append(("W", name, v))

    def barrier(self):
        snap = [(s, self.cnt[s]) for s in self.BAR_SEMS if self.cnt[s] > 0]
        for e in self.ENGS:
            for s, v in snap:
                self.steps[e].append(("W", s, v))


class Arena:

    def __init__(self, name):
        self.name = name
        self.regions = []

    def add(self, name, size, p0, p1):
        self.regions.append((name, size, p0, p1))

    def pack(self):
        def run(order):
            placed = {}
            for name, size, p0, p1 in order:
                pad = (size + 15) // 16 * 16
                spans = sorted((v for v in placed.values()
                                if not (v[3] < p0 or v[2] > p1)), key=lambda v: v[0])
                off = 0
                for (o2, s2, _, _, _) in spans:
                    if off + pad <= o2:
                        break
                    off = max(off, o2 + s2)
                placed[name] = (off, pad, p0, p1, size)
            tot = max((o + s for o, s, _, _, _ in placed.values()), default=0)
            return placed, tot
        best = None
        for key in (lambda r: (-r[1],), lambda r: (r[2], -r[1]),
                    lambda r: (-(r[3] - r[2]), -r[1]), lambda r: (r[2], r[3], -r[1])):
            placed, tot = run(sorted(self.regions, key=key))
            if best is None or tot < best[1]:
                best = (placed, tot)
        rng = np.random.default_rng(12345)
        base = sorted(self.regions, key=lambda r: (-r[1],))
        for _ in range(400):
            order = list(base)
            rng.shuffle(order)
            placed, tot = run(order)
            if tot < best[1]:
                best = (placed, tot)
        self.placed, self.total = best
        return self.total


def ts(i, n):
    return slice(i * n, (i + 1) * n)


class _StopBuild(Exception):
    pass


def build_program(stop_phase=None):
    nc = bass.Bass("TRN2", target_bir_lowering=False)
    P = Sched()
    SEMS = ("T", "V", "Sc", "LDG", "LDS", "STQ", "CC", "CCD", "PC",
            "LDW0", "LDW1", "LDW2", "LDW3")
    for s_ in SEMS:
        P.sem(s_)

    def din(name, shape, dt=F32):
        return nc.dram_tensor(name, list(shape), dt, kind="ExternalInput").ap()

    xT_in = din("xT", (DM, TPAD))
    xtm_in = din("xtm", (SLOC, DM))
    wq_in = din("wq", (DM, DM))
    wk_in = din("wk", (DM, DM))
    wv_in = din("wv", (DM, DM))
    wiT_in = din("wiT", (128, 32 * 1024))
    woT_in = din("woT", (FF, DM))
    gateT_in = din("gateT", (DM, E8))
    acatT_in = din("acatT", (DM, 32))
    bcatT_in = din("bcatT", (32, FF))
    tconv_in = din("tconv", (128, H * 3 * 64))
    eye_in = din("eye", (128, 128))
    izf_in = din("izf", (128, 16 * H))
    izff_in = din("izff", (128, 256))
    ipair_in = din("ipair", (16, 1024))
    im4_in = din("im4", (E8, 32))
    i7_in = din("i7", (128, 1024))
    i15_in = din("i15", (128, 1024))
    i13_in = din("i13", (128, 1024))
    bvb_in = din("bvb", (128, DM))
    bvlo_in = din("bvlo", (128, DM))
    bvhi_in = din("bvhi", (128, DM))
    bqc_in = din("bqc", (128, 8))
    bkc_in = din("bkc", (128, 8))
    wibc_in = din("wibc", (128, 32))
    gb16_in = din("gb16", (128, 128))
    p216_in = din("p216", (128, 128))
    wob_in = din("wob", (1, DM))
    y_out = nc.dram_tensor("y", [SLOC, DM], F32, kind="ExternalOutput").ap()

    x1b = nc.dram_tensor("x1b", [SLOC, DM], F32).ap()
    cc1i = nc.dram_tensor("cc1i", [128, 64], F32)
    cc1o = nc.dram_tensor("cc1o", [NCORE * 128, 64], F32)
    cc2i = nc.dram_tensor("cc2i", [128, 520], F32)
    cc2o = nc.dram_tensor("cc2o", [NCORE * 128, 520], F32)

    if _DBG:
        dbg = {nm: nc.dram_tensor("dbg_" + nm, list(shp), F32, kind="ExternalOutput").ap()
               for nm, shp in (("ql", (128, 512)), ("kl", (128, 512)),
                               ("vti", (64, 1024)), ("k3v", (128, 512)),
                               ("w2", (128, 512)), ("sumb", (128, 520)),
                               ("blob", (128, 520)))}

    BF = Arena("bf")
    FP = Arena("fp")
    for nm, sz, p0, p1 in (
        ("ln1", 8 * TPAD, 0, 4), ("ln2", 8 * SLOC, 11, 15),
        ("q", 8 * SLOC, 1, 7), ("k", 8 * SLOC, 2, 6), ("v", NTP * DM, 4, 9),
        ("inter", 32 * TC, 13, 15), ("vA", 17 * DM, 9, 10),
        ("e1", 8 * SLOC, 7, 10), ("woT", 32 * DM, 10, 15),
        ("w3", 8 * DM, 0, 3), ("wvh", 8 * 512, 1.5, 4), ("wkx", 4096, 1.5, 3),
        ("xbf", 512, 0, 0), ("xsq", 512, 0, 0),
        ("e3t0", 1024, 4.6, 4.6), ("e3t1", 1024, 4.6, 4.6),
        ("wib0", 1024, 13, 15), ("wib1", 1024, 13, 15),
        ("wib2", 1024, 13, 15), ("wib3", 1024, 13, 15),
        ("z", SLOC, 12, 15), ("mfm", SLOC, 12, 12), ("u", SLOC, 12, 12),
        ("ql", 512, 3, 9), ("kl", 512, 3, 7), ("w2", 512, 9, 10),
        ("lf_all", 512, 0, 3), ("tconv", H * 3 * 64, 8, 10),
        ("izbf", 16 * H, 0, 8), ("im4", 32, 12, 12), ("gateT", 8 * E8, 11, 12),
        ("acatT", 8 * 32, 11, 12), ("bcatT", FF, 12, 15), ("wob", DM, 12, 15),
        ("trash", DM, 10, 10), ("ones1b", 128, 0, 15), ("o128b", 16, 0, 15),
        ("zrow", 512, 0, 15),
    ):
        BF.add(nm, sz, p0, p1)
    for nm, sz, p0, p1 in (
        ("xch0", 2048, 0, 0), ("xch1", 2048, 0, 0), ("rows", 512, 0, 0),
        ("tmp384", 256, 0, 0), ("lf_loc", 64, 0, 3),
        ("e2", 1024, 5.0, 5.25), ("k2n", 1024, 5.0, 5.45),
        ("k2nt", 1024, 5.4, 5.9),
        ("vi", 1024, 5.45, 5.95), ("vti", 1024, 5.45, 5.95), ("kv", 1024, 5.5, 5.9),
        ("kvt", 1024, 5.5, 5.9), ("t1", 1024, 5.5, 5.9), ("t3", 1024, 5.5, 5.9),
        ("t5", 1024, 5.5, 5.9), ("vtd", 512, 5.9, 9),
        ("i7", 1024, 4, 5.9), ("i15", 1024, 4, 5.9), ("i13", 1024, 4, 5.9),
        ("rmcol", 16, 5.0, 5.9), ("rs8", 16, 5.0, 5.25), ("izff", 256, 0, 5.3),
        ("eye", 128, 0, 15), ("ipair", 1024, 6.5, 8),
        ("rz1", SLOC, 8, 8), ("blob", 528, 4.6, 9),
        ("g2a", 528, 4.6, 9), ("g2b", 528, 4.6, 9), ("sumb", 528, 4.6, 9), ("rz3", 8, 9, 9),
        ("k3v", 512, 9, 9), ("bvb", DM, 4, 4), ("bvlo", DM, 4, 4),
        ("bvhi", DM, 4, 4), ("bqc", 8, 0, 3), ("bkc", 8, 0, 3),
        ("wibc", 32, 12, 15), ("gb16", 128, 11, 12), ("p216", 128, 11, 12),
        ("xtm0", DM, 10, 10), ("xtm1", DM, 10, 10), ("ctmp", 512, 10, 10),
        ("x1c0", DM, 10, 10), ("x1c1", DM, 10, 10),
        ("s1", 16, 10, 11), ("s2", 16, 10, 11), ("st16", 256, 11, 11),
        ("r2rows", 2 * SLOC, 11, 11), ("r2mu", 256, 11, 11),
        ("r2sig", 256, 11, 11),
        ("x1r0", DM, 11, 15), ("x1r1", DM, 11, 15),
        ("x1r2", DM, 11, 15), ("x1r3", DM, 11, 15),
        ("tmp512", 512, 7, 11), ("glog", SLOC, 12, 12),
        ("lg", 128, 12, 12), ("eqw", 128, 12, 12), ("mask", 128, 12, 12),
        ("mx", 16, 11, 12),
        ("out0", DM, 13, 15), ("out1", DM, 13, 15), ("ones1f", 128, 0, 15),
        ("consts", 4, 0, 15),
    ):
        FP.add(nm, sz, p0, p1)
    nbf = BF.pack()
    nfp = FP.pack()
    assert 2 * nbf + 4 * nfp <= 208 * 1024, (nbf, nfp, 2 * nbf + 4 * nfp)

    es = ExitStack()
    BFA = es.enter_context(nc.sbuf_tensor("BFA", [128, nbf], BF16))
    FPA = es.enter_context(nc.sbuf_tensor("FPA", [128, nfp], F32))

    def R(name, parts=128):
        arena, t = (BF, BFA) if name in BF.placed else (FP, FPA)
        off, _, _, _, size = arena.placed[name]
        return t[0:parts, off:off + size]

    def Rs(name, lo, n, parts=(0, 128)):
        arena, t = (BF, BFA) if name in BF.placed else (FP, FPA)
        off = arena.placed[name][0]
        return t[parts[0]:parts[1], off + lo:off + lo + n]

    pb = [es.enter_context(nc.psum_tensor(f"pb{i}", [128, 512], F32)) for i in range(8)]
    sems = {}
    for s_ in SEMS:
        sems[s_] = es.enter_context(nc.semaphore(s_))
    SM = sems
    nc.const_aps.register(F32, EPS, Rs("consts", 0, 1))
    nc.const_aps.register(F32, -0.5, Rs("consts", 1, 1))

    bank_guard = {}

    def guard_bank(*bs):
        for b in bs:
            g = bank_guard.get(b)
            if g:
                P.wait("tensor", g[0], g[1])

    def consume_bank(*bs, sem="V"):
        for b in bs:
            bank_guard[b] = (sem, P.cnt[sem])

    def t_op(fn):
        P.raw("tensor", lambda e, fn=fn: fn(e).then_inc(SM["T"], 1))
        return P.inc("T", 1)

    def v_op(fn):
        P.wait("vector", "V", P.cnt["V"])
        P.raw("vector", lambda e, fn=fn: fn(e).then_inc(SM["V"], 1))
        return P.inc("V", 1)

    def s_op(fn):
        P.wait("scalar", "Sc", P.cnt["Sc"])
        P.raw("scalar", lambda e, fn=fn: fn(e).then_inc(SM["Sc"], 1))
        return P.inc("Sc", 1)

    def g_dma(fn):
        P.wait("gpsimd", "LDG", P.cnt["LDG"])
        P.wait("gpsimd", "CCD", P.cnt["CCD"])
        P.raw("gpsimd", fn)
        return P.inc("LDG", 16)

    def c_dma(fn):
        P.wait("gpsimd", "LDG", P.cnt["LDG"])
        P.wait("gpsimd", "CCD", P.cnt["CCD"])
        P.raw("gpsimd", fn)
        return P.inc("CCD", 16)

    def sy_dma(fn):
        P.wait("sync", "LDS", P.cnt["LDS"])
        P.raw("sync", fn)
        return P.inc("LDS", 16)

    def sc_dma(fn):
        P.wait("scalar", "STQ", P.cnt["STQ"])
        P.raw("scalar", fn)
        return P.inc("STQ", 16)

    def gload(dst, src):
        return g_dma(lambda e, d=dst, s=src: e.dma_start(out=d, in_=s).then_inc(SM["LDG"], 16))

    if _DBG:
        def dump(nm, ap_):
            for s_ in ("T", "V", "Sc", "LDS", "STQ"):
                P.wait("gpsimd", s_, P.cnt[s_])
            g_dma(lambda e, d=dbg[nm], s=ap_: e.dma_start(out=d, in_=s).then_inc(SM["LDG"], 16))
            for e_ in ("tensor", "vector", "scalar", "sync"):
                P.wait(e_, "LDG", P.cnt["LDG"])
    else:
        def dump(nm, ap_):
            pass

    def ckpt(n):
        if stop_phase is not None and n >= stop_phase:
            raise _StopBuild()

    def pslot(h):
        return ((h % 2) * 64, (h % 2) * 64 + 64, (h // 2) * 64)

    try:
        for nm, src_ in (("eye", eye_in), ("izff", izff_in),
                         ("bqc", bqc_in), ("bkc", bkc_in)):
            sy_dma(lambda e, d=R(nm, {"ipair": 16}.get(nm, 128)), s=src_:
                   e.dma_start(out=d, in_=s).then_inc(SM["LDS"], 16))
        ld_const = P.cnt["LDS"]

        ld_wq = gload(R("w3").rearrange("p (dc d) -> p dc d", dc=8),
                      wq_in.rearrange("(dc p) d -> p dc d", p=128))
        gload(R("izbf"), izf_in)

        def _memsets(e):
            e.memset(R("ones1b", 1), 1.0).then_inc(SM["V"], 1)
            e.memset(Rs("o128b", 0, 1), 1.0).then_inc(SM["V"], 1)
            e.memset(R("ones1f", 1), 1.0).then_inc(SM["V"], 1)
            e.memset(Rs("consts", 0, 1), EPS).then_inc(SM["V"], 1)
            e.memset(Rs("consts", 1, 1), -0.5).then_inc(SM["V"], 1)
            e.memset(R("zrow"), 0.0).then_inc(SM["V"], 1)
        P.raw("vector", _memsets)
        P.inc("V", 6)
        mset_v = P.cnt["V"]

        NCH = 9
        CW = TPAD // NCH
        war_x, war_xb = {}, {}
        for c in range(NCH):
            xb = "xch%d" % (c % 2)
            qeng, qsem = ("sync", "LDS") if c % 2 == 0 else ("gpsimd", "PC")
            if c >= 2:
                P.wait(qeng, "V", war_x[c % 2])
            if c % 2 == 0:
                xld = sy_dma(lambda e, b=R(xb), c=c: e.dma_start(
                    out=b.rearrange("p (dc t) -> p dc t", dc=8),
                    in_=xT_in.rearrange("(dc p) t -> p dc t", p=128)[:, :, ts(c, CW)],
                ).then_inc(SM["LDS"], 16))
            else:
                P.wait("gpsimd", "PC", P.cnt["PC"])
                P.raw("gpsimd", lambda e, b=R(xb), c=c: e.dma_start(
                    out=b.rearrange("p (dc t) -> p dc t", dc=8),
                    in_=xT_in.rearrange("(dc p) t -> p dc t", p=128)[:, :, ts(c, CW)],
                ).then_inc(SM["PC"], 16))
                xld = P.inc("PC", 16)
            P.wait("tensor", "V", mset_v)
            guard_bank(0, 2)
            P.wait("scalar", qsem, xld)
            P.wait("scalar", "LDS", ld_const)
            for dc in range(8):
                sl = (dc % 2) * CW
                if c + dc >= 2:
                    P.wait("scalar", "T", war_xb[dc % 2])
                s_op(lambda e, b=xb, dc=dc, sl=sl: e.copy(Rs("xbf", sl, CW), Rs(b, dc * CW, CW)))
                s_op(lambda e, b=xb, dc=dc, sl=sl: e.activation(Rs("xsq", sl, CW), Rs(b, dc * CW, CW), ACT.Square))
                P.wait("tensor", "Sc", P.cnt["Sc"])
                t_op(lambda e, sl=sl, st=(dc == 0), sp=(dc == 7): e.matmul(
                    pb[0][0:1, 0:CW], Rs("o128b", 0, 1), Rs("xbf", sl, CW), start=st, stop=sp))
                t_op(lambda e, sl=sl, st=(dc == 0), sp=(dc == 7): e.matmul(
                    pb[2][0:1, 0:CW], Rs("o128b", 0, 1), Rs("xsq", sl, CW), start=st, stop=sp))
                war_xb[dc % 2] = P.cnt["T"]
            P.wait("vector", "T", P.cnt["T"])
            v_op(lambda e: e.tensor_scalar(Rs("rows", 0, CW, (0, 1)), pb[0][0:1, 0:CW], 1.0 / DM, None, OP.mult))
            v_op(lambda e: e.tensor_scalar(Rs("rows", CW, CW, (0, 1)), pb[2][0:1, 0:CW], 1.0 / DM, None, OP.mult))
            v_op(lambda e: e.tensor_tensor(Rs("tmp384", 0, CW, (0, 1)), Rs("rows", 0, CW, (0, 1)), Rs("rows", 0, CW, (0, 1)), OP.mult))
            v_op(lambda e: e.tensor_tensor(Rs("rows", CW, CW, (0, 1)), Rs("rows", CW, CW, (0, 1)), Rs("tmp384", 0, CW, (0, 1)), OP.subtract))
            consume_bank(0, 2)
            P.wait("scalar", "V", P.cnt["V"])
            s_op(lambda e: e.activation(Rs("rows", CW, CW, (0, 1)), Rs("rows", CW, CW, (0, 1)), ACT.Ln, bias=EPS))
            s_op(lambda e: e.activation(Rs("rows", CW, CW, (0, 1)), Rs("rows", CW, CW, (0, 1)), ACT.Exp, scale=-0.5))
            P.wait("tensor", "Sc", P.cnt["Sc"])
            P.wait("tensor", "LDS", ld_const)
            guard_bank(4, 6)
            t_op(lambda e: e.matmul(pb[4][:, 0:CW], R("ones1f", 1), Rs("rows", 0, CW, (0, 1)), start=True, stop=True))
            t_op(lambda e: e.matmul(pb[6][:, 0:CW], R("ones1f", 1), Rs("rows", CW, CW, (0, 1)), start=True, stop=True))
            P.wait("vector", "T", P.cnt["T"])
            for dc in range(8):
                v_op(lambda e, b=xb, dc=dc: e.tensor_tensor(Rs("tmp384", 0, CW), Rs(b, dc * CW, CW), pb[4][:, 0:CW], OP.subtract))
                v_op(lambda e, dc=dc, c=c: e.tensor_tensor(
                    Rs("ln1", dc * TPAD + c * CW, CW), Rs("tmp384", 0, CW), pb[6][:, 0:CW], OP.mult))
            consume_bank(4, 6)
            war_x[c % 2] = P.cnt["V"]

        for dc in range(8):
            v_op(lambda e, dc=dc: e.reduce_sum(
                Rs("lf_loc", dc * 8, 8),
                Rs("ln1", dc * TPAD + 128, SLOC).rearrange("p (g t) -> p g t", g=8),
                axis=AX.X))
        v_op(lambda e: e.tensor_scalar(R("lf_loc"), R("lf_loc"), 1.0 / SEG, None, OP.mult))
        P.wait("gpsimd", "V", P.cnt["V"])
        c_dma(lambda e: e.dma_start(out=cc1i.ap(), in_=R("lf_loc")).then_inc(SM["CCD"], 16))
        P.wait("gpsimd", "CCD", P.cnt["CCD"])

        def _cc1(e):
            e.collective_compute("AllGather", OP.bypass, replica_groups=[list(range(NCORE))],
                                 ins=[cc1i.ap().opt()], outs=[cc1o.ap().opt()]).then_inc(SM["CC"], 1)
        P.raw("gpsimd", _cc1)
        P.inc("CC", 1)
        P.wait("gpsimd", "CC", P.cnt["CC"])
        c_dma(lambda e: e.dma_start(
            out=R("lf_all").rearrange("p (s c) -> p s c", s=8),
            in_=cc1o.ap().rearrange("(s p) c -> p s c", p=128)).then_inc(SM["CCD"], 16))
        cc1_done = P.cnt["CCD"]

        P.barrier()
        ckpt(1)

        wk_src = {dc: ("wvh", dc * 1024) for dc in range(4)}
        wk_src.update({dc: ("wkx", (dc - 4) * 1024) for dc in range(4, 8)})
        ld_wk = 0
        for dc in range(8):
            nm, off = wk_src[dc]
            ld_wk = gload(Rs(nm, off, 1024),
                          wk_in.rearrange("(dc p) d -> p dc d", p=128)[:, dc, :])

        def proj_qk(wld, out_nm, bias_nm, src_of=None):
            P.wait("tensor", "LDG", wld)
            for pp in range(8):
                for n in range(4):
                    b = [0, 2][(pp * 4 + n) % 2]
                    guard_bank(b)
                    for dc in range(8):
                        nm, off = ("w3", dc * 1024) if src_of is None else src_of[dc]
                        t_op(lambda e, b=b, nm=nm, off=off, pp=pp, n=n, dc=dc,
                             st=(dc == 0), sp=(dc == 7): e.matmul(
                            pb[b][:],
                            Rs(nm, off + pp * 128, 128),
                            Rs("ln1", dc * TPAD + 128 + n * 512, 512),
                            start=st, stop=sp))
                    P.wait("vector", "T", P.cnt["T"])
                    v_op(lambda e, b=b, pp=pp, n=n, o=out_nm, bs=bias_nm: e.tensor_scalar(
                        Rs(o, pp * SLOC + n * 512, 512),
                        pb[b][:], Rs(bs, pp, 1).opt(), SCALE, OP.add, OP.mult))
                    consume_bank(b)

        proj_qk(ld_wq, "q", "bqc")
        proj_qk(ld_wk, "k", "bkc", src_of=wk_src)

        P.barrier()
        ckpt(3)

        P.wait("tensor", "CCD", cc1_done)

        def proj_l(out_nm, bias_nm, src_of=None):
            for pp in range(8):
                b = [4, 6][pp % 2]
                guard_bank(b)
                for dc in range(8):
                    nm, off = ("w3", dc * 1024) if src_of is None else src_of[dc]
                    t_op(lambda e, b=b, nm=nm, off=off, pp=pp, dc=dc,
                         st=(dc == 0), sp=(dc == 7): e.matmul(
                        pb[b][:, 0:64],
                        Rs(nm, off + pp * 128, 128),
                        R("lf_all").rearrange("p (s c g) -> p s c g", s=8, c=8)[:, :, dc, :],
                        start=st, stop=sp))
                P.wait("vector", "T", P.cnt["T"])
                v_op(lambda e, b=b, pp=pp, o=out_nm, bs=bias_nm: e.tensor_scalar(
                    Rs(o, pp * 64, 64), pb[b][:, 0:64], Rs(bs, pp, 1).opt(), SCALE, OP.add, OP.mult))
                consume_bank(b)

        proj_l("kl", "bkc", src_of=wk_src)
        proj_l("ql", "bqc")
        dump("kl", R("kl"))
        dump("ql", R("ql"))

        P.barrier()
        ckpt(4)

        for nm, src_ in (("i7", i7_in), ("i15", i15_in), ("i13", i13_in),
                         ("bvb", bvb_in), ("bvlo", bvlo_in), ("bvhi", bvhi_in)):
            sy_dma(lambda e, d=R(nm), s=src_: e.dma_start(out=d, in_=s).then_inc(SM["LDS"], 16))
        ld_i = P.cnt["LDS"]
        P.wait("vector", "LDS", ld_i)
        for hf in range(2):
            P.wait("gpsimd", "T", P.cnt["T"])
            ld_wvh = gload(R("wvh").rearrange("p (dc d) -> p dc d", dc=8),
                           wv_in.rearrange("(dc p) d -> p dc d", p=128)[:, :, ts(hf, 512)])
            P.wait("tensor", "LDG", ld_wvh)
            for g in range(NTP):
                b = [0, 2][g % 2]
                guard_bank(b)
                for dc in range(8):
                    t_op(lambda e, hf=hf, b=b, g=g, dc=dc, st=(dc == 0), sp=(dc == 7): e.matmul(
                        pb[b][:],
                        Rs("ln1", dc * TPAD + g * 128, 128),
                        Rs("wvh", dc * 512, 512),
                        start=st, stop=sp))
                P.wait("vector", "T", P.cnt["T"])
                bv = "bvlo" if g == 0 else ("bvhi" if g == NTP - 1 else "bvb")
                v_op(lambda e, b=b, g=g, hf=hf, bv=bv: e.tensor_tensor(
                    Rs("v", g * DM + hf * 512, 512), pb[b][:], Rs(bv, hf * 512, 512), OP.add))
                consume_bank(b)

        P.barrier()
        ckpt(5)

        for sc in range(NT):
            enm = "e3t%d" % (sc % 2)
            guard_bank(0, 1)
            for h in range(H):
                r0, r1, c0 = pslot(h)
                bk = h % 2
                t_op(lambda e, bk=bk, h=h, sc=sc, r0=r0, r1=r1: e.matmul(
                    pb[bk][:, ts(h // 2, 64)],
                    Rs("k", (h // 2) * SLOC + sc * 128, 128, (r0, r1)),
                    Rs("ql", (h // 2) * 64, 64, (r0, r1)),
                    start=True, stop=True, tile_position=(r0, 0)))
            P.wait("scalar", "T", P.cnt["T"])
            s_op(lambda e, enm=enm: e.activation(
                Rs(enm, 0, 1024).rearrange("p (s two c) -> p s c two", two=2, c=64)[:, :, :, 0],
                pb[0][:], ACT.Exp))
            s_op(lambda e, enm=enm: e.activation(
                Rs(enm, 0, 1024).rearrange("p (s two c) -> p s c two", two=2, c=64)[:, :, :, 1],
                pb[1][:], ACT.Exp))
            consume_bank(0, 1, sem="Sc")
            P.wait("tensor", "Sc", P.cnt["Sc"])
            guard_bank(2, 4)
            for h in range(H):
                r0, r1, c0 = pslot(h)
                t_op(lambda e, enm=enm, h=h, r0=r0, c0=c0, sc=sc: e.matmul(
                    pb[4][r0:r0 + 64, c0:c0 + 64],
                    Rs(enm, (h % 8) * 64 + (h // 8) * 512, 64),
                    Rs("v", (sc + 1) * DM + h * 64, 64),
                    start=True, stop=True, tile_position=(0, r0)))
                t_op(lambda e, enm=enm, h=h, r0=r0: e.matmul(
                    pb[2][r0:r0 + 64, (h // 2):(h // 2) + 1],
                    Rs(enm, (h % 8) * 64 + (h // 8) * 512, 64),
                    R("o128b", 128)[:, 0:1],
                    start=True, stop=True, tile_position=(0, r0)))
            P.wait("vector", "T", P.cnt["T"])
            if sc == 0:
                v_op(lambda e: e.tensor_copy(Rs("blob", 0, 512), pb[4][:]))
                v_op(lambda e: e.tensor_copy(Rs("blob", 512, 8), pb[2][:, 0:8]))
            else:
                v_op(lambda e: e.tensor_tensor(Rs("blob", 0, 512), Rs("blob", 0, 512), pb[4][:], OP.add))
                v_op(lambda e: e.tensor_tensor(Rs("blob", 512, 8), Rs("blob", 512, 8), pb[2][:, 0:8], OP.add))
            consume_bank(2, 4)
        dump("blob", Rs("blob", 0, 520))
        P.wait("gpsimd", "V", P.cnt["V"])
        c_dma(lambda e: e.dma_start(out=cc2i.ap(), in_=Rs("blob", 0, 520)).then_inc(SM["CCD"], 16))
        P.wait("gpsimd", "CCD", P.cnt["CCD"])

        def _cc2(e):
            e.collective_compute("AllGather", OP.bypass, replica_groups=[list(range(NCORE))],
                                 ins=[cc2i.ap().opt()], outs=[cc2o.ap().opt()]).then_inc(SM["CC"], 1)
        P.raw("gpsimd", _cc2)
        P.inc("CC", 1)
        cc2_done = P.cnt["CC"]
        ckpt(7)

        P.barrier()
        ckpt(6)

        guard_bank(0, 1)
        for h in range(H):
            r0, r1, c0 = pslot(h)
            bk = h % 2
            t_op(lambda e, bk=bk, r0=r0, r1=r1, h=h: e.matmul(
                pb[bk][0:64, ts(h // 2, 64)],
                Rs("ql", (h // 2) * 64, 64, (r0, r1)),
                Rs("kl", (h // 2) * 64, 64, (r0, r1)),
                start=True, stop=True, tile_position=(r0, 0)))
        ckpt(5.05)
        P.wait("scalar", "T", P.cnt["T"])
        s_op(lambda e: e.activation(
            Rs("e2", 0, 1024, (0, 64)).rearrange("p (s two c) -> p s c two", two=2, c=64)[:, :, :, 0],
            pb[0][0:64, :], ACT.Exp))
        s_op(lambda e: e.activation(
            Rs("e2", 0, 1024, (0, 64)).rearrange("p (s two c) -> p s c two", two=2, c=64)[:, :, :, 1],
            pb[1][0:64, :], ACT.Exp))
        consume_bank(0, 1, sem="Sc")
        P.wait("vector", "Sc", P.cnt["Sc"])
        v_op(lambda e: e.reduce_sum(Rs("rs8", 0, 16, (0, 64)),
                                    Rs("e2", 0, 1024, (0, 64)).rearrange("p (a m) -> p a m", a=16), axis=AX.X))
        v_op(lambda e: e.reciprocal(Rs("rs8", 0, 16, (0, 64)), Rs("rs8", 0, 16, (0, 64))))
        for h in range(H):
            v_op(lambda e, h=h: e.tensor_scalar(Rs("k2n", h * 64, 64, (0, 64)), Rs("e2", h * 64, 64, (0, 64)),
                                                Rs("rs8", h, 1, (0, 64)).opt(), None, OP.mult))
        ckpt(5.1)
        P.wait("tensor", "V", P.cnt["V"])
        guard_bank(2)
        for h in range(H):
            t_op(lambda e, h=h: e.matmul(
                pb[2][0:16, 0:64],
                Rs("izff", h * 16, 16, (0, 64)),
                Rs("k2n", h * 64, 64, (0, 64)),
                start=(h == 0), stop=(h == H - 1)))
        P.wait("vector", "T", P.cnt["T"])
        v_op(lambda e: e.reduce_max(Rs("rmcol", 0, 1, (0, 16)), pb[2][0:16, 0:64], axis=AX.X))
        consume_bank(2)
        P.wait("tensor", "V", P.cnt["V"])
        guard_bank(4)
        t_op(lambda e: e.transpose(pb[4][0:1, 0:16], Rs("rmcol", 0, 1, (0, 16)), Rs("eye", 0, 16, (0, 16))))
        P.wait("vector", "T", P.cnt["T"])
        v_op(lambda e: e.reduce_max(Rs("rmcol", 1, 1, (0, 1)), pb[4][0:1, 0:16], axis=AX.X))
        consume_bank(4)
        v_op(lambda e: e.reciprocal(Rs("rmcol", 1, 1, (0, 1)), Rs("rmcol", 1, 1, (0, 1))))
        P.wait("tensor", "V", P.cnt["V"])
        guard_bank(6)
        t_op(lambda e: e.matmul(pb[6][:, 0:1], R("ones1f", 1), Rs("rmcol", 1, 1, (0, 1)), start=True, stop=True))
        P.wait("vector", "T", P.cnt["T"])
        v_op(lambda e: e.tensor_copy(Rs("rmcol", 2, 1), pb[6][:, 0:1]))
        consume_bank(6)
        ckpt(5.3)

        def mm16f(lhs_nm, rhs_nm, banks):
            guard_bank(*banks)
            for h in range(H):
                bk = banks[h // 8]
                t_op(lambda e, bk=bk, h=h, ln=lhs_nm, rn=rhs_nm: e.matmul(
                    pb[bk][0:64, ts(h % 8, 64)],
                    Rs(ln, h * 64, 64, (0, 64)),
                    Rs(rn, h * 64, 64, (0, 64)),
                    start=True, stop=True))

        def tr16f(in_nm, banks):
            guard_bank(*banks)
            for h in range(H):
                bk = banks[h // 8]
                t_op(lambda e, bk=bk, h=h, nm=in_nm: e.transpose(
                    pb[bk][0:64, ts(h % 8, 64)],
                    Rs(nm, h * 64, 64, (0, 64)),
                    Rs("eye", 0, 64, (0, 64))))

        def ev2(dst_nm, banks, fn):
            for i, bk in enumerate(banks):
                v_op(lambda e, i=i, bk=bk, d=dst_nm, fn=fn: fn(
                    e, Rs(d, i * 512, 512, (0, 64)), pb[bk][0:64, :], i))
            consume_bank(*banks)

        P.wait("tensor", "V", P.cnt["V"])
        tr16f("k2n", (4, 6))
        P.wait("vector", "T", P.cnt["T"])
        ev2("k2nt", (4, 6), lambda e, d, p, i: e.tensor_copy(d, p))
        v_op(lambda e: e.tensor_scalar(Rs("vi", 0, 1024, (0, 64)), Rs("k2nt", 0, 1024, (0, 64)),
                                       Rs("rmcol", 2, 1, (0, 64)).opt(), None, OP.mult))
        v_op(lambda e: e.tensor_scalar(Rs("vti", 0, 1024, (0, 64)), Rs("k2n", 0, 1024, (0, 64)),
                                       Rs("rmcol", 2, 1, (0, 64)).opt(), None, OP.mult))

        for it in range(6):
            ckpt(5.4 + it / 100.0)
            P.wait("tensor", "V", P.cnt["V"])
            mm16f("k2nt", "vi", (0, 2))
            P.wait("vector", "T", P.cnt["T"])
            ev2("kv", (0, 2), lambda e, d, p, i: e.tensor_copy(d, p))
            for i in range(2):
                v_op(lambda e, i=i: e.scalar_tensor_tensor(
                    Rs("t1", i * 512, 512, (0, 64)), pb[2 * i][0:64, :], -1.0,
                    Rs("i7", i * 512, 512, (0, 64)), OP.mult, OP.add))
            P.wait("tensor", "V", P.cnt["V"])
            tr16f("kv", (4, 6))
            P.wait("vector", "T", P.cnt["T"])
            ev2("kvt", (4, 6), lambda e, d, p, i: e.tensor_copy(d, p))
            P.wait("tensor", "V", P.cnt["V"])
            mm16f("kvt", "t1", (0, 2))
            P.wait("vector", "T", P.cnt["T"])
            ev2("t3", (0, 2), lambda e, d, p, i: e.scalar_tensor_tensor(
                d, p, -1.0, Rs("i15", i * 512, 512, (0, 64)), OP.mult, OP.add))
            P.wait("tensor", "V", P.cnt["V"])
            mm16f("kvt", "t3", (4, 6))
            P.wait("vector", "T", P.cnt["T"])
            ev2("t5", (4, 6), lambda e, d, p, i: e.scalar_tensor_tensor(
                d, p, -1.0, Rs("i13", i * 512, 512, (0, 64)), OP.mult, OP.add))
            P.wait("tensor", "V", P.cnt["V"])
            mm16f("vti", "t5", (0, 2))
            P.wait("vector", "T", P.cnt["T"])
            ev2("vi", (0, 2), lambda e, d, p, i: e.tensor_scalar(d, p, 0.25, None, OP.mult))
            P.wait("tensor", "V", P.cnt["V"])
            tr16f("vi", (4, 6))
            P.wait("vector", "T", P.cnt["T"])
            ev2("vti", (4, 6), lambda e, d, p, i: e.tensor_copy(d, p))

        ckpt(5.8)
        dump("vti", Rs("vti", 0, 1024, (0, 64)))
        P.wait("sync", "V", P.cnt["V"])
        for par in range(2):
            sy_dma(lambda e, par=par: e.dma_start(
                out=FPA[par * 64:(par + 1) * 64,
                        FP.placed["vtd"][0]:FP.placed["vtd"][0] + 512],
                in_=Rs("vti", 0, 1024, (0, 64)).rearrange(
                    "p (a two c) -> p a two c", a=8, two=2)[:, :, par, :],
            ).then_inc(SM["LDS"], 16))
        vtd_done = P.cnt["LDS"]


        ld_ipair = sy_dma(lambda e: e.dma_start(out=R("ipair", 16), in_=ipair_in).then_inc(SM["LDS"], 16))
        for n in range(4):
            for pp in range(8):
                be, bo = [(4, 1), (6, 3)][(n * 8 + pp) % 2]
                guard_bank(be, bo)
                for par in range(2):
                    b = be if par == 0 else bo
                    t_op(lambda e, b=b, pp=pp, n=n, par=par: e.matmul(
                        pb[b][par * 64:par * 64 + 64, :],
                        Rs("kl", pp * 64, 64, (par * 64, par * 64 + 64)),
                        Rs("q", pp * SLOC + n * 512, 512, (par * 64, par * 64 + 64)),
                        start=True, stop=True, tile_position=(par * 64, par * 64)))
                P.wait("scalar", "T", P.cnt["T"])
                s_op(lambda e, b=be, pp=pp, n=n: e.activation(
                    Rs("e1", pp * SLOC + n * 512, 512, (0, 64)), pb[b][0:64, :], ACT.Exp))
                s_op(lambda e, b=bo, pp=pp, n=n: e.activation(
                    Rs("e1", pp * SLOC + n * 512, 512, (64, 128)), pb[b][64:128, :], ACT.Exp))
                consume_bank(be, bo, sem="Sc")
        P.wait("tensor", "Sc", P.cnt["Sc"])
        ckpt(8)
        for n in range(4):
            be, bo = [(0, 1), (2, 3)][n % 2]
            guard_bank(be, bo)
            for h in range(H):
                r0, r1, c0 = pslot(h)
                b = be if h % 2 == 0 else bo
                t_op(lambda e, b=b, h=h, n=n, r0=r0, r1=r1, st=(h < 2), sp=(h >= H - 2): e.matmul(
                    pb[b][0:16, :],
                    Rs("izbf", h * 16, 16, (r0, r1)),
                    Rs("e1", (h // 2) * SLOC + n * 512, 512, (r0, r1)),
                    start=st, stop=sp, tile_position=(r0, 0)))
            P.wait("scalar", "T", P.cnt["T"])
            P.wait("scalar", "V", P.cnt["V"])
            s_op(lambda e, bo=bo: e.activation(Rs("tmp512", 0, 512, (0, 16)),
                                               pb[bo][0:16, :], ACT.Copy))
            P.wait("vector", "T", P.cnt["T"])
            P.wait("vector", "Sc", P.cnt["Sc"])
            v_op(lambda e, be=be: e.tensor_tensor(Rs("tmp512", 0, 512, (0, 16)),
                                                  pb[be][0:16, :], Rs("tmp512", 0, 512, (0, 16)), OP.add))
            v_op(lambda e, n=n: e.reciprocal(Rs("rz1", n * 512, 512, (0, 16)), Rs("tmp512", 0, 512, (0, 16))))
            consume_bank(be, bo)
        P.wait("tensor", "V", P.cnt["V"])
        P.wait("tensor", "LDS", ld_ipair)
        for pp in range(8):
            for n in range(4):
                b = [4, 6][(pp * 4 + n) % 2]
                guard_bank(b)
                t_op(lambda e, b=b, pp=pp, n=n: e.matmul(
                    pb[b][:], Rs("ipair", pp * 128, 128, (0, 16)), Rs("rz1", n * 512, 512, (0, 16)),
                    start=True, stop=True))
                P.wait("vector", "T", P.cnt["T"])
                v_op(lambda e, b=b, pp=pp, n=n: e.tensor_tensor(
                    Rs("e1", pp * SLOC + n * 512, 512),
                    Rs("e1", pp * SLOC + n * 512, 512),
                    pb[b][:], OP.mult))
                consume_bank(b)
        ckpt(9)

        P.wait("gpsimd", "CC", cc2_done)
        g2cnt, g2add = {}, {}
        for s8 in range(8):
            gb = "g2a" if s8 % 2 == 0 else "g2b"
            if s8 >= 2:
                P.wait("gpsimd", "V", g2add[s8 - 2])
            g2cnt[s8] = c_dma(lambda e, gb=gb, s8=s8: e.dma_start(
                out=Rs(gb, 0, 520),
                in_=cc2o.ap().rearrange("(s p) c -> p s c", p=128)[:, s8, :]).then_inc(SM["CCD"], 16))
            P.wait("vector", "CCD", g2cnt[s8])
            if s8 == 0:
                v_op(lambda e, gb=gb: e.tensor_copy(Rs("sumb", 0, 520), Rs(gb, 0, 520)))
            else:
                v_op(lambda e, gb=gb: e.tensor_tensor(Rs("sumb", 0, 520), Rs("sumb", 0, 520),
                                                      Rs(gb, 0, 520), OP.add))
            g2add[s8] = P.cnt["V"]
        v_op(lambda e: e.reciprocal(R("rz3"), Rs("sumb", 512, 8)))
        for a in range(8):
            v_op(lambda e, a=a: e.tensor_scalar(Rs("k3v", a * 64, 64), Rs("sumb", a * 64, 64),
                                                Rs("rz3", a, 1).opt(), None, OP.mult))
        dump("sumb", Rs("sumb", 0, 520))
        dump("k3v", Rs("k3v", 0, 512))
        P.wait("tensor", "V", P.cnt["V"])
        P.wait("tensor", "LDS", vtd_done)
        guard_bank(0, 1)
        for h in range(H):
            r0, r1, c0 = pslot(h)
            bk = h % 2
            t_op(lambda e, bk=bk, r0=r0, r1=r1, c0=c0: e.matmul(
                pb[bk][r0:r1, c0:c0 + 64],
                Rs("vtd", c0, 64, (r0, r1)),
                Rs("k3v", c0, 64, (r0, r1)),
                start=True, stop=True, tile_position=(r0, r0)))
        P.wait("vector", "T", P.cnt["T"])
        v_op(lambda e: e.tensor_copy(Rs("w2", 0, 512, (0, 64)), pb[0][0:64, :]))
        v_op(lambda e: e.tensor_copy(Rs("w2", 0, 512, (64, 128)), pb[1][64:128, :]))
        consume_bank(0, 1)
        dump("w2", R("w2"))

        P.wait("sync", "V", P.cnt["V"])
        sy_dma(lambda e: e.dma_start(
            out=BFA[0:32, BF.placed["vA"][0]:BF.placed["vA"][0] + 17 * DM],
            in_=Rs("v", 0, 17 * DM, (96, 128))).then_inc(SM["LDS"], 16))
        sy_dma(lambda e: e.dma_start(
            out=BFA[32:128, BF.placed["vA"][0]:BF.placed["vA"][0] + 17 * DM],
            in_=Rs("v", DM, 17 * DM, (0, 96))).then_inc(SM["LDS"], 16))
        va_done = P.cnt["LDS"]

        P.barrier()
        ckpt(10)

        ld_tconv = gload(R("tconv"), tconv_in)
        ld_wo = gload(R("woT").rearrange("p (fc d) -> p fc d", fc=32),
                      woT_in.rearrange("(fc p) d -> p fc d", p=128))
        P.wait("tensor", "LDG", ld_tconv)
        P.wait("tensor", "LDS", va_done)

        war_xtm, war_x1c = {}, {}
        for sc in range(NT):
            xnm = "xtm%d" % (sc % 2)
            if sc >= 2:
                P.wait("sync", "V", war_xtm[sc % 2])
            sy_dma(lambda e, d=R(xnm), sc=sc: e.dma_start(out=d, in_=xtm_in[ts(sc, 128), :]).then_inc(SM["LDS"], 16))
            bm = (0, 1, 2, 3) if sc % 2 == 0 else (4, 5, 6, 7)
            guard_bank(*bm)
            for par in range(2):
                for s in range(8):
                    h = 2 * s + par
                    t_op(lambda e, bk=bm[par], h=h, s=s, row=par * 64, sc=sc, st=(s == 0): e.matmul(
                        pb[bk][:, ts(s, 64)],
                        Rs("e1", (h // 2) * SLOC + sc * 128, 128, (row, row + 64)),
                        Rs("w2", (h // 2) * 64, 64, (row, row + 64)),
                        start=st, stop=False, tile_position=(row, 0)))
            t_op(lambda e, bk=bm[2]: e.matmul(pb[bk][:], Rs("zrow", 0, 128, (0, 1)),
                                    Rs("zrow", 0, 512, (0, 1)), start=True, stop=False))
            t_op(lambda e, bk=bm[3]: e.matmul(pb[bk][:], Rs("zrow", 0, 128, (64, 65)),
                                    Rs("zrow", 0, 512, (64, 65)), start=True, stop=False,
                                    tile_position=(64, 0)))
            for h in range(H):
                for m_par in range(2):
                    m = sc * 2 + m_par
                    if m % 2 == 0:
                        pieces = ((0, 0, 128, m // 2),)
                    else:
                        pieces = ((1, 64, 64, (m - 1) // 2), (2, 0, 64, (m + 1) // 2))
                    for (cls, base, kk, j) in pieces:
                        row = 64 if base == 64 else 0
                        bk = (0 if h % 2 == 0 else 2) if row == 0 else (1 if h % 2 == 1 else 3)
                        t_op(lambda e, bk=bm[bk], h=h, s=h // 2, ps0=m_par * 64, cls=cls,
                             base=base, kk=kk, j=j: e.matmul(
                            pb[bk][ps0:ps0 + 64, ts(s, 64)],
                            Rs("tconv", (h * 3 + cls) * 64, 64, (base, base + kk)),
                            Rs("vA", j * DM + h * 64, 64, (base, base + kk)),
                            start=False, stop=False, tile_position=(base, ps0)))
            t_op(lambda e, bk=bm[0]: e.matmul(pb[bk][:], Rs("zrow", 0, 128, (0, 1)),
                                    Rs("zrow", 0, 512, (0, 1)), start=False, stop=True))
            t_op(lambda e, bk=bm[2]: e.matmul(pb[bk][:], Rs("zrow", 0, 128, (0, 1)),
                                    Rs("zrow", 0, 512, (0, 1)), start=False, stop=True))
            t_op(lambda e, bk=bm[1]: e.matmul(pb[bk][:], Rs("zrow", 0, 128, (64, 65)),
                                    Rs("zrow", 0, 512, (64, 65)), start=False, stop=True,
                                    tile_position=(64, 0)))
            t_op(lambda e, bk=bm[3]: e.matmul(pb[bk][:], Rs("zrow", 0, 128, (64, 65)),
                                    Rs("zrow", 0, 512, (64, 65)), start=False, stop=True,
                                    tile_position=(64, 0)))
            P.wait("vector", "T", P.cnt["T"])
            P.wait("vector", "LDS", P.cnt["LDS"])
            cnm = "x1c%d" % (sc % 2)
            if sc >= 2:
                P.wait("vector", "STQ", war_x1c[sc % 2])

            def pstr(nm, par):
                return Rs(nm, 0, DM).rearrange("p (s two c) -> p s c two", two=2, c=64)[:, :, :, par]
            v_op(lambda e, xnm=xnm, bk=bm[0]: e.tensor_tensor(R("ctmp"), pstr(xnm, 0), pb[bk][:], OP.add))
            v_op(lambda e, cnm=cnm, bk=bm[3]: e.tensor_tensor(pstr(cnm, 0), R("ctmp"), pb[bk][:], OP.add))
            v_op(lambda e, xnm=xnm, bk=bm[1]: e.tensor_tensor(R("ctmp"), pstr(xnm, 1), pb[bk][:], OP.add))
            v_op(lambda e, cnm=cnm, bk=bm[2]: e.tensor_tensor(pstr(cnm, 1), R("ctmp"), pb[bk][:], OP.add))
            consume_bank(*bm)
            war_xtm[sc % 2] = P.cnt["V"]
            v_op(lambda e, cnm=cnm, sc=sc: e.reduce_sum(Rs("s1", sc, 1), Rs(cnm, 0, DM), axis=AX.X))
            P.wait("scalar", "V", P.cnt["V"])
            s_op(lambda e, cnm=cnm: e.activation(R("trash"), Rs(cnm, 0, DM), ACT.Square))
            P.wait("vector", "Sc", P.cnt["Sc"])
            v_op(lambda e, sc=sc: e.reduce_sum(Rs("s2", sc, 1), R("trash"), axis=AX.X))
            sc_dma(lambda e, cnm=cnm, sc=sc: e.dma_start(out=x1b[ts(sc, 128), :], in_=Rs(cnm, 0, DM)).then_inc(SM["STQ"], 16))
            war_x1c[sc % 2] = P.cnt["STQ"]

        P.barrier()
        ckpt(11)

        P.wait("vector", "STQ", P.cnt["STQ"])
        v_op(lambda e: e.tensor_scalar(R("s1"), R("s1"), 1.0 / DM, None, OP.mult))
        v_op(lambda e: e.tensor_tensor(R("mx"), R("s1"), R("s1"), OP.mult))
        v_op(lambda e: e.tensor_scalar(R("s2"), R("s2"), 1.0 / DM, None, OP.mult))
        v_op(lambda e: e.tensor_tensor(R("s2"), R("s2"), R("mx"), OP.subtract))
        P.wait("scalar", "V", P.cnt["V"])
        s_op(lambda e: e.activation(R("s2"), R("s2"), ACT.Ln, bias=EPS))
        s_op(lambda e: e.activation(R("s2"), R("s2"), ACT.Exp, scale=-0.5))
        P.wait("tensor", "Sc", P.cnt["Sc"])
        guard_bank(0)
        t_op(lambda e: e.transpose(pb[0][0:16, 0:128], R("s1"), R("eye")))
        t_op(lambda e: e.transpose(pb[0][0:16, 128:256], R("s2"), R("eye")))
        P.wait("vector", "T", P.cnt["T"])
        v_op(lambda e: e.tensor_copy(Rs("st16", 0, 256, (0, 16)), pb[0][0:16, 0:256]))
        consume_bank(0)
        P.wait("sync", "V", P.cnt["V"])
        sy_dma(lambda e: e.dma_start(out=Rs("r2rows", 0, SLOC, (0, 1)), in_=Rs("st16", 0, 128, (0, 16))).then_inc(SM["LDS"], 16))
        sy_dma(lambda e: e.dma_start(out=Rs("r2rows", SLOC, SLOC, (0, 1)), in_=Rs("st16", 128, 128, (0, 16))).then_inc(SM["LDS"], 16))
        P.wait("tensor", "LDS", P.cnt["LDS"])

        war_x1r, x1r_cnt = {}, {}
        for scg in range(8):
            for w in range(2):
                b = [4, 6][w]
                guard_bank(b)
                t_op(lambda e, b=b, scg=scg, w=w: e.matmul(
                    pb[b][:, 0:256], R("ones1f", 1), Rs("r2rows", w * SLOC + scg * 256, 256, (0, 1)),
                    start=True, stop=True))
                P.wait("vector", "T", P.cnt["T"])
                dnm = "r2mu" if w == 0 else "r2sig"
                v_op(lambda e, b=b, d=dnm: e.tensor_copy(Rs(d, 0, 256), pb[b][:, 0:256]))
                consume_bank(b)
            for j in range(2):
                bi = scg * 2 + j
                slot = bi % 4
                qeng, qsem = [("sync", "LDS"), ("scalar", "STQ"),
                              ("gpsimd", "LDG")][bi % 3]
                if bi >= 4:
                    P.wait(qeng, "T", war_x1r[slot])
                if qeng == "sync":
                    x1r_cnt[bi] = (qsem, sy_dma(lambda e, slot=slot, bi=bi: e.dma_start(
                        out=R("x1r%d" % slot), in_=x1b[ts(bi, 128), :]).then_inc(SM["LDS"], 16)))
                elif qeng == "scalar":
                    x1r_cnt[bi] = (qsem, sc_dma(lambda e, slot=slot, bi=bi: e.dma_start(
                        out=R("x1r%d" % slot), in_=x1b[ts(bi, 128), :]).then_inc(SM["STQ"], 16)))
                else:
                    x1r_cnt[bi] = (qsem, gload(R("x1r%d" % slot), x1b[ts(bi, 128), :]))
            for j in range(2):
                qsem, cnt = x1r_cnt[scg * 2 + j]
                P.wait("tensor", qsem, cnt)
            for dc in range(8):
                b = [0, 2][dc % 2]
                guard_bank(b)
                for j in range(2):
                    t_op(lambda e, b=b, dc=dc, j=j, slot=(scg * 2 + j) % 4: e.transpose(
                        pb[b][:, ts(j, 128)], Rs("x1r%d" % slot, dc * 128, 128), R("eye")))
                P.wait("vector", "T", P.cnt["T"])
                v_op(lambda e, b=b: e.tensor_tensor(
                    Rs("tmp512", 0, 256), pb[b][:, 0:256], Rs("r2mu", 0, 256), OP.subtract))
                v_op(lambda e, dc=dc, scg=scg: e.tensor_tensor(
                    Rs("ln2", dc * SLOC + scg * 256, 256),
                    Rs("tmp512", 0, 256), Rs("r2sig", 0, 256), OP.mult))
                consume_bank(b)
            for j in range(2):
                war_x1r[(scg * 2 + j) % 4] = P.cnt["T"]

        P.barrier()
        ckpt(12)

        ld_gate = gload(R("gateT").rearrange("p (dc e) -> p dc e", dc=8),
                        gateT_in.rearrange("(dc p) e -> p dc e", p=128))
        ld_acat = gload(R("acatT").rearrange("p (dc r) -> p dc r", dc=8),
                        acatT_in.rearrange("(dc p) r -> p dc r", p=128))
        ld_im4 = gload(R("im4", 8), im4_in)
        for nm, src_ in (("gb16", gb16_in), ("p216", p216_in)):
            sy_dma(lambda e, d=R(nm), s=src_: e.dma_start(out=d, in_=s).then_inc(SM["LDS"], 16))
        P.wait("tensor", "LDS", P.cnt["LDS"])
        P.wait("tensor", "LDG", ld_acat)
        for n in range(4):
            b = [0, 2][n % 2]
            guard_bank(b)
            for dc in range(8):
                t_op(lambda e, b=b, dc=dc, n=n, st=(dc == 0), sp=(dc == 7): e.matmul(
                    pb[b][0:8, :], Rs("gateT", dc * 8, 8),
                    Rs("ln2", dc * SLOC + n * 512, 512), start=st, stop=sp))
            P.wait("vector", "T", P.cnt["T"])
            v_op(lambda e, b=b, n=n: e.tensor_copy(Rs("glog", n * 512, 512, (0, 8)), pb[b][0:8, :]))
            consume_bank(b)
            b2 = [4, 6][n % 2]
            guard_bank(b2)
            for dc in range(8):
                t_op(lambda e, b=b2, dc=dc, n=n, st=(dc == 0), sp=(dc == 7): e.matmul(
                    pb[b][0:32, :], Rs("acatT", dc * 32, 32),
                    Rs("ln2", dc * SLOC + n * 512, 512), start=st, stop=sp))
            P.wait("vector", "T", P.cnt["T"])
            v_op(lambda e, b=b2, n=n: e.tensor_copy(Rs("u", n * 512, 512, (0, 32)), pb[b][0:32, :]))
            consume_bank(b2)
        P.wait("tensor", "V", P.cnt["V"])
        guard_bank(0)
        for sc in range(NT):
            t_op(lambda e, sc=sc: e.transpose(pb[0][:, ts(sc, 8)], Rs("glog", sc * 128, 128, (0, 8)),
                                              Rs("eye", 0, 8, (0, 8))))
        P.wait("vector", "T", P.cnt["T"])
        v_op(lambda e: e.tensor_tensor(R("lg"), pb[0][:, 0:128], R("gb16"), OP.add))
        consume_bank(0)
        v_op(lambda e: e.reduce_max(R("mx"), R("lg").rearrange("p (a b) -> p a b", a=16), axis=AX.X))
        for sc in range(NT):
            v_op(lambda e, sc=sc: e.tensor_scalar(Rs("eqw", sc * 8, 8), Rs("lg", sc * 8, 8),
                                                  Rs("mx", sc, 1).opt(), None, OP.is_ge))
        v_op(lambda e: e.tensor_tensor(R("eqw"), R("eqw"), R("p216"), OP.mult))
        v_op(lambda e: e.reduce_max(R("mx"), R("eqw").rearrange("p (a b) -> p a b", a=16), axis=AX.X))
        for sc in range(NT):
            v_op(lambda e, sc=sc: e.tensor_scalar(Rs("mask", sc * 8, 8), Rs("eqw", sc * 8, 8),
                                                  Rs("mx", sc, 1).opt(), None, OP.is_ge))
        P.wait("tensor", "V", P.cnt["V"])
        for grp in range(4):
            b = [4, 6][grp % 2]
            guard_bank(b)
            for j in range(4):
                sc = grp * 4 + j
                t_op(lambda e, b=b, j=j, sc=sc: e.transpose(
                    pb[b][0:8, ts(j, 128)], Rs("mask", sc * 8, 8), R("eye")))
            P.wait("vector", "T", P.cnt["T"])
            v_op(lambda e, b=b, grp=grp: e.tensor_copy(Rs("mfm", grp * 512, 512, (0, 8)), pb[b][0:8, :]))
            consume_bank(b)
        P.wait("tensor", "V", P.cnt["V"])
        P.wait("tensor", "LDG", ld_im4)
        for n in range(4):
            b = [0, 2][n % 2]
            guard_bank(b)
            t_op(lambda e, b=b, n=n: e.matmul(pb[b][0:32, :], R("im4", 8), Rs("mfm", n * 512, 512, (0, 8)),
                                              start=True, stop=True))
            P.wait("vector", "T", P.cnt["T"])
            v_op(lambda e, b=b, n=n: e.tensor_tensor(Rs("z", n * 512, 512, (0, 32)),
                                                     Rs("u", n * 512, 512, (0, 32)), pb[b][0:32, :], OP.mult))
            consume_bank(b)

        P.barrier()
        ckpt(13)

        ld_bcat = gload(R("bcatT", 32), bcatT_in)
        ld_wob = gload(R("wob", 1), wob_in)
        sy_dma(lambda e: e.dma_start(out=R("wibc"), in_=wibc_in).then_inc(SM["LDS"], 16))
        ld_wibc = P.cnt["LDS"]
        P.wait("tensor", "LDG", ld_wo)
        P.wait("tensor", "LDG", ld_wob)
        P.wait("scalar", "LDS", ld_wibc)
        wi_guard = {}
        war_wo, war_out = {}, {}
        for tc in range(NTC):
            for fc in range(32):
                slot = fc % 4
                if tc * 32 + fc >= 4:
                    P.wait("gpsimd", "T", wi_guard[slot])
                P.raw("gpsimd", lambda e, fc=fc, slot=slot: e.dma_start(
                    out=R("wib%d" % slot), in_=wiT_in[:, ts(fc, 1024)]).then_inc(SM["LDW%d" % slot], 16))
                P.inc("LDW%d" % slot, 16)
                P.wait("tensor", "LDW%d" % slot, P.cnt["LDW%d" % slot])
                b = [0, 2][fc % 2]
                guard_bank(b)
                for dc in range(8):
                    t_op(lambda e, b=b, slot=slot, dc=dc, tc=tc, st=(dc == 0): e.matmul(
                        pb[b][:], Rs("wib%d" % slot, dc * 128, 128),
                        Rs("ln2", dc * SLOC + tc * TC, TC), start=st, stop=False))
                t_op(lambda e, b=b, fc=fc, tc=tc: e.matmul(
                    pb[b][:], Rs("bcatT", fc * 128, 128, (0, 32)),
                    Rs("z", tc * TC, TC, (0, 32)), start=False, stop=True))
                wi_guard[slot] = P.cnt["T"]
                P.wait("scalar", "T", P.cnt["T"])
                s_op(lambda e, b=b, fc=fc: e.activation(
                    Rs("inter", fc * TC, TC), pb[b][:], ACT.Relu, bias=Rs("wibc", fc, 1).opt()))
                consume_bank(b, sem="Sc")
            P.wait("tensor", "Sc", P.cnt["Sc"])
            for tt in range(TC // 128):
                sc = tc * (TC // 128) + tt
                xj = sc % 4
                if sc >= 4:
                    P.wait("sync", "V", war_wo[xj])
                sy_dma(lambda e, xj=xj, sc=sc: e.dma_start(out=R("x1r%d" % xj), in_=x1b[ts(sc, 128), :]).then_inc(SM["LDS"], 16))
                for dh in range(2):
                    b = [4, 6][(tt * 2 + dh) % 2]
                    guard_bank(b)
                    for fc in range(32):
                        t_op(lambda e, b=b, fc=fc, tt=tt, dh=dh, st=(fc == 0): e.matmul(
                            pb[b][:], Rs("inter", fc * TC + tt * 128, 128),
                            Rs("woT", fc * DM + dh * 512, 512), start=st, stop=False))
                    t_op(lambda e, b=b, dh=dh: e.matmul(
                        pb[b][:], R("ones1b", 1), Rs("wob", dh * 512, 512, (0, 1)), start=False, stop=True))
                    P.wait("vector", "T", P.cnt["T"])
                    P.wait("vector", "LDS", P.cnt["LDS"])
                    onm = "out%d" % (sc % 2)
                    if sc >= 2 and dh == 0:
                        P.wait("vector", "STQ", war_out[sc % 2])
                    v_op(lambda e, b=b, onm=onm, xj=xj, dh=dh: e.tensor_tensor(
                        Rs(onm, dh * 512, 512), Rs("x1r%d" % xj, dh * 512, 512), pb[b][:], OP.add))
                    consume_bank(b)
                war_wo[xj] = P.cnt["V"]
                P.wait("scalar", "V", P.cnt["V"])
                sc_dma(lambda e, onm="out%d" % (sc % 2), sc=sc: e.dma_start(
                    out=y_out[ts(sc, 128), :], in_=Rs(onm, 0, DM)).then_inc(SM["STQ"], 16))
                war_out[sc % 2] = P.cnt["STQ"]

        P.wait("scalar", "STQ", P.cnt["STQ"])
    except _StopBuild:
        pass

    with nc.Block() as block:
        def run(eng_name):
            def body(e):
                for st_ in P.steps[eng_name]:
                    if isinstance(st_, tuple):
                        e.wait_ge(sems[st_[1]], st_[2])
                    else:
                        st_(e)
            return body
        block.tensor(run("tensor"))
        block.vector(run("vector"))
        block.scalar(run("scalar"))
        block.gpsimd(run("gpsimd"))
        block.sync(run("sync"))

    _elide_ldweights(nc)
    es.close()
    return nc


_MM_FLDS = ("replication_resolution", "replication_shift_amnt", "replication_num_rows",
            "start_tensor_calc", "stop_tensor_calc", "is_transpose", "is_fmap_onezero",
            "is_weight_onezero", "ifmap_quant_offset", "weights_quant_offset", "tile_size",
            "tile_position", "perf_mode", "bass_skip_group_check", "name", "engine",
            "debug", "ins", "outs", "sync_info", "emission_index", "descendants")


def _elide_ldweights(nc):
    n = 0
    for fn in nc.m.functions:
        for bl in fn.blocks:
            if "_PE_" not in bl.name:
                continue
            il = bl.instructions
            prev = None
            for i, ins in enumerate(il):
                tn = type(ins).__name__
                if tn == "InstEventSemaphore":
                    continue
                if tn != "InstMatmult" or ins.is_transpose:
                    prev = None
                    continue
                try:
                    a = ins.ins[1]
                    if "bfloat16" not in str(a.dtype) or "bfloat16" not in str(ins.ins[0].dtype):
                        prev = None
                        continue
                    if a.ap[1][1] != ins.outs[0].ap[0][1]:
                        prev = None
                        continue
                    k = (a.memref, a.offset, str(a.ap), str(a.dtype), ins.tile_position)
                except Exception:
                    prev = None
                    continue
                if k == prev:
                    kw = {f: getattr(ins, f) for f in _MM_FLDS}
                    kw["ldweights"] = False
                    il[i] = mybir.InstMatmult(**kw)
                    n += 1
                else:
                    prev = k
    return n


_CACHE = {}


def _host_stage(inputs):
    f32 = np.float32
    x = np.asarray(inputs["L0_attn_hidden_states"], f32)[0]
    Wq = np.asarray(inputs["L0_attn_Wq"], f32)
    bq = np.asarray(inputs["L0_attn_bq"], f32)
    Wk = np.asarray(inputs["L0_attn_Wk"], f32)
    bk = np.asarray(inputs["L0_attn_bk"], f32)
    Wv = np.asarray(inputs["L0_attn_Wv"], f32)
    bv = np.asarray(inputs["L0_attn_bv"], f32)
    conv_w = np.asarray(inputs["L0_attn_conv_w"], f32)
    gW = np.asarray(inputs["L0_ffn_gate_W"], f32)
    gb = np.asarray(inputs["L0_ffn_gate_b"], f32)
    wiW = np.asarray(inputs["L0_ffn_wi_W"], f32)
    wib = np.asarray(inputs["L0_ffn_wi_b"], f32)
    woW = np.asarray(inputs["L0_ffn_wo_W"], f32)
    wob = np.asarray(inputs["L0_ffn_wo_b"], f32)
    lA = np.asarray(inputs["L0_ffn_lora_A"], f32)
    lB = np.asarray(inputs["L0_ffn_lora_B"], f32)

    xpad = np.zeros((S + 256, DM), f32)
    xpad[128:128 + S] = x

    wiT = np.ascontiguousarray(wiW.T)
    wiT_pm = np.ascontiguousarray(
        wiT.reshape(8, 128, 32, 128).transpose(1, 2, 0, 3)).reshape(128, 32 * 1024)
    woT = np.ascontiguousarray(woW.T)
    gateT = np.ascontiguousarray(gW.T)
    acatT = np.ascontiguousarray(lA.reshape(32, DM).T)
    bcatT = np.ascontiguousarray(lB.transpose(0, 2, 1).reshape(32, FF))

    w65 = conv_w[:, 0, :, 0]
    Tfull = np.zeros((H, 128, 64), f32)
    for p in range(64):
        Tfull[:, p:p + 65, p] = w65
    tconv = np.zeros((H, 3, 128, 64), f32)
    tconv[:, 0] = Tfull
    tconv[:, 1, 64:128] = Tfull[:, 0:64]
    tconv[:, 2, 0:64] = Tfull[:, 64:128]
    tconv_pm = np.ascontiguousarray(tconv.transpose(2, 0, 1, 3)).reshape(128, H * 3 * 64)

    eye = np.eye(128, dtype=f32)
    izf = np.zeros((128, 16 * H), f32)
    for h in range(H):
        izf[(h % 2) * 64:(h % 2) * 64 + 64, 16 * h + h] = 1.0
    izff = np.zeros((128, 256), f32)
    for h in range(H):
        izff[0:64, 16 * h + h] = 1.0
    ipair = np.zeros((16, 8, 128), f32)
    for pp in range(8):
        ipair[2 * pp, pp, 0:64] = 1.0
        ipair[2 * pp + 1, pp, 64:128] = 1.0
    ipair = ipair.reshape(16, 1024)
    im4 = np.zeros((E8, 32), f32)
    for e_ in range(E8):
        im4[e_, 4 * e_:4 * e_ + 4] = 1.0
    eyep = np.zeros((128, 1024), f32)
    for a in range(16):
        eyep[0:64, a * 64:(a + 1) * 64] = np.eye(64)
    bvb = np.broadcast_to(bv, (128, DM)).copy()
    bqc = np.ascontiguousarray(bq.reshape(8, 128).T)
    bkc = np.ascontiguousarray(bk.reshape(8, 128).T)
    wibc = np.ascontiguousarray(wib.reshape(32, 128).T)
    gb16 = np.tile(np.broadcast_to(gb, (128, 8)), (1, 16))
    p216 = np.tile((2.0 ** -np.arange(8, dtype=f32))[None, :], (128, 16))

    shared = dict(wq=Wq, wk=Wk, wv=Wv, wiT=wiT_pm, woT=woT, gateT=gateT,
                  acatT=acatT, bcatT=bcatT, tconv=tconv_pm, eye=eye, izf=izf,
                  izff=izff, ipair=ipair, im4=im4, i7=7.0 * eyep,
                  i15=15.0 * eyep, i13=13.0 * eyep, bvb=bvb, bqc=bqc, bkc=bkc,
                  wibc=wibc, gb16=gb16, p216=p216, wob=wob.reshape(1, DM))
    in_maps = []
    for c in range(NCORE):
        sl = xpad[c * SLOC: c * SLOC + TPAD]
        m = dict(shared)
        m["xT"] = np.ascontiguousarray(sl.T)
        m["xtm"] = np.ascontiguousarray(x[c * SLOC:(c + 1) * SLOC])
        m["bvlo"] = np.zeros_like(bvb) if c == 0 else bvb
        m["bvhi"] = np.zeros_like(bvb) if c == NCORE - 1 else bvb
        in_maps.append(m)
    return in_maps


def kernel(**inputs):
    in_maps = _host_stage(inputs)
    if "nc" not in _CACHE:
        _CACHE["nc"] = build_program()
    res = run_bass_kernel_spmd(_CACHE["nc"], in_maps, core_ids=list(range(NCORE)))
    out = np.concatenate([np.asarray(r["y"], np.float32) for r in res.results], axis=0)
    return out.reshape(1, S, DM)


# version 18
# speedup vs baseline: 1.0087x; 1.0021x over previous
import os
import numpy as np
from contextlib import ExitStack

import concourse.bass as bass
import concourse.mybir as mybir
from concourse.bass_utils import run_bass_kernel_spmd
from concourse.alu_op_type import AluOpType as OP

F32 = mybir.dt.float32
BF16 = mybir.dt.bfloat16
ACT = mybir.ActivationFunctionType
AX = mybir.AxisListType

_DBG = os.environ.get("KDBG") == "1"

NCORE = 8
S, DM, H, DH, LM, SEG = 16384, 1024, 16, 64, 64, 256
E8, R4, FF, KSZ = 8, 4, 4096, 65
SLOC = S // NCORE
NT = SLOC // 128
NTP = NT + 2
TPAD = NTP * 128
SCALE = float(DH) ** -0.25
EPS = 1e-5
TC = 512
NTC = SLOC // TC


class Sched:
    ENGS = ("tensor", "vector", "scalar", "gpsimd", "sync")
    BAR_SEMS = ("T", "V", "Sc", "LDG", "LDS", "STQ")

    def __init__(self):
        self.steps = {e: [] for e in self.ENGS}
        self.cnt = {}

    def sem(self, name):
        self.cnt.setdefault(name, 0)

    def raw(self, eng, fn):
        self.steps[eng].append(fn)

    def inc(self, name, k):
        self.cnt[name] += k
        return self.cnt[name]

    def wait(self, eng, name, val=None):
        v = self.cnt[name] if val is None else val
        if v <= 0:
            return
        self.steps[eng].append(("W", name, v))

    def barrier(self):
        snap = [(s, self.cnt[s]) for s in self.BAR_SEMS if self.cnt[s] > 0]
        for e in self.ENGS:
            for s, v in snap:
                self.steps[e].append(("W", s, v))


class Arena:

    def __init__(self, name):
        self.name = name
        self.regions = []

    def add(self, name, size, p0, p1):
        self.regions.append((name, size, p0, p1))

    def pack(self):
        def run(order):
            placed = {}
            for name, size, p0, p1 in order:
                pad = (size + 15) // 16 * 16
                spans = sorted((v for v in placed.values()
                                if not (v[3] < p0 or v[2] > p1)), key=lambda v: v[0])
                off = 0
                for (o2, s2, _, _, _) in spans:
                    if off + pad <= o2:
                        break
                    off = max(off, o2 + s2)
                placed[name] = (off, pad, p0, p1, size)
            tot = max((o + s for o, s, _, _, _ in placed.values()), default=0)
            return placed, tot
        best = None
        for key in (lambda r: (-r[1],), lambda r: (r[2], -r[1]),
                    lambda r: (-(r[3] - r[2]), -r[1]), lambda r: (r[2], r[3], -r[1])):
            placed, tot = run(sorted(self.regions, key=key))
            if best is None or tot < best[1]:
                best = (placed, tot)
        rng = np.random.default_rng(12345)
        base = sorted(self.regions, key=lambda r: (-r[1],))
        for _ in range(400):
            order = list(base)
            rng.shuffle(order)
            placed, tot = run(order)
            if tot < best[1]:
                best = (placed, tot)
        self.placed, self.total = best
        return self.total


def ts(i, n):
    return slice(i * n, (i + 1) * n)


class _StopBuild(Exception):
    pass


def build_program(stop_phase=None):
    nc = bass.Bass("TRN2", target_bir_lowering=False)
    P = Sched()
    SEMS = ("T", "V", "Sc", "LDG", "LDS", "STQ", "CC", "CCD",
            "LDW0", "LDW1", "LDW2", "LDW3")
    for s_ in SEMS:
        P.sem(s_)

    def din(name, shape, dt=F32):
        return nc.dram_tensor(name, list(shape), dt, kind="ExternalInput").ap()

    xT_in = din("xT", (DM, TPAD))
    xtm_in = din("xtm", (SLOC, DM))
    wq_in = din("wq", (DM, DM))
    wk_in = din("wk", (DM, DM))
    wv_in = din("wv", (DM, DM))
    wiT_in = din("wiT", (128, 32 * 1024))
    woT_in = din("woT", (FF, DM))
    gateT_in = din("gateT", (DM, E8))
    acatT_in = din("acatT", (DM, 32))
    bcatT_in = din("bcatT", (32, FF))
    tconv_in = din("tconv", (128, H * 3 * 64))
    eye_in = din("eye", (128, 128))
    izf_in = din("izf", (128, 16 * H))
    izff_in = din("izff", (128, 256))
    ipair_in = din("ipair", (16, 1024))
    im4_in = din("im4", (E8, 32))
    i7_in = din("i7", (128, 1024))
    i15_in = din("i15", (128, 1024))
    i13_in = din("i13", (128, 1024))
    bvb_in = din("bvb", (128, DM))
    bvlo_in = din("bvlo", (128, DM))
    bvhi_in = din("bvhi", (128, DM))
    bqc_in = din("bqc", (128, 8))
    bkc_in = din("bkc", (128, 8))
    wibc_in = din("wibc", (128, 32))
    gb16_in = din("gb16", (128, 128))
    p216_in = din("p216", (128, 128))
    wob_in = din("wob", (1, DM))
    y_out = nc.dram_tensor("y", [SLOC, DM], F32, kind="ExternalOutput").ap()

    x1b = nc.dram_tensor("x1b", [SLOC, DM], F32).ap()
    cc1i = nc.dram_tensor("cc1i", [128, 64], F32)
    cc1o = nc.dram_tensor("cc1o", [NCORE * 128, 64], F32)
    cc2i = nc.dram_tensor("cc2i", [128, 520], F32)
    cc2o = nc.dram_tensor("cc2o", [NCORE * 128, 520], F32)

    if _DBG:
        dbg = {nm: nc.dram_tensor("dbg_" + nm, list(shp), F32, kind="ExternalOutput").ap()
               for nm, shp in (("ql", (128, 512)), ("kl", (128, 512)),
                               ("vti", (64, 1024)), ("k3v", (128, 512)),
                               ("w2", (128, 512)), ("sumb", (128, 520)),
                               ("blob", (128, 520)))}

    BF = Arena("bf")
    FP = Arena("fp")
    for nm, sz, p0, p1 in (
        ("ln1", 8 * TPAD, 0, 4), ("ln2", 8 * SLOC, 11, 15),
        ("q", 8 * SLOC, 1, 7), ("k", 8 * SLOC, 2, 6), ("v", NTP * DM, 4, 9),
        ("inter", 32 * TC, 13, 15), ("vA", 17 * DM, 9, 10),
        ("e1", 8 * SLOC, 7, 10), ("woT", 32 * DM, 10, 15),
        ("w3", 8 * DM, 0, 3), ("wvh", 8 * 512, 1.5, 4), ("wkx", 4096, 1.5, 3),
        ("xbf", 1024, 0, 0), ("xsq", 1024, 0, 0),
        ("e3t0", 1024, 4.6, 4.6), ("e3t1", 1024, 4.6, 4.6),
        ("wib0", 1024, 13, 15), ("wib1", 1024, 13, 15),
        ("wib2", 1024, 13, 15), ("wib3", 1024, 13, 15),
        ("z", SLOC, 12, 15), ("mfm", SLOC, 12, 12), ("u", SLOC, 12, 12),
        ("ql", 512, 3, 9), ("kl", 512, 3, 7), ("w2", 512, 9, 10),
        ("lf_all", 512, 0, 3), ("tconv", H * 3 * 64, 8, 10),
        ("izbf", 16 * H, 0, 8), ("im4", 32, 12, 12), ("gateT", 8 * E8, 11, 12),
        ("acatT", 8 * 32, 11, 12), ("bcatT", FF, 12, 15), ("wob", DM, 12, 15),
        ("trash", DM, 10, 10), ("ones1b", 128, 0, 15), ("o128b", 16, 0, 15),
        ("zrow", 512, 0, 15),
    ):
        BF.add(nm, sz, p0, p1)
    for nm, sz, p0, p1 in (
        ("xch0", 2048, 0, 0), ("xch1", 2048, 0, 0), ("rows", 512, 0, 0),
        ("rows1", 512, 0, 0), ("tmp384", 256, 0, 0), ("tmp3841", 256, 0, 0), ("lf_loc", 64, 0, 3),
        ("e2", 1024, 5.0, 5.25), ("k2n", 1024, 5.0, 5.45),
        ("k2nt", 1024, 5.4, 5.9),
        ("vi", 1024, 5.45, 5.95), ("vti", 1024, 5.45, 5.95), ("kv", 1024, 5.5, 5.9),
        ("kvt", 1024, 5.5, 5.9), ("t1", 1024, 5.5, 5.9), ("t3", 1024, 5.5, 5.9),
        ("t5", 1024, 5.5, 5.9), ("vtd", 512, 5.9, 9),
        ("i7", 1024, 4, 5.9), ("i15", 1024, 4, 5.9), ("i13", 1024, 4, 5.9),
        ("rmcol", 16, 5.0, 5.9), ("rs8", 16, 5.0, 5.25), ("izff", 256, 0, 5.3),
        ("eye", 128, 0, 15), ("ipair", 1024, 6.5, 8),
        ("rz1", SLOC, 8, 8), ("blob", 528, 4.6, 9),
        ("g2a", 528, 4.6, 9), ("g2b", 528, 4.6, 9), ("sumb", 528, 4.6, 9), ("rz3", 8, 9, 9),
        ("k3v", 512, 9, 9), ("bvb", DM, 4, 4), ("bvlo", DM, 4, 4),
        ("bvhi", DM, 4, 4), ("bqc", 8, 0, 3), ("bkc", 8, 0, 3),
        ("wibc", 32, 12, 15), ("gb16", 128, 11, 12), ("p216", 128, 11, 12),
        ("xtm0", DM, 10, 10), ("xtm1", DM, 10, 10), ("ctmp", 512, 10, 10),
        ("x1c0", DM, 10, 10), ("x1c1", DM, 10, 10),
        ("s1", 16, 10, 11), ("s2", 16, 10, 11), ("st16", 256, 11, 11),
        ("r2rows", 2 * SLOC, 11, 11), ("r2mu", 256, 11, 11),
        ("r2sig", 256, 11, 11),
        ("x1r0", DM, 11, 15), ("x1r1", DM, 11, 15),
        ("x1r2", DM, 11, 15), ("x1r3", DM, 11, 15),
        ("tmp512", 512, 7, 11), ("glog", SLOC, 12, 12),
        ("lg", 128, 12, 12), ("eqw", 128, 12, 12), ("mask", 128, 12, 12),
        ("mx", 16, 11, 12),
        ("out0", DM, 13, 15), ("out1", DM, 13, 15), ("ones1f", 128, 0, 15),
        ("consts", 4, 0, 15),
    ):
        FP.add(nm, sz, p0, p1)
    nbf = BF.pack()
    nfp = FP.pack()
    assert 2 * nbf + 4 * nfp <= 208 * 1024, (nbf, nfp, 2 * nbf + 4 * nfp)

    es = ExitStack()
    BFA = es.enter_context(nc.sbuf_tensor("BFA", [128, nbf], BF16))
    FPA = es.enter_context(nc.sbuf_tensor("FPA", [128, nfp], F32))

    def R(name, parts=128):
        arena, t = (BF, BFA) if name in BF.placed else (FP, FPA)
        off, _, _, _, size = arena.placed[name]
        return t[0:parts, off:off + size]

    def Rs(name, lo, n, parts=(0, 128)):
        arena, t = (BF, BFA) if name in BF.placed else (FP, FPA)
        off = arena.placed[name][0]
        return t[parts[0]:parts[1], off + lo:off + lo + n]

    pb = [es.enter_context(nc.psum_tensor(f"pb{i}", [128, 512], F32)) for i in range(8)]
    sems = {}
    for s_ in SEMS:
        sems[s_] = es.enter_context(nc.semaphore(s_))
    SM = sems
    nc.const_aps.register(F32, EPS, Rs("consts", 0, 1))
    nc.const_aps.register(F32, -0.5, Rs("consts", 1, 1))

    bank_guard = {}

    def guard_bank(*bs):
        for b in bs:
            g = bank_guard.get(b)
            if g:
                P.wait("tensor", g[0], g[1])

    def consume_bank(*bs, sem="V"):
        for b in bs:
            bank_guard[b] = (sem, P.cnt[sem])

    def t_op(fn):
        P.raw("tensor", lambda e, fn=fn: fn(e).then_inc(SM["T"], 1))
        return P.inc("T", 1)

    def v_op(fn):
        P.wait("vector", "V", P.cnt["V"])
        P.raw("vector", lambda e, fn=fn: fn(e).then_inc(SM["V"], 1))
        return P.inc("V", 1)

    def s_op(fn):
        P.wait("scalar", "Sc", P.cnt["Sc"])
        P.raw("scalar", lambda e, fn=fn: fn(e).then_inc(SM["Sc"], 1))
        return P.inc("Sc", 1)

    def g_dma(fn):
        P.wait("gpsimd", "LDG", P.cnt["LDG"])
        P.wait("gpsimd", "CCD", P.cnt["CCD"])
        P.raw("gpsimd", fn)
        return P.inc("LDG", 16)

    def c_dma(fn):
        P.wait("gpsimd", "LDG", P.cnt["LDG"])
        P.wait("gpsimd", "CCD", P.cnt["CCD"])
        P.raw("gpsimd", fn)
        return P.inc("CCD", 16)

    def sy_dma(fn):
        P.wait("sync", "LDS", P.cnt["LDS"])
        P.raw("sync", fn)
        return P.inc("LDS", 16)

    def sc_dma(fn):
        P.wait("scalar", "STQ", P.cnt["STQ"])
        P.raw("scalar", fn)
        return P.inc("STQ", 16)

    def gload(dst, src):
        return g_dma(lambda e, d=dst, s=src: e.dma_start(out=d, in_=s).then_inc(SM["LDG"], 16))

    if _DBG:
        def dump(nm, ap_):
            for s_ in ("T", "V", "Sc", "LDS", "STQ"):
                P.wait("gpsimd", s_, P.cnt[s_])
            g_dma(lambda e, d=dbg[nm], s=ap_: e.dma_start(out=d, in_=s).then_inc(SM["LDG"], 16))
            for e_ in ("tensor", "vector", "scalar", "sync"):
                P.wait(e_, "LDG", P.cnt["LDG"])
    else:
        def dump(nm, ap_):
            pass

    def ckpt(n):
        if stop_phase is not None and n >= stop_phase:
            raise _StopBuild()

    def pslot(h):
        return ((h % 2) * 64, (h % 2) * 64 + 64, (h // 2) * 64)

    try:
        for nm, src_ in (("eye", eye_in), ("izff", izff_in),
                         ("bqc", bqc_in), ("bkc", bkc_in)):
            sy_dma(lambda e, d=R(nm, {"ipair": 16}.get(nm, 128)), s=src_:
                   e.dma_start(out=d, in_=s).then_inc(SM["LDS"], 16))
        ld_const = P.cnt["LDS"]

        ld_wq = gload(R("w3").rearrange("p (dc d) -> p dc d", dc=8),
                      wq_in.rearrange("(dc p) d -> p dc d", p=128))
        gload(R("izbf"), izf_in)

        def _memsets(e):
            e.memset(R("ones1b", 1), 1.0).then_inc(SM["V"], 1)
            e.memset(Rs("o128b", 0, 1), 1.0).then_inc(SM["V"], 1)
            e.memset(R("ones1f", 1), 1.0).then_inc(SM["V"], 1)
            e.memset(Rs("consts", 0, 1), EPS).then_inc(SM["V"], 1)
            e.memset(Rs("consts", 1, 1), -0.5).then_inc(SM["V"], 1)
            e.memset(R("zrow"), 0.0).then_inc(SM["V"], 1)
        P.raw("vector", _memsets)
        P.inc("V", 6)
        mset_v = P.cnt["V"]

        NCH = 9
        CW = TPAD // NCH
        war_x, war_xb = {}, {}
        for c in range(NCH):
            xb = "xch%d" % (c % 2)
            qeng, qsem = ("sync", "LDS") if c % 2 == 0 else ("scalar", "STQ")
            if c >= 2:
                P.wait(qeng, "V", war_x[c % 2])
            if c % 2 == 0:
                xld = sy_dma(lambda e, b=R(xb), c=c: e.dma_start(
                    out=b.rearrange("p (dc t) -> p dc t", dc=8),
                    in_=xT_in.rearrange("(dc p) t -> p dc t", p=128)[:, :, ts(c, CW)],
                ).then_inc(SM["LDS"], 16))
            else:
                xld = sc_dma(lambda e, b=R(xb), c=c: e.dma_start(
                    out=b.rearrange("p (dc t) -> p dc t", dc=8),
                    in_=xT_in.rearrange("(dc p) t -> p dc t", p=128)[:, :, ts(c, CW)],
                ).then_inc(SM["STQ"], 16))
            P.wait("tensor", "V", mset_v)
            bs = (0, 2) if c % 2 == 0 else (4, 6)
            rn = "rows" if c % 2 == 0 else "rows1"
            tn = "tmp384" if c % 2 == 0 else "tmp3841"
            guard_bank(*bs)
            P.wait("scalar", qsem, xld)
            P.wait("scalar", "LDS", ld_const)
            for dc in range(8):
                sl = (dc % 2) * CW + (c % 2) * 2 * CW
                if c >= 2 or dc >= 2:
                    P.wait("scalar", "T", war_xb[(dc % 2, c % 2)])
                s_op(lambda e, b=xb, dc=dc, sl=sl: e.copy(Rs("xbf", sl, CW), Rs(b, dc * CW, CW)))
                s_op(lambda e, b=xb, dc=dc, sl=sl: e.activation(Rs("xsq", sl, CW), Rs(b, dc * CW, CW), ACT.Square))
                P.wait("tensor", "Sc", P.cnt["Sc"])
                t_op(lambda e, bk=bs[0], sl=sl, st=(dc == 0), sp=(dc == 7): e.matmul(
                    pb[bk][0:1, 0:CW], Rs("o128b", 0, 1), Rs("xbf", sl, CW), start=st, stop=sp))
                t_op(lambda e, bk=bs[1], sl=sl, st=(dc == 0), sp=(dc == 7): e.matmul(
                    pb[bk][0:1, 0:CW], Rs("o128b", 0, 1), Rs("xsq", sl, CW), start=st, stop=sp))
                war_xb[(dc % 2, c % 2)] = P.cnt["T"]
            P.wait("vector", "T", P.cnt["T"])
            v_op(lambda e, bk=bs[0], rn=rn: e.tensor_scalar(Rs(rn, 0, CW, (0, 1)), pb[bk][0:1, 0:CW], 1.0 / DM, None, OP.mult))
            v_op(lambda e, bk=bs[1], rn=rn: e.tensor_scalar(Rs(rn, CW, CW, (0, 1)), pb[bk][0:1, 0:CW], 1.0 / DM, None, OP.mult))
            v_op(lambda e, rn=rn, tn=tn: e.tensor_tensor(Rs(tn, 0, CW, (0, 1)), Rs(rn, 0, CW, (0, 1)), Rs(rn, 0, CW, (0, 1)), OP.mult))
            v_op(lambda e, rn=rn, tn=tn: e.tensor_tensor(Rs(rn, CW, CW, (0, 1)), Rs(rn, CW, CW, (0, 1)), Rs(tn, 0, CW, (0, 1)), OP.subtract))
            consume_bank(*bs)
            P.wait("scalar", "V", P.cnt["V"])
            s_op(lambda e, rn=rn: e.activation(Rs(rn, CW, CW, (0, 1)), Rs(rn, CW, CW, (0, 1)), ACT.Ln, bias=EPS))
            s_op(lambda e, rn=rn: e.activation(Rs(rn, CW, CW, (0, 1)), Rs(rn, CW, CW, (0, 1)), ACT.Exp, scale=-0.5))
            P.wait("tensor", "Sc", P.cnt["Sc"])
            P.wait("tensor", "LDS", ld_const)
            guard_bank(*bs)
            t_op(lambda e, bk=bs[0], rn=rn: e.matmul(pb[bk][:, 0:CW], R("ones1f", 1), Rs(rn, 0, CW, (0, 1)), start=True, stop=True))
            t_op(lambda e, bk=bs[1], rn=rn: e.matmul(pb[bk][:, 0:CW], R("ones1f", 1), Rs(rn, CW, CW, (0, 1)), start=True, stop=True))
            P.wait("vector", "T", P.cnt["T"])
            for dc in range(8):
                v_op(lambda e, b=xb, dc=dc, bk=bs[0], tn=tn: e.tensor_tensor(Rs(tn, 0, CW), Rs(b, dc * CW, CW), pb[bk][:, 0:CW], OP.subtract))
                v_op(lambda e, dc=dc, c=c, bk=bs[1], tn=tn: e.tensor_tensor(
                    Rs("ln1", dc * TPAD + c * CW, CW), Rs(tn, 0, CW), pb[bk][:, 0:CW], OP.mult))
            consume_bank(*bs)
            war_x[c % 2] = P.cnt["V"]

        for dc in range(8):
            v_op(lambda e, dc=dc: e.reduce_sum(
                Rs("lf_loc", dc * 8, 8),
                Rs("ln1", dc * TPAD + 128, SLOC).rearrange("p (g t) -> p g t", g=8),
                axis=AX.X))
        v_op(lambda e: e.tensor_scalar(R("lf_loc"), R("lf_loc"), 1.0 / SEG, None, OP.mult))
        P.wait("gpsimd", "V", P.cnt["V"])
        c_dma(lambda e: e.dma_start(out=cc1i.ap(), in_=R("lf_loc")).then_inc(SM["CCD"], 16))
        P.wait("gpsimd", "CCD", P.cnt["CCD"])

        def _cc1(e):
            e.collective_compute("AllGather", OP.bypass, replica_groups=[list(range(NCORE))],
                                 ins=[cc1i.ap().opt()], outs=[cc1o.ap().opt()]).then_inc(SM["CC"], 1)
        P.raw("gpsimd", _cc1)
        P.inc("CC", 1)
        P.wait("gpsimd", "CC", P.cnt["CC"])
        c_dma(lambda e: e.dma_start(
            out=R("lf_all").rearrange("p (s c) -> p s c", s=8),
            in_=cc1o.ap().rearrange("(s p) c -> p s c", p=128)).then_inc(SM["CCD"], 16))
        cc1_done = P.cnt["CCD"]

        P.barrier()
        ckpt(1)

        wk_src = {dc: ("wvh", dc * 1024) for dc in range(4)}
        wk_src.update({dc: ("wkx", (dc - 4) * 1024) for dc in range(4, 8)})
        ld_wk = 0
        for dc in range(8):
            nm, off = wk_src[dc]
            ld_wk = gload(Rs(nm, off, 1024),
                          wk_in.rearrange("(dc p) d -> p dc d", p=128)[:, dc, :])

        def proj_qk(wld, out_nm, bias_nm, src_of=None):
            P.wait("tensor", "LDG", wld)
            for pp in range(8):
                for n in range(4):
                    b = [0, 2][(pp * 4 + n) % 2]
                    guard_bank(b)
                    for dc in range(8):
                        nm, off = ("w3", dc * 1024) if src_of is None else src_of[dc]
                        t_op(lambda e, b=b, nm=nm, off=off, pp=pp, n=n, dc=dc,
                             st=(dc == 0), sp=(dc == 7): e.matmul(
                            pb[b][:],
                            Rs(nm, off + pp * 128, 128),
                            Rs("ln1", dc * TPAD + 128 + n * 512, 512),
                            start=st, stop=sp))
                    P.wait("vector", "T", P.cnt["T"])
                    v_op(lambda e, b=b, pp=pp, n=n, o=out_nm, bs=bias_nm: e.tensor_scalar(
                        Rs(o, pp * SLOC + n * 512, 512),
                        pb[b][:], Rs(bs, pp, 1).opt(), SCALE, OP.add, OP.mult))
                    consume_bank(b)

        proj_qk(ld_wq, "q", "bqc")
        proj_qk(ld_wk, "k", "bkc", src_of=wk_src)

        P.barrier()
        ckpt(3)

        P.wait("tensor", "CCD", cc1_done)

        def proj_l(out_nm, bias_nm, src_of=None):
            for pp in range(8):
                b = [4, 6][pp % 2]
                guard_bank(b)
                for dc in range(8):
                    nm, off = ("w3", dc * 1024) if src_of is None else src_of[dc]
                    t_op(lambda e, b=b, nm=nm, off=off, pp=pp, dc=dc,
                         st=(dc == 0), sp=(dc == 7): e.matmul(
                        pb[b][:, 0:64],
                        Rs(nm, off + pp * 128, 128),
                        R("lf_all").rearrange("p (s c g) -> p s c g", s=8, c=8)[:, :, dc, :],
                        start=st, stop=sp))
                P.wait("vector", "T", P.cnt["T"])
                v_op(lambda e, b=b, pp=pp, o=out_nm, bs=bias_nm: e.tensor_scalar(
                    Rs(o, pp * 64, 64), pb[b][:, 0:64], Rs(bs, pp, 1).opt(), SCALE, OP.add, OP.mult))
                consume_bank(b)

        proj_l("kl", "bkc", src_of=wk_src)
        proj_l("ql", "bqc")
        dump("kl", R("kl"))
        dump("ql", R("ql"))

        P.barrier()
        ckpt(4)

        for nm, src_ in (("i7", i7_in), ("i15", i15_in), ("i13", i13_in),
                         ("bvb", bvb_in), ("bvlo", bvlo_in), ("bvhi", bvhi_in)):
            sy_dma(lambda e, d=R(nm), s=src_: e.dma_start(out=d, in_=s).then_inc(SM["LDS"], 16))
        ld_i = P.cnt["LDS"]
        P.wait("vector", "LDS", ld_i)
        for hf in range(2):
            P.wait("gpsimd", "T", P.cnt["T"])
            ld_wvh = gload(R("wvh").rearrange("p (dc d) -> p dc d", dc=8),
                           wv_in.rearrange("(dc p) d -> p dc d", p=128)[:, :, ts(hf, 512)])
            P.wait("tensor", "LDG", ld_wvh)
            for g in range(NTP):
                b = [0, 2][g % 2]
                guard_bank(b)
                for dc in range(8):
                    t_op(lambda e, hf=hf, b=b, g=g, dc=dc, st=(dc == 0), sp=(dc == 7): e.matmul(
                        pb[b][:],
                        Rs("ln1", dc * TPAD + g * 128, 128),
                        Rs("wvh", dc * 512, 512),
                        start=st, stop=sp))
                P.wait("vector", "T", P.cnt["T"])
                bv = "bvlo" if g == 0 else ("bvhi" if g == NTP - 1 else "bvb")
                v_op(lambda e, b=b, g=g, hf=hf, bv=bv: e.tensor_tensor(
                    Rs("v", g * DM + hf * 512, 512), pb[b][:], Rs(bv, hf * 512, 512), OP.add))
                consume_bank(b)

        P.barrier()
        ckpt(5)

        for sc in range(NT):
            enm = "e3t%d" % (sc % 2)
            guard_bank(0, 1)
            for h in range(H):
                r0, r1, c0 = pslot(h)
                bk = h % 2
                t_op(lambda e, bk=bk, h=h, sc=sc, r0=r0, r1=r1: e.matmul(
                    pb[bk][:, ts(h // 2, 64)],
                    Rs("k", (h // 2) * SLOC + sc * 128, 128, (r0, r1)),
                    Rs("ql", (h // 2) * 64, 64, (r0, r1)),
                    start=True, stop=True, tile_position=(r0, 0)))
            P.wait("scalar", "T", P.cnt["T"])
            s_op(lambda e, enm=enm: e.activation(
                Rs(enm, 0, 1024).rearrange("p (s two c) -> p s c two", two=2, c=64)[:, :, :, 0],
                pb[0][:], ACT.Exp))
            s_op(lambda e, enm=enm: e.activation(
                Rs(enm, 0, 1024).rearrange("p (s two c) -> p s c two", two=2, c=64)[:, :, :, 1],
                pb[1][:], ACT.Exp))
            consume_bank(0, 1, sem="Sc")
            P.wait("tensor", "Sc", P.cnt["Sc"])
            guard_bank(2, 4)
            for h in range(H):
                r0, r1, c0 = pslot(h)
                t_op(lambda e, enm=enm, h=h, r0=r0, c0=c0, sc=sc: e.matmul(
                    pb[4][r0:r0 + 64, c0:c0 + 64],
                    Rs(enm, (h % 8) * 64 + (h // 8) * 512, 64),
                    Rs("v", (sc + 1) * DM + h * 64, 64),
                    start=True, stop=True, tile_position=(0, r0)))
                t_op(lambda e, enm=enm, h=h, r0=r0: e.matmul(
                    pb[2][r0:r0 + 64, (h // 2):(h // 2) + 1],
                    Rs(enm, (h % 8) * 64 + (h // 8) * 512, 64),
                    R("o128b", 128)[:, 0:1],
                    start=True, stop=True, tile_position=(0, r0)))
            P.wait("vector", "T", P.cnt["T"])
            if sc == 0:
                v_op(lambda e: e.tensor_copy(Rs("blob", 0, 512), pb[4][:]))
                v_op(lambda e: e.tensor_copy(Rs("blob", 512, 8), pb[2][:, 0:8]))
            else:
                v_op(lambda e: e.tensor_tensor(Rs("blob", 0, 512), Rs("blob", 0, 512), pb[4][:], OP.add))
                v_op(lambda e: e.tensor_tensor(Rs("blob", 512, 8), Rs("blob", 512, 8), pb[2][:, 0:8], OP.add))
            consume_bank(2, 4)
        dump("blob", Rs("blob", 0, 520))
        P.wait("gpsimd", "V", P.cnt["V"])
        c_dma(lambda e: e.dma_start(out=cc2i.ap(), in_=Rs("blob", 0, 520)).then_inc(SM["CCD"], 16))
        P.wait("gpsimd", "CCD", P.cnt["CCD"])

        def _cc2(e):
            e.collective_compute("AllGather", OP.bypass, replica_groups=[list(range(NCORE))],
                                 ins=[cc2i.ap().opt()], outs=[cc2o.ap().opt()]).then_inc(SM["CC"], 1)
        P.raw("gpsimd", _cc2)
        P.inc("CC", 1)
        cc2_done = P.cnt["CC"]
        ckpt(7)

        P.barrier()
        ckpt(6)

        guard_bank(0, 1)
        for h in range(H):
            r0, r1, c0 = pslot(h)
            bk = h % 2
            t_op(lambda e, bk=bk, r0=r0, r1=r1, h=h: e.matmul(
                pb[bk][0:64, ts(h // 2, 64)],
                Rs("ql", (h // 2) * 64, 64, (r0, r1)),
                Rs("kl", (h // 2) * 64, 64, (r0, r1)),
                start=True, stop=True, tile_position=(r0, 0)))
        ckpt(5.05)
        P.wait("scalar", "T", P.cnt["T"])
        s_op(lambda e: e.activation(
            Rs("e2", 0, 1024, (0, 64)).rearrange("p (s two c) -> p s c two", two=2, c=64)[:, :, :, 0],
            pb[0][0:64, :], ACT.Exp))
        s_op(lambda e: e.activation(
            Rs("e2", 0, 1024, (0, 64)).rearrange("p (s two c) -> p s c two", two=2, c=64)[:, :, :, 1],
            pb[1][0:64, :], ACT.Exp))
        consume_bank(0, 1, sem="Sc")
        P.wait("vector", "Sc", P.cnt["Sc"])
        v_op(lambda e: e.reduce_sum(Rs("rs8", 0, 16, (0, 64)),
                                    Rs("e2", 0, 1024, (0, 64)).rearrange("p (a m) -> p a m", a=16), axis=AX.X))
        v_op(lambda e: e.reciprocal(Rs("rs8", 0, 16, (0, 64)), Rs("rs8", 0, 16, (0, 64))))
        for h in range(H):
            v_op(lambda e, h=h: e.tensor_scalar(Rs("k2n", h * 64, 64, (0, 64)), Rs("e2", h * 64, 64, (0, 64)),
                                                Rs("rs8", h, 1, (0, 64)).opt(), None, OP.mult))
        ckpt(5.1)
        P.wait("tensor", "V", P.cnt["V"])
        guard_bank(2)
        for h in range(H):
            t_op(lambda e, h=h: e.matmul(
                pb[2][0:16, 0:64],
                Rs("izff", h * 16, 16, (0, 64)),
                Rs("k2n", h * 64, 64, (0, 64)),
                start=(h == 0), stop=(h == H - 1)))
        P.wait("vector", "T", P.cnt["T"])
        v_op(lambda e: e.reduce_max(Rs("rmcol", 0, 1, (0, 16)), pb[2][0:16, 0:64], axis=AX.X))
        consume_bank(2)
        P.wait("tensor", "V", P.cnt["V"])
        guard_bank(4)
        t_op(lambda e: e.transpose(pb[4][0:1, 0:16], Rs("rmcol", 0, 1, (0, 16)), Rs("eye", 0, 16, (0, 16))))
        P.wait("vector", "T", P.cnt["T"])
        v_op(lambda e: e.reduce_max(Rs("rmcol", 1, 1, (0, 1)), pb[4][0:1, 0:16], axis=AX.X))
        consume_bank(4)
        v_op(lambda e: e.reciprocal(Rs("rmcol", 1, 1, (0, 1)), Rs("rmcol", 1, 1, (0, 1))))
        P.wait("tensor", "V", P.cnt["V"])
        guard_bank(6)
        t_op(lambda e: e.matmul(pb[6][:, 0:1], R("ones1f", 1), Rs("rmcol", 1, 1, (0, 1)), start=True, stop=True))
        P.wait("vector", "T", P.cnt["T"])
        v_op(lambda e: e.tensor_copy(Rs("rmcol", 2, 1), pb[6][:, 0:1]))
        consume_bank(6)
        ckpt(5.3)

        def mm16f(lhs_nm, rhs_nm, banks):
            guard_bank(*banks)
            for h in range(H):
                bk = banks[h // 8]
                t_op(lambda e, bk=bk, h=h, ln=lhs_nm, rn=rhs_nm: e.matmul(
                    pb[bk][0:64, ts(h % 8, 64)],
                    Rs(ln, h * 64, 64, (0, 64)),
                    Rs(rn, h * 64, 64, (0, 64)),
                    start=True, stop=True))

        def tr16f(in_nm, banks):
            guard_bank(*banks)
            for h in range(H):
                bk = banks[h // 8]
                t_op(lambda e, bk=bk, h=h, nm=in_nm: e.transpose(
                    pb[bk][0:64, ts(h % 8, 64)],
                    Rs(nm, h * 64, 64, (0, 64)),
                    Rs("eye", 0, 64, (0, 64))))

        def ev2(dst_nm, banks, fn):
            for i, bk in enumerate(banks):
                v_op(lambda e, i=i, bk=bk, d=dst_nm, fn=fn: fn(
                    e, Rs(d, i * 512, 512, (0, 64)), pb[bk][0:64, :], i))
            consume_bank(*banks)

        P.wait("tensor", "V", P.cnt["V"])
        tr16f("k2n", (4, 6))
        P.wait("vector", "T", P.cnt["T"])
        ev2("k2nt", (4, 6), lambda e, d, p, i: e.tensor_copy(d, p))
        v_op(lambda e: e.tensor_scalar(Rs("vi", 0, 1024, (0, 64)), Rs("k2nt", 0, 1024, (0, 64)),
                                       Rs("rmcol", 2, 1, (0, 64)).opt(), None, OP.mult))
        v_op(lambda e: e.tensor_scalar(Rs("vti", 0, 1024, (0, 64)), Rs("k2n", 0, 1024, (0, 64)),
                                       Rs("rmcol", 2, 1, (0, 64)).opt(), None, OP.mult))

        for it in range(6):
            ckpt(5.4 + it / 100.0)
            P.wait("tensor", "V", P.cnt["V"])
            mm16f("k2nt", "vi", (0, 2))
            P.wait("vector", "T", P.cnt["T"])
            ev2("kv", (0, 2), lambda e, d, p, i: e.tensor_copy(d, p))
            for i in range(2):
                v_op(lambda e, i=i: e.scalar_tensor_tensor(
                    Rs("t1", i * 512, 512, (0, 64)), pb[2 * i][0:64, :], -1.0,
                    Rs("i7", i * 512, 512, (0, 64)), OP.mult, OP.add))
            P.wait("tensor", "V", P.cnt["V"])
            tr16f("kv", (4, 6))
            P.wait("vector", "T", P.cnt["T"])
            ev2("kvt", (4, 6), lambda e, d, p, i: e.tensor_copy(d, p))
            P.wait("tensor", "V", P.cnt["V"])
            mm16f("kvt", "t1", (0, 2))
            P.wait("vector", "T", P.cnt["T"])
            ev2("t3", (0, 2), lambda e, d, p, i: e.scalar_tensor_tensor(
                d, p, -1.0, Rs("i15", i * 512, 512, (0, 64)), OP.mult, OP.add))
            P.wait("tensor", "V", P.cnt["V"])
            mm16f("kvt", "t3", (4, 6))
            P.wait("vector", "T", P.cnt["T"])
            ev2("t5", (4, 6), lambda e, d, p, i: e.scalar_tensor_tensor(
                d, p, -1.0, Rs("i13", i * 512, 512, (0, 64)), OP.mult, OP.add))
            P.wait("tensor", "V", P.cnt["V"])
            mm16f("vti", "t5", (0, 2))
            P.wait("vector", "T", P.cnt["T"])
            ev2("vi", (0, 2), lambda e, d, p, i: e.tensor_scalar(d, p, 0.25, None, OP.mult))
            P.wait("tensor", "V", P.cnt["V"])
            tr16f("vi", (4, 6))
            P.wait("vector", "T", P.cnt["T"])
            ev2("vti", (4, 6), lambda e, d, p, i: e.tensor_copy(d, p))

        ckpt(5.8)
        dump("vti", Rs("vti", 0, 1024, (0, 64)))
        P.wait("sync", "V", P.cnt["V"])
        for par in range(2):
            sy_dma(lambda e, par=par: e.dma_start(
                out=FPA[par * 64:(par + 1) * 64,
                        FP.placed["vtd"][0]:FP.placed["vtd"][0] + 512],
                in_=Rs("vti", 0, 1024, (0, 64)).rearrange(
                    "p (a two c) -> p a two c", a=8, two=2)[:, :, par, :],
            ).then_inc(SM["LDS"], 16))
        vtd_done = P.cnt["LDS"]


        ld_ipair = sy_dma(lambda e: e.dma_start(out=R("ipair", 16), in_=ipair_in).then_inc(SM["LDS"], 16))
        for n in range(4):
            for pp in range(8):
                be, bo = [(4, 1), (6, 3)][(n * 8 + pp) % 2]
                guard_bank(be, bo)
                for par in range(2):
                    b = be if par == 0 else bo
                    t_op(lambda e, b=b, pp=pp, n=n, par=par: e.matmul(
                        pb[b][par * 64:par * 64 + 64, :],
                        Rs("kl", pp * 64, 64, (par * 64, par * 64 + 64)),
                        Rs("q", pp * SLOC + n * 512, 512, (par * 64, par * 64 + 64)),
                        start=True, stop=True, tile_position=(par * 64, par * 64)))
                P.wait("scalar", "T", P.cnt["T"])
                s_op(lambda e, b=be, pp=pp, n=n: e.activation(
                    Rs("e1", pp * SLOC + n * 512, 512, (0, 64)), pb[b][0:64, :], ACT.Exp))
                s_op(lambda e, b=bo, pp=pp, n=n: e.activation(
                    Rs("e1", pp * SLOC + n * 512, 512, (64, 128)), pb[b][64:128, :], ACT.Exp))
                consume_bank(be, bo, sem="Sc")
        P.wait("tensor", "Sc", P.cnt["Sc"])
        ckpt(8)
        for n in range(4):
            be, bo = [(0, 1), (2, 3)][n % 2]
            guard_bank(be, bo)
            for h in range(H):
                r0, r1, c0 = pslot(h)
                b = be if h % 2 == 0 else bo
                t_op(lambda e, b=b, h=h, n=n, r0=r0, r1=r1, st=(h < 2), sp=(h >= H - 2): e.matmul(
                    pb[b][0:16, :],
                    Rs("izbf", h * 16, 16, (r0, r1)),
                    Rs("e1", (h // 2) * SLOC + n * 512, 512, (r0, r1)),
                    start=st, stop=sp, tile_position=(r0, 0)))
            P.wait("scalar", "T", P.cnt["T"])
            P.wait("scalar", "V", P.cnt["V"])
            s_op(lambda e, bo=bo: e.activation(Rs("tmp512", 0, 512, (0, 16)),
                                               pb[bo][0:16, :], ACT.Copy))
            P.wait("vector", "T", P.cnt["T"])
            P.wait("vector", "Sc", P.cnt["Sc"])
            v_op(lambda e, be=be: e.tensor_tensor(Rs("tmp512", 0, 512, (0, 16)),
                                                  pb[be][0:16, :], Rs("tmp512", 0, 512, (0, 16)), OP.add))
            v_op(lambda e, n=n: e.reciprocal(Rs("rz1", n * 512, 512, (0, 16)), Rs("tmp512", 0, 512, (0, 16))))
            consume_bank(be, bo)
        P.wait("tensor", "V", P.cnt["V"])
        P.wait("tensor", "LDS", ld_ipair)
        for pp in range(8):
            for n in range(4):
                b = [4, 6][(pp * 4 + n) % 2]
                guard_bank(b)
                t_op(lambda e, b=b, pp=pp, n=n: e.matmul(
                    pb[b][:], Rs("ipair", pp * 128, 128, (0, 16)), Rs("rz1", n * 512, 512, (0, 16)),
                    start=True, stop=True))
                P.wait("vector", "T", P.cnt["T"])
                v_op(lambda e, b=b, pp=pp, n=n: e.tensor_tensor(
                    Rs("e1", pp * SLOC + n * 512, 512),
                    Rs("e1", pp * SLOC + n * 512, 512),
                    pb[b][:], OP.mult))
                consume_bank(b)
        ckpt(9)

        P.wait("gpsimd", "CC", cc2_done)
        g2cnt, g2add = {}, {}
        for s8 in range(8):
            gb = "g2a" if s8 % 2 == 0 else "g2b"
            if s8 >= 2:
                P.wait("gpsimd", "V", g2add[s8 - 2])
            g2cnt[s8] = c_dma(lambda e, gb=gb, s8=s8: e.dma_start(
                out=Rs(gb, 0, 520),
                in_=cc2o.ap().rearrange("(s p) c -> p s c", p=128)[:, s8, :]).then_inc(SM["CCD"], 16))
            P.wait("vector", "CCD", g2cnt[s8])
            if s8 == 0:
                v_op(lambda e, gb=gb: e.tensor_copy(Rs("sumb", 0, 520), Rs(gb, 0, 520)))
            else:
                v_op(lambda e, gb=gb: e.tensor_tensor(Rs("sumb", 0, 520), Rs("sumb", 0, 520),
                                                      Rs(gb, 0, 520), OP.add))
            g2add[s8] = P.cnt["V"]
        v_op(lambda e: e.reciprocal(R("rz3"), Rs("sumb", 512, 8)))
        for a in range(8):
            v_op(lambda e, a=a: e.tensor_scalar(Rs("k3v", a * 64, 64), Rs("sumb", a * 64, 64),
                                                Rs("rz3", a, 1).opt(), None, OP.mult))
        dump("sumb", Rs("sumb", 0, 520))
        dump("k3v", Rs("k3v", 0, 512))
        P.wait("tensor", "V", P.cnt["V"])
        P.wait("tensor", "LDS", vtd_done)
        guard_bank(0, 1)
        for h in range(H):
            r0, r1, c0 = pslot(h)
            bk = h % 2
            t_op(lambda e, bk=bk, r0=r0, r1=r1, c0=c0: e.matmul(
                pb[bk][r0:r1, c0:c0 + 64],
                Rs("vtd", c0, 64, (r0, r1)),
                Rs("k3v", c0, 64, (r0, r1)),
                start=True, stop=True, tile_position=(r0, r0)))
        P.wait("vector", "T", P.cnt["T"])
        v_op(lambda e: e.tensor_copy(Rs("w2", 0, 512, (0, 64)), pb[0][0:64, :]))
        v_op(lambda e: e.tensor_copy(Rs("w2", 0, 512, (64, 128)), pb[1][64:128, :]))
        consume_bank(0, 1)
        dump("w2", R("w2"))

        P.wait("sync", "V", P.cnt["V"])
        sy_dma(lambda e: e.dma_start(
            out=BFA[0:32, BF.placed["vA"][0]:BF.placed["vA"][0] + 17 * DM],
            in_=Rs("v", 0, 17 * DM, (96, 128))).then_inc(SM["LDS"], 16))
        sy_dma(lambda e: e.dma_start(
            out=BFA[32:128, BF.placed["vA"][0]:BF.placed["vA"][0] + 17 * DM],
            in_=Rs("v", DM, 17 * DM, (0, 96))).then_inc(SM["LDS"], 16))
        va_done = P.cnt["LDS"]

        P.barrier()
        ckpt(10)

        ld_tconv = gload(R("tconv"), tconv_in)
        ld_wo = gload(R("woT").rearrange("p (fc d) -> p fc d", fc=32),
                      woT_in.rearrange("(fc p) d -> p fc d", p=128))
        P.wait("tensor", "LDG", ld_tconv)
        P.wait("tensor", "LDS", va_done)

        war_xtm, war_x1c = {}, {}
        for sc in range(NT):
            xnm = "xtm%d" % (sc % 2)
            if sc >= 2:
                P.wait("sync", "V", war_xtm[sc % 2])
            sy_dma(lambda e, d=R(xnm), sc=sc: e.dma_start(out=d, in_=xtm_in[ts(sc, 128), :]).then_inc(SM["LDS"], 16))
            bm = (0, 1, 2, 3) if sc % 2 == 0 else (4, 5, 6, 7)
            guard_bank(*bm)
            for par in range(2):
                for s in range(8):
                    h = 2 * s + par
                    t_op(lambda e, bk=bm[par], h=h, s=s, row=par * 64, sc=sc, st=(s == 0): e.matmul(
                        pb[bk][:, ts(s, 64)],
                        Rs("e1", (h // 2) * SLOC + sc * 128, 128, (row, row + 64)),
                        Rs("w2", (h // 2) * 64, 64, (row, row + 64)),
                        start=st, stop=False, tile_position=(row, 0)))
            t_op(lambda e, bk=bm[2]: e.matmul(pb[bk][:], Rs("zrow", 0, 128, (0, 1)),
                                    Rs("zrow", 0, 512, (0, 1)), start=True, stop=False))
            t_op(lambda e, bk=bm[3]: e.matmul(pb[bk][:], Rs("zrow", 0, 128, (64, 65)),
                                    Rs("zrow", 0, 512, (64, 65)), start=True, stop=False,
                                    tile_position=(64, 0)))
            for h in range(H):
                for m_par in range(2):
                    m = sc * 2 + m_par
                    if m % 2 == 0:
                        pieces = ((0, 0, 128, m // 2),)
                    else:
                        pieces = ((1, 64, 64, (m - 1) // 2), (2, 0, 64, (m + 1) // 2))
                    for (cls, base, kk, j) in pieces:
                        row = 64 if base == 64 else 0
                        bk = (0 if h % 2 == 0 else 2) if row == 0 else (1 if h % 2 == 1 else 3)
                        t_op(lambda e, bk=bm[bk], h=h, s=h // 2, ps0=m_par * 64, cls=cls,
                             base=base, kk=kk, j=j: e.matmul(
                            pb[bk][ps0:ps0 + 64, ts(s, 64)],
                            Rs("tconv", (h * 3 + cls) * 64, 64, (base, base + kk)),
                            Rs("vA", j * DM + h * 64, 64, (base, base + kk)),
                            start=False, stop=False, tile_position=(base, ps0)))
            t_op(lambda e, bk=bm[0]: e.matmul(pb[bk][:], Rs("zrow", 0, 128, (0, 1)),
                                    Rs("zrow", 0, 512, (0, 1)), start=False, stop=True))
            t_op(lambda e, bk=bm[2]: e.matmul(pb[bk][:], Rs("zrow", 0, 128, (0, 1)),
                                    Rs("zrow", 0, 512, (0, 1)), start=False, stop=True))
            t_op(lambda e, bk=bm[1]: e.matmul(pb[bk][:], Rs("zrow", 0, 128, (64, 65)),
                                    Rs("zrow", 0, 512, (64, 65)), start=False, stop=True,
                                    tile_position=(64, 0)))
            t_op(lambda e, bk=bm[3]: e.matmul(pb[bk][:], Rs("zrow", 0, 128, (64, 65)),
                                    Rs("zrow", 0, 512, (64, 65)), start=False, stop=True,
                                    tile_position=(64, 0)))
            P.wait("vector", "T", P.cnt["T"])
            P.wait("vector", "LDS", P.cnt["LDS"])
            cnm = "x1c%d" % (sc % 2)
            if sc >= 2:
                P.wait("vector", "STQ", war_x1c[sc % 2])

            def pstr(nm, par):
                return Rs(nm, 0, DM).rearrange("p (s two c) -> p s c two", two=2, c=64)[:, :, :, par]
            v_op(lambda e, xnm=xnm, bk=bm[0]: e.tensor_tensor(R("ctmp"), pstr(xnm, 0), pb[bk][:], OP.add))
            v_op(lambda e, cnm=cnm, bk=bm[3]: e.tensor_tensor(pstr(cnm, 0), R("ctmp"), pb[bk][:], OP.add))
            v_op(lambda e, xnm=xnm, bk=bm[1]: e.tensor_tensor(R("ctmp"), pstr(xnm, 1), pb[bk][:], OP.add))
            v_op(lambda e, cnm=cnm, bk=bm[2]: e.tensor_tensor(pstr(cnm, 1), R("ctmp"), pb[bk][:], OP.add))
            consume_bank(*bm)
            war_xtm[sc % 2] = P.cnt["V"]
            v_op(lambda e, cnm=cnm, sc=sc: e.reduce_sum(Rs("s1", sc, 1), Rs(cnm, 0, DM), axis=AX.X))
            P.wait("scalar", "V", P.cnt["V"])
            s_op(lambda e, cnm=cnm: e.activation(R("trash"), Rs(cnm, 0, DM), ACT.Square))
            P.wait("vector", "Sc", P.cnt["Sc"])
            v_op(lambda e, sc=sc: e.reduce_sum(Rs("s2", sc, 1), R("trash"), axis=AX.X))
            sc_dma(lambda e, cnm=cnm, sc=sc: e.dma_start(out=x1b[ts(sc, 128), :], in_=Rs(cnm, 0, DM)).then_inc(SM["STQ"], 16))
            war_x1c[sc % 2] = P.cnt["STQ"]

        P.barrier()
        ckpt(11)

        P.wait("vector", "STQ", P.cnt["STQ"])
        v_op(lambda e: e.tensor_scalar(R("s1"), R("s1"), 1.0 / DM, None, OP.mult))
        v_op(lambda e: e.tensor_tensor(R("mx"), R("s1"), R("s1"), OP.mult))
        v_op(lambda e: e.tensor_scalar(R("s2"), R("s2"), 1.0 / DM, None, OP.mult))
        v_op(lambda e: e.tensor_tensor(R("s2"), R("s2"), R("mx"), OP.subtract))
        P.wait("scalar", "V", P.cnt["V"])
        s_op(lambda e: e.activation(R("s2"), R("s2"), ACT.Ln, bias=EPS))
        s_op(lambda e: e.activation(R("s2"), R("s2"), ACT.Exp, scale=-0.5))
        P.wait("tensor", "Sc", P.cnt["Sc"])
        guard_bank(0)
        t_op(lambda e: e.transpose(pb[0][0:16, 0:128], R("s1"), R("eye")))
        t_op(lambda e: e.transpose(pb[0][0:16, 128:256], R("s2"), R("eye")))
        P.wait("vector", "T", P.cnt["T"])
        v_op(lambda e: e.tensor_copy(Rs("st16", 0, 256, (0, 16)), pb[0][0:16, 0:256]))
        consume_bank(0)
        P.wait("sync", "V", P.cnt["V"])
        sy_dma(lambda e: e.dma_start(out=Rs("r2rows", 0, SLOC, (0, 1)), in_=Rs("st16", 0, 128, (0, 16))).then_inc(SM["LDS"], 16))
        sy_dma(lambda e: e.dma_start(out=Rs("r2rows", SLOC, SLOC, (0, 1)), in_=Rs("st16", 128, 128, (0, 16))).then_inc(SM["LDS"], 16))
        P.wait("tensor", "LDS", P.cnt["LDS"])

        war_x1r, x1r_cnt = {}, {}
        for scg in range(8):
            for w in range(2):
                b = [4, 6][w]
                guard_bank(b)
                t_op(lambda e, b=b, scg=scg, w=w: e.matmul(
                    pb[b][:, 0:256], R("ones1f", 1), Rs("r2rows", w * SLOC + scg * 256, 256, (0, 1)),
                    start=True, stop=True))
                P.wait("vector", "T", P.cnt["T"])
                dnm = "r2mu" if w == 0 else "r2sig"
                v_op(lambda e, b=b, d=dnm: e.tensor_copy(Rs(d, 0, 256), pb[b][:, 0:256]))
                consume_bank(b)
            for j in range(2):
                bi = scg * 2 + j
                slot = bi % 4
                qeng, qsem = [("sync", "LDS"), ("scalar", "STQ"),
                              ("gpsimd", "LDG")][bi % 3]
                if bi >= 4:
                    P.wait(qeng, "T", war_x1r[slot])
                if qeng == "sync":
                    x1r_cnt[bi] = (qsem, sy_dma(lambda e, slot=slot, bi=bi: e.dma_start(
                        out=R("x1r%d" % slot), in_=x1b[ts(bi, 128), :]).then_inc(SM["LDS"], 16)))
                elif qeng == "scalar":
                    x1r_cnt[bi] = (qsem, sc_dma(lambda e, slot=slot, bi=bi: e.dma_start(
                        out=R("x1r%d" % slot), in_=x1b[ts(bi, 128), :]).then_inc(SM["STQ"], 16)))
                else:
                    x1r_cnt[bi] = (qsem, gload(R("x1r%d" % slot), x1b[ts(bi, 128), :]))
            for j in range(2):
                qsem, cnt = x1r_cnt[scg * 2 + j]
                P.wait("tensor", qsem, cnt)
            for dc in range(8):
                b = [0, 2][dc % 2]
                guard_bank(b)
                for j in range(2):
                    t_op(lambda e, b=b, dc=dc, j=j, slot=(scg * 2 + j) % 4: e.transpose(
                        pb[b][:, ts(j, 128)], Rs("x1r%d" % slot, dc * 128, 128), R("eye")))
                P.wait("vector", "T", P.cnt["T"])
                v_op(lambda e, b=b: e.tensor_tensor(
                    Rs("tmp512", 0, 256), pb[b][:, 0:256], Rs("r2mu", 0, 256), OP.subtract))
                v_op(lambda e, dc=dc, scg=scg: e.tensor_tensor(
                    Rs("ln2", dc * SLOC + scg * 256, 256),
                    Rs("tmp512", 0, 256), Rs("r2sig", 0, 256), OP.mult))
                consume_bank(b)
            for j in range(2):
                war_x1r[(scg * 2 + j) % 4] = P.cnt["T"]

        P.barrier()
        ckpt(12)

        ld_gate = gload(R("gateT").rearrange("p (dc e) -> p dc e", dc=8),
                        gateT_in.rearrange("(dc p) e -> p dc e", p=128))
        ld_acat = gload(R("acatT").rearrange("p (dc r) -> p dc r", dc=8),
                        acatT_in.rearrange("(dc p) r -> p dc r", p=128))
        ld_im4 = gload(R("im4", 8), im4_in)
        for nm, src_ in (("gb16", gb16_in), ("p216", p216_in)):
            sy_dma(lambda e, d=R(nm), s=src_: e.dma_start(out=d, in_=s).then_inc(SM["LDS"], 16))
        P.wait("tensor", "LDS", P.cnt["LDS"])
        P.wait("tensor", "LDG", ld_acat)
        for n in range(4):
            b = [0, 2][n % 2]
            guard_bank(b)
            for dc in range(8):
                t_op(lambda e, b=b, dc=dc, n=n, st=(dc == 0), sp=(dc == 7): e.matmul(
                    pb[b][0:8, :], Rs("gateT", dc * 8, 8),
                    Rs("ln2", dc * SLOC + n * 512, 512), start=st, stop=sp))
            P.wait("vector", "T", P.cnt["T"])
            v_op(lambda e, b=b, n=n: e.tensor_copy(Rs("glog", n * 512, 512, (0, 8)), pb[b][0:8, :]))
            consume_bank(b)
            b2 = [4, 6][n % 2]
            guard_bank(b2)
            for dc in range(8):
                t_op(lambda e, b=b2, dc=dc, n=n, st=(dc == 0), sp=(dc == 7): e.matmul(
                    pb[b][0:32, :], Rs("acatT", dc * 32, 32),
                    Rs("ln2", dc * SLOC + n * 512, 512), start=st, stop=sp))
            P.wait("vector", "T", P.cnt["T"])
            v_op(lambda e, b=b2, n=n: e.tensor_copy(Rs("u", n * 512, 512, (0, 32)), pb[b][0:32, :]))
            consume_bank(b2)
        P.wait("tensor", "V", P.cnt["V"])
        guard_bank(0)
        for sc in range(NT):
            t_op(lambda e, sc=sc: e.transpose(pb[0][:, ts(sc, 8)], Rs("glog", sc * 128, 128, (0, 8)),
                                              Rs("eye", 0, 8, (0, 8))))
        P.wait("vector", "T", P.cnt["T"])
        v_op(lambda e: e.tensor_tensor(R("lg"), pb[0][:, 0:128], R("gb16"), OP.add))
        consume_bank(0)
        v_op(lambda e: e.reduce_max(R("mx"), R("lg").rearrange("p (a b) -> p a b", a=16), axis=AX.X))
        for sc in range(NT):
            v_op(lambda e, sc=sc: e.tensor_scalar(Rs("eqw", sc * 8, 8), Rs("lg", sc * 8, 8),
                                                  Rs("mx", sc, 1).opt(), None, OP.is_ge))
        v_op(lambda e: e.tensor_tensor(R("eqw"), R("eqw"), R("p216"), OP.mult))
        v_op(lambda e: e.reduce_max(R("mx"), R("eqw").rearrange("p (a b) -> p a b", a=16), axis=AX.X))
        for sc in range(NT):
            v_op(lambda e, sc=sc: e.tensor_scalar(Rs("mask", sc * 8, 8), Rs("eqw", sc * 8, 8),
                                                  Rs("mx", sc, 1).opt(), None, OP.is_ge))
        P.wait("tensor", "V", P.cnt["V"])
        for grp in range(4):
            b = [4, 6][grp % 2]
            guard_bank(b)
            for j in range(4):
                sc = grp * 4 + j
                t_op(lambda e, b=b, j=j, sc=sc: e.transpose(
                    pb[b][0:8, ts(j, 128)], Rs("mask", sc * 8, 8), R("eye")))
            P.wait("vector", "T", P.cnt["T"])
            v_op(lambda e, b=b, grp=grp: e.tensor_copy(Rs("mfm", grp * 512, 512, (0, 8)), pb[b][0:8, :]))
            consume_bank(b)
        P.wait("tensor", "V", P.cnt["V"])
        P.wait("tensor", "LDG", ld_im4)
        for n in range(4):
            b = [0, 2][n % 2]
            guard_bank(b)
            t_op(lambda e, b=b, n=n: e.matmul(pb[b][0:32, :], R("im4", 8), Rs("mfm", n * 512, 512, (0, 8)),
                                              start=True, stop=True))
            P.wait("vector", "T", P.cnt["T"])
            v_op(lambda e, b=b, n=n: e.tensor_tensor(Rs("z", n * 512, 512, (0, 32)),
                                                     Rs("u", n * 512, 512, (0, 32)), pb[b][0:32, :], OP.mult))
            consume_bank(b)

        P.barrier()
        ckpt(13)

        ld_bcat = gload(R("bcatT", 32), bcatT_in)
        ld_wob = gload(R("wob", 1), wob_in)
        sy_dma(lambda e: e.dma_start(out=R("wibc"), in_=wibc_in).then_inc(SM["LDS"], 16))
        ld_wibc = P.cnt["LDS"]
        P.wait("tensor", "LDG", ld_wo)
        P.wait("tensor", "LDG", ld_wob)
        P.wait("scalar", "LDS", ld_wibc)
        wi_guard = {}
        war_wo, war_out = {}, {}
        for tc in range(NTC):
            for fc in range(32):
                slot = fc % 4
                if tc * 32 + fc >= 4:
                    P.wait("gpsimd", "T", wi_guard[slot])
                P.raw("gpsimd", lambda e, fc=fc, slot=slot: e.dma_start(
                    out=R("wib%d" % slot), in_=wiT_in[:, ts(fc, 1024)]).then_inc(SM["LDW%d" % slot], 16))
                P.inc("LDW%d" % slot, 16)
                P.wait("tensor", "LDW%d" % slot, P.cnt["LDW%d" % slot])
                b = [0, 2][fc % 2]
                guard_bank(b)
                for dc in range(8):
                    t_op(lambda e, b=b, slot=slot, dc=dc, tc=tc, st=(dc == 0): e.matmul(
                        pb[b][:], Rs("wib%d" % slot, dc * 128, 128),
                        Rs("ln2", dc * SLOC + tc * TC, TC), start=st, stop=False))
                t_op(lambda e, b=b, fc=fc, tc=tc: e.matmul(
                    pb[b][:], Rs("bcatT", fc * 128, 128, (0, 32)),
                    Rs("z", tc * TC, TC, (0, 32)), start=False, stop=True))
                wi_guard[slot] = P.cnt["T"]
                P.wait("scalar", "T", P.cnt["T"])
                s_op(lambda e, b=b, fc=fc: e.activation(
                    Rs("inter", fc * TC, TC), pb[b][:], ACT.Relu, bias=Rs("wibc", fc, 1).opt()))
                consume_bank(b, sem="Sc")
            P.wait("tensor", "Sc", P.cnt["Sc"])
            for tt in range(TC // 128):
                sc = tc * (TC // 128) + tt
                xj = sc % 4
                if sc >= 4:
                    P.wait("sync", "V", war_wo[xj])
                sy_dma(lambda e, xj=xj, sc=sc: e.dma_start(out=R("x1r%d" % xj), in_=x1b[ts(sc, 128), :]).then_inc(SM["LDS"], 16))
                for dh in range(2):
                    b = [4, 6][(tt * 2 + dh) % 2]
                    guard_bank(b)
                    for fc in range(32):
                        t_op(lambda e, b=b, fc=fc, tt=tt, dh=dh, st=(fc == 0): e.matmul(
                            pb[b][:], Rs("inter", fc * TC + tt * 128, 128),
                            Rs("woT", fc * DM + dh * 512, 512), start=st, stop=False))
                    t_op(lambda e, b=b, dh=dh: e.matmul(
                        pb[b][:], R("ones1b", 1), Rs("wob", dh * 512, 512, (0, 1)), start=False, stop=True))
                    P.wait("vector", "T", P.cnt["T"])
                    P.wait("vector", "LDS", P.cnt["LDS"])
                    onm = "out%d" % (sc % 2)
                    if sc >= 2 and dh == 0:
                        P.wait("vector", "STQ", war_out[sc % 2])
                    v_op(lambda e, b=b, onm=onm, xj=xj, dh=dh: e.tensor_tensor(
                        Rs(onm, dh * 512, 512), Rs("x1r%d" % xj, dh * 512, 512), pb[b][:], OP.add))
                    consume_bank(b)
                war_wo[xj] = P.cnt["V"]
                P.wait("scalar", "V", P.cnt["V"])
                sc_dma(lambda e, onm="out%d" % (sc % 2), sc=sc: e.dma_start(
                    out=y_out[ts(sc, 128), :], in_=Rs(onm, 0, DM)).then_inc(SM["STQ"], 16))
                war_out[sc % 2] = P.cnt["STQ"]

        P.wait("scalar", "STQ", P.cnt["STQ"])
    except _StopBuild:
        pass

    with nc.Block() as block:
        def run(eng_name):
            def body(e):
                for st_ in P.steps[eng_name]:
                    if isinstance(st_, tuple):
                        e.wait_ge(sems[st_[1]], st_[2])
                    else:
                        st_(e)
            return body
        block.tensor(run("tensor"))
        block.vector(run("vector"))
        block.scalar(run("scalar"))
        block.gpsimd(run("gpsimd"))
        block.sync(run("sync"))

    _elide_ldweights(nc)
    es.close()
    return nc


_MM_FLDS = ("replication_resolution", "replication_shift_amnt", "replication_num_rows",
            "start_tensor_calc", "stop_tensor_calc", "is_transpose", "is_fmap_onezero",
            "is_weight_onezero", "ifmap_quant_offset", "weights_quant_offset", "tile_size",
            "tile_position", "perf_mode", "bass_skip_group_check", "name", "engine",
            "debug", "ins", "outs", "sync_info", "emission_index", "descendants")


def _elide_ldweights(nc):
    n = 0
    for fn in nc.m.functions:
        for bl in fn.blocks:
            if "_PE_" not in bl.name:
                continue
            il = bl.instructions
            prev = None
            for i, ins in enumerate(il):
                tn = type(ins).__name__
                if tn == "InstEventSemaphore":
                    continue
                if tn != "InstMatmult" or ins.is_transpose:
                    prev = None
                    continue
                try:
                    a = ins.ins[1]
                    if "bfloat16" not in str(a.dtype) or "bfloat16" not in str(ins.ins[0].dtype):
                        prev = None
                        continue
                    if a.ap[1][1] != ins.outs[0].ap[0][1]:
                        prev = None
                        continue
                    k = (a.memref, a.offset, str(a.ap), str(a.dtype), ins.tile_position)
                except Exception:
                    prev = None
                    continue
                if k == prev:
                    kw = {f: getattr(ins, f) for f in _MM_FLDS}
                    kw["ldweights"] = False
                    il[i] = mybir.InstMatmult(**kw)
                    n += 1
                else:
                    prev = k
    return n


_CACHE = {}


def _host_stage(inputs):
    f32 = np.float32
    x = np.asarray(inputs["L0_attn_hidden_states"], f32)[0]
    Wq = np.asarray(inputs["L0_attn_Wq"], f32)
    bq = np.asarray(inputs["L0_attn_bq"], f32)
    Wk = np.asarray(inputs["L0_attn_Wk"], f32)
    bk = np.asarray(inputs["L0_attn_bk"], f32)
    Wv = np.asarray(inputs["L0_attn_Wv"], f32)
    bv = np.asarray(inputs["L0_attn_bv"], f32)
    conv_w = np.asarray(inputs["L0_attn_conv_w"], f32)
    gW = np.asarray(inputs["L0_ffn_gate_W"], f32)
    gb = np.asarray(inputs["L0_ffn_gate_b"], f32)
    wiW = np.asarray(inputs["L0_ffn_wi_W"], f32)
    wib = np.asarray(inputs["L0_ffn_wi_b"], f32)
    woW = np.asarray(inputs["L0_ffn_wo_W"], f32)
    wob = np.asarray(inputs["L0_ffn_wo_b"], f32)
    lA = np.asarray(inputs["L0_ffn_lora_A"], f32)
    lB = np.asarray(inputs["L0_ffn_lora_B"], f32)

    xpad = np.zeros((S + 256, DM), f32)
    xpad[128:128 + S] = x

    wiT = np.ascontiguousarray(wiW.T)
    wiT_pm = np.ascontiguousarray(
        wiT.reshape(8, 128, 32, 128).transpose(1, 2, 0, 3)).reshape(128, 32 * 1024)
    woT = np.ascontiguousarray(woW.T)
    gateT = np.ascontiguousarray(gW.T)
    acatT = np.ascontiguousarray(lA.reshape(32, DM).T)
    bcatT = np.ascontiguousarray(lB.transpose(0, 2, 1).reshape(32, FF))

    w65 = conv_w[:, 0, :, 0]
    Tfull = np.zeros((H, 128, 64), f32)
    for p in range(64):
        Tfull[:, p:p + 65, p] = w65
    tconv = np.zeros((H, 3, 128, 64), f32)
    tconv[:, 0] = Tfull
    tconv[:, 1, 64:128] = Tfull[:, 0:64]
    tconv[:, 2, 0:64] = Tfull[:, 64:128]
    tconv_pm = np.ascontiguousarray(tconv.transpose(2, 0, 1, 3)).reshape(128, H * 3 * 64)

    eye = np.eye(128, dtype=f32)
    izf = np.zeros((128, 16 * H), f32)
    for h in range(H):
        izf[(h % 2) * 64:(h % 2) * 64 + 64, 16 * h + h] = 1.0
    izff = np.zeros((128, 256), f32)
    for h in range(H):
        izff[0:64, 16 * h + h] = 1.0
    ipair = np.zeros((16, 8, 128), f32)
    for pp in range(8):
        ipair[2 * pp, pp, 0:64] = 1.0
        ipair[2 * pp + 1, pp, 64:128] = 1.0
    ipair = ipair.reshape(16, 1024)
    im4 = np.zeros((E8, 32), f32)
    for e_ in range(E8):
        im4[e_, 4 * e_:4 * e_ + 4] = 1.0
    eyep = np.zeros((128, 1024), f32)
    for a in range(16):
        eyep[0:64, a * 64:(a + 1) * 64] = np.eye(64)
    bvb = np.broadcast_to(bv, (128, DM)).copy()
    bqc = np.ascontiguousarray(bq.reshape(8, 128).T)
    bkc = np.ascontiguousarray(bk.reshape(8, 128).T)
    wibc = np.ascontiguousarray(wib.reshape(32, 128).T)
    gb16 = np.tile(np.broadcast_to(gb, (128, 8)), (1, 16))
    p216 = np.tile((2.0 ** -np.arange(8, dtype=f32))[None, :], (128, 16))

    shared = dict(wq=Wq, wk=Wk, wv=Wv, wiT=wiT_pm, woT=woT, gateT=gateT,
                  acatT=acatT, bcatT=bcatT, tconv=tconv_pm, eye=eye, izf=izf,
                  izff=izff, ipair=ipair, im4=im4, i7=7.0 * eyep,
                  i15=15.0 * eyep, i13=13.0 * eyep, bvb=bvb, bqc=bqc, bkc=bkc,
                  wibc=wibc, gb16=gb16, p216=p216, wob=wob.reshape(1, DM))
    in_maps = []
    for c in range(NCORE):
        sl = xpad[c * SLOC: c * SLOC + TPAD]
        m = dict(shared)
        m["xT"] = np.ascontiguousarray(sl.T)
        m["xtm"] = np.ascontiguousarray(x[c * SLOC:(c + 1) * SLOC])
        m["bvlo"] = np.zeros_like(bvb) if c == 0 else bvb
        m["bvhi"] = np.zeros_like(bvb) if c == NCORE - 1 else bvb
        in_maps.append(m)
    return in_maps


def kernel(**inputs):
    in_maps = _host_stage(inputs)
    if "nc" not in _CACHE:
        _CACHE["nc"] = build_program()
    res = run_bass_kernel_spmd(_CACHE["nc"], in_maps, core_ids=list(range(NCORE)))
    out = np.concatenate([np.asarray(r["y"], np.float32) for r in res.results], axis=0)
    return out.reshape(1, S, DM)


# version 19
# speedup vs baseline: 1.0264x; 1.0176x over previous
import os
import numpy as np
from contextlib import ExitStack

import concourse.bass as bass
import concourse.mybir as mybir
from concourse.bass_utils import run_bass_kernel_spmd
from concourse.alu_op_type import AluOpType as OP

F32 = mybir.dt.float32
BF16 = mybir.dt.bfloat16
ACT = mybir.ActivationFunctionType
AX = mybir.AxisListType

_DBG = os.environ.get("KDBG") == "1"

NCORE = 8
S, DM, H, DH, LM, SEG = 16384, 1024, 16, 64, 64, 256
E8, R4, FF, KSZ = 8, 4, 4096, 65
SLOC = S // NCORE
NT = SLOC // 128
NTP = NT + 2
TPAD = NTP * 128
SCALE = float(DH) ** -0.25
EPS = 1e-5
TC = 512
NTC = SLOC // TC


class Sched:
    ENGS = ("tensor", "vector", "scalar", "gpsimd", "sync")
    BAR_SEMS = ("T", "V", "Sc", "LDG", "LDS", "STQ")

    def __init__(self):
        self.steps = {e: [] for e in self.ENGS}
        self.cnt = {}

    def sem(self, name):
        self.cnt.setdefault(name, 0)

    def raw(self, eng, fn):
        self.steps[eng].append(fn)

    def inc(self, name, k):
        self.cnt[name] += k
        return self.cnt[name]

    def wait(self, eng, name, val=None):
        v = self.cnt[name] if val is None else val
        if v <= 0:
            return
        self.steps[eng].append(("W", name, v))

    def barrier(self):
        snap = [(s, self.cnt[s]) for s in self.BAR_SEMS if self.cnt[s] > 0]
        for e in self.ENGS:
            for s, v in snap:
                self.steps[e].append(("W", s, v))


class Arena:

    def __init__(self, name):
        self.name = name
        self.regions = []

    def add(self, name, size, p0, p1):
        self.regions.append((name, size, p0, p1))

    def pack(self):
        def run(order):
            placed = {}
            for name, size, p0, p1 in order:
                pad = (size + 15) // 16 * 16
                spans = sorted((v for v in placed.values()
                                if not (v[3] < p0 or v[2] > p1)), key=lambda v: v[0])
                off = 0
                for (o2, s2, _, _, _) in spans:
                    if off + pad <= o2:
                        break
                    off = max(off, o2 + s2)
                placed[name] = (off, pad, p0, p1, size)
            tot = max((o + s for o, s, _, _, _ in placed.values()), default=0)
            return placed, tot
        best = None
        for key in (lambda r: (-r[1],), lambda r: (r[2], -r[1]),
                    lambda r: (-(r[3] - r[2]), -r[1]), lambda r: (r[2], r[3], -r[1])):
            placed, tot = run(sorted(self.regions, key=key))
            if best is None or tot < best[1]:
                best = (placed, tot)
        rng = np.random.default_rng(12345)
        base = sorted(self.regions, key=lambda r: (-r[1],))
        for _ in range(400):
            order = list(base)
            rng.shuffle(order)
            placed, tot = run(order)
            if tot < best[1]:
                best = (placed, tot)
        self.placed, self.total = best
        return self.total


def ts(i, n):
    return slice(i * n, (i + 1) * n)


class _StopBuild(Exception):
    pass


def build_program(stop_phase=None):
    nc = bass.Bass("TRN2", target_bir_lowering=False)
    P = Sched()
    SEMS = ("T", "V", "Sc", "LDG", "LDS", "STQ", "CC", "CCD",
            "LDW0", "LDW1", "LDW2", "LDW3")
    for s_ in SEMS:
        P.sem(s_)

    def din(name, shape, dt=F32):
        return nc.dram_tensor(name, list(shape), dt, kind="ExternalInput").ap()

    xT_in = din("xT", (DM, TPAD))
    xtm_in = din("xtm", (SLOC, DM))
    wq_in = din("wq", (DM, DM))
    wk_in = din("wk", (DM, DM))
    wv_in = din("wv", (DM, DM))
    wiT_in = din("wiT", (128, 32 * 1024))
    woT_in = din("woT", (FF, DM))
    gateT_in = din("gateT", (DM, E8))
    acatT_in = din("acatT", (DM, 32))
    bcatT_in = din("bcatT", (32, FF))
    tconv_in = din("tconv", (128, H * 3 * 64))
    eye_in = din("eye", (128, 128))
    izf_in = din("izf", (128, 16 * H))
    izff_in = din("izff", (128, 256))
    ipair_in = din("ipair", (16, 1024))
    im4_in = din("im4", (E8, 32))
    i7_in = din("i7", (128, 1024))
    i15_in = din("i15", (128, 1024))
    i13_in = din("i13", (128, 1024))
    bvb_in = din("bvb", (128, DM))
    bvlo_in = din("bvlo", (128, DM))
    bvhi_in = din("bvhi", (128, DM))
    bqc_in = din("bqc", (128, 8))
    bkc_in = din("bkc", (128, 8))
    wibc_in = din("wibc", (128, 32))
    gb16_in = din("gb16", (128, 128))
    p216_in = din("p216", (128, 128))
    wob_in = din("wob", (1, DM))
    y_out = nc.dram_tensor("y", [SLOC, DM], F32, kind="ExternalOutput").ap()

    x1b = nc.dram_tensor("x1b", [SLOC, DM], F32).ap()
    cc1i = nc.dram_tensor("cc1i", [128, 64], F32)
    cc1o = nc.dram_tensor("cc1o", [NCORE * 128, 64], F32)
    cc2i = nc.dram_tensor("cc2i", [128, 520], F32)
    cc2o = nc.dram_tensor("cc2o", [128, 520], F32)

    if _DBG:
        dbg = {nm: nc.dram_tensor("dbg_" + nm, list(shp), F32, kind="ExternalOutput").ap()
               for nm, shp in (("ql", (128, 512)), ("kl", (128, 512)),
                               ("vti", (64, 1024)), ("k3v", (128, 512)),
                               ("w2", (128, 512)), ("sumb", (128, 520)),
                               ("blob", (128, 520)))}

    BF = Arena("bf")
    FP = Arena("fp")
    for nm, sz, p0, p1 in (
        ("ln1", 8 * TPAD, 0, 4), ("ln2", 8 * SLOC, 11, 15),
        ("q", 8 * SLOC, 1, 7), ("k", 8 * SLOC, 2, 6), ("v", NTP * DM, 4, 9),
        ("inter", 32 * TC, 13, 15), ("vA", 17 * DM, 9, 10),
        ("e1", 8 * SLOC, 7, 10), ("woT", 32 * DM, 10, 15),
        ("w3", 8 * DM, 0, 3), ("wvh", 8 * 512, 1.5, 4), ("wkx", 4096, 1.5, 3),
        ("xbf", 512, 0, 0), ("xsq", 512, 0, 0),
        ("e3t0", 1024, 4.6, 4.6), ("e3t1", 1024, 4.6, 4.6),
        ("wib0", 1024, 13, 15), ("wib1", 1024, 13, 15),
        ("wib2", 1024, 13, 15), ("wib3", 1024, 13, 15),
        ("z", SLOC, 12, 15), ("mfm", SLOC, 12, 12), ("u", SLOC, 12, 12),
        ("ql", 512, 3, 9), ("kl", 512, 3, 7), ("w2", 512, 9, 10),
        ("lf_all", 512, 0, 3), ("tconv", H * 3 * 64, 8, 10),
        ("izbf", 16 * H, 0, 8), ("im4", 32, 12, 12), ("gateT", 8 * E8, 11, 12),
        ("acatT", 8 * 32, 11, 12), ("bcatT", FF, 12, 15), ("wob", DM, 12, 15),
        ("trash", DM, 10, 10), ("ones1b", 128, 0, 15), ("o128b", 16, 0, 15),
        ("zrow", 512, 0, 15),
    ):
        BF.add(nm, sz, p0, p1)
    for nm, sz, p0, p1 in (
        ("xch0", 2048, 0, 0), ("xch1", 2048, 0, 0), ("rows", 512, 0, 0),
        ("tmp384", 256, 0, 0), ("lf_loc", 64, 0, 3),
        ("e2", 1024, 5.0, 5.25), ("k2n", 1024, 5.0, 5.45),
        ("k2nt", 1024, 5.4, 5.9),
        ("vi", 1024, 5.45, 5.95), ("vti", 1024, 5.45, 5.95), ("kv", 1024, 5.5, 5.9),
        ("kvt", 1024, 5.5, 5.9), ("t1", 1024, 5.5, 5.9), ("t3", 1024, 5.5, 5.9),
        ("t5", 1024, 5.5, 5.9), ("vtd", 512, 5.9, 9),
        ("i7", 1024, 4, 5.9), ("i15", 1024, 4, 5.9), ("i13", 1024, 4, 5.9),
        ("rmcol", 16, 5.0, 5.9), ("rs8", 16, 5.0, 5.25), ("izff", 256, 0, 5.3),
        ("eye", 128, 0, 15), ("ipair", 1024, 6.5, 8),
        ("rz1", SLOC, 8, 8), ("blob", 528, 4.6, 9),
        ("sumb", 528, 4.6, 9), ("rz3", 8, 9, 9),
        ("k3v", 512, 9, 9), ("bvb", DM, 4, 4), ("bvlo", DM, 4, 4),
        ("bvhi", DM, 4, 4), ("bqc", 8, 0, 3), ("bkc", 8, 0, 3),
        ("wibc", 32, 12, 15), ("gb16", 128, 11, 12), ("p216", 128, 11, 12),
        ("xtm0", DM, 10, 10), ("xtm1", DM, 10, 10), ("ctmp", 512, 10, 10),
        ("x1c0", DM, 10, 10), ("x1c1", DM, 10, 10),
        ("s1", 16, 10, 11), ("s2", 16, 10, 11), ("st16", 256, 11, 11),
        ("r2rows", 2 * SLOC, 11, 11), ("r2mu", 256, 11, 11),
        ("r2sig", 256, 11, 11),
        ("x1r0", DM, 11, 15), ("x1r1", DM, 11, 15),
        ("x1r2", DM, 11, 15), ("x1r3", DM, 11, 15),
        ("tmp512", 512, 7, 11), ("glog", SLOC, 12, 12),
        ("lg", 128, 12, 12), ("eqw", 128, 12, 12), ("mask", 128, 12, 12),
        ("mx", 16, 11, 12),
        ("out0", DM, 13, 15), ("out1", DM, 13, 15), ("ones1f", 128, 0, 15),
        ("consts", 4, 0, 15),
    ):
        FP.add(nm, sz, p0, p1)
    nbf = BF.pack()
    nfp = FP.pack()
    assert 2 * nbf + 4 * nfp <= 208 * 1024, (nbf, nfp, 2 * nbf + 4 * nfp)

    es = ExitStack()
    BFA = es.enter_context(nc.sbuf_tensor("BFA", [128, nbf], BF16))
    FPA = es.enter_context(nc.sbuf_tensor("FPA", [128, nfp], F32))

    def R(name, parts=128):
        arena, t = (BF, BFA) if name in BF.placed else (FP, FPA)
        off, _, _, _, size = arena.placed[name]
        return t[0:parts, off:off + size]

    def Rs(name, lo, n, parts=(0, 128)):
        arena, t = (BF, BFA) if name in BF.placed else (FP, FPA)
        off = arena.placed[name][0]
        return t[parts[0]:parts[1], off + lo:off + lo + n]

    pb = [es.enter_context(nc.psum_tensor(f"pb{i}", [128, 512], F32)) for i in range(8)]
    sems = {}
    for s_ in SEMS:
        sems[s_] = es.enter_context(nc.semaphore(s_))
    SM = sems
    nc.const_aps.register(F32, EPS, Rs("consts", 0, 1))
    nc.const_aps.register(F32, -0.5, Rs("consts", 1, 1))

    bank_guard = {}

    def guard_bank(*bs):
        for b in bs:
            g = bank_guard.get(b)
            if g:
                P.wait("tensor", g[0], g[1])

    def consume_bank(*bs, sem="V"):
        for b in bs:
            bank_guard[b] = (sem, P.cnt[sem])

    def t_op(fn):
        P.raw("tensor", lambda e, fn=fn: fn(e).then_inc(SM["T"], 1))
        return P.inc("T", 1)

    def v_op(fn):
        P.wait("vector", "V", P.cnt["V"])
        P.raw("vector", lambda e, fn=fn: fn(e).then_inc(SM["V"], 1))
        return P.inc("V", 1)

    def s_op(fn):
        P.wait("scalar", "Sc", P.cnt["Sc"])
        P.raw("scalar", lambda e, fn=fn: fn(e).then_inc(SM["Sc"], 1))
        return P.inc("Sc", 1)

    def g_dma(fn):
        P.wait("gpsimd", "LDG", P.cnt["LDG"])
        P.wait("gpsimd", "CCD", P.cnt["CCD"])
        P.raw("gpsimd", fn)
        return P.inc("LDG", 16)

    def c_dma(fn):
        P.wait("gpsimd", "LDG", P.cnt["LDG"])
        P.wait("gpsimd", "CCD", P.cnt["CCD"])
        P.raw("gpsimd", fn)
        return P.inc("CCD", 16)

    def sy_dma(fn):
        P.wait("sync", "LDS", P.cnt["LDS"])
        P.raw("sync", fn)
        return P.inc("LDS", 16)

    def sc_dma(fn):
        P.wait("scalar", "STQ", P.cnt["STQ"])
        P.raw("scalar", fn)
        return P.inc("STQ", 16)

    def gload(dst, src):
        return g_dma(lambda e, d=dst, s=src: e.dma_start(out=d, in_=s).then_inc(SM["LDG"], 16))

    if _DBG:
        def dump(nm, ap_):
            for s_ in ("T", "V", "Sc", "LDS", "STQ"):
                P.wait("gpsimd", s_, P.cnt[s_])
            g_dma(lambda e, d=dbg[nm], s=ap_: e.dma_start(out=d, in_=s).then_inc(SM["LDG"], 16))
            for e_ in ("tensor", "vector", "scalar", "sync"):
                P.wait(e_, "LDG", P.cnt["LDG"])
    else:
        def dump(nm, ap_):
            pass

    def ckpt(n):
        if stop_phase is not None and n >= stop_phase:
            raise _StopBuild()

    def pslot(h):
        return ((h % 2) * 64, (h % 2) * 64 + 64, (h // 2) * 64)

    try:
        for nm, src_ in (("eye", eye_in), ("izff", izff_in),
                         ("bqc", bqc_in), ("bkc", bkc_in)):
            sy_dma(lambda e, d=R(nm, {"ipair": 16}.get(nm, 128)), s=src_:
                   e.dma_start(out=d, in_=s).then_inc(SM["LDS"], 16))
        ld_const = P.cnt["LDS"]

        ld_wq = gload(R("w3").rearrange("p (dc d) -> p dc d", dc=8),
                      wq_in.rearrange("(dc p) d -> p dc d", p=128))
        gload(R("izbf"), izf_in)

        def _memsets(e):
            e.memset(R("ones1b", 1), 1.0).then_inc(SM["V"], 1)
            e.memset(Rs("o128b", 0, 1), 1.0).then_inc(SM["V"], 1)
            e.memset(R("ones1f", 1), 1.0).then_inc(SM["V"], 1)
            e.memset(Rs("consts", 0, 1), EPS).then_inc(SM["V"], 1)
            e.memset(Rs("consts", 1, 1), -0.5).then_inc(SM["V"], 1)
            e.memset(R("zrow"), 0.0).then_inc(SM["V"], 1)
        P.raw("vector", _memsets)
        P.inc("V", 6)
        mset_v = P.cnt["V"]

        NCH = 9
        CW = TPAD // NCH
        war_x, war_xb = {}, {}
        for c in range(NCH):
            xb = "xch%d" % (c % 2)
            qeng, qsem = ("sync", "LDS") if c % 2 == 0 else ("scalar", "STQ")
            if c >= 2:
                P.wait(qeng, "V", war_x[c % 2])
            if c % 2 == 0:
                xld = sy_dma(lambda e, b=R(xb), c=c: e.dma_start(
                    out=b.rearrange("p (dc t) -> p dc t", dc=8),
                    in_=xT_in.rearrange("(dc p) t -> p dc t", p=128)[:, :, ts(c, CW)],
                ).then_inc(SM["LDS"], 16))
            else:
                xld = sc_dma(lambda e, b=R(xb), c=c: e.dma_start(
                    out=b.rearrange("p (dc t) -> p dc t", dc=8),
                    in_=xT_in.rearrange("(dc p) t -> p dc t", p=128)[:, :, ts(c, CW)],
                ).then_inc(SM["STQ"], 16))
            P.wait("tensor", "V", mset_v)
            guard_bank(0, 2)
            P.wait("scalar", qsem, xld)
            P.wait("scalar", "LDS", ld_const)
            for dc in range(8):
                sl = (dc % 2) * CW
                if c + dc >= 2:
                    P.wait("scalar", "T", war_xb[dc % 2])
                s_op(lambda e, b=xb, dc=dc, sl=sl: e.copy(Rs("xbf", sl, CW), Rs(b, dc * CW, CW)))
                s_op(lambda e, b=xb, dc=dc, sl=sl: e.activation(Rs("xsq", sl, CW), Rs(b, dc * CW, CW), ACT.Square))
                P.wait("tensor", "Sc", P.cnt["Sc"])
                t_op(lambda e, sl=sl, st=(dc == 0), sp=(dc == 7): e.matmul(
                    pb[0][0:1, 0:CW], Rs("o128b", 0, 1), Rs("xbf", sl, CW), start=st, stop=sp))
                t_op(lambda e, sl=sl, st=(dc == 0), sp=(dc == 7): e.matmul(
                    pb[2][0:1, 0:CW], Rs("o128b", 0, 1), Rs("xsq", sl, CW), start=st, stop=sp))
                war_xb[dc % 2] = P.cnt["T"]
            P.wait("vector", "T", P.cnt["T"])
            v_op(lambda e: e.tensor_scalar(Rs("rows", 0, CW, (0, 1)), pb[0][0:1, 0:CW], 1.0 / DM, None, OP.mult))
            v_op(lambda e: e.tensor_scalar(Rs("rows", CW, CW, (0, 1)), pb[2][0:1, 0:CW], 1.0 / DM, None, OP.mult))
            v_op(lambda e: e.tensor_tensor(Rs("tmp384", 0, CW, (0, 1)), Rs("rows", 0, CW, (0, 1)), Rs("rows", 0, CW, (0, 1)), OP.mult))
            v_op(lambda e: e.tensor_tensor(Rs("rows", CW, CW, (0, 1)), Rs("rows", CW, CW, (0, 1)), Rs("tmp384", 0, CW, (0, 1)), OP.subtract))
            consume_bank(0, 2)
            P.wait("scalar", "V", P.cnt["V"])
            s_op(lambda e: e.activation(Rs("rows", CW, CW, (0, 1)), Rs("rows", CW, CW, (0, 1)), ACT.Ln, bias=EPS))
            s_op(lambda e: e.activation(Rs("rows", CW, CW, (0, 1)), Rs("rows", CW, CW, (0, 1)), ACT.Exp, scale=-0.5))
            P.wait("tensor", "Sc", P.cnt["Sc"])
            P.wait("tensor", "LDS", ld_const)
            guard_bank(4, 6)
            t_op(lambda e: e.matmul(pb[4][:, 0:CW], R("ones1f", 1), Rs("rows", 0, CW, (0, 1)), start=True, stop=True))
            t_op(lambda e: e.matmul(pb[6][:, 0:CW], R("ones1f", 1), Rs("rows", CW, CW, (0, 1)), start=True, stop=True))
            P.wait("vector", "T", P.cnt["T"])
            for dc in range(8):
                v_op(lambda e, b=xb, dc=dc: e.tensor_tensor(Rs("tmp384", 0, CW), Rs(b, dc * CW, CW), pb[4][:, 0:CW], OP.subtract))
                v_op(lambda e, dc=dc, c=c: e.tensor_tensor(
                    Rs("ln1", dc * TPAD + c * CW, CW), Rs("tmp384", 0, CW), pb[6][:, 0:CW], OP.mult))
            consume_bank(4, 6)
            war_x[c % 2] = P.cnt["V"]

        for dc in range(8):
            v_op(lambda e, dc=dc: e.reduce_sum(
                Rs("lf_loc", dc * 8, 8),
                Rs("ln1", dc * TPAD + 128, SLOC).rearrange("p (g t) -> p g t", g=8),
                axis=AX.X))
        v_op(lambda e: e.tensor_scalar(R("lf_loc"), R("lf_loc"), 1.0 / SEG, None, OP.mult))
        P.wait("gpsimd", "V", P.cnt["V"])
        c_dma(lambda e: e.dma_start(out=cc1i.ap(), in_=R("lf_loc")).then_inc(SM["CCD"], 16))
        P.wait("gpsimd", "CCD", P.cnt["CCD"])

        def _cc1(e):
            e.collective_compute("AllGather", OP.bypass, replica_groups=[list(range(NCORE))],
                                 ins=[cc1i.ap().opt()], outs=[cc1o.ap().opt()]).then_inc(SM["CC"], 1)
        P.raw("gpsimd", _cc1)
        P.inc("CC", 1)
        P.wait("gpsimd", "CC", P.cnt["CC"])
        c_dma(lambda e: e.dma_start(
            out=R("lf_all").rearrange("p (s c) -> p s c", s=8),
            in_=cc1o.ap().rearrange("(s p) c -> p s c", p=128)).then_inc(SM["CCD"], 16))
        cc1_done = P.cnt["CCD"]

        P.barrier()
        ckpt(1)

        wk_src = {dc: ("wvh", dc * 1024) for dc in range(4)}
        wk_src.update({dc: ("wkx", (dc - 4) * 1024) for dc in range(4, 8)})
        ld_wk = 0
        for dc in range(8):
            nm, off = wk_src[dc]
            ld_wk = gload(Rs(nm, off, 1024),
                          wk_in.rearrange("(dc p) d -> p dc d", p=128)[:, dc, :])

        def proj_qk(wld, out_nm, bias_nm, src_of=None):
            P.wait("tensor", "LDG", wld)
            for pp in range(8):
                for n in range(4):
                    b = [0, 2][(pp * 4 + n) % 2]
                    guard_bank(b)
                    for dc in range(8):
                        nm, off = ("w3", dc * 1024) if src_of is None else src_of[dc]
                        t_op(lambda e, b=b, nm=nm, off=off, pp=pp, n=n, dc=dc,
                             st=(dc == 0), sp=(dc == 7): e.matmul(
                            pb[b][:],
                            Rs(nm, off + pp * 128, 128),
                            Rs("ln1", dc * TPAD + 128 + n * 512, 512),
                            start=st, stop=sp))
                    P.wait("vector", "T", P.cnt["T"])
                    v_op(lambda e, b=b, pp=pp, n=n, o=out_nm, bs=bias_nm: e.tensor_scalar(
                        Rs(o, pp * SLOC + n * 512, 512),
                        pb[b][:], Rs(bs, pp, 1).opt(), SCALE, OP.add, OP.mult))
                    consume_bank(b)

        proj_qk(ld_wq, "q", "bqc")
        proj_qk(ld_wk, "k", "bkc", src_of=wk_src)

        P.barrier()
        ckpt(3)

        P.wait("tensor", "CCD", cc1_done)

        def proj_l(out_nm, bias_nm, src_of=None):
            for pp in range(8):
                b = [4, 6][pp % 2]
                guard_bank(b)
                for dc in range(8):
                    nm, off = ("w3", dc * 1024) if src_of is None else src_of[dc]
                    t_op(lambda e, b=b, nm=nm, off=off, pp=pp, dc=dc,
                         st=(dc == 0), sp=(dc == 7): e.matmul(
                        pb[b][:, 0:64],
                        Rs(nm, off + pp * 128, 128),
                        R("lf_all").rearrange("p (s c g) -> p s c g", s=8, c=8)[:, :, dc, :],
                        start=st, stop=sp))
                P.wait("vector", "T", P.cnt["T"])
                v_op(lambda e, b=b, pp=pp, o=out_nm, bs=bias_nm: e.tensor_scalar(
                    Rs(o, pp * 64, 64), pb[b][:, 0:64], Rs(bs, pp, 1).opt(), SCALE, OP.add, OP.mult))
                consume_bank(b)

        proj_l("kl", "bkc", src_of=wk_src)
        proj_l("ql", "bqc")
        dump("kl", R("kl"))
        dump("ql", R("ql"))

        P.barrier()
        ckpt(4)

        for nm, src_ in (("i7", i7_in), ("i15", i15_in), ("i13", i13_in),
                         ("bvb", bvb_in), ("bvlo", bvlo_in), ("bvhi", bvhi_in)):
            sy_dma(lambda e, d=R(nm), s=src_: e.dma_start(out=d, in_=s).then_inc(SM["LDS"], 16))
        ld_i = P.cnt["LDS"]
        P.wait("vector", "LDS", ld_i)
        for hf in range(2):
            P.wait("gpsimd", "T", P.cnt["T"])
            ld_wvh = gload(R("wvh").rearrange("p (dc d) -> p dc d", dc=8),
                           wv_in.rearrange("(dc p) d -> p dc d", p=128)[:, :, ts(hf, 512)])
            P.wait("tensor", "LDG", ld_wvh)
            for g in range(NTP):
                b = [0, 2][g % 2]
                guard_bank(b)
                for dc in range(8):
                    t_op(lambda e, hf=hf, b=b, g=g, dc=dc, st=(dc == 0), sp=(dc == 7): e.matmul(
                        pb[b][:],
                        Rs("ln1", dc * TPAD + g * 128, 128),
                        Rs("wvh", dc * 512, 512),
                        start=st, stop=sp))
                P.wait("vector", "T", P.cnt["T"])
                bv = "bvlo" if g == 0 else ("bvhi" if g == NTP - 1 else "bvb")
                v_op(lambda e, b=b, g=g, hf=hf, bv=bv: e.tensor_tensor(
                    Rs("v", g * DM + hf * 512, 512), pb[b][:], Rs(bv, hf * 512, 512), OP.add))
                consume_bank(b)

        P.barrier()
        ckpt(5)

        for sc in range(NT):
            enm = "e3t%d" % (sc % 2)
            guard_bank(0, 1)
            for h in range(H):
                r0, r1, c0 = pslot(h)
                bk = h % 2
                t_op(lambda e, bk=bk, h=h, sc=sc, r0=r0, r1=r1: e.matmul(
                    pb[bk][:, ts(h // 2, 64)],
                    Rs("k", (h // 2) * SLOC + sc * 128, 128, (r0, r1)),
                    Rs("ql", (h // 2) * 64, 64, (r0, r1)),
                    start=True, stop=True, tile_position=(r0, 0)))
            P.wait("scalar", "T", P.cnt["T"])
            s_op(lambda e, enm=enm: e.activation(
                Rs(enm, 0, 1024).rearrange("p (s two c) -> p s c two", two=2, c=64)[:, :, :, 0],
                pb[0][:], ACT.Exp))
            s_op(lambda e, enm=enm: e.activation(
                Rs(enm, 0, 1024).rearrange("p (s two c) -> p s c two", two=2, c=64)[:, :, :, 1],
                pb[1][:], ACT.Exp))
            consume_bank(0, 1, sem="Sc")
            P.wait("tensor", "Sc", P.cnt["Sc"])
            guard_bank(2, 4)
            for h in range(H):
                r0, r1, c0 = pslot(h)
                t_op(lambda e, enm=enm, h=h, r0=r0, c0=c0, sc=sc: e.matmul(
                    pb[4][r0:r0 + 64, c0:c0 + 64],
                    Rs(enm, (h % 8) * 64 + (h // 8) * 512, 64),
                    Rs("v", (sc + 1) * DM + h * 64, 64),
                    start=True, stop=True, tile_position=(0, r0)))
                t_op(lambda e, enm=enm, h=h, r0=r0: e.matmul(
                    pb[2][r0:r0 + 64, (h // 2):(h // 2) + 1],
                    Rs(enm, (h % 8) * 64 + (h // 8) * 512, 64),
                    R("o128b", 128)[:, 0:1],
                    start=True, stop=True, tile_position=(0, r0)))
            P.wait("vector", "T", P.cnt["T"])
            if sc == 0:
                v_op(lambda e: e.tensor_copy(Rs("blob", 0, 512), pb[4][:]))
                v_op(lambda e: e.tensor_copy(Rs("blob", 512, 8), pb[2][:, 0:8]))
            else:
                v_op(lambda e: e.tensor_tensor(Rs("blob", 0, 512), Rs("blob", 0, 512), pb[4][:], OP.add))
                v_op(lambda e: e.tensor_tensor(Rs("blob", 512, 8), Rs("blob", 512, 8), pb[2][:, 0:8], OP.add))
            consume_bank(2, 4)
        dump("blob", Rs("blob", 0, 520))
        P.wait("gpsimd", "V", P.cnt["V"])
        c_dma(lambda e: e.dma_start(out=cc2i.ap(), in_=Rs("blob", 0, 520)).then_inc(SM["CCD"], 16))
        P.wait("gpsimd", "CCD", P.cnt["CCD"])

        def _cc2(e):
            e.collective_compute("AllReduce", OP.add, replica_groups=[list(range(NCORE))],
                                 ins=[cc2i.ap().opt()], outs=[cc2o.ap().opt()]).then_inc(SM["CC"], 1)
        P.raw("gpsimd", _cc2)
        P.inc("CC", 1)
        cc2_done = P.cnt["CC"]
        ckpt(7)

        P.barrier()
        ckpt(6)

        guard_bank(0, 1)
        for h in range(H):
            r0, r1, c0 = pslot(h)
            bk = h % 2
            t_op(lambda e, bk=bk, r0=r0, r1=r1, h=h: e.matmul(
                pb[bk][0:64, ts(h // 2, 64)],
                Rs("ql", (h // 2) * 64, 64, (r0, r1)),
                Rs("kl", (h // 2) * 64, 64, (r0, r1)),
                start=True, stop=True, tile_position=(r0, 0)))
        ckpt(5.05)
        P.wait("scalar", "T", P.cnt["T"])
        s_op(lambda e: e.activation(
            Rs("e2", 0, 1024, (0, 64)).rearrange("p (s two c) -> p s c two", two=2, c=64)[:, :, :, 0],
            pb[0][0:64, :], ACT.Exp))
        s_op(lambda e: e.activation(
            Rs("e2", 0, 1024, (0, 64)).rearrange("p (s two c) -> p s c two", two=2, c=64)[:, :, :, 1],
            pb[1][0:64, :], ACT.Exp))
        consume_bank(0, 1, sem="Sc")
        P.wait("vector", "Sc", P.cnt["Sc"])
        v_op(lambda e: e.reduce_sum(Rs("rs8", 0, 16, (0, 64)),
                                    Rs("e2", 0, 1024, (0, 64)).rearrange("p (a m) -> p a m", a=16), axis=AX.X))
        v_op(lambda e: e.reciprocal(Rs("rs8", 0, 16, (0, 64)), Rs("rs8", 0, 16, (0, 64))))
        for h in range(H):
            v_op(lambda e, h=h: e.tensor_scalar(Rs("k2n", h * 64, 64, (0, 64)), Rs("e2", h * 64, 64, (0, 64)),
                                                Rs("rs8", h, 1, (0, 64)).opt(), None, OP.mult))
        ckpt(5.1)
        P.wait("tensor", "V", P.cnt["V"])
        guard_bank(2)
        for h in range(H):
            t_op(lambda e, h=h: e.matmul(
                pb[2][0:16, 0:64],
                Rs("izff", h * 16, 16, (0, 64)),
                Rs("k2n", h * 64, 64, (0, 64)),
                start=(h == 0), stop=(h == H - 1)))
        P.wait("vector", "T", P.cnt["T"])
        v_op(lambda e: e.reduce_max(Rs("rmcol", 0, 1, (0, 16)), pb[2][0:16, 0:64], axis=AX.X))
        consume_bank(2)
        P.wait("tensor", "V", P.cnt["V"])
        guard_bank(4)
        t_op(lambda e: e.transpose(pb[4][0:1, 0:16], Rs("rmcol", 0, 1, (0, 16)), Rs("eye", 0, 16, (0, 16))))
        P.wait("vector", "T", P.cnt["T"])
        v_op(lambda e: e.reduce_max(Rs("rmcol", 1, 1, (0, 1)), pb[4][0:1, 0:16], axis=AX.X))
        consume_bank(4)
        v_op(lambda e: e.reciprocal(Rs("rmcol", 1, 1, (0, 1)), Rs("rmcol", 1, 1, (0, 1))))
        P.wait("tensor", "V", P.cnt["V"])
        guard_bank(6)
        t_op(lambda e: e.matmul(pb[6][:, 0:1], R("ones1f", 1), Rs("rmcol", 1, 1, (0, 1)), start=True, stop=True))
        P.wait("vector", "T", P.cnt["T"])
        v_op(lambda e: e.tensor_copy(Rs("rmcol", 2, 1), pb[6][:, 0:1]))
        consume_bank(6)
        ckpt(5.3)

        def mm16f(lhs_nm, rhs_nm, banks):
            guard_bank(*banks)
            for h in range(H):
                bk = banks[h // 8]
                t_op(lambda e, bk=bk, h=h, ln=lhs_nm, rn=rhs_nm: e.matmul(
                    pb[bk][0:64, ts(h % 8, 64)],
                    Rs(ln, h * 64, 64, (0, 64)),
                    Rs(rn, h * 64, 64, (0, 64)),
                    start=True, stop=True))

        def tr16f(in_nm, banks):
            guard_bank(*banks)
            for h in range(H):
                bk = banks[h // 8]
                t_op(lambda e, bk=bk, h=h, nm=in_nm: e.transpose(
                    pb[bk][0:64, ts(h % 8, 64)],
                    Rs(nm, h * 64, 64, (0, 64)),
                    Rs("eye", 0, 64, (0, 64))))

        def ev2(dst_nm, banks, fn):
            for i, bk in enumerate(banks):
                v_op(lambda e, i=i, bk=bk, d=dst_nm, fn=fn: fn(
                    e, Rs(d, i * 512, 512, (0, 64)), pb[bk][0:64, :], i))
            consume_bank(*banks)

        P.wait("tensor", "V", P.cnt["V"])
        tr16f("k2n", (4, 6))
        P.wait("vector", "T", P.cnt["T"])
        ev2("k2nt", (4, 6), lambda e, d, p, i: e.tensor_copy(d, p))
        v_op(lambda e: e.tensor_scalar(Rs("vi", 0, 1024, (0, 64)), Rs("k2nt", 0, 1024, (0, 64)),
                                       Rs("rmcol", 2, 1, (0, 64)).opt(), None, OP.mult))
        v_op(lambda e: e.tensor_scalar(Rs("vti", 0, 1024, (0, 64)), Rs("k2n", 0, 1024, (0, 64)),
                                       Rs("rmcol", 2, 1, (0, 64)).opt(), None, OP.mult))

        for it in range(6):
            ckpt(5.4 + it / 100.0)
            P.wait("tensor", "V", P.cnt["V"])
            mm16f("k2nt", "vi", (0, 2))
            P.wait("vector", "T", P.cnt["T"])
            ev2("kv", (0, 2), lambda e, d, p, i: e.tensor_copy(d, p))
            for i in range(2):
                v_op(lambda e, i=i: e.scalar_tensor_tensor(
                    Rs("t1", i * 512, 512, (0, 64)), pb[2 * i][0:64, :], -1.0,
                    Rs("i7", i * 512, 512, (0, 64)), OP.mult, OP.add))
            P.wait("tensor", "V", P.cnt["V"])
            tr16f("kv", (4, 6))
            P.wait("vector", "T", P.cnt["T"])
            ev2("kvt", (4, 6), lambda e, d, p, i: e.tensor_copy(d, p))
            P.wait("tensor", "V", P.cnt["V"])
            mm16f("kvt", "t1", (0, 2))
            P.wait("vector", "T", P.cnt["T"])
            ev2("t3", (0, 2), lambda e, d, p, i: e.scalar_tensor_tensor(
                d, p, -1.0, Rs("i15", i * 512, 512, (0, 64)), OP.mult, OP.add))
            P.wait("tensor", "V", P.cnt["V"])
            mm16f("kvt", "t3", (4, 6))
            P.wait("vector", "T", P.cnt["T"])
            ev2("t5", (4, 6), lambda e, d, p, i: e.scalar_tensor_tensor(
                d, p, -1.0, Rs("i13", i * 512, 512, (0, 64)), OP.mult, OP.add))
            P.wait("tensor", "V", P.cnt["V"])
            mm16f("vti", "t5", (0, 2))
            P.wait("vector", "T", P.cnt["T"])
            ev2("vi", (0, 2), lambda e, d, p, i: e.tensor_scalar(d, p, 0.25, None, OP.mult))
            P.wait("tensor", "V", P.cnt["V"])
            tr16f("vi", (4, 6))
            P.wait("vector", "T", P.cnt["T"])
            ev2("vti", (4, 6), lambda e, d, p, i: e.tensor_copy(d, p))

        ckpt(5.8)
        dump("vti", Rs("vti", 0, 1024, (0, 64)))
        P.wait("sync", "V", P.cnt["V"])
        for par in range(2):
            sy_dma(lambda e, par=par: e.dma_start(
                out=FPA[par * 64:(par + 1) * 64,
                        FP.placed["vtd"][0]:FP.placed["vtd"][0] + 512],
                in_=Rs("vti", 0, 1024, (0, 64)).rearrange(
                    "p (a two c) -> p a two c", a=8, two=2)[:, :, par, :],
            ).then_inc(SM["LDS"], 16))
        vtd_done = P.cnt["LDS"]


        ld_ipair = sy_dma(lambda e: e.dma_start(out=R("ipair", 16), in_=ipair_in).then_inc(SM["LDS"], 16))
        for n in range(4):
            for pp in range(8):
                be, bo = [(4, 1), (6, 3)][(n * 8 + pp) % 2]
                guard_bank(be, bo)
                for par in range(2):
                    b = be if par == 0 else bo
                    t_op(lambda e, b=b, pp=pp, n=n, par=par: e.matmul(
                        pb[b][par * 64:par * 64 + 64, :],
                        Rs("kl", pp * 64, 64, (par * 64, par * 64 + 64)),
                        Rs("q", pp * SLOC + n * 512, 512, (par * 64, par * 64 + 64)),
                        start=True, stop=True, tile_position=(par * 64, par * 64)))
                P.wait("scalar", "T", P.cnt["T"])
                s_op(lambda e, b=be, pp=pp, n=n: e.activation(
                    Rs("e1", pp * SLOC + n * 512, 512, (0, 64)), pb[b][0:64, :], ACT.Exp))
                s_op(lambda e, b=bo, pp=pp, n=n: e.activation(
                    Rs("e1", pp * SLOC + n * 512, 512, (64, 128)), pb[b][64:128, :], ACT.Exp))
                consume_bank(be, bo, sem="Sc")
        P.wait("tensor", "Sc", P.cnt["Sc"])
        ckpt(8)
        for n in range(4):
            be, bo = [(0, 1), (2, 3)][n % 2]
            guard_bank(be, bo)
            for h in range(H):
                r0, r1, c0 = pslot(h)
                b = be if h % 2 == 0 else bo
                t_op(lambda e, b=b, h=h, n=n, r0=r0, r1=r1, st=(h < 2), sp=(h >= H - 2): e.matmul(
                    pb[b][0:16, :],
                    Rs("izbf", h * 16, 16, (r0, r1)),
                    Rs("e1", (h // 2) * SLOC + n * 512, 512, (r0, r1)),
                    start=st, stop=sp, tile_position=(r0, 0)))
            P.wait("scalar", "T", P.cnt["T"])
            P.wait("scalar", "V", P.cnt["V"])
            s_op(lambda e, bo=bo: e.activation(Rs("tmp512", 0, 512, (0, 16)),
                                               pb[bo][0:16, :], ACT.Copy))
            P.wait("vector", "T", P.cnt["T"])
            P.wait("vector", "Sc", P.cnt["Sc"])
            v_op(lambda e, be=be: e.tensor_tensor(Rs("tmp512", 0, 512, (0, 16)),
                                                  pb[be][0:16, :], Rs("tmp512", 0, 512, (0, 16)), OP.add))
            v_op(lambda e, n=n: e.reciprocal(Rs("rz1", n * 512, 512, (0, 16)), Rs("tmp512", 0, 512, (0, 16))))
            consume_bank(be, bo)
        P.wait("tensor", "V", P.cnt["V"])
        P.wait("tensor", "LDS", ld_ipair)
        for pp in range(8):
            for n in range(4):
                b = [4, 6][(pp * 4 + n) % 2]
                guard_bank(b)
                t_op(lambda e, b=b, pp=pp, n=n: e.matmul(
                    pb[b][:], Rs("ipair", pp * 128, 128, (0, 16)), Rs("rz1", n * 512, 512, (0, 16)),
                    start=True, stop=True))
                P.wait("vector", "T", P.cnt["T"])
                v_op(lambda e, b=b, pp=pp, n=n: e.tensor_tensor(
                    Rs("e1", pp * SLOC + n * 512, 512),
                    Rs("e1", pp * SLOC + n * 512, 512),
                    pb[b][:], OP.mult))
                consume_bank(b)
        ckpt(9)

        P.wait("gpsimd", "CC", cc2_done)
        g2c = c_dma(lambda e: e.dma_start(
            out=Rs("sumb", 0, 520), in_=cc2o.ap()).then_inc(SM["CCD"], 16))
        P.wait("vector", "CCD", g2c)
        v_op(lambda e: e.reciprocal(R("rz3"), Rs("sumb", 512, 8)))
        for a in range(8):
            v_op(lambda e, a=a: e.tensor_scalar(Rs("k3v", a * 64, 64), Rs("sumb", a * 64, 64),
                                                Rs("rz3", a, 1).opt(), None, OP.mult))
        dump("sumb", Rs("sumb", 0, 520))
        dump("k3v", Rs("k3v", 0, 512))
        P.wait("tensor", "V", P.cnt["V"])
        P.wait("tensor", "LDS", vtd_done)
        guard_bank(0, 1)
        for h in range(H):
            r0, r1, c0 = pslot(h)
            bk = h % 2
            t_op(lambda e, bk=bk, r0=r0, r1=r1, c0=c0: e.matmul(
                pb[bk][r0:r1, c0:c0 + 64],
                Rs("vtd", c0, 64, (r0, r1)),
                Rs("k3v", c0, 64, (r0, r1)),
                start=True, stop=True, tile_position=(r0, r0)))
        P.wait("vector", "T", P.cnt["T"])
        v_op(lambda e: e.tensor_copy(Rs("w2", 0, 512, (0, 64)), pb[0][0:64, :]))
        v_op(lambda e: e.tensor_copy(Rs("w2", 0, 512, (64, 128)), pb[1][64:128, :]))
        consume_bank(0, 1)
        dump("w2", R("w2"))

        P.wait("sync", "V", P.cnt["V"])
        sy_dma(lambda e: e.dma_start(
            out=BFA[0:32, BF.placed["vA"][0]:BF.placed["vA"][0] + 17 * DM],
            in_=Rs("v", 0, 17 * DM, (96, 128))).then_inc(SM["LDS"], 16))
        sy_dma(lambda e: e.dma_start(
            out=BFA[32:128, BF.placed["vA"][0]:BF.placed["vA"][0] + 17 * DM],
            in_=Rs("v", DM, 17 * DM, (0, 96))).then_inc(SM["LDS"], 16))
        va_done = P.cnt["LDS"]

        P.barrier()
        ckpt(10)

        ld_tconv = gload(R("tconv"), tconv_in)
        ld_wo = gload(R("woT").rearrange("p (fc d) -> p fc d", fc=32),
                      woT_in.rearrange("(fc p) d -> p fc d", p=128))
        P.wait("tensor", "LDG", ld_tconv)
        P.wait("tensor", "LDS", va_done)

        war_xtm, war_x1c = {}, {}
        for sc in range(NT):
            xnm = "xtm%d" % (sc % 2)
            if sc >= 2:
                P.wait("sync", "V", war_xtm[sc % 2])
            sy_dma(lambda e, d=R(xnm), sc=sc: e.dma_start(out=d, in_=xtm_in[ts(sc, 128), :]).then_inc(SM["LDS"], 16))
            bm = (0, 1, 2, 3) if sc % 2 == 0 else (4, 5, 6, 7)
            guard_bank(*bm)
            for par in range(2):
                for s in range(8):
                    h = 2 * s + par
                    t_op(lambda e, bk=bm[par], h=h, s=s, row=par * 64, sc=sc, st=(s == 0): e.matmul(
                        pb[bk][:, ts(s, 64)],
                        Rs("e1", (h // 2) * SLOC + sc * 128, 128, (row, row + 64)),
                        Rs("w2", (h // 2) * 64, 64, (row, row + 64)),
                        start=st, stop=False, tile_position=(row, 0)))
            t_op(lambda e, bk=bm[2]: e.matmul(pb[bk][:], Rs("zrow", 0, 128, (0, 1)),
                                    Rs("zrow", 0, 512, (0, 1)), start=True, stop=False))
            t_op(lambda e, bk=bm[3]: e.matmul(pb[bk][:], Rs("zrow", 0, 128, (64, 65)),
                                    Rs("zrow", 0, 512, (64, 65)), start=True, stop=False,
                                    tile_position=(64, 0)))
            for h in range(H):
                for m_par in range(2):
                    m = sc * 2 + m_par
                    if m % 2 == 0:
                        pieces = ((0, 0, 128, m // 2),)
                    else:
                        pieces = ((1, 64, 64, (m - 1) // 2), (2, 0, 64, (m + 1) // 2))
                    for (cls, base, kk, j) in pieces:
                        row = 64 if base == 64 else 0
                        bk = (0 if h % 2 == 0 else 2) if row == 0 else (1 if h % 2 == 1 else 3)
                        t_op(lambda e, bk=bm[bk], h=h, s=h // 2, ps0=m_par * 64, cls=cls,
                             base=base, kk=kk, j=j: e.matmul(
                            pb[bk][ps0:ps0 + 64, ts(s, 64)],
                            Rs("tconv", (h * 3 + cls) * 64, 64, (base, base + kk)),
                            Rs("vA", j * DM + h * 64, 64, (base, base + kk)),
                            start=False, stop=False, tile_position=(base, ps0)))
            t_op(lambda e, bk=bm[0]: e.matmul(pb[bk][:], Rs("zrow", 0, 128, (0, 1)),
                                    Rs("zrow", 0, 512, (0, 1)), start=False, stop=True))
            t_op(lambda e, bk=bm[2]: e.matmul(pb[bk][:], Rs("zrow", 0, 128, (0, 1)),
                                    Rs("zrow", 0, 512, (0, 1)), start=False, stop=True))
            t_op(lambda e, bk=bm[1]: e.matmul(pb[bk][:], Rs("zrow", 0, 128, (64, 65)),
                                    Rs("zrow", 0, 512, (64, 65)), start=False, stop=True,
                                    tile_position=(64, 0)))
            t_op(lambda e, bk=bm[3]: e.matmul(pb[bk][:], Rs("zrow", 0, 128, (64, 65)),
                                    Rs("zrow", 0, 512, (64, 65)), start=False, stop=True,
                                    tile_position=(64, 0)))
            P.wait("vector", "T", P.cnt["T"])
            P.wait("vector", "LDS", P.cnt["LDS"])
            cnm = "x1c%d" % (sc % 2)
            if sc >= 2:
                P.wait("vector", "STQ", war_x1c[sc % 2])

            def pstr(nm, par):
                return Rs(nm, 0, DM).rearrange("p (s two c) -> p s c two", two=2, c=64)[:, :, :, par]
            v_op(lambda e, xnm=xnm, bk=bm[0]: e.tensor_tensor(R("ctmp"), pstr(xnm, 0), pb[bk][:], OP.add))
            v_op(lambda e, cnm=cnm, bk=bm[3]: e.tensor_tensor(pstr(cnm, 0), R("ctmp"), pb[bk][:], OP.add))
            v_op(lambda e, xnm=xnm, bk=bm[1]: e.tensor_tensor(R("ctmp"), pstr(xnm, 1), pb[bk][:], OP.add))
            v_op(lambda e, cnm=cnm, bk=bm[2]: e.tensor_tensor(pstr(cnm, 1), R("ctmp"), pb[bk][:], OP.add))
            consume_bank(*bm)
            war_xtm[sc % 2] = P.cnt["V"]
            v_op(lambda e, cnm=cnm, sc=sc: e.reduce_sum(Rs("s1", sc, 1), Rs(cnm, 0, DM), axis=AX.X))
            P.wait("scalar", "V", P.cnt["V"])
            s_op(lambda e, cnm=cnm: e.activation(R("trash"), Rs(cnm, 0, DM), ACT.Square))
            P.wait("vector", "Sc", P.cnt["Sc"])
            v_op(lambda e, sc=sc: e.reduce_sum(Rs("s2", sc, 1), R("trash"), axis=AX.X))
            sc_dma(lambda e, cnm=cnm, sc=sc: e.dma_start(out=x1b[ts(sc, 128), :], in_=Rs(cnm, 0, DM)).then_inc(SM["STQ"], 16))
            war_x1c[sc % 2] = P.cnt["STQ"]

        P.barrier()
        ckpt(11)

        P.wait("vector", "STQ", P.cnt["STQ"])
        v_op(lambda e: e.tensor_scalar(R("s1"), R("s1"), 1.0 / DM, None, OP.mult))
        v_op(lambda e: e.tensor_tensor(R("mx"), R("s1"), R("s1"), OP.mult))
        v_op(lambda e: e.tensor_scalar(R("s2"), R("s2"), 1.0 / DM, None, OP.mult))
        v_op(lambda e: e.tensor_tensor(R("s2"), R("s2"), R("mx"), OP.subtract))
        P.wait("scalar", "V", P.cnt["V"])
        s_op(lambda e: e.activation(R("s2"), R("s2"), ACT.Ln, bias=EPS))
        s_op(lambda e: e.activation(R("s2"), R("s2"), ACT.Exp, scale=-0.5))
        P.wait("tensor", "Sc", P.cnt["Sc"])
        guard_bank(0)
        t_op(lambda e: e.transpose(pb[0][0:16, 0:128], R("s1"), R("eye")))
        t_op(lambda e: e.transpose(pb[0][0:16, 128:256], R("s2"), R("eye")))
        P.wait("vector", "T", P.cnt["T"])
        v_op(lambda e: e.tensor_copy(Rs("st16", 0, 256, (0, 16)), pb[0][0:16, 0:256]))
        consume_bank(0)
        P.wait("sync", "V", P.cnt["V"])
        sy_dma(lambda e: e.dma_start(out=Rs("r2rows", 0, SLOC, (0, 1)), in_=Rs("st16", 0, 128, (0, 16))).then_inc(SM["LDS"], 16))
        sy_dma(lambda e: e.dma_start(out=Rs("r2rows", SLOC, SLOC, (0, 1)), in_=Rs("st16", 128, 128, (0, 16))).then_inc(SM["LDS"], 16))
        P.wait("tensor", "LDS", P.cnt["LDS"])

        war_x1r, x1r_cnt = {}, {}
        for scg in range(8):
            for w in range(2):
                b = [4, 6][w]
                guard_bank(b)
                t_op(lambda e, b=b, scg=scg, w=w: e.matmul(
                    pb[b][:, 0:256], R("ones1f", 1), Rs("r2rows", w * SLOC + scg * 256, 256, (0, 1)),
                    start=True, stop=True))
                P.wait("vector", "T", P.cnt["T"])
                dnm = "r2mu" if w == 0 else "r2sig"
                v_op(lambda e, b=b, d=dnm: e.tensor_copy(Rs(d, 0, 256), pb[b][:, 0:256]))
                consume_bank(b)
            for j in range(2):
                bi = scg * 2 + j
                slot = bi % 4
                qeng, qsem = [("sync", "LDS"), ("scalar", "STQ"),
                              ("gpsimd", "LDG")][bi % 3]
                if bi >= 4:
                    P.wait(qeng, "T", war_x1r[slot])
                if qeng == "sync":
                    x1r_cnt[bi] = (qsem, sy_dma(lambda e, slot=slot, bi=bi: e.dma_start(
                        out=R("x1r%d" % slot), in_=x1b[ts(bi, 128), :]).then_inc(SM["LDS"], 16)))
                elif qeng == "scalar":
                    x1r_cnt[bi] = (qsem, sc_dma(lambda e, slot=slot, bi=bi: e.dma_start(
                        out=R("x1r%d" % slot), in_=x1b[ts(bi, 128), :]).then_inc(SM["STQ"], 16)))
                else:
                    x1r_cnt[bi] = (qsem, gload(R("x1r%d" % slot), x1b[ts(bi, 128), :]))
            for j in range(2):
                qsem, cnt = x1r_cnt[scg * 2 + j]
                P.wait("tensor", qsem, cnt)
            for dc in range(8):
                b = [0, 2][dc % 2]
                guard_bank(b)
                for j in range(2):
                    t_op(lambda e, b=b, dc=dc, j=j, slot=(scg * 2 + j) % 4: e.transpose(
                        pb[b][:, ts(j, 128)], Rs("x1r%d" % slot, dc * 128, 128), R("eye")))
                P.wait("vector", "T", P.cnt["T"])
                v_op(lambda e, b=b: e.tensor_tensor(
                    Rs("tmp512", 0, 256), pb[b][:, 0:256], Rs("r2mu", 0, 256), OP.subtract))
                v_op(lambda e, dc=dc, scg=scg: e.tensor_tensor(
                    Rs("ln2", dc * SLOC + scg * 256, 256),
                    Rs("tmp512", 0, 256), Rs("r2sig", 0, 256), OP.mult))
                consume_bank(b)
            for j in range(2):
                war_x1r[(scg * 2 + j) % 4] = P.cnt["T"]

        P.barrier()
        ckpt(12)

        ld_gate = gload(R("gateT").rearrange("p (dc e) -> p dc e", dc=8),
                        gateT_in.rearrange("(dc p) e -> p dc e", p=128))
        ld_acat = gload(R("acatT").rearrange("p (dc r) -> p dc r", dc=8),
                        acatT_in.rearrange("(dc p) r -> p dc r", p=128))
        ld_im4 = gload(R("im4", 8), im4_in)
        for nm, src_ in (("gb16", gb16_in), ("p216", p216_in)):
            sy_dma(lambda e, d=R(nm), s=src_: e.dma_start(out=d, in_=s).then_inc(SM["LDS"], 16))
        P.wait("tensor", "LDS", P.cnt["LDS"])
        P.wait("tensor", "LDG", ld_acat)
        for n in range(4):
            b = [0, 2][n % 2]
            guard_bank(b)
            for dc in range(8):
                t_op(lambda e, b=b, dc=dc, n=n, st=(dc == 0), sp=(dc == 7): e.matmul(
                    pb[b][0:8, :], Rs("gateT", dc * 8, 8),
                    Rs("ln2", dc * SLOC + n * 512, 512), start=st, stop=sp))
            P.wait("vector", "T", P.cnt["T"])
            v_op(lambda e, b=b, n=n: e.tensor_copy(Rs("glog", n * 512, 512, (0, 8)), pb[b][0:8, :]))
            consume_bank(b)
            b2 = [4, 6][n % 2]
            guard_bank(b2)
            for dc in range(8):
                t_op(lambda e, b=b2, dc=dc, n=n, st=(dc == 0), sp=(dc == 7): e.matmul(
                    pb[b][0:32, :], Rs("acatT", dc * 32, 32),
                    Rs("ln2", dc * SLOC + n * 512, 512), start=st, stop=sp))
            P.wait("vector", "T", P.cnt["T"])
            v_op(lambda e, b=b2, n=n: e.tensor_copy(Rs("u", n * 512, 512, (0, 32)), pb[b][0:32, :]))
            consume_bank(b2)
        P.wait("tensor", "V", P.cnt["V"])
        guard_bank(0)
        for sc in range(NT):
            t_op(lambda e, sc=sc: e.transpose(pb[0][:, ts(sc, 8)], Rs("glog", sc * 128, 128, (0, 8)),
                                              Rs("eye", 0, 8, (0, 8))))
        P.wait("vector", "T", P.cnt["T"])
        v_op(lambda e: e.tensor_tensor(R("lg"), pb[0][:, 0:128], R("gb16"), OP.add))
        consume_bank(0)
        v_op(lambda e: e.reduce_max(R("mx"), R("lg").rearrange("p (a b) -> p a b", a=16), axis=AX.X))
        for sc in range(NT):
            v_op(lambda e, sc=sc: e.tensor_scalar(Rs("eqw", sc * 8, 8), Rs("lg", sc * 8, 8),
                                                  Rs("mx", sc, 1).opt(), None, OP.is_ge))
        v_op(lambda e: e.tensor_tensor(R("eqw"), R("eqw"), R("p216"), OP.mult))
        v_op(lambda e: e.reduce_max(R("mx"), R("eqw").rearrange("p (a b) -> p a b", a=16), axis=AX.X))
        for sc in range(NT):
            v_op(lambda e, sc=sc: e.tensor_scalar(Rs("mask", sc * 8, 8), Rs("eqw", sc * 8, 8),
                                                  Rs("mx", sc, 1).opt(), None, OP.is_ge))
        P.wait("tensor", "V", P.cnt["V"])
        for grp in range(4):
            b = [4, 6][grp % 2]
            guard_bank(b)
            for j in range(4):
                sc = grp * 4 + j
                t_op(lambda e, b=b, j=j, sc=sc: e.transpose(
                    pb[b][0:8, ts(j, 128)], Rs("mask", sc * 8, 8), R("eye")))
            P.wait("vector", "T", P.cnt["T"])
            v_op(lambda e, b=b, grp=grp: e.tensor_copy(Rs("mfm", grp * 512, 512, (0, 8)), pb[b][0:8, :]))
            consume_bank(b)
        P.wait("tensor", "V", P.cnt["V"])
        P.wait("tensor", "LDG", ld_im4)
        for n in range(4):
            b = [0, 2][n % 2]
            guard_bank(b)
            t_op(lambda e, b=b, n=n: e.matmul(pb[b][0:32, :], R("im4", 8), Rs("mfm", n * 512, 512, (0, 8)),
                                              start=True, stop=True))
            P.wait("vector", "T", P.cnt["T"])
            v_op(lambda e, b=b, n=n: e.tensor_tensor(Rs("z", n * 512, 512, (0, 32)),
                                                     Rs("u", n * 512, 512, (0, 32)), pb[b][0:32, :], OP.mult))
            consume_bank(b)

        P.barrier()
        ckpt(13)

        ld_bcat = gload(R("bcatT", 32), bcatT_in)
        ld_wob = gload(R("wob", 1), wob_in)
        sy_dma(lambda e: e.dma_start(out=R("wibc"), in_=wibc_in).then_inc(SM["LDS"], 16))
        ld_wibc = P.cnt["LDS"]
        P.wait("tensor", "LDG", ld_wo)
        P.wait("tensor", "LDG", ld_wob)
        P.wait("scalar", "LDS", ld_wibc)
        wi_guard = {}
        war_wo, war_out = {}, {}
        for tc in range(NTC):
            for fc in range(32):
                slot = fc % 4
                if tc * 32 + fc >= 4:
                    P.wait("gpsimd", "T", wi_guard[slot])
                P.raw("gpsimd", lambda e, fc=fc, slot=slot: e.dma_start(
                    out=R("wib%d" % slot), in_=wiT_in[:, ts(fc, 1024)]).then_inc(SM["LDW%d" % slot], 16))
                P.inc("LDW%d" % slot, 16)
                P.wait("tensor", "LDW%d" % slot, P.cnt["LDW%d" % slot])
                b = [0, 2][fc % 2]
                guard_bank(b)
                for dc in range(8):
                    t_op(lambda e, b=b, slot=slot, dc=dc, tc=tc, st=(dc == 0): e.matmul(
                        pb[b][:], Rs("wib%d" % slot, dc * 128, 128),
                        Rs("ln2", dc * SLOC + tc * TC, TC), start=st, stop=False))
                t_op(lambda e, b=b, fc=fc, tc=tc: e.matmul(
                    pb[b][:], Rs("bcatT", fc * 128, 128, (0, 32)),
                    Rs("z", tc * TC, TC, (0, 32)), start=False, stop=True))
                wi_guard[slot] = P.cnt["T"]
                P.wait("scalar", "T", P.cnt["T"])
                s_op(lambda e, b=b, fc=fc: e.activation(
                    Rs("inter", fc * TC, TC), pb[b][:], ACT.Relu, bias=Rs("wibc", fc, 1).opt()))
                consume_bank(b, sem="Sc")
            P.wait("tensor", "Sc", P.cnt["Sc"])
            for tt in range(TC // 128):
                sc = tc * (TC // 128) + tt
                xj = sc % 4
                if sc >= 4:
                    P.wait("sync", "V", war_wo[xj])
                sy_dma(lambda e, xj=xj, sc=sc: e.dma_start(out=R("x1r%d" % xj), in_=x1b[ts(sc, 128), :]).then_inc(SM["LDS"], 16))
                for dh in range(2):
                    b = [4, 6][(tt * 2 + dh) % 2]
                    guard_bank(b)
                    for fc in range(32):
                        t_op(lambda e, b=b, fc=fc, tt=tt, dh=dh, st=(fc == 0): e.matmul(
                            pb[b][:], Rs("inter", fc * TC + tt * 128, 128),
                            Rs("woT", fc * DM + dh * 512, 512), start=st, stop=False))
                    t_op(lambda e, b=b, dh=dh: e.matmul(
                        pb[b][:], R("ones1b", 1), Rs("wob", dh * 512, 512, (0, 1)), start=False, stop=True))
                    P.wait("vector", "T", P.cnt["T"])
                    P.wait("vector", "LDS", P.cnt["LDS"])
                    onm = "out%d" % (sc % 2)
                    if sc >= 2 and dh == 0:
                        P.wait("vector", "STQ", war_out[sc % 2])
                    v_op(lambda e, b=b, onm=onm, xj=xj, dh=dh: e.tensor_tensor(
                        Rs(onm, dh * 512, 512), Rs("x1r%d" % xj, dh * 512, 512), pb[b][:], OP.add))
                    consume_bank(b)
                war_wo[xj] = P.cnt["V"]
                P.wait("scalar", "V", P.cnt["V"])
                sc_dma(lambda e, onm="out%d" % (sc % 2), sc=sc: e.dma_start(
                    out=y_out[ts(sc, 128), :], in_=Rs(onm, 0, DM)).then_inc(SM["STQ"], 16))
                war_out[sc % 2] = P.cnt["STQ"]

        P.wait("scalar", "STQ", P.cnt["STQ"])
    except _StopBuild:
        pass

    with nc.Block() as block:
        def run(eng_name):
            def body(e):
                for st_ in P.steps[eng_name]:
                    if isinstance(st_, tuple):
                        e.wait_ge(sems[st_[1]], st_[2])
                    else:
                        st_(e)
            return body
        block.tensor(run("tensor"))
        block.vector(run("vector"))
        block.scalar(run("scalar"))
        block.gpsimd(run("gpsimd"))
        block.sync(run("sync"))

    _elide_ldweights(nc)
    es.close()
    return nc


_MM_FLDS = ("replication_resolution", "replication_shift_amnt", "replication_num_rows",
            "start_tensor_calc", "stop_tensor_calc", "is_transpose", "is_fmap_onezero",
            "is_weight_onezero", "ifmap_quant_offset", "weights_quant_offset", "tile_size",
            "tile_position", "perf_mode", "bass_skip_group_check", "name", "engine",
            "debug", "ins", "outs", "sync_info", "emission_index", "descendants")


def _elide_ldweights(nc):
    n = 0
    for fn in nc.m.functions:
        for bl in fn.blocks:
            if "_PE_" not in bl.name:
                continue
            il = bl.instructions
            prev = None
            for i, ins in enumerate(il):
                tn = type(ins).__name__
                if tn == "InstEventSemaphore":
                    continue
                if tn != "InstMatmult" or ins.is_transpose:
                    prev = None
                    continue
                try:
                    a = ins.ins[1]
                    if "bfloat16" not in str(a.dtype) or "bfloat16" not in str(ins.ins[0].dtype):
                        prev = None
                        continue
                    if a.ap[1][1] != ins.outs[0].ap[0][1]:
                        prev = None
                        continue
                    k = (a.memref, a.offset, str(a.ap), str(a.dtype), ins.tile_position)
                except Exception:
                    prev = None
                    continue
                if k == prev:
                    kw = {f: getattr(ins, f) for f in _MM_FLDS}
                    kw["ldweights"] = False
                    il[i] = mybir.InstMatmult(**kw)
                    n += 1
                else:
                    prev = k
    return n


_CACHE = {}


def _host_stage(inputs):
    f32 = np.float32
    x = np.asarray(inputs["L0_attn_hidden_states"], f32)[0]
    Wq = np.asarray(inputs["L0_attn_Wq"], f32)
    bq = np.asarray(inputs["L0_attn_bq"], f32)
    Wk = np.asarray(inputs["L0_attn_Wk"], f32)
    bk = np.asarray(inputs["L0_attn_bk"], f32)
    Wv = np.asarray(inputs["L0_attn_Wv"], f32)
    bv = np.asarray(inputs["L0_attn_bv"], f32)
    conv_w = np.asarray(inputs["L0_attn_conv_w"], f32)
    gW = np.asarray(inputs["L0_ffn_gate_W"], f32)
    gb = np.asarray(inputs["L0_ffn_gate_b"], f32)
    wiW = np.asarray(inputs["L0_ffn_wi_W"], f32)
    wib = np.asarray(inputs["L0_ffn_wi_b"], f32)
    woW = np.asarray(inputs["L0_ffn_wo_W"], f32)
    wob = np.asarray(inputs["L0_ffn_wo_b"], f32)
    lA = np.asarray(inputs["L0_ffn_lora_A"], f32)
    lB = np.asarray(inputs["L0_ffn_lora_B"], f32)

    xpad = np.zeros((S + 256, DM), f32)
    xpad[128:128 + S] = x

    wiT = np.ascontiguousarray(wiW.T)
    wiT_pm = np.ascontiguousarray(
        wiT.reshape(8, 128, 32, 128).transpose(1, 2, 0, 3)).reshape(128, 32 * 1024)
    woT = np.ascontiguousarray(woW.T)
    gateT = np.ascontiguousarray(gW.T)
    acatT = np.ascontiguousarray(lA.reshape(32, DM).T)
    bcatT = np.ascontiguousarray(lB.transpose(0, 2, 1).reshape(32, FF))

    w65 = conv_w[:, 0, :, 0]
    Tfull = np.zeros((H, 128, 64), f32)
    for p in range(64):
        Tfull[:, p:p + 65, p] = w65
    tconv = np.zeros((H, 3, 128, 64), f32)
    tconv[:, 0] = Tfull
    tconv[:, 1, 64:128] = Tfull[:, 0:64]
    tconv[:, 2, 0:64] = Tfull[:, 64:128]
    tconv_pm = np.ascontiguousarray(tconv.transpose(2, 0, 1, 3)).reshape(128, H * 3 * 64)

    eye = np.eye(128, dtype=f32)
    izf = np.zeros((128, 16 * H), f32)
    for h in range(H):
        izf[(h % 2) * 64:(h % 2) * 64 + 64, 16 * h + h] = 1.0
    izff = np.zeros((128, 256), f32)
    for h in range(H):
        izff[0:64, 16 * h + h] = 1.0
    ipair = np.zeros((16, 8, 128), f32)
    for pp in range(8):
        ipair[2 * pp, pp, 0:64] = 1.0
        ipair[2 * pp + 1, pp, 64:128] = 1.0
    ipair = ipair.reshape(16, 1024)
    im4 = np.zeros((E8, 32), f32)
    for e_ in range(E8):
        im4[e_, 4 * e_:4 * e_ + 4] = 1.0
    eyep = np.zeros((128, 1024), f32)
    for a in range(16):
        eyep[0:64, a * 64:(a + 1) * 64] = np.eye(64)
    bvb = np.broadcast_to(bv, (128, DM)).copy()
    bqc = np.ascontiguousarray(bq.reshape(8, 128).T)
    bkc = np.ascontiguousarray(bk.reshape(8, 128).T)
    wibc = np.ascontiguousarray(wib.reshape(32, 128).T)
    gb16 = np.tile(np.broadcast_to(gb, (128, 8)), (1, 16))
    p216 = np.tile((2.0 ** -np.arange(8, dtype=f32))[None, :], (128, 16))

    shared = dict(wq=Wq, wk=Wk, wv=Wv, wiT=wiT_pm, woT=woT, gateT=gateT,
                  acatT=acatT, bcatT=bcatT, tconv=tconv_pm, eye=eye, izf=izf,
                  izff=izff, ipair=ipair, im4=im4, i7=7.0 * eyep,
                  i15=15.0 * eyep, i13=13.0 * eyep, bvb=bvb, bqc=bqc, bkc=bkc,
                  wibc=wibc, gb16=gb16, p216=p216, wob=wob.reshape(1, DM))
    in_maps = []
    for c in range(NCORE):
        sl = xpad[c * SLOC: c * SLOC + TPAD]
        m = dict(shared)
        m["xT"] = np.ascontiguousarray(sl.T)
        m["xtm"] = np.ascontiguousarray(x[c * SLOC:(c + 1) * SLOC])
        m["bvlo"] = np.zeros_like(bvb) if c == 0 else bvb
        m["bvhi"] = np.zeros_like(bvb) if c == NCORE - 1 else bvb
        in_maps.append(m)
    return in_maps


def kernel(**inputs):
    in_maps = _host_stage(inputs)
    if "nc" not in _CACHE:
        _CACHE["nc"] = build_program()
    res = run_bass_kernel_spmd(_CACHE["nc"], in_maps, core_ids=list(range(NCORE)))
    out = np.concatenate([np.asarray(r["y"], np.float32) for r in res.results], axis=0)
    return out.reshape(1, S, DM)
